# version 01
# baseline (speedup 1.0000x reference)
import math
from contextlib import ExitStack

import numpy as np
import ml_dtypes

import concourse.bass as bass
import concourse.mybir as mybir
from concourse.bass_utils import run_bass_kernel_spmd

F32 = mybir.dt.float32
BF16 = mybir.dt.bfloat16
I32 = mybir.dt.int32
AF = mybir.ActivationFunctionType
ALU = mybir.AluOpType
AX = mybir.AxisListType
BF = ml_dtypes.bfloat16

NCORES = 8
NB = 2
S = 8192
D = 1024
TOK = 2048
PI = math.pi
TWO_PI = 2.0 * math.pi

ENGINES = ("tensor", "vector", "scalar", "sync", "gpsimd")
DMA_ENGINES = {"sync", "gpsimd"}


class Prog:

    def __init__(self, name):
        self.nc = bass.Bass("TRN2", target_bir_lowering=False)
        self.ops = []
        self.st = ExitStack()
        self._names = set()
        self.din = {}
        self.dout = {}

    def inp(self, name, shape, dt=F32):
        ap = self.nc.dram_tensor(name, shape, dt, kind="ExternalInput").ap()
        self.din[name] = ap
        return ap

    def out(self, name, shape, dt=F32):
        ap = self.nc.dram_tensor(name, shape, dt, kind="ExternalOutput").ap()
        self.dout[name] = ap
        return ap

    def sb(self, name, shape, dt=F32):
        assert name not in self._names, name
        self._names.add(name)
        return self.st.enter_context(self.nc.sbuf_tensor(name, shape, dt))

    def ps(self, name, shape=None, dt=F32):
        assert name not in self._names, name
        self._names.add(name)
        return self.st.enter_context(
            self.nc.psum_tensor(name, shape or [128, 512], dt))

    @staticmethod
    def _key(x):
        t = getattr(x, "tensor", x)
        return id(t)

    def rec(self, engine, emit, ins, outs, is_dma=False):
        self.ops.append((engine, emit,
                         [self._key(a) for a in ins if a is not None],
                         [self._key(a) for a in outs if a is not None],
                         is_dma))

    def dma(self, q, dst, src):
        self.rec(q, lambda e: e.dma_start(out=dst, in_=src), [src], [dst],
                 is_dma=True)

    def mm(self, psum, lhsTs, rhss, extra_ins=()):
        if not isinstance(lhsTs, (list, tuple)):
            lhsTs, rhss = [lhsTs], [rhss]
        n = len(lhsTs)

        def emit(t):
            for i in range(n):
                inst = t.matmul(psum, lhsTs[i], rhss[i],
                                start=(i == 0), stop=(i == n - 1))
            return inst
        self.rec("tensor", emit, list(lhsTs) + list(rhss) + list(extra_ins),
                 [psum])

    def mm_more(self, psum, lhsTs, rhss, start, stop):
        if not isinstance(lhsTs, (list, tuple)):
            lhsTs, rhss = [lhsTs], [rhss]
        n = len(lhsTs)

        def emit(t):
            for i in range(n):
                inst = t.matmul(psum, lhsTs[i], rhss[i],
                                start=(start and i == 0),
                                stop=(stop and i == n - 1))
            return inst
        self.rec("tensor", emit, list(lhsTs) + list(rhss), [psum])

    def transpose(self, psum_out, in_ap, ident):
        self.rec("tensor", lambda t: t.transpose(psum_out, in_ap, ident),
                 [in_ap, ident], [psum_out])

    def act(self, out, in_, fn, bias=0.0, scale=1.0, accum_out=None):
        b = bias if not isinstance(bias, (int, float)) else None
        s_ = scale if not isinstance(scale, (int, float)) else None
        self.rec("scalar",
                 lambda s: s.activation(out, in_, fn, bias=bias, scale=scale,
                                        accum_out=accum_out),
                 [in_, b, s_], [out, accum_out])

    def tt(self, out, a, b, op, eng="vector"):
        self.rec(eng, lambda v: v.tensor_tensor(out, a, b, op), [a, b], [out])

    def ts(self, out, a, s1, s2, op0, op1=None, eng="vector", accum_out=None):
        if op1 is None:
            self.rec(eng, lambda v: v.tensor_scalar(out, a, s1, None, op0,
                                                    accum_out=accum_out),
                     [a, s1 if not np.isscalar(s1) else None], [out, accum_out])
        else:
            self.rec(eng, lambda v: v.tensor_scalar(out, a, s1, s2, op0, op1,
                                                    accum_out=accum_out),
                     [a, s1 if not np.isscalar(s1) else None,
                      s2 if not np.isscalar(s2) else None], [out, accum_out])

    def stt(self, out, in0, scalar, in1, op0, op1, eng="vector"):
        self.rec(eng,
                 lambda v: v.scalar_tensor_tensor(out, in0, scalar, in1,
                                                  op0, op1),
                 [in0, in1, scalar if not np.isscalar(scalar) else None],
                 [out])

    def copy(self, out, in_, eng="vector"):
        if eng == "scalar":
            self.rec(eng, lambda s: s.activation(out, in_, AF.Copy),
                     [in_], [out])
        else:
            self.rec(eng, lambda v: v.tensor_copy(out, in_), [in_], [out])

    def reduce(self, out, in_, op, absval=False, negate=False, eng="vector"):
        self.rec(eng,
                 lambda v: v.tensor_reduce(out, in_, AX.X, op,
                                           apply_absolute_value=absval,
                                           negate=negate),
                 [in_], [out])

    def scan(self, out, d0, d1, init, op0, op1):
        self.rec("vector",
                 lambda v: v.tensor_tensor_scan(out, d0, d1, init, op0, op1),
                 [d0, d1, init if not np.isscalar(init) else None], [out])

    def recip(self, out, in_):
        self.rec("scalar", lambda s: s.activation(out, in_, AF.Ln),
                 [in_], [out])
        cm1 = self._neg1()
        np_ = out.shape[0]
        self.rec("scalar",
                 lambda s: s.activation(out, out, AF.Exp,
                                        scale=cm1[0:np_, 0:1]),
                 [out, cm1], [out])

    def _neg1(self):
        if not hasattr(self, "_cneg1"):
            self._cneg1 = self.sb("_cneg1", [128, 1], F32)
            self.memset(self._cneg1[:, :], -1.0)
        return self._cneg1

    def memset(self, ap, c, eng="vector"):
        self.rec(eng, lambda v: v.memset(ap, c), [], [ap])

    def select(self, out, mask, tv, fv):
        self.rec("vector", lambda v: v.select(out, mask, tv, fv),
                 [mask, tv, fv], [out])

    def constcol(self, name, value, rows=128):
        t = self.sb(name, [rows, 1], F32)
        self.memset(t[:, :], value)
        return t

    def finalize(self):
        n = len(self.ops)
        last_write = {}
        readers = {}
        deps = [set() for _ in range(n)]
        for i, (eng, emit, ins, outs, is_dma) in enumerate(self.ops):
            for k in ins:
                if k in last_write:
                    deps[i].add(last_write[k])
            for k in outs:
                if k in last_write:
                    deps[i].add(last_write[k])
                for r in readers.get(k, ()):
                    deps[i].add(r)
            for k in ins:
                readers.setdefault(k, []).append(i)
            for k in outs:
                last_write[k] = i
                readers[k] = []

        engs = [o[0] for o in self.ops]
        isdma = [o[4] for o in self.ops]
        cnt = {}
        run_cnt = {e: 0 for e in ENGINES}
        for i in range(n):
            run_cnt[engs[i]] += 16 if isdma[i] else 1
            cnt[i] = run_cnt[engs[i]]
        need_wait = [[] for _ in range(n)]
        for i in range(n):
            for j in deps[i]:
                if (engs[j] == engs[i] == "tensor"
                        and not isdma[j] and not isdma[i]):
                    continue
                need_wait[i].append((engs[j], cnt[j]))

        sems = {e: self.st.enter_context(self.nc.semaphore("sem_" + e))
                for e in ENGINES}

        with self.nc.Block() as block:
            def run_engine(eng_name, eng):
                hi = {e: 0 for e in ENGINES}
                for i in range(n):
                    if engs[i] != eng_name:
                        continue
                    for (we, wv) in need_wait[i]:
                        if wv > hi[we]:
                            eng.wait_ge(sems[we], wv)
                            hi[we] = wv
                    inst = self.ops[i][1](eng)
                    inst.then_inc(sems[eng_name], 16 if isdma[i] else 1)
                for e in ENGINES:
                    if run_cnt[e] > hi[e] and e != eng_name:
                        eng.wait_ge(sems[e], run_cnt[e])

            @block.tensor
            def _(t):
                run_engine("tensor", t)

            @block.vector
            def _(v):
                run_engine("vector", v)

            @block.scalar
            def _(s):
                run_engine("scalar", s)

            @block.sync
            def _(s):
                run_engine("sync", s)

            @block.gpsimd
            def _(g):
                run_engine("gpsimd", g)

        self.st.close()
        return self.nc


def col_splits(n, maxn=512):
    k = (n + maxn - 1) // maxn
    base, rem = divmod(n, k)
    out = []
    i = 0
    for j in range(k):
        w = base + (1 if j < rem else 0)
        out.append((i, i + w))
        i += w
    return out


def _t(a):
    return np.ascontiguousarray(np.asarray(a, np.float32).T)


def _f(a):
    return np.ascontiguousarray(np.asarray(a, np.float32))


def _col(a):
    return np.ascontiguousarray(np.asarray(a, np.float32).reshape(-1, 1))


_BUILD_CACHE = {}


def get_nc(key, fn):
    if key not in _BUILD_CACHE:
        _BUILD_CACHE[key] = fn()
    return _BUILD_CACHE[key]


def run8(nc, in_maps):
    return run_bass_kernel_spmd(nc, in_maps, core_ids=list(range(NCORES))).results



def feat_stats(p, X, nch, cols, ones_col, sq, psA, psB, row_sum, row_ssq):
    for (a, b) in col_splits(cols, 512):
        w = b - a
        p.mm(psA[0:1, 0:w], [ones_col[:, 0:1]] * nch,
             [X[:, k * cols + a: k * cols + b] for k in range(nch)])
        for k in range(nch):
            p.act(sq[k % 2][:, 0:w], X[:, k * cols + a: k * cols + b],
                  AF.Square)
            p.mm_more(psB[0:1, 0:w], ones_col[:, 0:1], sq[k % 2][:, 0:w],
                      start=(k == 0), stop=(k == nch - 1))
        p.copy(row_sum[0:1, a:b], psA[0:1, 0:w])
        p.copy(row_ssq[0:1, a:b], psB[0:1, 0:w])


def ln_rows(p, cols, row_sum, row_ssq, row_rs, row_mr, d=D, eps=1e-5,
            rms=False):
    c = slice(0, cols)
    if not rms:
        p.ts(row_mr[0:1, c], row_sum[0:1, c], 1.0 / d, None, ALU.mult)
        p.tt(row_rs[0:1, c], row_mr[0:1, c], row_mr[0:1, c], ALU.mult)
        p.stt(row_rs[0:1, c], row_ssq[0:1, c], 1.0 / d, row_rs[0:1, c],
              ALU.mult, ALU.subtract)
        p.ts(row_rs[0:1, c], row_rs[0:1, c], eps, None, ALU.add)
    else:
        p.ts(row_rs[0:1, c], row_ssq[0:1, c], 1.0 / d, eps, ALU.mult, ALU.add)
    p.act(row_sum[0:1, c], row_rs[0:1, c], AF.Sqrt)
    p.recip(row_rs[0:1, c], row_sum[0:1, c])
    if not rms:
        p.tt(row_mr[0:1, c], row_mr[0:1, c], row_rs[0:1, c], ALU.mult)
        p.ts(row_mr[0:1, c], row_mr[0:1, c], -1.0, None, ALU.mult)


def apply_ln(p, Xin, Xout, nch, cols, ones_row, row_rs, row_mr, PS0, PS1,
             gain_cols=None, bias_cols=None):
    for (a, b) in col_splits(cols, 512):
        w = b - a
        p.mm(PS0[:, 0:w], ones_row[0:1, :], row_rs[0:1, a:b])
        if row_mr is not None:
            p.mm(PS1[:, 0:w], ones_row[0:1, :], row_mr[0:1, a:b])
        for k in range(nch):
            sl = slice(k * cols + a, k * cols + b)
            p.tt(Xout[:, sl], Xin[:, sl], PS0[:, 0:w], ALU.mult)
            if row_mr is not None:
                p.tt(Xout[:, sl], Xout[:, sl], PS1[:, 0:w], ALU.add)
            if gain_cols is not None:
                p.ts(Xout[:, sl], Xout[:, sl], gain_cols[:, k:k + 1], None,
                     ALU.mult)
            if bias_cols is not None:
                p.ts(Xout[:, sl], Xout[:, sl], bias_cols[:, k:k + 1], None,
                     ALU.add)


def proj(p, W, X, OUT, psums, nk, nout, cols, xcols=None, bias=None,
         act=None, scale=1.0, out_cols=None, obase=0, wout=None):
    xcols = xcols if xcols is not None else cols
    out_cols = out_cols if out_cols is not None else cols
    wout = wout if wout is not None else nout * 128
    for o in range(nout):
        for (a, b) in col_splits(cols, 512):
            w = b - a
            ps = psums[o % len(psums)]
            p.mm(ps[:, 0:w],
                 [W[:, k * wout + o * 128: k * wout + o * 128 + 128]
                  for k in range(nk)],
                 [X[:, k * xcols + a: k * xcols + b] for k in range(nk)])
            sl = slice(o * out_cols + obase + a, o * out_cols + obase + b)
            fn = act or (AF.Identity if bias is not None else AF.Copy)
            p.act(OUT[:, sl], ps[:, 0:w], fn,
                  bias=(bias[:, o:o + 1] if bias is not None else 0.0),
                  scale=scale)



CH = 512
NCH_T = TOK // CH
CB = CH + 1


def build_l1():
    p = Prog("l1")
    for name, shape in [
        ("xT", [D, TOK + 1]), ("ppT", [D, TOK + 1]),
        ("Wpv", [D, D]), ("Wk", [D, D]), ("Wkv", [D, D]),
        ("Wg1", [2 * D, D]), ("Wb1", [D, 512]),
        ("Wg2", [D, 1]), ("Wb2", [512, 2]), ("wqkvT", [D, 24]),
        ("bpv", [128, 8]), ("bk", [128, 8]), ("bkv", [128, 8]),
        ("bg1", [128, 8]), ("bb1", [128, 4]),
        ("bg2", [1, 1]), ("bb2a", [1, 1]), ("bb2b", [1, 1]),
        ("wg_sh", [352, D]), ("wu_sh", [352, D]), ("wd_sh", [128, 2816]),
        ("idf32", [128, 128]), ("fmask", [128, 1]),
    ]:
        p.inp(name, shape)
    for name, shape, dt in [
        ("xnT", [D, TOK], F32),
        ("pos_cT", [D, TOK], BF16), ("pos_sT", [D, TOK], BF16),
        ("kcT", [D, TOK], BF16), ("ksT", [D, TOK], BF16),
        ("a_cT", [D, TOK], BF16), ("a_sT", [D, TOK], BF16),
        ("m_cT", [D, TOK], BF16), ("m_sT", [D, TOK], BF16),
        ("vg", [1, TOK], F32), ("bwr", [2, TOK], F32),
        ("wr0", [8, TOK], F32),
        ("wgq_sh", [352, D], BF16), ("wuq_sh", [352, D], BF16),
        ("wdq_sh", [128, 2816], BF16), ("dbg", [128, 128], F32),
    ]:
        p.out(name, shape, dt)

    din, dout = p.din, p.dout
    wg1 = p.sb("wg1", [128, 16 * D], BF16)
    wb1 = p.sb("wb1", [128, 8 * 512], BF16)
    wg2 = p.sb("wg2", [128, 8 * 1], BF16)
    wb2 = p.sb("wb2", [128, 4 * 2], BF16)
    wqkv = p.sb("wqkv", [128, 8 * 24], F32)
    WA = p.sb("WA", [128, 8 * D], BF16)
    for nm, dst, kch, ow in [("Wg1", wg1, 16, D), ("Wb1", wb1, 8, 512),
                             ("Wg2", wg2, 8, 1), ("Wb2", wb2, 4, 2),
                             ("wqkvT", wqkv, 8, 24)]:
        for k in range(kch):
            p.dma("gpsimd", dst[:, k * ow:(k + 1) * ow],
                  din[nm][k * 128:(k + 1) * 128, :])
    for k in range(8):
        p.dma("sync", wqkv[:, k * 24:(k + 1) * 24],
              din["wqkvT"][k * 128:(k + 1) * 128, :])
    bias_s = {}
    for nm in ["bpv", "bk", "bkv", "bg1", "bb1", "bg2", "bb2a", "bb2b"]:
        shp = {"bb1": [128, 4], "bg2": [1, 1], "bb2a": [1, 1],
               "bb2b": [1, 1]}.get(nm, [128, 8])
        bias_s[nm] = p.sb(nm + "_s", shp, F32)
        p.dma("sync", bias_s[nm][:, :], din[nm])
    idf = p.sb("idf", [128, 128], F32)
    p.dma("sync", idf[:, :], din["idf32"])
    fmask_s = p.sb("fmask_s", [128, 1], F32)
    p.dma("sync", fmask_s[:, :], din["fmask"])
    ones_col = p.sb("ones_col", [128, 1], F32)
    ones_row = p.sb("ones_row", [1, 128], F32)
    p.memset(ones_col[:, :], 1.0)
    p.memset(ones_row[:, :], 1.0)
    C_PI2 = p.constcol("C_PI2", PI / 2)
    C_PI = p.constcol("C_PI", PI)

    X = p.sb("X", [128, 8 * CB], F32)
    XNf = p.sb("XNf", [128, 8 * CB], F32)
    XNb = p.sb("XNb", [128, 8 * CB], BF16)
    PC = p.sb("PC", [128, 8 * CB], F32)
    PS = p.sb("PS", [128, 8 * CB], F32)
    SC = p.sb("SC", [128, 8 * CB], F32)
    VV = p.sb("VV", [128, 8 * CB], F32)
    GH = p.sb("GH", [128, 8 * CH], BF16)
    BH = p.sb("BH", [128, 4 * CH], BF16)
    sq0_ = p.sb("sq0", [128, 512], F32)
    sq = [sq0_, sq0_]
    row_sum = p.sb("row_sum", [1, CB], F32)
    row_ssq = p.sb("row_ssq", [1, CB], F32)
    row_rs = p.sb("row_rs", [1, CB], F32)
    row_mr = p.sb("row_mr", [1, CB], F32)
    vg_row = p.sb("vg_row", [1, CB], F32)
    bw_a = p.sb("bw_a", [1, CB], F32)
    bw_b = p.sb("bw_b", [1, CB], F32)
    MIX = p.sb("MIX", [128, 24], F32)
    RAT = p.sb("RAT", [128, 64], F32)
    RSC = p.sb("RSC", [128, 64], F32)
    RM = p.sb("RM", [128, 8], F32)
    RL2 = p.sb("RL2", [128, 8], F32)
    RM2 = p.sb("RM2", [128, 8], F32)
    RV = p.sb("RV", [128, 10], F32)
    WRT = p.sb("WRT", [128, 8], F32)
    RRC = p.sb("RRC", [128, 8], F32)
    WRO = p.sb("WRO", [8, 128], F32)
    QA = XNf[:, 0:1024]
    QC1 = XNf[:, 1024:2048]
    QC2 = XNf[:, 2048:3072]
    QS = XNf[:, 3072:3136]
    QR = XNf[:, 3200:3264]
    QO = XNb[:, 0:1024]

    psA = p.ps("psA", [1, 512])
    psB = p.ps("psB", [1, 512])
    P0 = p.ps("P0")
    P1 = p.ps("P1")
    P2 = p.ps("P2")
    P3 = p.ps("P3")
    PR = p.ps("PR", [128, 128])

    def fp4_qd(A, Srec, R, C1, C2, O, rows, colw):
        nblk = colw // 16
        rs, cw, nb = slice(0, rows), slice(0, colw), slice(0, nblk)
        p.reduce(Srec[rs, nb], A[rs, cw].rearrange("p (g b) -> p g b", b=16),
                 ALU.max, absval=True)
        p.ts(Srec[rs, nb], Srec[rs, nb], 1.0 / 6.0, 1e-12, ALU.mult, ALU.max)
        p.recip(R[rs, nb], Srec[rs, nb])
        p.act(C1[rs, cw], A[rs, cw], AF.Abs)
        rb = R[rs, nb].rearrange("p (g one) -> p g one", one=1) \
            .broadcast_to([rows, nblk, 16])
        p.tt(C1[rs, cw].rearrange("p (g b) -> p g b", b=16),
             C1[rs, cw].rearrange("p (g b) -> p g b", b=16), rb, ALU.mult)
        p.ts(C2[rs, cw], C1[rs, cw], 0.25, None, ALU.is_gt)
        for m in (0.75, 1.25, 1.75):
            p.stt(C2[rs, cw], C1[rs, cw], m, C2[rs, cw], ALU.is_gt, ALU.add)
        p.ts(C2[rs, cw], C2[rs, cw], 0.5, None, ALU.mult)
        for m in (2.5, 3.5, 5.0, 5.0):
            p.stt(C2[rs, cw], C1[rs, cw], m, C2[rs, cw], ALU.is_gt, ALU.add)
        sb_ = Srec[rs, nb].rearrange("p (g one) -> p g one", one=1) \
            .broadcast_to([rows, nblk, 16])
        p.tt(C2[rs, cw].rearrange("p (g b) -> p g b", b=16),
             C2[rs, cw].rearrange("p (g b) -> p g b", b=16), sb_, ALU.mult)
        p.act(C1[rs, cw], A[rs, cw], AF.Sign)
        p.tt(O[rs, cw], C2[rs, cw], C1[rs, cw], ALU.mult)

    for (srcn, dstn, nrows, cw) in [("wg_sh", "wgq_sh", 352, 1024),
                                    ("wu_sh", "wuq_sh", 352, 1024),
                                    ("wd_sh", "wdq_sh", 128, 2816)]:
        pieces = [(0, 1024)] if cw == 1024 else \
            [(i * 704, (i + 1) * 704) for i in range(4)]
        r0 = 0
        while r0 < nrows:
            r1 = min(r0 + 128, nrows)
            rr = r1 - r0
            for (c0, c1) in pieces:
                pw = c1 - c0
                p.dma("sync", QA[0:rr, 0:pw], din[srcn][r0:r1, c0:c1])
                fp4_qd(QA, QS, QR, QC1, QC2, QO, rr, pw)
                p.dma("sync", dout[dstn][r0:r1, c0:c1], QO[0:rr, 0:pw])
            r0 = r1

    n8 = slice(0, 8 * CB)
    for c in range(NCH_T):
        xa = c * CH

        for k in range(8):
            p.dma("sync", X[:, k * CB:(k + 1) * CB],
                  din["xT"][k * 128:(k + 1) * 128, xa:xa + CB])
            p.dma("sync", PC[:, k * CB:(k + 1) * CB],
                  din["ppT"][k * 128:(k + 1) * 128, xa:xa + CB])

        feat_stats(p, X, 8, CB, ones_col, sq, psA, psB, row_sum, row_ssq)
        ln_rows(p, CB, row_sum, row_ssq, row_rs, row_mr)
        apply_ln(p, X, XNf, 8, CB, ones_row, row_rs, row_mr, P0, P1)
        p.copy(XNb[:, :], XNf[:, :], eng="scalar")
        for k in range(8):
            p.dma("sync", dout["xnT"][k * 128:(k + 1) * 128, xa:xa + CH],
                  XNf[:, k * CB + 1:(k + 1) * CB])

        p.ts(SC[:, n8], PC[:, n8], 1.0 / TWO_PI, None, ALU.mult)
        p.copy(VV[:, n8].bitcast(I32), SC[:, n8])
        p.copy(SC[:, n8], VV[:, n8].bitcast(I32))
        p.stt(PS[:, n8], SC[:, n8], -TWO_PI, PC[:, n8], ALU.mult, ALU.add)
        p.ts(SC[:, n8], PS[:, n8], PI / 2, -TWO_PI, ALU.is_gt, ALU.mult)
        p.tt(SC[:, n8], SC[:, n8], PS[:, n8], ALU.add)
        p.act(PC[:, n8], SC[:, n8], AF.Sin, bias=C_PI2[:, 0:1])
        p.act(PS[:, n8], PS[:, n8], AF.Sin)
        for k in range(8):
            p.dma("gpsimd", dout["pos_cT"][k * 128:(k + 1) * 128, xa:xa + CH],
                  PC[:, k * CB + 1:(k + 1) * CB])
            p.dma("gpsimd", dout["pos_sT"][k * 128:(k + 1) * 128, xa:xa + CH],
                  PS[:, k * CB + 1:(k + 1) * CB])

        for k in range(8):
            p.dma("gpsimd", WA[:, k * D:(k + 1) * D],
                  din["Wpv"][k * 128:(k + 1) * 128, :])
        proj(p, WA, XNb, VV, [P0, P1], 8, 8, CB, bias=bias_s["bpv"])
        for k in range(8):
            sl = slice(k * CB, (k + 1) * CB)
            p.tt(SC[:, sl], PC[:, sl], VV[:, sl], ALU.mult)
        for k in range(8):
            p.dma("gpsimd", dout["a_cT"][k * 128:(k + 1) * 128, xa:xa + CH],
                  SC[:, k * CB + 1:(k + 1) * CB])
        for k in range(8):
            sl = slice(k * CB, (k + 1) * CB)
            p.tt(PC[:, sl], PS[:, sl], VV[:, sl], ALU.mult)
        for k in range(8):
            p.dma("gpsimd", dout["a_sT"][k * 128:(k + 1) * 128, xa:xa + CH],
                  PC[:, k * CB + 1:(k + 1) * CB])

        for k in range(8):
            p.dma("gpsimd", WA[:, k * D:(k + 1) * D],
                  din["Wkv"][k * 128:(k + 1) * 128, :])
        proj(p, WA, XNb, VV, [P2, P3], 8, 8, CB, bias=bias_s["bkv"])

        for k in range(8):
            p.dma("gpsimd", WA[:, k * D:(k + 1) * D],
                  din["Wk"][k * 128:(k + 1) * 128, :])
        proj(p, WA, XNb, SC, [P0, P1], 8, 8, CB, bias=bias_s["bk"],
             act=AF.Tanh)
        p.act(PS[:, n8], SC[:, n8], AF.Sin, scale=C_PI[:, 0:1])
        p.ts(PC[:, n8], SC[:, n8], 0.5, -TWO_PI, ALU.is_gt, ALU.mult)
        p.stt(PC[:, n8], SC[:, n8], PI, PC[:, n8], ALU.mult, ALU.add)
        p.act(PC[:, n8], PC[:, n8], AF.Sin, bias=C_PI2[:, 0:1])
        for k in range(8):
            p.dma("gpsimd", dout["kcT"][k * 128:(k + 1) * 128, xa:xa + CH],
                  PC[:, k * CB + 1:(k + 1) * CB])
            p.dma("gpsimd", dout["ksT"][k * 128:(k + 1) * 128, xa:xa + CH],
                  PS[:, k * CB + 1:(k + 1) * CB])

        for o in range(8):
            ps = [P0, P1][o % 2]
            p.mm(ps[:, 0:CH],
                 [wg1[:, k * D + o * 128: k * D + o * 128 + 128]
                  for k in range(16)],
                 [XNb[:, k * CB + 1:(k + 1) * CB] for k in range(8)] +
                 [XNb[:, k * CB + 0:(k + 1) * CB - 1] for k in range(8)])
            p.act(GH[:, o * CH:(o + 1) * CH], ps[:, 0:CH], AF.Gelu,
                  bias=bias_s["bg1"][:, o:o + 1])
        p.mm(psA[0:1, 0:CH],
             [wg2[:, k:k + 1] for k in range(8)],
             [GH[:, k * CH:(k + 1) * CH] for k in range(8)])
        p.act(vg_row[0:1, 1:CB], psA[0:1, 0:CH], AF.Sigmoid,
              bias=bias_s["bg2"][0:1, 0:1])
        p.dma("sync", dout["vg"][0:1, xa:xa + CH], vg_row[0:1, 1:CB])

        for o in range(4):
            ps = [P2, P3][o % 2]
            p.mm(ps[:, 0:CH],
                 [wb1[:, k * 512 + o * 128: k * 512 + o * 128 + 128]
                  for k in range(8)],
                 [XNb[:, k * CB + 1:(k + 1) * CB] for k in range(8)])
            p.act(BH[:, o * CH:(o + 1) * CH], ps[:, 0:CH], AF.Gelu,
                  bias=bias_s["bb1"][:, o:o + 1])
        p.mm(psA[0:1, 0:CH],
             [wb2[:, 2 * k:2 * k + 1] for k in range(4)],
             [BH[:, k * CH:(k + 1) * CH] for k in range(4)])
        p.mm(psB[0:1, 0:CH],
             [wb2[:, 2 * k + 1:2 * k + 2] for k in range(4)],
             [BH[:, k * CH:(k + 1) * CH] for k in range(4)])
        p.act(bw_a[0:1, 1:CB], psA[0:1, 0:CH], AF.Identity,
              bias=bias_s["bb2a"][0:1, 0:1])
        p.act(bw_b[0:1, 1:CB], psB[0:1, 0:CH], AF.Identity,
              bias=bias_s["bb2b"][0:1, 0:1])
        p.tt(bw_a[0:1, 1:CB], bw_a[0:1, 1:CB], bw_b[0:1, 1:CB], ALU.subtract)
        p.act(bw_a[0:1, 1:CB], bw_a[0:1, 1:CB], AF.Sigmoid)
        p.ts(bw_b[0:1, 1:CB], bw_a[0:1, 1:CB], -1.0, 1.0, ALU.mult, ALU.add)
        p.dma("sync", dout["bwr"][0:1, xa:xa + CH], bw_a[0:1, 1:CB])
        p.dma("sync", dout["bwr"][1:2, xa:xa + CH], bw_b[0:1, 1:CB])

        p.mm(P2[:, 0:CH], ones_row[0:1, :], vg_row[0:1, 1:CB])
        for k in range(8):
            slt = slice(k * CB + 1, (k + 1) * CB)
            sls = slice(k * CB + 0, (k + 1) * CB - 1)
            p.tt(SC[:, slt], PC[:, sls], VV[:, slt], ALU.mult)
            p.tt(SC[:, slt], SC[:, slt], P2[:, 0:CH], ALU.mult)
            p.tt(X[:, slt], PS[:, sls], VV[:, slt], ALU.mult)
            p.tt(X[:, slt], X[:, slt], P2[:, 0:CH], ALU.mult)
        if c == 0:
            for k in range(8):
                p.ts(SC[:, k * CB + 1:k * CB + 2],
                     SC[:, k * CB + 1:k * CB + 2],
                     fmask_s[:, 0:1], None, ALU.mult)
                p.ts(X[:, k * CB + 1:k * CB + 2],
                     X[:, k * CB + 1:k * CB + 2],
                     fmask_s[:, 0:1], None, ALU.mult)
        for k in range(8):
            p.dma("gpsimd", dout["m_cT"][k * 128:(k + 1) * 128, xa:xa + CH],
                  SC[:, k * CB + 1:(k + 1) * CB])
            p.dma("gpsimd", dout["m_sT"][k * 128:(k + 1) * 128, xa:xa + CH],
                  X[:, k * CB + 1:(k + 1) * CB])

        for i in range(CH // 128):
            tbase = 1 + i * 128
            p.mm(PR[:, 0:24],
                 [XNf[:, k * CB + tbase: k * CB + tbase + 128]
                  for k in range(8)],
                 [wqkv[:, k * 24:(k + 1) * 24] for k in range(8)])
            p.copy(MIX[:, 0:24], PR[:, 0:24])
            for e in range(8):
                p.tt(RAT[:, 8 * e:8 * e + 8],
                     MIX[:, e:e + 1].broadcast_to([128, 8]),
                     MIX[:, 8:16], ALU.mult)
            g3 = RAT[:, 0:64].rearrange("p (e j) -> p e j", j=8)
            p.reduce(RM[:, 0:8], g3, ALU.max)
            p.tt(RSC[:, 0:64].rearrange("p (e j) -> p e j", j=8), g3,
                 RM[:, 0:8].rearrange("p (e one) -> p e one", one=1)
                 .broadcast_to([128, 8, 8]), ALU.subtract)
            p.act(RSC[:, 0:64], RSC[:, 0:64], AF.Exp)
            s3 = RSC[:, 0:64].rearrange("p (e j) -> p e j", j=8)
            p.reduce(RM[:, 0:8], s3, ALU.add)
            p.recip(RRC[:, 0:8], RM[:, 0:8])
            p.tt(RSC[:, 0:64].rearrange("p (e j) -> p e j", j=8), s3,
                 RRC[:, 0:8].rearrange("p (e one) -> p e one", one=1)
                 .broadcast_to([128, 8, 8]), ALU.mult)
            p.tt(RSC[:, 0:64].rearrange("p (e j) -> p e j", j=8), s3,
                 MIX[:, 16:24].rearrange("p (one j) -> p one j", one=1)
                 .broadcast_to([128, 8, 8]), ALU.mult)
            p.reduce(RM[:, 0:8], s3, ALU.add)
            p.reduce(RV[:, 0:1], RM[:, 0:8], ALU.max)
            p.ts(RL2[:, 0:8], RM[:, 0:8], RV[:, 0:1], None, ALU.is_equal)
            p.stt(RM2[:, 0:8], RL2[:, 0:8], -1e30, RM[:, 0:8],
                  ALU.mult, ALU.add)
            p.reduce(RV[:, 1:2], RM2[:, 0:8], ALU.max)
            p.ts(WRT[:, 0:8], RM2[:, 0:8], RV[:, 1:2], None, ALU.is_equal)
            p.tt(RV[:, 2:3], RV[:, 1:2], RV[:, 0:1], ALU.subtract)
            p.act(RV[:, 3:4], RV[:, 2:3], AF.Exp)
            p.ts(RV[:, 4:5], RV[:, 3:4], 1.0, None, ALU.add)
            p.recip(RV[:, 6:7], RV[:, 4:5])
            p.tt(RV[:, 5:6], RV[:, 3:4], RV[:, 6:7], ALU.mult)
            p.ts(RL2[:, 0:8], RL2[:, 0:8], RV[:, 6:7], None, ALU.mult)
            p.ts(WRT[:, 0:8], WRT[:, 0:8], RV[:, 5:6], None, ALU.mult)
            p.tt(WRT[:, 0:8], WRT[:, 0:8], RL2[:, 0:8], ALU.add)
            if c == 0 and i == 0:
                p.dma("sync", dout["dbg"][:, 0:24], MIX[:, 0:24])
                p.dma("sync", dout["dbg"][:, 24:88], RSC[:, 0:64])
                p.dma("sync", dout["dbg"][:, 88:96], RM[:, 0:8])
                p.dma("sync", dout["dbg"][:, 96:104], RL2[:, 0:8])
                p.dma("sync", dout["dbg"][:, 104:112], WRT[:, 0:8])
                p.dma("sync", dout["dbg"][:, 112:118], RV[:, 0:6])
            p.transpose(PR[0:8, 0:128], WRT[:, 0:8], idf[:, :])
            p.copy(WRO[0:8, 0:128], PR[0:8, 0:128])
            p.dma("sync", dout["wr0"][0:8, xa + i * 128: xa + (i + 1) * 128],
                  WRO[0:8, 0:128])

    return p.finalize()


def host_l1(inputs, state):
    nc = get_nc("l1", build_l1)
    x = np.asarray(inputs["L0_attn_x"], np.float32)
    pp = np.asarray(inputs["L0_attn_pos_phases"], np.float32)[:S]
    xf = x.reshape(NB * S, D)
    in_maps = []
    idf32 = np.eye(128, dtype=np.float32)
    Wg1 = _f(inputs["L0_attn_Wg1"])
    com = dict(
        Wpv=_f(inputs["L0_attn_Wpv"]), Wk=_f(inputs["L0_attn_Wk"]),
        Wkv=_f(inputs["L0_attn_Wkv"]), Wg1=Wg1,
        Wb1=_f(inputs["L0_attn_Wb1"]), Wg2=_f(inputs["L0_attn_Wg2"]),
        Wb2=_f(inputs["L0_attn_Wb2"]), wqkvT=_t(inputs["L0_ffn_wqkv"]),
        bpv=_f(inputs["L0_attn_bpv"]).reshape(8, 128).T.copy(),
        bk=_f(inputs["L0_attn_bk"]).reshape(8, 128).T.copy(),
        bkv=_f(inputs["L0_attn_bkv"]).reshape(8, 128).T.copy(),
        bg1=_f(inputs["L0_attn_bg1"]).reshape(8, 128).T.copy(),
        bb1=_f(inputs["L0_attn_bb1"]).reshape(4, 128).T.copy(),
        bg2=_f(inputs["L0_attn_bg2"]).reshape(1, 1),
        bb2a=_f(inputs["L0_attn_bb2"])[0].reshape(1, 1),
        bb2b=_f(inputs["L0_attn_bb2"])[1].reshape(1, 1),
        idf32=idf32,
    )
    wg = _f(inputs["L2_ffn_w_gate"])
    wu = _f(inputs["L2_ffn_w_up"])
    wd = _f(inputs["L2_ffn_w_down"])
    for c in range(NCORES):
        b, t0 = divmod(c * TOK, S)
        xs = np.zeros((D, TOK + 1), np.float32)
        xs[:, 1:] = xf[b * S + t0: b * S + t0 + TOK].T
        if t0 > 0:
            xs[:, 0] = xf[b * S + t0 - 1]
        pps = np.zeros((D, TOK + 1), np.float32)
        pps[:, 1:] = pp[t0:t0 + TOK].T
        if t0 > 0:
            pps[:, 0] = pp[t0 - 1]
        m = dict(com)
        m["xT"] = xs
        m["ppT"] = pps
        m["fmask"] = np.full((128, 1), 0.0 if t0 == 0 else 1.0, np.float32)
        m["wg_sh"] = wg[c * 352:(c + 1) * 352]
        m["wu_sh"] = wu[c * 352:(c + 1) * 352]
        m["wd_sh"] = wd[c * 128:(c + 1) * 128]
        in_maps.append(m)
    res = run8(nc, in_maps)

    state["l1"] = res
    for name in ["xnT", "pos_cT", "pos_sT", "kcT", "ksT",
                 "a_cT", "a_sT", "m_cT", "m_sT"]:
        state[name] = np.concatenate([np.asarray(res[c][name], np.float32)
                                      for c in range(NCORES)], axis=1)
    for name in ["vg", "bwr", "wr0"]:
        state[name] = np.concatenate([res[c][name] for c in range(NCORES)],
                                     axis=1)
    state["wgq"] = np.concatenate([res[c]["wgq_sh"] for c in range(NCORES)], 0)
    state["wuq"] = np.concatenate([res[c]["wuq_sh"] for c in range(NCORES)], 0)
    state["wdq"] = np.concatenate([res[c]["wdq_sh"] for c in range(NCORES)], 0)
    return state




def build_l2():
    p = Prog("l2")
    for name, shape, dt in [
        ("a_c", [256, S], BF16), ("a_s", [256, S], BF16),
        ("m_c", [256, S], BF16), ("m_s", [256, S], BF16),
        ("pc", [256, S], BF16), ("psn", [256, S], BF16),
        ("kc", [256, S], BF16), ("ks", [256, S], BF16),
        ("vgf", [1, S], F32), ("bwf", [2, S], F32),
    ]:
        p.inp(name, shape, dt)
    p.out("bl", [256, S], F32)
    din, dout = p.din, p.dout

    PW = 2048
    NP = S // PW
    ones_col = p.sb("ones_col", [128, 1], F32)
    ones_row = p.sb("ones_row", [1, 128], F32)
    p.memset(ones_col[:, :], 1.0)
    p.memset(ones_row[:, :], 1.0)
    ONES = p.sb("ONES", [128, PW], F32)
    p.memset(ONES[:, :], 1.0)
    ACC = p.sb("ACC", [128, PW], F32)
    SCN = p.sb("SCN", [128, PW], F32)
    SCI = p.sb("SCI", [128, PW], F32)
    PCF = p.sb("PCF", [128, PW], F32)
    INB = p.sb("INB", [128, PW], BF16)
    TMP = p.sb("TMP", [128, PW], F32)
    CAR = p.sb("CAR", [128, 16], F32)
    VGR = p.sb("VGR", [1, PW], F32)
    GCR = p.sb("GCR", [1, PW], F32)
    W0R = p.sb("W0R", [1, PW], F32)
    W1R = p.sb("W1R", [1, PW], F32)
    GCC = p.sb("GCC", [1, 2], F32)
    P0 = p.ps("P0")
    P1 = p.ps("P1")

    p.memset(CAR[:, :], 0.0)
    p.memset(GCC[:, :], 0.0)

    for pc_ in range(NP):
        a = pc_ * PW
        p.dma("sync", VGR[0:1, :], din["vgf"][0:1, a:a + PW])
        p.scan(GCR[0:1, :], ONES[0:1, :], VGR[0:1, :], GCC[0:1, 0:1],
               ALU.mult, ALU.add)
        p.copy(GCC[0:1, 0:1], GCR[0:1, PW - 1:PW])
        p.ts(TMP[0:1, 0:PW], GCR[0:1, :], 1.0, None, ALU.max)
        p.act(TMP[0:1, 0:PW], TMP[0:1, 0:PW], AF.Sqrt)
        p.recip(W1R[0:1, :], TMP[0:1, 0:PW])
        p.dma("sync", W0R[0:1, :], din["bwf"][0:1, a:a + PW])
        p.ts(W0R[0:1, :], W0R[0:1, :], 1.0 / 32.0, None, ALU.mult)
        p.dma("sync", TMP[0:1, 0:PW], din["bwf"][1:2, a:a + PW])
        p.tt(W1R[0:1, :], W1R[0:1, :], TMP[0:1, 0:PW], ALU.mult)
        p.ts(W1R[0:1, :], W1R[0:1, :], 1.0 / 32.0, None, ALU.mult)

        for half in range(2):
            r0 = half * 128
            carb = 8 * half

            def scan_in(name, car_i):
                p.dma("sync", INB[:, :], din[name][r0:r0 + 128, a:a + PW])
                p.copy(SCI[:, :], INB[:, :])
                p.scan(SCN[:, :], ONES[:, :], SCI[:, :],
                       CAR[:, carb + car_i:carb + car_i + 1],
                       ALU.mult, ALU.add)
                p.copy(CAR[:, carb + car_i:carb + car_i + 1],
                       SCN[:, PW - 1:PW])

            def load_f32(name):
                p.dma("sync", INB[:, :], din[name][r0:r0 + 128, a:a + PW])
                p.copy(PCF[:, :], INB[:, :])

            scan_in("a_c", 0)
            load_f32("pc")
            p.tt(ACC[:, :], SCN[:, :], PCF[:, :], ALU.mult)
            scan_in("a_s", 1)
            load_f32("psn")
            p.tt(TMP[:, :], SCN[:, :], PCF[:, :], ALU.mult)
            p.tt(ACC[:, :], ACC[:, :], TMP[:, :], ALU.add)
            for (sa, sb_) in col_splits(PW, 512):
                w = sb_ - sa
                p.mm(P0[:, 0:w], ones_row[0:1, :], W0R[0:1, sa:sb_])
                p.tt(ACC[:, sa:sb_], ACC[:, sa:sb_], P0[:, 0:w], ALU.mult)
            scan_in("m_c", 2)
            load_f32("kc")
            p.tt(TMP[:, :], SCN[:, :], PCF[:, :], ALU.mult)
            scan_in("m_s", 3)
            load_f32("ks")
            p.tt(PCF[:, :], SCN[:, :], PCF[:, :], ALU.mult)
            p.tt(TMP[:, :], TMP[:, :], PCF[:, :], ALU.add)
            for (sa, sb_) in col_splits(PW, 512):
                w = sb_ - sa
                p.mm(P1[:, 0:w], ones_row[0:1, :], W1R[0:1, sa:sb_])
                p.tt(TMP[:, sa:sb_], TMP[:, sa:sb_], P1[:, 0:w], ALU.mult)
            p.tt(ACC[:, :], ACC[:, :], TMP[:, :], ALU.add)
            p.dma("sync", dout["bl"][r0:r0 + 128, a:a + PW], ACC[:, :])
    return p.finalize()


def host_l2(inputs, state):
    nc = get_nc("l2", build_l2)
    in_maps = []
    for c in range(NCORES):
        b, f0 = c // 4, (c % 4) * 256
        m = {}
        for nm, key in [("a_c", "a_cT"), ("a_s", "a_sT"), ("m_c", "m_cT"),
                        ("m_s", "m_sT"), ("pc", "pos_cT"), ("psn", "pos_sT"),
                        ("kc", "kcT"), ("ks", "ksT")]:
            full = state[key]
            m[nm] = np.ascontiguousarray(
                full[f0:f0 + 256, b * S:(b + 1) * S]).astype(BF)
        m["vgf"] = np.ascontiguousarray(state["vg"][0:1, b * S:(b + 1) * S])
        m["bwf"] = np.ascontiguousarray(state["bwr"][:, b * S:(b + 1) * S])
        in_maps.append(m)
    res = run8(nc, in_maps)
    bl = np.zeros((NB, D, S), np.float32)
    for c in range(NCORES):
        b, f0 = c // 4, (c % 4) * 256
        bl[b, f0:f0 + 256] = res[c]["bl"]
    state["blT"] = np.ascontiguousarray(
        np.concatenate([bl[0], bl[1]], axis=1))
    return state



def build_l3():
    p = Prog("l3")
    for name, shape, dt in [
        ("blT", [D, TOK], F32), ("xnT", [D, TOK], F32), ("xT", [D, TOK], F32),
        ("wr0", [8, TOK], F32),
        ("Wo", [D, D], F32), ("bo", [128, 8], F32),
        ("ln_g", [128, 8], F32), ("ln_b", [128, 8], F32),
        ("w1L", [8 * 22 * 128, 8 * 256], F32),
        ("w2L", [8 * 8 * 128, 22 * 128], F32),
    ]:
        p.inp(name, shape, dt)
    p.out("x1T", [D, TOK], F32)
    p.out("xn1T", [D, TOK], F32)
    din, dout = p.din, p.dout

    ones_col = p.sb("ones_col", [128, 1], F32)
    ones_row = p.sb("ones_row", [1, 128], F32)
    p.memset(ones_col[:, :], 1.0)
    p.memset(ones_row[:, :], 1.0)
    HCH = 1024
    XN = p.sb("XN", [128, 8 * HCH], F32)
    XNb = p.sb("XNb", [128, 8 * HCH], BF16)
    BL = p.sb("BL", [128, 8 * HCH], F32)
    LNB = p.sb("LNB", [128, 8 * HCH], BF16)
    SS = p.sb("SS", [128, 8 * HCH], F32)
    MA = XN
    XO = BL
    HE = p.sb("HE", [128, 22 * 512], BF16)
    W1B = p.sb("W1B", [128, 8 * 256], BF16)
    W2B = p.sb("W2B", [128, 22 * 128], BF16)
    WRB = p.sb("WRB", [128, HCH], F32)
    WRE = p.sb("WRE", [1, HCH], F32)
    T1 = p.sb("T1", [128, 512], F32)
    T2 = p.sb("T2", [128, 512], F32)
    sq0_ = p.sb("sq0", [128, 512], F32)
    sq = [sq0_, sq0_]
    row_sum = p.sb("row_sum", [1, HCH], F32)
    row_ssq = p.sb("row_ssq", [1, HCH], F32)
    row_rs = p.sb("row_rs", [1, HCH], F32)
    row_mr = p.sb("row_mr", [1, HCH], F32)
    gcol = {}
    for nm in ["ln_g", "ln_b", "bo"]:
        gcol[nm] = p.sb(nm + "_s", [128, 8], F32)
        p.dma("sync", gcol[nm][:, :], din[nm])
    WOB = p.sb("WOB", [128, 8 * D], BF16)
    for k in range(8):
        p.dma("gpsimd", WOB[:, k * D:(k + 1) * D],
              din["Wo"][k * 128:(k + 1) * 128, :])
    psA = p.ps("psA", [1, 512])
    psB = p.ps("psB", [1, 512])
    P0 = p.ps("P0")
    P1 = p.ps("P1")
    P2 = p.ps("P2")
    P3 = p.ps("P3")

    for hf in range(2):
        xa = hf * HCH
        for k in range(8):
            p.dma("sync", BL[:, k * HCH:(k + 1) * HCH],
                  din["blT"][k * 128:(k + 1) * 128, xa:xa + HCH])
            p.dma("sync", XN[:, k * HCH:(k + 1) * HCH],
                  din["xnT"][k * 128:(k + 1) * 128, xa:xa + HCH])
        p.copy(XNb[:, :], XN[:, :], eng="scalar")

        feat_stats(p, BL, 8, HCH, ones_col, sq, psA, psB, row_sum, row_ssq)
        ln_rows(p, HCH, row_sum, row_ssq, row_rs, row_mr)
        apply_ln(p, BL, BL, 8, HCH, ones_row, row_rs, row_mr, P0, P1,
                 gain_cols=gcol["ln_g"], bias_cols=gcol["ln_b"])
        p.copy(LNB[:, :], BL[:, :], eng="scalar")
        for o in range(8):
            for (a, b) in col_splits(HCH, 512):
                w = b - a
                ps = [P0, P1][o % 2]
                p.mm(ps[:, 0:w],
                     [WOB[:, k * D + o * 128: k * D + o * 128 + 128]
                      for k in range(8)],
                     [LNB[:, k * HCH + a: k * HCH + b] for k in range(8)])
                p.act(SS[:, o * HCH + a: o * HCH + b], ps[:, 0:w],
                      AF.Identity, bias=gcol["bo"][:, o:o + 1])
                p.tt(SS[:, o * HCH + a: o * HCH + b],
                     SS[:, o * HCH + a: o * HCH + b],
                     XN[:, o * HCH + a: o * HCH + b], ALU.add)

        p.memset(MA[:, :], 0.0)
        for e in range(8):
            p.dma("sync", WRE[0:1, 0:HCH], din["wr0"][e:e + 1, xa:xa + HCH])
            for (a, b) in col_splits(HCH, 512):
                w = b - a
                p.mm(P2[:, 0:w], ones_row[0:1, :], WRE[0:1, a:b])
                p.copy(WRB[:, a:b], P2[:, 0:w])
            for tb in range(HCH // 512):
                ta = tb * 512
                for h in range(22):
                    r0_ = (e * 22 + h) * 128
                    p.dma("gpsimd", W1B[:, :], din["w1L"][r0_:r0_ + 128, :])
                    p.mm(P2[:, 0:512],
                         [W1B[:, k * 256: k * 256 + 128] for k in range(8)],
                         [XNb[:, k * HCH + ta: k * HCH + ta + 512]
                          for k in range(8)])
                    p.mm(P3[:, 0:512],
                         [W1B[:, k * 256 + 128: k * 256 + 256]
                          for k in range(8)],
                         [XNb[:, k * HCH + ta: k * HCH + ta + 512]
                          for k in range(8)])
                    p.act(T1[:, 0:512], P2[:, 0:512], AF.Silu)
                    p.tt(T2[:, 0:512], T1[:, 0:512], P3[:, 0:512], ALU.mult)
                    p.tt(HE[:, h * 512:(h + 1) * 512], T2[:, 0:512],
                         WRB[:, ta:ta + 512], ALU.mult)
                for d in range(8):
                    r0_ = (e * 8 + d) * 128
                    p.dma("gpsimd", W2B[:, :], din["w2L"][r0_:r0_ + 128, :])
                    ps = [P2, P3][d % 2]
                    p.mm(ps[:, 0:512],
                         [W2B[:, h * 128:(h + 1) * 128] for h in range(22)],
                         [HE[:, h * 512:(h + 1) * 512] for h in range(22)])
                    p.tt(MA[:, d * HCH + ta: d * HCH + ta + 512],
                         MA[:, d * HCH + ta: d * HCH + ta + 512],
                         ps[:, 0:512], ALU.add)

        for k in range(8):
            p.dma("sync", XO[:, k * HCH:(k + 1) * HCH],
                  din["xT"][k * 128:(k + 1) * 128, xa:xa + HCH])
        p.act(MA[:, :], MA[:, :], AF.Silu)
        p.tt(MA[:, :], MA[:, :], SS[:, :], ALU.mult)
        p.tt(XO[:, :], XO[:, :], MA[:, :], ALU.add)
        for k in range(8):
            p.dma("sync", dout["x1T"][k * 128:(k + 1) * 128, xa:xa + HCH],
                  XO[:, k * HCH:(k + 1) * HCH])
        feat_stats(p, XO, 8, HCH, ones_col, sq, psA, psB, row_sum, row_ssq)
        ln_rows(p, HCH, row_sum, row_ssq, row_rs, row_mr)
        apply_ln(p, XO, XO, 8, HCH, ones_row, row_rs, row_mr, P0, P1)
        for k in range(8):
            p.dma("sync", dout["xn1T"][k * 128:(k + 1) * 128, xa:xa + HCH],
                  XO[:, k * HCH:(k + 1) * HCH])
    return p.finalize()


def host_l3(inputs, state):
    nc = get_nc("l3", build_l3)
    w1 = np.asarray(inputs["L0_ffn_w1"], np.float32)
    w2 = np.asarray(inputs["L0_ffn_w2"], np.float32)
    w1T = w1.transpose(0, 2, 1)
    w1L = np.zeros((8, 22, 128, 8 * 256), np.float32)
    w1a = w1T[:, :, :2816].reshape(8, 8, 128, 22, 128)
    w1g = w1T[:, :, 2816:].reshape(8, 8, 128, 22, 128)
    for h in range(22):
        for k in range(8):
            w1L[:, h, :, k * 256:k * 256 + 128] = w1a[:, k, :, h]
            w1L[:, h, :, k * 256 + 128:k * 256 + 256] = w1g[:, k, :, h]

    w2T = w2.transpose(0, 2, 1)
    w2L = np.zeros((8, 8, 128, 22 * 128), np.float32)
    w2r = w2T.reshape(8, 22, 128, 8, 128)
    for d in range(8):
        for h in range(22):
            w2L[:, d, :, h * 128:(h + 1) * 128] = w2r[:, h, :, d]
    x = np.asarray(inputs["L0_attn_x"], np.float32).reshape(NB * S, D)
    com = dict(
        Wo=_f(inputs["L0_attn_Wo"]),
        bo=_f(inputs["L0_attn_bo"]).reshape(8, 128).T.copy(),
        ln_g=_f(inputs["L0_attn_ln_g"]).reshape(8, 128).T.copy(),
        ln_b=_f(inputs["L0_attn_ln_b"]).reshape(8, 128).T.copy(),
        w1L=np.ascontiguousarray(w1L.reshape(8 * 22 * 128, 8 * 256)),
        w2L=np.ascontiguousarray(w2L.reshape(8 * 8 * 128, 22 * 128)),
    )
    in_maps = []
    for c in range(NCORES):
        m = dict(com)
        m["blT"] = np.ascontiguousarray(state["blT"][:, c * TOK:(c + 1) * TOK])
        m["xnT"] = np.ascontiguousarray(state["xnT"][:, c * TOK:(c + 1) * TOK])
        m["xT"] = np.ascontiguousarray(x[c * TOK:(c + 1) * TOK].T)
        m["wr0"] = np.ascontiguousarray(state["wr0"][:, c * TOK:(c + 1) * TOK])
        in_maps.append(m)
    res = run8(nc, in_maps)
    state["x1T"] = np.concatenate([res[c]["x1T"] for c in range(NCORES)], 1)
    state["xn1T"] = np.concatenate([res[c]["xn1T"] for c in range(NCORES)], 1)
    return state




def build_l4():
    p = Prog("l4")
    for name, shape, dt in [
        ("xn1T", [D, TOK], F32),
        ("wq", [D, D], F32), ("wk", [D, D], F32), ("wv", [D, D], F32),
        ("projN", [64, 64], F32),
        ("wqT", [D, D], F32), ("wqTr", [D, D], F32),
        ("wkT", [D, D], F32), ("wkTr", [D, D], F32),
        ("wvT", [D, D], F32),
        ("g1c", [128, 8], F32), ("b1c", [128, 8], F32),
        ("anwc", [128, 8], F32),
        ("COSF", [128, TOK], F32), ("SINF", [128, TOK], F32),
    ]:
        p.inp(name, shape, dt)
    for name in ["qpT", "kpT", "vT", "q2rT", "k2rT", "v2T"]:
        p.out(name, [D, TOK], BF16)
    din, dout = p.din, p.dout

    ones_col = p.sb("ones_col", [128, 1], F32)
    ones_row = p.sb("ones_row", [1, 128], F32)
    p.memset(ones_col[:, :], 1.0)
    p.memset(ones_row[:, :], 1.0)
    HCH = 1024
    XN1 = p.sb("XN1", [128, 8 * HCH], F32)
    HB = p.sb("HB", [128, 8 * HCH], BF16)
    Z1 = p.sb("Z1", [128, 8 * HCH], F32)
    Z1b = p.sb("Z1b", [128, 8 * HCH], BF16)
    OUTB = p.sb("OUTB", [128, 8 * HCH], BF16)
    T1 = p.sb("T1", [128, 512], F32)
    T2 = p.sb("T2", [128, 512], F32)
    WA = p.sb("WA", [128, 8 * D], BF16)
    PRJ = p.sb("PRJ", [128, 64], BF16)
    COSs = p.sb("COSs", [128, HCH], F32)
    SINs = p.sb("SINs", [128, HCH], F32)
    sq0_ = p.sb("sq0", [128, 512], F32)
    sq = [sq0_, sq0_]
    row_sum = p.sb("row_sum", [1, HCH], F32)
    row_ssq = p.sb("row_ssq", [1, HCH], F32)
    row_rs = p.sb("row_rs", [1, HCH], F32)
    row_mr = p.sb("row_mr", [1, HCH], F32)
    gc = {}
    for nm in ["g1c", "b1c", "anwc"]:
        gc[nm] = p.sb(nm + "_s", [128, 8], F32)
        p.dma("sync", gc[nm][:, :], din[nm])
    p.dma("gpsimd", PRJ[0:64, :], din["projN"])
    p.dma("gpsimd", PRJ[64:128, :], din["projN"])
    psA = p.ps("psA", [1, 512])
    psB = p.ps("psB", [1, 512])
    P0 = p.ps("P0")
    P1 = p.ps("P1")
    P2 = p.ps("P2")
    P3 = p.ps("P3")

    def loadW(nm):
        for k in range(8):
            p.dma("gpsimd", WA[:, k * D:(k + 1) * D],
                  din[nm][k * 128:(k + 1) * 128, :])

    for hf in range(2):
        xa = hf * HCH
        for k in range(8):
            p.dma("sync", XN1[:, k * HCH:(k + 1) * HCH],
                  din["xn1T"][k * 128:(k + 1) * 128, xa:xa + HCH])
        p.dma("sync", COSs[:, :], din["COSF"][:, xa:xa + HCH])
        p.dma("sync", SINs[:, :], din["SINF"][:, xa:xa + HCH])

        feat_stats(p, XN1, 8, HCH, ones_col, sq, psA, psB, row_sum, row_ssq)
        ln_rows(p, HCH, row_sum, row_ssq, row_rs, row_mr)
        apply_ln(p, XN1, Z1, 8, HCH, ones_row, row_rs, row_mr, P0, P1,
                 gain_cols=gc["g1c"], bias_cols=gc["b1c"])
        p.copy(HB[:, :], Z1[:, :], eng="scalar")

        for (wn, on) in [("wq", "qpT"), ("wk", "kpT")]:
            loadW(wn)
            proj(p, WA, HB, Z1b, [P0, P1], 8, 8, HCH)
            for h in range(16):
                o, r0 = h // 2, (h % 2) * 64
                for (a, b) in col_splits(HCH, 512):
                    w = b - a
                    ps = [P2, P3][h % 2]
                    p.mm(ps[0:64, 0:w], PRJ[r0:r0 + 64, :],
                         Z1b[r0:r0 + 64, o * HCH + a: o * HCH + b])
                    rsl = slice(r0, r0 + 64)
                    p.act(T1[rsl, 0:w], ps[0:64, 0:w], AF.Relu)
                    p.tt(T2[rsl, 0:w], ps[0:64, 0:w], T1[rsl, 0:w],
                         ALU.subtract)
                    p.act(T2[rsl, 0:w], T2[rsl, 0:w], AF.Exp)
                    p.tt(OUTB[rsl, o * HCH + a: o * HCH + b],
                         T2[rsl, 0:w], T1[rsl, 0:w], ALU.add)
            for k in range(8):
                p.dma("sync", dout[on][k * 128:(k + 1) * 128, xa:xa + HCH],
                      OUTB[:, k * HCH:(k + 1) * HCH])

        loadW("wv")
        proj(p, WA, HB, OUTB, [P0, P1], 8, 8, HCH)
        for k in range(8):
            p.dma("sync", dout["vT"][k * 128:(k + 1) * 128, xa:xa + HCH],
                  OUTB[:, k * HCH:(k + 1) * HCH])

        feat_stats(p, XN1, 8, HCH, ones_col, sq, psA, psB, row_sum, row_ssq)
        ln_rows(p, HCH, row_sum, row_ssq, row_rs, row_mr, eps=1e-6, rms=True)
        apply_ln(p, XN1, Z1, 8, HCH, ones_row, row_rs, None, P0, P1,
                 gain_cols=gc["anwc"])
        p.copy(HB[:, :], Z1[:, :], eng="scalar")

        for (w1n, w2n, on) in [("wqT", "wqTr", "q2rT"), ("wkT", "wkTr", "k2rT")]:
            loadW(w1n)
            proj(p, WA, HB, Z1, [P0, P1], 8, 8, HCH)
            loadW(w2n)
            for o in range(8):
                for (a, b) in col_splits(HCH, 512):
                    w = b - a
                    ps = [P2, P3][o % 2]
                    p.mm(ps[:, 0:w],
                         [WA[:, k * D + o * 128: k * D + o * 128 + 128]
                          for k in range(8)],
                         [HB[:, k * HCH + a: k * HCH + b] for k in range(8)])
                    sl = slice(o * HCH + a, o * HCH + b)
                    p.tt(T1[:, 0:w], ps[:, 0:w], SINs[:, a:b], ALU.mult)
                    p.tt(T2[:, 0:w], Z1[:, sl], COSs[:, a:b], ALU.mult)
                    p.tt(OUTB[:, sl], T1[:, 0:w], T2[:, 0:w], ALU.add)
            for k in range(8):
                p.dma("sync", dout[on][k * 128:(k + 1) * 128, xa:xa + HCH],
                      OUTB[:, k * HCH:(k + 1) * HCH])

        loadW("wvT")
        proj(p, WA, HB, OUTB, [P0, P1], 8, 8, HCH)
        for k in range(8):
            p.dma("sync", dout["v2T"][k * 128:(k + 1) * 128, xa:xa + HCH],
                  OUTB[:, k * HCH:(k + 1) * HCH])
    return p.finalize()


def host_l4(inputs, state):
    nc = get_nc("l4", build_l4)

    def rot_cols(w):
        wr = np.asarray(w, np.float32).copy()
        wr2 = wr.reshape(wr.shape[0], 16, 2, 32)
        return np.ascontiguousarray(wr2[:, :, ::-1].reshape(wr.shape))

    inv_freq = 1.0 / 10000.0 ** (np.arange(0, 64, 2, dtype=np.float32) / 64)
    pos = np.arange(S, dtype=np.float32)
    fr = pos[:, None] * inv_freq[None, :]
    cosf = np.cos(fr).T
    sinf = np.sin(fr).T
    wqT = _t(inputs["L1_ffn_wq"])
    wkT = _t(inputs["L1_ffn_wk"])
    com = dict(
        wq=_f(inputs["L1_attn_wq"]), wk=_f(inputs["L1_attn_wk"]),
        wv=_f(inputs["L1_attn_wv"]), projN=_f(inputs["L1_attn_proj"]),
        wqT=wqT, wqTr=rot_cols(wqT), wkT=wkT, wkTr=rot_cols(wkT),
        wvT=_t(inputs["L1_ffn_wv"]),
        g1c=_f(inputs["L1_attn_g1"]).reshape(8, 128).T.copy(),
        b1c=_f(inputs["L1_attn_beta1"]).reshape(8, 128).T.copy(),
        anwc=_f(inputs["L1_ffn_attn_norm_w"]).reshape(8, 128).T.copy(),
    )
    in_maps = []
    for c in range(NCORES):
        b, t0 = divmod(c * TOK, S)
        CO = np.zeros((128, TOK), np.float32)
        SI = np.zeros((128, TOK), np.float32)
        cs = cosf[:, t0:t0 + TOK]
        sn = sinf[:, t0:t0 + TOK]
        for q in range(4):
            CO[q * 32:(q + 1) * 32] = cs
            SI[q * 32:(q + 1) * 32] = (-sn if q % 2 == 0 else sn)
        m = dict(com)
        m["xn1T"] = np.ascontiguousarray(state["xn1T"][:, c * TOK:(c + 1) * TOK])
        m["COSF"] = CO
        m["SINF"] = SI
        in_maps.append(m)
    res = run8(nc, in_maps)
    for nm in ["qpT", "kpT", "vT", "q2rT", "k2rT", "v2T"]:
        state[nm] = np.concatenate(
            [np.asarray(res[c][nm], np.float32) for c in range(NCORES)], 1)
    return state




NQB = 16


def build_l5():
    p = Prog("l5")
    for name in ["qp", "kp", "vp", "q2", "k2", "v2"]:
        p.inp(name, [256, S], BF16)
    for v in range(4):
        p.inp("MBA%d" % v, [128, 512], F32)
        p.inp("MK%d" % v, [128, 512], F32)
    p.inp("TRIU", [128, 128], F32)
    p.inp("idf32", [128, 128], F32)
    p.out("ap_o", [256, S], BF16)
    p.out("ao_o", [256, S], BF16)
    din, dout = p.din, p.dout
    scr = p.nc.dram_tensor("scr", [8, 128], F32).ap()

    ones_row = p.sb("ones_row", [1, 128], F32)
    ones_bf = p.sb("ones_bf", [128, 1], BF16)
    p.memset(ones_row[:, :], 1.0)
    p.memset(ones_bf[:, :], 1.0)
    idf = p.sb("idf", [128, 128], F32)
    p.dma("sync", idf[:, :], din["idf32"])
    idb = p.sb("idb", [128, 128], BF16)
    p.copy(idb[:, :], idf[:, :])
    MBA = [p.sb("sMBA%d" % v, [128, 512], F32) for v in range(4)]
    MK = [p.sb("sMK%d" % v, [128, 512], F32) for v in range(4)]
    for v in range(4):
        p.dma("sync", MBA[v][:, :], din["MBA%d" % v])
        p.dma("sync", MK[v][:, :], din["MK%d" % v])
    TRI = p.sb("TRI", [128, 128], F32)
    p.dma("sync", TRI[:, :], din["TRIU"])

    QH = p.sb("QH", [64, S], BF16)
    KH = p.sb("KH", [64, S], BF16)
    VH = p.sb("VH", [64, S], BF16)
    VTOK = p.sb("VTOK", [128, 64 * 64], BF16)
    KTOK = p.sb("KTOK", [128, 64 * 64], BF16)
    STb = p.sb("STb", [128, 128], BF16)
    NUMF = p.sb("NUMF", [64, 128], F32)
    KVf = p.sb("KVf", [64, 64], F32)
    KVb = p.sb("KVb", [64, 64], BF16)
    KSf = p.sb("KSf", [64, 1], F32)
    KSb = p.sb("KSb", [64, 1], BF16)
    DRow = p.sb("DRow", [1, 128], F32)
    OST = p.sb("OST", [64, S], BF16)
    MXC = p.sb("MXC", [128, 4], F32)
    MXT = p.sb("MXT", [4, 128], F32)
    MR = p.sb("MR", [1, 512], F32)
    MBs = p.sb("MBs", [128, 512], F32)
    ET = p.sb("ET", [128, 512], BF16)
    T1 = p.sb("T1", [128, 512], F32)
    T2 = p.sb("T2", [128, 512], F32)
    OF = p.sb("OF", [64, 512], F32)
    DR = p.sb("DR", [1, 512], F32)
    PP = [p.ps("PP%d" % i) for i in range(8)]
    PPB5 = PP[5][:, 0:256].bitcast(BF16)

    for h in range(4):
        r0 = h * 64
        p.dma("sync", QH[:, :], din["qp"][r0:r0 + 64, :])
        p.dma("sync", KH[:, :], din["kp"][r0:r0 + 64, :])
        p.dma("sync", VH[:, :], din["vp"][r0:r0 + 64, :])
        for c in range(64):
            cs = slice(c * 128, (c + 1) * 128)
            p.transpose(PPB5[0:128, 0:64], VH[:, cs], idb[0:64, 0:64])
            p.copy(VTOK[:, c * 64:(c + 1) * 64], PPB5[0:128, 0:64])
            p.transpose(PPB5[0:128, 64:128], KH[:, cs], idb[0:64, 0:64])
            p.copy(KTOK[:, c * 64:(c + 1) * 64], PPB5[0:128, 64:128])
        p.memset(KVf[:, :], 0.0)
        p.memset(KVb[:, :], 0.0)
        p.memset(KSf[:, :], 0.0)
        p.memset(KSb[:, :], 0.0)
        for c in range(64):
            cs = slice(c * 128, (c + 1) * 128)
            p.mm(PP[0][:, 0:128], KH[:, cs], QH[:, cs])
            p.tt(STb[:, :], PP[0][:, 0:128], TRI[:, :], ALU.mult)

            def emitnum(t, cs=cs, c=c):
                t.matmul(PP[1][0:64, 0:128], VTOK[:, c * 64:(c + 1) * 64],
                         STb[:, :], start=True, stop=False)
                return t.matmul(PP[1][0:64, 0:128], KVb[:, :],
                                QH[:, cs], start=False, stop=True)
            p.rec("tensor", emitnum, [VTOK, STb, KVb, QH], [PP[1]])

            def emitden(t, cs=cs):
                t.matmul(PP[2][0:1, 0:128], ones_bf[:, 0:1],
                         STb[:, :], start=True, stop=False)
                return t.matmul(PP[2][0:1, 0:128], KSb[:, 0:1],
                                QH[:, cs], start=False, stop=True)
            p.rec("tensor", emitden, [ones_bf, STb, KSb, QH], [PP[2]])
            p.copy(NUMF[:, :], PP[1][0:64, 0:128])
            p.ts(DRow[0:1, :], PP[2][0:1, 0:128], 1e-6, None, ALU.add)
            p.recip(DRow[0:1, :], DRow[0:1, :])
            p.mm(PP[3][0:64, 0:128], ones_row[0:1, 0:64], DRow[0:1, :])
            p.tt(OST[:, cs], NUMF[:, :], PP[3][0:64, 0:128], ALU.mult)
            p.mm(PP[4][0:64, 0:64], KTOK[:, c * 64:(c + 1) * 64],
                 VTOK[:, c * 64:(c + 1) * 64])
            p.tt(KVf[:, :], KVf[:, :], PP[4][0:64, 0:64], ALU.add)
            p.copy(KVb[:, :], KVf[:, :])
            p.reduce(T2[0:64, 0:1], KH[:, cs], ALU.add)
            p.tt(KSf[:, 0:1], KSf[:, 0:1], T2[0:64, 0:1], ALU.add)
            p.copy(KSb[:, 0:1], KSf[:, 0:1])
        p.dma("sync", dout["ap_o"][r0:r0 + 64, :], OST[:, :])

    for h in range(4):
        r0 = h * 64
        p.dma("sync", QH[:, :], din["q2"][r0:r0 + 64, :])
        p.dma("sync", KH[:, :], din["k2"][r0:r0 + 64, :])
        p.dma("sync", VH[:, :], din["v2"][r0:r0 + 64, :])
        p.ts(QH[:, :], QH[:, :], 0.125, None, ALU.mult)
        for c in range(64):
            p.transpose(PPB5[0:128, 0:64], VH[:, c * 128:(c + 1) * 128],
                        idb[0:64, 0:64])
            p.copy(VTOK[:, c * 64:(c + 1) * 64], PPB5[0:128, 0:64])
        for qb in range(NQB):
            Q0b = qb * 512
            p.memset(MXC[:, :], -3e38)
            for qt in range(4):
                Q0 = Q0b + qt * 128
                for j in range(Q0 // 512 + 1):
                    ps = PP[(j % 2)]
                    p.mm(ps[:, 0:512], QH[:, Q0:Q0 + 128],
                         KH[:, j * 512:(j + 1) * 512])
                    v_ = (Q0 - j * 512) // 128
                    if 0 <= v_ <= 3:
                        p.tt(T1[:, 0:512], ps[:, 0:512], MBA[v_][:, :],
                             ALU.add)
                        p.reduce(T1[:, 0:1], T1[:, 0:512], ALU.max)
                    else:
                        p.reduce(T1[:, 0:1], ps[:, 0:512], ALU.max)
                    p.tt(MXC[:, qt:qt + 1], MXC[:, qt:qt + 1],
                         T1[:, 0:1], ALU.max)
            p.transpose(PP[2][0:4, 0:128], MXC[:, 0:4], idf[:, :])
            p.copy(MXT[0:4, :], PP[2][0:4, 0:128])
            p.dma("sync", scr[0:4, :], MXT[0:4, :])
            p.dma("sync", MR[0:1, 0:512],
                  scr[0:4, :].rearrange("a b -> (a b)")[None, :])
            p.ts(MR[0:1, :], MR[0:1, :], -1.0, None, ALU.mult)
            p.mm(PP[3][:, 0:512], ones_row[0:1, :], MR[0:1, 0:512])
            p.copy(MBs[:, :], PP[3][:, 0:512])
            nk = (Q0b + 512) // 128
            first = True
            for ck in range(nk):
                K0 = ck * 128
                ps = PP[(ck % 2)]
                p.mm(ps[:, 0:512], KH[:, K0:K0 + 128], QH[:, Q0b:Q0b + 512])
                p.tt(T1[:, 0:512], ps[:, 0:512], MBs[:, :], ALU.add)
                p.act(ET[:, :], T1[:, 0:512], AF.Exp)
                u = (K0 - Q0b) // 128
                if 0 <= u <= 3:
                    p.tt(ET[:, :], ET[:, :], MK[u][:, :], ALU.mult)
                p.mm_more(PP[6][0:64, 0:512], VTOK[:, ck * 64:(ck + 1) * 64],
                          ET[:, :], start=first, stop=(ck == nk - 1))
                p.mm_more(PP[7][0:1, 0:512], ones_bf[:, 0:1], ET[:, :],
                          start=first, stop=(ck == nk - 1))
                first = False
            p.recip(DR[0:1, :], PP[7][0:1, 0:512])
            p.mm(PP[3][0:64, 0:512], ones_row[0:1, 0:64], DR[0:1, 0:512])
            p.copy(OF[0:64, :], PP[6][0:64, 0:512])
            p.tt(OST[:, Q0b:Q0b + 512], OF[0:64, :], PP[3][0:64, 0:512],
                 ALU.mult)
        p.dma("sync", dout["ao_o"][r0:r0 + 64, :], OST[:, :])
    return p.finalize()


def host_l5(inputs, state):
    nc = get_nc("l5", build_l5)
    MBAs, MKs = [], []
    for v in range(4):
        qq = np.arange(128)[:, None]
        kk = np.arange(512)[None, :]
        MBAs.append(np.where(kk <= v * 128 + qq, 0.0, -1e30).astype(np.float32))
        MKs.append((v * 128 + np.arange(128)[:, None]
                    <= np.arange(512)[None, :]).astype(np.float32))
    tri = (np.arange(128)[:, None] <= np.arange(128)[None, :]) \
        .astype(np.float32)
    com = {"TRIU": tri, "idf32": np.eye(128, dtype=np.float32)}
    for v in range(4):
        com["MBA%d" % v] = MBAs[v]
        com["MK%d" % v] = MKs[v]
    in_maps = []
    for c in range(NCORES):
        b, hg = c // 4, c % 4
        rows = slice(hg * 256, hg * 256 + 256)
        cols = slice(b * S, (b + 1) * S)
        m = dict(com)
        for nm, key in [("qp", "qpT"), ("kp", "kpT"), ("vp", "vT"),
                        ("q2", "q2rT"), ("k2", "k2rT"), ("v2", "v2T")]:
            m[nm] = np.ascontiguousarray(state[key][rows, cols]).astype(BF)
        in_maps.append(m)
    res = run8(nc, in_maps)
    ap = np.zeros((NB, D, S), np.float32)
    ao = np.zeros((NB, D, S), np.float32)
    for c in range(NCORES):
        b, hg = c // 4, c % 4
        ap[b, hg * 256:hg * 256 + 256] = np.asarray(res[c]["ap_o"], np.float32)
        ao[b, hg * 256:hg * 256 + 256] = np.asarray(res[c]["ao_o"], np.float32)
    state["apT"] = np.ascontiguousarray(np.concatenate([ap[0], ap[1]], 1))
    state["aoT"] = np.ascontiguousarray(np.concatenate([ao[0], ao[1]], 1))
    return state




def emit_top2(p, PRps, MIX, RM, RL2, RM2, RV, WRT, RRC):
    p.copy(RM[:, 0:8], PRps)
    p.reduce(RV[:, 0:1], RM[:, 0:8], ALU.max)
    p.ts(RL2[:, 0:8], RM[:, 0:8], RV[:, 0:1], None, ALU.is_equal)
    p.stt(RM2[:, 0:8], RL2[:, 0:8], -1e30, RM[:, 0:8], ALU.mult, ALU.add)
    p.reduce(RV[:, 1:2], RM2[:, 0:8], ALU.max)
    p.ts(WRT[:, 0:8], RM2[:, 0:8], RV[:, 1:2], None, ALU.is_equal)
    p.tt(RV[:, 2:3], RV[:, 1:2], RV[:, 0:1], ALU.subtract)
    p.act(RV[:, 3:4], RV[:, 2:3], AF.Exp)
    p.ts(RV[:, 4:5], RV[:, 3:4], 1.0, None, ALU.add)
    p.recip(RV[:, 6:7], RV[:, 4:5])
    p.tt(RV[:, 5:6], RV[:, 3:4], RV[:, 6:7], ALU.mult)
    p.ts(RL2[:, 0:8], RL2[:, 0:8], RV[:, 6:7], None, ALU.mult)
    p.ts(WRT[:, 0:8], WRT[:, 0:8], RV[:, 5:6], None, ALU.mult)
    p.tt(WRT[:, 0:8], WRT[:, 0:8], RL2[:, 0:8], ALU.add)



def build_l6a():
    p = Prog("l6a")
    for name, shape, dt in [
        ("xn1T", [D, TOK], F32), ("x1T", [D, TOK], F32),
        ("apT", [D, TOK], BF16), ("aoT", [D, TOK], BF16),
        ("wo1", [D, D], F32), ("w1f", [D, 4096], F32),
        ("w2f", [4096, D], F32),
        ("b1fc", [128, 32], F32), ("b2fc", [128, 8], F32),
        ("g2c", [128, 8], F32), ("be2c", [128, 8], F32),
        ("woT2", [D, D], F32), ("mnwc", [128, 8], F32),
        ("gwT", [D, 8], F32),
        ("w1L2", [8 * 32 * 128, 8 * 128], F32),
        ("w2L2", [8 * 8 * 128, 32 * 128], F32),
        ("b1c2", [128, 8 * 32], F32), ("b2nat", [8, D], F32),
        ("idf32", [128, 128], F32),
    ]:
        p.inp(name, shape, dt)
    p.out("x2T", [D, TOK], F32)
    p.out("xn2T", [D, TOK], F32)
    din, dout = p.din, p.dout
    scr = p.nc.dram_tensor("scr2", [8, TOK], F32).ap()

    ones_col = p.sb("ones_col", [128, 1], F32)
    ones_row = p.sb("ones_row", [1, 128], F32)
    p.memset(ones_col[:, :], 1.0)
    p.memset(ones_row[:, :], 1.0)
    idf = p.sb("idf", [128, 128], F32)
    p.dma("sync", idf[:, :], din["idf32"])
    HCH = 1024
    BUF1 = p.sb("BUF1", [128, 8 * HCH], F32)
    XATT = p.sb("XATT", [128, 8 * HCH], F32)
    HRES = p.sb("HRES", [128, 8 * HCH], F32)
    ABuf = p.sb("ABuf", [128, 8 * HCH], BF16)
    HBb = p.sb("HBb", [128, 8 * HCH], BF16)
    HID = p.sb("HID", [128, 32 * 512], BF16)
    WA = p.sb("WA", [128, 8 * D], BF16)
    W1B = p.sb("W1B", [128, 8 * 128], BF16)
    W2B = p.sb("W2B", [128, 16 * 128], BF16)
    GW = p.sb("GW", [128, 8 * 8], F32)
    B2S = p.sb("B2S", [8, 128], F32)
    WRB = p.sb("WRB", [128, 512], F32)
    MIX = p.sb("MIX", [128, 8], F32)
    RM = p.sb("RM", [128, 8], F32)
    RL2 = p.sb("RL2", [128, 8], F32)
    RM2 = p.sb("RM2", [128, 8], F32)
    RV = p.sb("RV", [128, 10], F32)
    WRT = p.sb("WRT", [128, 8], F32)
    RRC = p.sb("RRC", [128, 8], F32)
    T1 = p.sb("T1", [128, 512], F32)
    sq0_ = p.sb("sq0", [128, 512], F32)
    sq = [sq0_, sq0_]
    row_sum = p.sb("row_sum", [1, HCH], F32)
    row_ssq = p.sb("row_ssq", [1, HCH], F32)
    row_rs = p.sb("row_rs", [1, HCH], F32)
    row_mr = p.sb("row_mr", [1, HCH], F32)
    gc = {}
    for nm in ["b1fc", "b2fc", "g2c", "be2c", "mnwc", "b1c2"]:
        shp = {"b1fc": [128, 32], "b1c2": [128, 8 * 32]}.get(nm, [128, 8])
        gc[nm] = p.sb(nm + "_s", shp, F32)
        p.dma("sync", gc[nm][:, :], din[nm])
    for k in range(8):
        p.dma("sync", GW[:, k * 8:(k + 1) * 8],
              din["gwT"][k * 128:(k + 1) * 128, :])

    psA = p.ps("psA", [1, 512])
    psB = p.ps("psB", [1, 512])
    P0 = p.ps("P0")
    P1 = p.ps("P1")
    P2 = p.ps("P2")
    P3 = p.ps("P3")
    PR = p.ps("PR", [128, 128])

    def loadW(nm):
        for k in range(8):
            p.dma("gpsimd", WA[:, k * D:(k + 1) * D],
                  din[nm][k * 128:(k + 1) * 128, :])

    for hf in range(2):
        xa = hf * HCH
        for k in range(8):
            p.dma("sync", BUF1[:, k * HCH:(k + 1) * HCH],
                  din["xn1T"][k * 128:(k + 1) * 128, xa:xa + HCH])
            p.dma("sync", ABuf[:, k * HCH:(k + 1) * HCH],
                  din["apT"][k * 128:(k + 1) * 128, xa:xa + HCH])
        loadW("wo1")
        proj(p, WA, ABuf, XATT, [P0, P1], 8, 8, HCH)
        p.tt(XATT[:, :], XATT[:, :], BUF1[:, :], ALU.add)
        feat_stats(p, XATT, 8, HCH, ones_col, sq, psA, psB, row_sum, row_ssq)
        ln_rows(p, HCH, row_sum, row_ssq, row_rs, row_mr)
        apply_ln(p, XATT, HRES, 8, HCH, ones_row, row_rs, row_mr, P0, P1,
                 gain_cols=gc["g2c"], bias_cols=gc["be2c"])
        p.copy(HBb[:, :], HRES[:, :], eng="scalar")
        for tb in range(HCH // 512):
            ta = tb * 512
            for o in range(32):
                p.dma("gpsimd", W1B[:, :].rearrange("p (k n) -> p k n", n=128),
                      din["w1f"][0:D, o * 128:(o + 1) * 128]
                      .rearrange("(k p) n -> p k n", p=128))
                ps = [P0, P1][o % 2]
                p.mm(ps[:, 0:512],
                     [W1B[:, k * 128:(k + 1) * 128] for k in range(8)],
                     [HBb[:, k * HCH + ta: k * HCH + ta + 512]
                      for k in range(8)])
                p.act(HID[:, o * 512:(o + 1) * 512], ps[:, 0:512], AF.Gelu,
                      bias=gc["b1fc"][:, o:o + 1])
            for d in range(8):
                ps = [P2, P3][d % 2]
                for half in range(2):
                    hb = half * 16
                    p.dma("gpsimd",
                          W2B[:, :].rearrange("p (k n) -> p k n", n=128),
                          din["w2f"][hb * 128:(hb + 16) * 128,
                                     d * 128:(d + 1) * 128]
                          .rearrange("(k p) n -> p k n", p=128))
                    p.mm_more(ps[:, 0:512],
                              [W2B[:, h * 128:(h + 1) * 128]
                               for h in range(16)],
                              [HID[:, (hb + h) * 512:(hb + h + 1) * 512]
                               for h in range(16)],
                              start=(half == 0), stop=(half == 1))
                p.act(T1[:, 0:512], ps[:, 0:512], AF.Identity,
                      bias=gc["b2fc"][:, d:d + 1])
                p.tt(XATT[:, d * HCH + ta: d * HCH + ta + 512],
                     XATT[:, d * HCH + ta: d * HCH + ta + 512],
                     T1[:, 0:512], ALU.add)
        for k in range(8):
            p.dma("sync", ABuf[:, k * HCH:(k + 1) * HCH],
                  din["aoT"][k * 128:(k + 1) * 128, xa:xa + HCH])
        loadW("woT2")
        proj(p, WA, ABuf, HRES, [P0, P1], 8, 8, HCH)
        p.tt(HRES[:, :], HRES[:, :], BUF1[:, :], ALU.add)
        feat_stats(p, HRES, 8, HCH, ones_col, sq, psA, psB, row_sum, row_ssq)
        ln_rows(p, HCH, row_sum, row_ssq, row_rs, row_mr, eps=1e-6, rms=True)
        apply_ln(p, HRES, BUF1, 8, HCH, ones_row, row_rs, None, P0, P1,
                 gain_cols=gc["mnwc"])
        p.copy(HBb[:, :], BUF1[:, :], eng="scalar")
        for i in range(HCH // 128):
            tbase = i * 128
            p.mm(PR[:, 0:8],
                 [BUF1[:, k * HCH + tbase: k * HCH + tbase + 128]
                  for k in range(8)],
                 [GW[:, k * 8:(k + 1) * 8] for k in range(8)])
            emit_top2(p, PR[:, 0:8], MIX, RM, RL2, RM2, RV, WRT, RRC)
            p.transpose(PR[0:8, 0:128], WRT[:, 0:8], idf[:, :])
            p.copy(T1[0:8, 0:128], PR[0:8, 0:128])
            p.dma("sync", scr[0:8, xa + tbase: xa + tbase + 128],
                  T1[0:8, 0:128])
        for e in range(8):
            p.dma("sync", row_mr[0:1, 0:HCH], scr[e:e + 1, xa:xa + HCH])
            for tb in range(HCH // 512):
                ta = tb * 512
                p.mm(P2[:, 0:512], ones_row[0:1, :], row_mr[0:1, ta:ta + 512])
                p.copy(WRB[:, 0:512], P2[:, 0:512])
                for h in range(32):
                    r0_ = (e * 32 + h) * 128
                    p.dma("gpsimd", W1B[:, :], din["w1L2"][r0_:r0_ + 128, :])
                    ps = [P0, P1][h % 2]
                    p.mm(ps[:, 0:512],
                         [W1B[:, k * 128:(k + 1) * 128] for k in range(8)],
                         [HBb[:, k * HCH + ta: k * HCH + ta + 512]
                          for k in range(8)])
                    p.act(T1[:, 0:512], ps[:, 0:512], AF.Gelu,
                          bias=gc["b1c2"][:, e * 32 + h: e * 32 + h + 1])
                    p.tt(HID[:, h * 512:(h + 1) * 512], T1[:, 0:512],
                         WRB[:, 0:512], ALU.mult)
                for d in range(8):
                    r0_ = (e * 8 + d) * 128
                    ps = [P2, P3][d % 2]
                    for half in range(2):
                        hb = half * 16
                        p.dma("gpsimd", W2B[:, :],
                              din["w2L2"][r0_:r0_ + 128,
                                          hb * 128:(hb + 16) * 128])
                        p.mm_more(ps[:, 0:512],
                                  [W2B[:, h * 128:(h + 1) * 128]
                                   for h in range(16)],
                                  [HID[:, (hb + h) * 512:(hb + h + 1) * 512]
                                   for h in range(16)],
                                  start=(half == 0), stop=(half == 1))
                    p.tt(HRES[:, d * HCH + ta: d * HCH + ta + 512],
                         HRES[:, d * HCH + ta: d * HCH + ta + 512],
                         ps[:, 0:512], ALU.add)
        for d in range(8):
            p.dma("sync", B2S[0:8, 0:128],
                  din["b2nat"][0:8, d * 128:(d + 1) * 128])
            for tb in range(HCH // 512):
                ta = tb * 512
                ps = [P0, P1][tb % 2]
                def eb2(t, ps=ps):
                    return t.matmul(ps[:, 0:512], B2S[0:8, 0:128],
                                    T1[0:8, 0:512], start=True,
                                    stop=True)
                p.dma("sync", T1[0:8, 0:512],
                      scr[0:8, xa + ta: xa + ta + 512])
                p.rec("tensor", eb2, [B2S, T1], [ps])
                p.tt(HRES[:, d * HCH + ta: d * HCH + ta + 512],
                     HRES[:, d * HCH + ta: d * HCH + ta + 512],
                     ps[:, 0:512], ALU.add)
        p.act(HRES[:, :], HRES[:, :], AF.Silu)
        p.tt(XATT[:, :], XATT[:, :], HRES[:, :], ALU.mult)
        for k in range(8):
            p.dma("sync", BUF1[:, k * HCH:(k + 1) * HCH],
                  din["x1T"][k * 128:(k + 1) * 128, xa:xa + HCH])
        p.tt(BUF1[:, :], BUF1[:, :], XATT[:, :], ALU.add)
        for k in range(8):
            p.dma("sync", dout["x2T"][k * 128:(k + 1) * 128, xa:xa + HCH],
                  BUF1[:, k * HCH:(k + 1) * HCH])
        feat_stats(p, BUF1, 8, HCH, ones_col, sq, psA, psB, row_sum, row_ssq)
        ln_rows(p, HCH, row_sum, row_ssq, row_rs, row_mr)
        apply_ln(p, BUF1, BUF1, 8, HCH, ones_row, row_rs, row_mr, P0, P1)
        for k in range(8):
            p.dma("sync", dout["xn2T"][k * 128:(k + 1) * 128, xa:xa + HCH],
                  BUF1[:, k * HCH:(k + 1) * HCH])
    return p.finalize()


def host_l6a(inputs, state):
    nc = get_nc("l6a", build_l6a)
    w1 = np.asarray(inputs["L1_ffn_w1"], np.float32)
    w2 = np.asarray(inputs["L1_ffn_w2"], np.float32)
    w1T = w1.transpose(0, 2, 1)
    w1L = np.zeros((8, 32, 128, 8 * 128), np.float32)
    w1r = w1T.reshape(8, 8, 128, 32, 128)
    for h in range(32):
        for k in range(8):
            w1L[:, h, :, k * 128:(k + 1) * 128] = w1r[:, k, :, h]
    w2T = w2.transpose(0, 2, 1)
    w2L = np.zeros((8, 8, 128, 32 * 128), np.float32)
    w2r = w2T.reshape(8, 32, 128, 8, 128)
    for d in range(8):
        for h in range(32):
            w2L[:, d, :, h * 128:(h + 1) * 128] = w2r[:, h, :, d]
    b1 = np.asarray(inputs["L1_ffn_b1"], np.float32)
    b1c2 = np.zeros((128, 8 * 32), np.float32)
    for e in range(8):
        b1c2[:, e * 32:(e + 1) * 32] = b1[e].reshape(32, 128).T
    com = dict(
        wo1=_f(inputs["L1_attn_wo"]), w1f=_f(inputs["L1_attn_w1"]),
        w2f=_f(inputs["L1_attn_w2"]),
        b1fc=_f(inputs["L1_attn_b1"]).reshape(32, 128).T.copy(),
        b2fc=_f(inputs["L1_attn_b2"]).reshape(8, 128).T.copy(),
        g2c=_f(inputs["L1_attn_g2"]).reshape(8, 128).T.copy(),
        be2c=_f(inputs["L1_attn_beta2"]).reshape(8, 128).T.copy(),
        woT2=_t(inputs["L1_ffn_wo"]),
        mnwc=_f(inputs["L1_ffn_moe_norm_w"]).reshape(8, 128).T.copy(),
        gwT=_t(inputs["L1_ffn_gate_w"]),
        w1L2=np.ascontiguousarray(w1L.reshape(8 * 32 * 128, 8 * 128)),
        w2L2=np.ascontiguousarray(w2L.reshape(8 * 8 * 128, 32 * 128)),
        b1c2=b1c2, b2nat=_f(inputs["L1_ffn_b2"]),
        idf32=np.eye(128, dtype=np.float32),
    )
    in_maps = []
    for c in range(NCORES):
        m = dict(com)
        for nm, key in [("xn1T", "xn1T"), ("x1T", "x1T"),
                        ("apT", "apT"), ("aoT", "aoT")]:
            arr = state[key][:, c * TOK:(c + 1) * TOK]
            m[nm] = np.ascontiguousarray(arr).astype(
                BF if nm in ("apT", "aoT") else np.float32)
        in_maps.append(m)
    res = run8(nc, in_maps)
    state["x2T"] = np.concatenate([res[c]["x2T"] for c in range(NCORES)], 1)
    state["xn2T"] = np.concatenate([res[c]["xn2T"] for c in range(NCORES)], 1)
    return state




def build_l6b():
    import os
    SKIP = os.environ.get("L6B_SKIP", "")
    p = Prog("l6b")
    for name, shape, dt in [
        ("xn2T", [D, TOK], F32),
        ("qwT", [D, D], F32), ("kwT", [D, D], F32), ("vwT", [D, D], F32),
        ("qbc", [128, 8], F32), ("kbc", [128, 8], F32), ("vbc", [128, 8], F32),
        ("qgc", [128, 8], F32), ("qBc", [128, 8], F32),
        ("kgc", [128, 8], F32), ("kBc", [128, 8], F32),
        ("vgc", [128, 8], F32), ("vBc", [128, 8], F32),
        ("idf32", [128, 128], F32),
    ]:
        p.inp(name, shape, dt)
    p.out("q3T", [D, TOK], BF16)
    p.out("k3T", [D, TOK], BF16)
    p.out("v3T", [D, TOK], BF16)
    p.out("xqT", [D, TOK], BF16)
    din, dout = p.din, p.dout

    ones_col = p.sb("ones_col", [128, 1], F32)
    ones_row = p.sb("ones_row", [1, 128], F32)
    p.memset(ones_col[:, :], 1.0)
    p.memset(ones_row[:, :], 1.0)
    idf = p.sb("idf", [128, 128], F32)
    p.dma("sync", idf[:, :], din["idf32"])
    idb = p.sb("idb", [128, 128], BF16)
    p.copy(idb[:, :], idf[:, :])
    HCH = 1024
    XN2 = p.sb("XN2", [128, 8 * HCH], F32)
    XNN = p.sb("XNN", [128, 8 * HCH], F32)
    QF = p.sb("QF", [128, 8 * HCH], F32)
    NB_ = p.sb("NBB", [128, 8 * HCH], BF16)
    WA = p.sb("WA", [128, 8 * D], BF16)
    T1 = p.sb("T1", [128, 512], F32)
    QT = p.sb("QT", [128, 1024], F32)
    QS = p.sb("QSc", [128, 64], F32)
    QR = p.sb("QRc", [128, 64], F32)
    QC1 = p.sb("QC1c", [128, 1024], F32)
    QC2 = p.sb("QC2c", [128, 1024], F32)
    QO = p.sb("QOc", [128, 1024], BF16)
    sq0_ = p.sb("sq0", [128, 512], F32)
    sq = [sq0_, sq0_]
    row_sum = p.sb("row_sum", [1, HCH], F32)
    row_ssq = p.sb("row_ssq", [1, HCH], F32)
    row_rs = p.sb("row_rs", [1, HCH], F32)
    row_mr = p.sb("row_mr", [1, HCH], F32)
    gc = {}
    for nm in ["qbc", "kbc", "vbc", "qgc", "qBc", "kgc", "kBc", "vgc", "vBc"]:
        gc[nm] = p.sb(nm + "_s", [128, 8], F32)
        p.dma("sync", gc[nm][:, :], din[nm])
    psA = p.ps("psA", [1, 512])
    psB = p.ps("psB", [1, 512])
    P0 = p.ps("P0")
    P1 = p.ps("P1")
    P2 = p.ps("P2")
    P3 = p.ps("P3")

    for hf in range(2):
        xa = hf * HCH
        for k in range(8):
            p.dma("sync", XN2[:, k * HCH:(k + 1) * HCH],
                  din["xn2T"][k * 128:(k + 1) * 128, xa:xa + HCH])
        feat_stats(p, XN2, 8, HCH, ones_col, sq, psA, psB, row_sum, row_ssq)
        ln_rows(p, HCH, row_sum, row_ssq, row_rs, row_mr)
        apply_ln(p, XN2, XNN, 8, HCH, ones_row, row_rs, row_mr, P0, P1)

        plist = [] if "proj" in SKIP else [
                ("qwT", "qbc", "qgc", "qBc", "q3T", True),
                ("kwT", "kbc", "kgc", "kBc", "k3T", True),
                ("vwT", "vbc", "vgc", "vBc", "v3T", False)]
        for (wn, bn, gn, Bn, on, do_norm) in plist:
            for k in range(8):
                sl = slice(k * HCH, (k + 1) * HCH)
                p.ts(QF[:, sl], XNN[:, sl], gc[gn][:, k:k + 1], None, ALU.mult)
                p.ts(QF[:, sl], QF[:, sl], gc[Bn][:, k:k + 1], None, ALU.add)
            p.copy(NB_[:, :], QF[:, :], eng="scalar")
            loadW(p, WA, din[wn])
            proj(p, WA, NB_, QF, [P0, P1], 8, 8, HCH, bias=gc[bn])
            if do_norm:
                for (a, b) in col_splits(HCH, 512):
                    w = b - a
                    for k in range(8):
                        p.act(sq[0][:, 0:w], QF[:, k * HCH + a: k * HCH + b],
                              AF.Square)
                        p.mm_more(psA[0:1, 0:w], ones_col[:, 0:1],
                                  sq[0][:, 0:w], start=(k == 0),
                                  stop=(k == 7))
                    p.copy(row_sum[0:1, a:b], psA[0:1, 0:w])
                p.act(row_sum[0:1, 0:HCH], row_sum[0:1, 0:HCH], AF.Sqrt)
                p.ts(row_sum[0:1, 0:HCH], row_sum[0:1, 0:HCH], 1e-6, None,
                     ALU.max)
                p.recip(row_rs[0:1, 0:HCH], row_sum[0:1, 0:HCH])
                apply_ln(p, QF, QF, 8, HCH, ones_row, row_rs, None, P2, P3)
            p.copy(NB_[:, :], QF[:, :], eng="scalar")
            for k in range(8):
                p.dma("sync", dout[on][k * 128:(k + 1) * 128, xa:xa + HCH],
                      NB_[:, k * HCH:(k + 1) * HCH])

        for tt_ in range([] if False else (0 if "quant" in SKIP else HCH // 128)):
            tb = tt_ * 128
            for k in range(8):
                p.transpose(P2[:, 0:128], XN2[:, k * HCH + tb: k * HCH + tb + 128],
                            idf[:, :])
                p.copy(QT[:, k * 128:(k + 1) * 128], P2[:, 0:128])
            if "fp4" not in SKIP:
                fp4_qd_generic(p, QT, QS, QR, QC1, QC2, None, 128, 1024,
                               out_f32=True)
            else:
                p.copy(QC2[:, :], QT[:, :])
            for k in range(8):
                if "back" in SKIP:
                    continue
                p.transpose(P3[:, 0:128], QC2[:, k * 128:(k + 1) * 128],
                            idf[:, :])
                p.copy(NB_[:, k * HCH + tb: k * HCH + tb + 128],
                       P3[:, 0:128])
        for k in range(8):
            p.dma("sync", dout["xqT"][k * 128:(k + 1) * 128, xa:xa + HCH],
                  NB_[:, k * HCH:(k + 1) * HCH])
    return p.finalize()


def loadW(p, WA, dsrc):
    for k in range(8):
        p.dma("gpsimd", WA[:, k * D:(k + 1) * D], dsrc[k * 128:(k + 1) * 128, :])


def fp4_qd_generic(p, A, Srec, R, C1, C2, O, rows, colw, out_f32=False):
    nblk = colw // 16
    rs, cw, nb = slice(0, rows), slice(0, colw), slice(0, nblk)
    p.reduce(Srec[rs, nb], A[rs, cw].rearrange("p (g b) -> p g b", b=16),
             ALU.max, absval=True)
    p.ts(Srec[rs, nb], Srec[rs, nb], 1.0 / 6.0, 1e-12, ALU.mult, ALU.max)
    p.recip(R[rs, nb], Srec[rs, nb])
    p.act(C1[rs, cw], A[rs, cw], AF.Abs)
    rb = R[rs, nb].rearrange("p (g one) -> p g one", one=1) \
        .broadcast_to([rows, nblk, 16])
    p.tt(C1[rs, cw].rearrange("p (g b) -> p g b", b=16),
         C1[rs, cw].rearrange("p (g b) -> p g b", b=16), rb, ALU.mult)
    p.ts(C2[rs, cw], C1[rs, cw], 0.25, None, ALU.is_gt)
    for m in (0.75, 1.25, 1.75):
        p.stt(C2[rs, cw], C1[rs, cw], m, C2[rs, cw], ALU.is_gt, ALU.add)
    p.ts(C2[rs, cw], C2[rs, cw], 0.5, None, ALU.mult)
    for m in (2.5, 3.5, 5.0, 5.0):
        p.stt(C2[rs, cw], C1[rs, cw], m, C2[rs, cw], ALU.is_gt, ALU.add)
    sb_ = Srec[rs, nb].rearrange("p (g one) -> p g one", one=1) \
        .broadcast_to([rows, nblk, 16])
    p.tt(C2[rs, cw].rearrange("p (g b) -> p g b", b=16),
         C2[rs, cw].rearrange("p (g b) -> p g b", b=16), sb_, ALU.mult)
    p.act(C1[rs, cw], A[rs, cw], AF.Sign)
    if out_f32:
        p.tt(C2[rs, cw], C2[rs, cw], C1[rs, cw], ALU.mult)
    else:
        p.tt(O[rs, cw], C2[rs, cw], C1[rs, cw], ALU.mult)


def host_l6b(inputs, state):
    nc = get_nc("l6b", build_l6b)

    def col8(v):
        return _f(v).reshape(8, 128).T.copy()

    com = dict(
        qwT=_t(inputs["L2_attn_q_w"]), kwT=_t(inputs["L2_attn_k_w"]),
        vwT=_t(inputs["L2_attn_v_w"]),
        qbc=col8(inputs["L2_attn_q_b"]), kbc=col8(inputs["L2_attn_k_b"]),
        vbc=col8(inputs["L2_attn_v_b"]),
        qgc=col8(inputs["L2_attn_ln_q_g"]), qBc=col8(inputs["L2_attn_ln_q_b"]),
        kgc=col8(inputs["L2_attn_ln_k_g"]), kBc=col8(inputs["L2_attn_ln_k_b"]),
        vgc=col8(inputs["L2_attn_ln_v_g"]), vBc=col8(inputs["L2_attn_ln_v_b"]),
        idf32=np.eye(128, dtype=np.float32),
    )
    in_maps = []
    for c in range(NCORES):
        m = dict(com)
        m["xn2T"] = np.ascontiguousarray(state["xn2T"][:, c * TOK:(c + 1) * TOK])
        in_maps.append(m)
    res = run8(nc, in_maps)
    for nm in ["q3T", "k3T", "v3T", "xqT"]:
        state[nm] = np.concatenate(
            [np.asarray(res[c][nm], np.float32) for c in range(NCORES)], 1)
    return state




def build_l7():
    p = Prog("l7")
    for name in ["q3", "k3", "v3"]:
        p.inp(name, [256, S], BF16)
    p.inp("betap", [1, 16], F32)
    p.inp("selh", [4, 128], F32)
    p.inp("idf32", [128, 128], F32)
    p.out("o3_o", [256, S], BF16)
    din, dout = p.din, p.dout

    ones_row = p.sb("ones_row", [1, 128], F32)
    ones_bf = p.sb("ones_bf", [128, 1], BF16)
    p.memset(ones_row[:, :], 1.0)
    p.memset(ones_bf[:, :], 1.0)
    idf = p.sb("idf", [128, 128], F32)
    p.dma("sync", idf[:, :], din["idf32"])
    idb = p.sb("idb", [128, 128], BF16)
    p.copy(idb[:, :], idf[:, :])
    QH = p.sb("QH", [64, S], BF16)
    KH = p.sb("KH", [64, S], BF16)
    VH = p.sb("VH", [64, S], BF16)
    VTOK = p.sb("VTOK", [128, 64 * 64], BF16)
    ET = p.sb("ET", [128, 512], BF16)
    OF = p.sb("OF", [64, 512], F32)
    DR = p.sb("DR", [1, 512], F32)
    OST = p.sb("OST", [64, S], BF16)
    BR = p.sb("BR", [1, 16], F32)
    BC = p.sb("BC", [64, 1], F32)
    SEL = p.sb("SEL", [4, 128], F32)
    T0 = p.sb("T0", [1, 16], F32)
    PP = [p.ps("PP%d" % i) for i in range(8)]
    PPB5 = PP[5][:, 0:256].bitcast(BF16)

    p.dma("sync", SEL[0:4, :], din["selh"])
    p.dma("sync", BR[0:1, :], din["betap"])
    p.act(T0[0:1, :], BR[0:1, :], AF.Abs)
    p.ts(T0[0:1, :], T0[0:1, :], -1.0, None, ALU.mult)
    p.act(T0[0:1, :], T0[0:1, :], AF.Exp)
    p.act(T0[0:1, :], T0[0:1, :], AF.Ln, bias=1.0)
    p.act(BR[0:1, :], BR[0:1, :], AF.Relu)
    p.tt(BR[0:1, :], BR[0:1, :], T0[0:1, :], ALU.add)
    p.ts(BR[0:1, :], BR[0:1, :], 0.1, None, ALU.max)
    p.ts(BR[0:1, :], BR[0:1, :], 10.0, None, ALU.min)
    p.ts(BR[0:1, :], BR[0:1, :], 0.125, None, ALU.mult)

    for h in range(4):
        r0 = h * 64
        p.dma("sync", QH[:, :], din["q3"][r0:r0 + 64, :])
        p.dma("sync", KH[:, :], din["k3"][r0:r0 + 64, :])
        p.dma("sync", VH[:, :], din["v3"][r0:r0 + 64, :])
        p.rec("tensor",
              lambda t, h=h: t.matmul(PP[3][0:64, 0:1], ones_row[0:1, 0:64],
                                      BR[0:1, h:h + 1], start=True, stop=True),
              [ones_row, BR], [PP[3]])
        p.copy(BC[0:64, 0:1], PP[3][0:64, 0:1])
        p.ts(QH[:, :], QH[:, :], BC[0:64, 0:1], None, ALU.mult)
        for c in range(64):
            p.transpose(PPB5[0:128, 0:64], VH[:, c * 128:(c + 1) * 128],
                        idb[0:64, 0:64])
            p.copy(VTOK[:, c * 64:(c + 1) * 64], PPB5[0:128, 0:64])
        for qb in range(NQB):
            Q0b = qb * 512
            first = True
            for ck in range(64):
                K0 = ck * 128
                ps = PP[(ck % 2)]
                p.mm(ps[:, 0:512], KH[:, K0:K0 + 128], QH[:, Q0b:Q0b + 512])
                p.act(ET[:, :], ps[:, 0:512], AF.Exp)
                p.mm_more(PP[6][0:64, 0:512], VTOK[:, ck * 64:(ck + 1) * 64],
                          ET[:, :], start=first, stop=(ck == 63))
                p.mm_more(PP[7][0:1, 0:512], ones_bf[:, 0:1], ET[:, :],
                          start=first, stop=(ck == 63))
                first = False
            p.recip(DR[0:1, :], PP[7][0:1, 0:512])
            p.mm(PP[3][0:64, 0:512], ones_row[0:1, 0:64], DR[0:1, 0:512])
            p.copy(OF[0:64, :], PP[6][0:64, 0:512])
            p.tt(OST[:, Q0b:Q0b + 512], OF[0:64, :], PP[3][0:64, 0:512],
                 ALU.mult)
        p.dma("sync", dout["o3_o"][r0:r0 + 64, :], OST[:, :])
    return p.finalize()


def host_l7(inputs, state):
    nc = get_nc("l7", build_l7)
    com = dict(idf32=np.eye(128, dtype=np.float32),
               betap=_f(inputs["L2_attn_beta_param"]).reshape(1, 16),
               selh=np.zeros((4, 128), np.float32))
    in_maps = []
    for c in range(NCORES):
        b, hg = c // 4, c % 4
        rows = slice(hg * 256, hg * 256 + 256)
        cols = slice(b * S, (b + 1) * S)
        m = dict(com)
        bp = _f(inputs["L2_attn_beta_param"]).reshape(16)
        m["betap"] = np.ascontiguousarray(
            np.concatenate([bp[hg * 4:hg * 4 + 4],
                            np.zeros(12, np.float32)]).reshape(1, 16))
        for nm, key in [("q3", "q3T"), ("k3", "k3T"), ("v3", "v3T")]:
            m[nm] = np.ascontiguousarray(state[key][rows, cols]).astype(BF)
        in_maps.append(m)
    res = run8(nc, in_maps)
    o3 = np.zeros((NB, D, S), np.float32)
    for c in range(NCORES):
        b, hg = c // 4, c % 4
        o3[b, hg * 256:hg * 256 + 256] = np.asarray(res[c]["o3_o"], np.float32)
    state["o3T"] = np.ascontiguousarray(np.concatenate([o3[0], o3[1]], 1))
    return state



def build_l8():
    p = Prog("l8")
    for name, shape, dt in [
        ("xn2T", [D, TOK], F32), ("x2T", [D, TOK], F32),
        ("o3T", [D, TOK], BF16), ("xqT", [D, TOK], BF16),
        ("owT", [D, D], F32), ("obc", [128, 8], F32),
        ("wgq", [D, 2816], BF16), ("wuq", [D, 2816], BF16),
        ("wdqT", [2816, D], BF16),
        ("wjT", [D, D], F32), ("wjN", [D, D], F32), ("wjvT", [D, D], F32),
        ("gamc", [128, 8], F32), ("betc", [128, 8], F32),
        ("muN", [D, 256], F32), ("scl", [1, 1], F32),
        ("idf32", [128, 128], F32),
    ]:
        p.inp(name, shape, dt)
    p.out("x3T", [D, TOK], F32)
    p.out("xn3T", [D, TOK], F32)
    p.out("sc4", [1, TOK], F32)
    p.out("valT", [D, TOK], BF16)
    p.out("xninT", [D, TOK], BF16)
    p.out("lgT", [TOK, 256], F32)
    din, dout = p.din, p.dout

    ones_col = p.sb("ones_col", [128, 1], F32)
    ones_row = p.sb("ones_row", [1, 128], F32)
    p.memset(ones_col[:, :], 1.0)
    p.memset(ones_row[:, :], 1.0)
    idf = p.sb("idf", [128, 128], F32)
    p.dma("sync", idf[:, :], din["idf32"])
    HCH = 1024
    XN2 = p.sb("XN2", [128, 8 * HCH], F32)
    SS = p.sb("SS", [128, 8 * HCH], F32)
    FF = p.sb("FF", [128, 8 * HCH], F32)
    XB = p.sb("XB", [128, 8 * HCH], BF16)
    HQ = p.sb("HQB", [128, 22 * 512], BF16)
    QS = p.sb("QSc", [128, 176], F32)
    QR = p.sb("QRc", [128, 176], F32)
    QC2 = p.sb("QC2c", [128, 2816], F32)
    WA = p.sb("WA", [128, 8 * D], BF16)
    WG = p.sb("WGc", [128, 8 * 512], BF16)
    T1 = p.sb("T1", [128, 512], F32)
    MUU = p.sb("MUU", [128, 8 * 256], BF16)
    sq0_ = p.sb("sq0", [128, 512], F32)
    sq = [sq0_, sq0_]
    row_sum = p.sb("row_sum", [1, HCH], F32)
    row_ssq = p.sb("row_ssq", [1, HCH], F32)
    row_rs = p.sb("row_rs", [1, HCH], F32)
    row_mr = p.sb("row_mr", [1, HCH], F32)
    mrow = p.sb("mrow", [1, 256], F32)
    QT = XN2[:, 0:2816]
    XUB = HQ[:, 0:8 * HCH]
    QC1 = XN2[:, 2816 + 256:2816 + 256 + 2816]
    MUF = FF[:, 0:8 * 256]
    gc = {}
    for nm in ["obc", "gamc", "betc"]:
        gc[nm] = p.sb(nm + "_s", [128, 8], F32)
        p.dma("sync", gc[nm][:, :], din[nm])
    scl_s = p.sb("scl_s", [1, 1], F32)
    p.dma("sync", scl_s[:, :], din["scl"])
    psA = p.ps("psA", [1, 512])
    psB = p.ps("psB", [1, 512])
    P0 = p.ps("P0")
    P1 = p.ps("P1")
    P2 = p.ps("P2")
    P3 = p.ps("P3")
    PR = p.ps("PR", [128, 256])

    for k in range(8):
        p.dma("sync", MUF[:, k * 256:(k + 1) * 256],
              din["muN"][k * 128:(k + 1) * 128, :])
    for k in range(8):
        p.act(sq[0][:, 0:256], MUF[:, k * 256:(k + 1) * 256], AF.Square)
        p.mm_more(psA[0:1, 0:256], ones_col[:, 0:1], sq[0][:, 0:256],
                  start=(k == 0), stop=(k == 7))
    p.copy(mrow[0:1, 0:256], psA[0:1, 0:256])
    p.act(mrow[0:1, 0:256], mrow[0:1, 0:256], AF.Sqrt)
    p.ts(mrow[0:1, 0:256], mrow[0:1, 0:256], 1e-12, None, ALU.max)
    p.recip(row_mr[0:1, 0:256], mrow[0:1, 0:256])
    p.mm(PR[:, 0:256], ones_row[0:1, :], row_mr[0:1, 0:256])
    for k in range(8):
        p.tt(MUU[:, k * 256:(k + 1) * 256], MUF[:, k * 256:(k + 1) * 256],
             PR[:, 0:256], ALU.mult)

    for hf in range(2):
        xa = hf * HCH
        for k in range(8):
            p.dma("sync", XN2[:, k * HCH:(k + 1) * HCH],
                  din["xn2T"][k * 128:(k + 1) * 128, xa:xa + HCH])
            p.dma("sync", XB[:, k * HCH:(k + 1) * HCH],
                  din["o3T"][k * 128:(k + 1) * 128, xa:xa + HCH])
        loadW(p, WA, din["owT"])
        proj(p, WA, XB, SS, [P0, P1], 8, 8, HCH, bias=gc["obc"])
        p.tt(SS[:, :], SS[:, :], XN2[:, :], ALU.add)
        for k in range(8):
            p.dma("sync", XB[:, k * HCH:(k + 1) * HCH],
                  din["xqT"][k * 128:(k + 1) * 128, xa:xa + HCH])
        for tt_ in range(HCH // 128):
            tb = tt_ * 128
            for gu in range(2):
                wname = "wgq" if gu == 0 else "wuq"
                for hs in range(6):
                    ha = hs * 512
                    hw = min(512, 2816 - ha)
                    p.dma("gpsimd",
                          WG[:, 0:8 * hw].rearrange("p (k n) -> p k n", n=hw),
                          din[wname][:, ha:ha + hw]
                          .rearrange("(k p) n -> p k n", p=128))
                    ps = [P0, P1][gu]
                    p.mm(ps[:, 0:hw],
                         [XB[:, k * HCH + tb: k * HCH + tb + 128]
                          for k in range(8)],
                         [WG[:, k * hw:(k + 1) * hw] for k in range(8)])
                    if gu == 0:
                        p.act(QT[:, ha:ha + hw], ps[:, 0:hw], AF.Silu)
                    else:
                        p.tt(QT[:, ha:ha + hw], QT[:, ha:ha + hw],
                             ps[:, 0:hw], ALU.mult)
            fp4_qd_generic(p, QT, QS, QR, QC1, QC2, None, 128, 2816,
                           out_f32=True)
            for k in range(22):
                p.transpose(P2[:, 0:128], QC2[:, k * 128:(k + 1) * 128],
                            idf[:, :])
                p.copy(HQ[:, k * 512 + tt_ % 4 * 128:
                          k * 512 + tt_ % 4 * 128 + 128], P2[:, 0:128])
            if tt_ % 4 == 3:
                ta = (tt_ - 3) * 128
                for d in range(8):
                    p.dma("gpsimd",
                          WA[:, 0:22 * 128]
                          .rearrange("p (k n) -> p k n", n=128),
                          din["wdqT"][:, d * 128:(d + 1) * 128]
                          .rearrange("(k p) n -> p k n", p=128))
                    ps = [P2, P3][d % 2]
                    p.mm(ps[:, 0:512],
                         [WA[:, h * 128:(h + 1) * 128] for h in range(22)],
                         [HQ[:, h * 512:(h + 1) * 512] for h in range(22)])
                    p.copy(FF[:, d * HCH + ta: d * HCH + ta + 512],
                           ps[:, 0:512])
        p.act(FF[:, :], FF[:, :], AF.Silu)
        p.tt(FF[:, :], FF[:, :], SS[:, :], ALU.mult)
        for k in range(8):
            p.dma("sync", XN2[:, k * HCH:(k + 1) * HCH],
                  din["x2T"][k * 128:(k + 1) * 128, xa:xa + HCH])
        p.tt(XN2[:, :], XN2[:, :], FF[:, :], ALU.add)
        for k in range(8):
            p.dma("sync", dout["x3T"][k * 128:(k + 1) * 128, xa:xa + HCH],
                  XN2[:, k * HCH:(k + 1) * HCH])
        feat_stats(p, XN2, 8, HCH, ones_col, sq, psA, psB, row_sum, row_ssq)
        ln_rows(p, HCH, row_sum, row_ssq, row_rs, row_mr)
        apply_ln(p, XN2, XN2, 8, HCH, ones_row, row_rs, row_mr, P0, P1)
        for k in range(8):
            p.dma("sync", dout["xn3T"][k * 128:(k + 1) * 128, xa:xa + HCH],
                  XN2[:, k * HCH:(k + 1) * HCH])
        XN3 = XN2
        p.copy(XB[:, :], XN3[:, :], eng="scalar")
        loadW(p, WA, din["wjT"])
        proj(p, WA, XB, FF, [P0, P1], 8, 8, HCH)
        p.copy(XUB[:, :], FF[:, :], eng="scalar")
        loadW(p, WA, din["wjN"])
        proj(p, WA, XUB, FF, [P2, P3], 8, 8, HCH)
        p.tt(FF[:, :], FF[:, :], XN3[:, :], ALU.mult)
        for (a, b) in col_splits(HCH, 512):
            w = b - a
            p.mm(psA[0:1, 0:w], [ones_col[:, 0:1]] * 8,
                 [FF[:, k * HCH + a: k * HCH + b] for k in range(8)])
            p.ts(row_sum[0:1, a:b], psA[0:1, 0:w], 1.0 / 32.0, None, ALU.mult)
        p.dma("sync", dout["sc4"][0:1, xa:xa + HCH], row_sum[0:1, 0:HCH])
        loadW(p, WA, din["wjvT"])
        proj(p, WA, XB, XUB, [P0, P1], 8, 8, HCH)
        for k in range(8):
            p.dma("sync", dout["valT"][k * 128:(k + 1) * 128, xa:xa + HCH],
                  XUB[:, k * HCH:(k + 1) * HCH])
        feat_stats(p, XN3, 8, HCH, ones_col, sq, psA, psB, row_sum, row_ssq)
        ln_rows(p, HCH, row_sum, row_ssq, row_rs, row_mr)
        apply_ln(p, XN3, FF, 8, HCH, ones_row, row_rs, row_mr, P0, P1,
                 gain_cols=gc["gamc"], bias_cols=gc["betc"])
        p.copy(XB[:, :], FF[:, :], eng="scalar")
        for k in range(8):
            p.dma("sync", dout["xninT"][k * 128:(k + 1) * 128, xa:xa + HCH],
                  XB[:, k * HCH:(k + 1) * HCH])
        for (a, b) in col_splits(HCH, 512):
            w = b - a
            for k in range(8):
                p.act(sq[0][:, 0:w], FF[:, k * HCH + a: k * HCH + b],
                      AF.Square)
                p.mm_more(psA[0:1, 0:w], ones_col[:, 0:1], sq[0][:, 0:w],
                          start=(k == 0), stop=(k == 7))
            p.copy(row_sum[0:1, a:b], psA[0:1, 0:w])
        p.act(row_sum[0:1, 0:HCH], row_sum[0:1, 0:HCH], AF.Sqrt)
        p.ts(row_sum[0:1, 0:HCH], row_sum[0:1, 0:HCH], 1e-12, None, ALU.max)
        p.recip(row_rs[0:1, 0:HCH], row_sum[0:1, 0:HCH])
        p.ts(row_rs[0:1, 0:HCH], row_rs[0:1, 0:HCH], scl_s[0:1, 0:1], None,
             ALU.mult)
        apply_ln(p, FF, FF, 8, HCH, ones_row, row_rs, None, P2, P3)
        p.copy(XUB[:, :], FF[:, :], eng="scalar")
        for i in range(HCH // 128):
            tbase = i * 128
            p.mm(PR[:, 0:256],
                 [XUB[:, k * HCH + tbase: k * HCH + tbase + 128]
                  for k in range(8)],
                 [MUU[:, k * 256:(k + 1) * 256] for k in range(8)])
            p.copy(T1[:, 0:256], PR[:, 0:256])
            p.dma("sync", dout["lgT"][xa + tbase: xa + tbase + 128, :],
                  T1[:, 0:256])
    return p.finalize()


def host_l8(inputs, state):
    nc = get_nc("l8", build_l8)
    wgq = state["wgq"].astype(np.float32)
    wuq = state["wuq"].astype(np.float32)
    wdq = state["wdq"].astype(np.float32)
    com = dict(
        owT=_t(inputs["L2_attn_o_w"]),
        obc=_f(inputs["L2_attn_o_b"]).reshape(8, 128).T.copy(),
        wgq=np.ascontiguousarray(wgq.T).astype(BF),
        wuq=np.ascontiguousarray(wuq.T).astype(BF),
        wdqT=np.ascontiguousarray(wdq.T).astype(BF),
        wjT=_t(inputs["L3_attn_wj_w"]), wjN=_f(inputs["L3_attn_wj_w"]),
        wjvT=_t(inputs["L3_attn_wjv_w"]),
        gamc=_f(inputs["L3_ffn_ln_gamma"]).reshape(8, 128).T.copy(),
        betc=_f(inputs["L3_ffn_ln_beta"]).reshape(8, 128).T.copy(),
        muN=np.ascontiguousarray(
            np.asarray(inputs["L3_ffn_mu"], np.float32).reshape(D, 256)),
        scl=_f(inputs["L3_ffn_scale"]).reshape(1, 1),
        idf32=np.eye(128, dtype=np.float32),
    )
    in_maps = []
    for c in range(NCORES):
        m = dict(com)
        for nm in ["xn2T", "x2T"]:
            m[nm] = np.ascontiguousarray(state[nm][:, c * TOK:(c + 1) * TOK])
        for nm in ["o3T", "xqT"]:
            m[nm] = np.ascontiguousarray(
                state[nm][:, c * TOK:(c + 1) * TOK]).astype(BF)
        in_maps.append(m)
    res = run8(nc, in_maps)
    for nm in ["x3T", "xn3T", "valT", "xninT"]:
        state[nm] = np.concatenate(
            [np.asarray(res[c][nm], np.float32) for c in range(NCORES)], 1)
    state["sc4"] = np.concatenate([res[c]["sc4"] for c in range(NCORES)], 1)
    state["lgT"] = np.concatenate([res[c]["lgT"] for c in range(NCORES)], 0)
    return state




def build_l9():
    p = Prog("l9")
    p.inp("scR", [1, S], F32)
    p.inp("scP", [128, 64], F32)
    p.inp("valS", [256, S], BF16)
    p.inp("lgS", [64, S], F32)
    p.inp("xnintok", [S, D], BF16)
    p.inp("w1S", [4 * 8 * 128, 256], F32)
    p.inp("w2S", [4 * 2 * 128, 1024], F32)
    p.inp("b1S", [128, 4 * 2], F32)
    p.inp("b2S", [128, 4 * 8], F32)
    for v in range(4):
        p.inp("MK%d" % v, [128, 512], F32)
    p.inp("idf32", [128, 128], F32)
    p.out("o4_o", [256, S], BF16)
    p.out("soT", [D, 64], F32)
    din, dout = p.din, p.dout

    ones_row = p.sb("ones_row", [1, 128], F32)
    ones_bf = p.sb("ones_bf", [128, 1], BF16)
    p.memset(ones_row[:, :], 1.0)
    p.memset(ones_bf[:, :], 1.0)
    idf = p.sb("idf", [128, 128], F32)
    p.dma("sync", idf[:, :], din["idf32"])
    idb = p.sb("idb", [128, 128], BF16)
    p.copy(idb[:, :], idf[:, :])
    MK = [p.sb("sMK%d" % v, [128, 512], F32) for v in range(4)]
    for v in range(4):
        p.dma("sync", MK[v][:, :], din["MK%d" % v])
    SCR = p.sb("SCR", [1, S], F32)
    PMX = p.sb("PMX", [1, 2048], F32)
    PMN = p.sb("PMN", [1, 2048], F32)
    RMX = p.sb("RMX", [1, S], F32)
    CARY = p.sb("CARY", [1, 4], F32)
    SCP = p.sb("SCP", [128, 64], F32)
    VH2 = p.sb("VH2", [128, S], BF16)
    VT2 = p.sb("VT2", [128, 64 * 128], BF16)
    SQB = p.sb("SQB", [128, 512], F32)
    MBX = p.sb("MBX", [128, 512], F32)
    ET = p.sb("ET", [128, 512], BF16)
    T1 = p.sb("T1", [128, 512], F32)
    OF = T1
    DR = p.sb("DR", [1, 512], F32)
    OST2 = p.sb("OST2", [128, S], BF16)
    PP = [p.ps("PP%d" % i) for i in range(8)]
    PPB5 = PP[5][:, 0:256].bitcast(BF16)

    p.dma("sync", SCR[0:1, :], din["scR"])
    p.dma("sync", SCP[:, :], din["scP"])
    p.memset(CARY[0:1, :], -3e38)
    for pc_ in range(S // 2048):
        a = pc_ * 2048
        sl = slice(a, a + 2048)
        p.scan(PMX[0:1, :], SCR[0:1, sl], SCR[0:1, sl], CARY[0:1, 0:1],
               ALU.max, ALU.bypass)
        p.copy(CARY[0:1, 0:1], PMX[0:1, 2047:2048])
        p.ts(RMX[0:1, sl], SCR[0:1, sl], -1.0, None, ALU.mult)
        p.scan(PMN[0:1, :], RMX[0:1, sl], RMX[0:1, sl], CARY[0:1, 1:2],
               ALU.max, ALU.bypass)
        p.copy(CARY[0:1, 1:2], PMN[0:1, 2047:2048])
        p.ts(PMN[0:1, :], PMN[0:1, :], -1.0, None, ALU.mult)
        p.tt(PMX[0:1, :], SCR[0:1, sl], PMX[0:1, :], ALU.mult)
        p.tt(PMN[0:1, :], SCR[0:1, sl], PMN[0:1, :], ALU.mult)
        p.tt(PMX[0:1, :], PMX[0:1, :], PMN[0:1, :], ALU.max)
        p.ts(RMX[0:1, sl], PMX[0:1, :], -1.0, None, ALU.mult)

    for half in range(2):
        r0 = half * 128
        p.dma("sync", VH2[:, :], din["valS"][r0:r0 + 128, :])
        for c in range(64):
            p.transpose(PPB5[0:128, 0:128], VH2[:, c * 128:(c + 1) * 128],
                        idb[:, :])
            p.copy(VT2[:, c * 128:(c + 1) * 128], PPB5[0:128, 0:128])
        for qb in range(NQB):
            Q0b = qb * 512
            p.mm(PP[3][:, 0:512], ones_row[0:1, :], SCR[0:1, Q0b:Q0b + 512])
            p.copy(SQB[:, :], PP[3][:, 0:512])
            p.mm(PP[3][:, 0:512], ones_row[0:1, :], RMX[0:1, Q0b:Q0b + 512])
            p.copy(MBX[:, :], PP[3][:, 0:512])
            nk = (Q0b + 512) // 128
            first = True
            for ck in range(nk):
                K0 = ck * 128
                p.ts(T1[:, 0:512], SQB[:, :], SCP[:, ck:ck + 1], None,
                     ALU.mult)
                p.tt(T1[:, 0:512], T1[:, 0:512], MBX[:, :], ALU.add)
                p.act(ET[:, :], T1[:, 0:512], AF.Exp)
                u = (K0 - Q0b) // 128
                if 0 <= u <= 3:
                    p.tt(ET[:, :], ET[:, :], MK[u][:, :], ALU.mult)
                p.mm_more(PP[6][:, 0:512], VT2[:, ck * 128:(ck + 1) * 128],
                          ET[:, :], start=first, stop=(ck == nk - 1))
                p.mm_more(PP[7][0:1, 0:512], ones_bf[:, 0:1], ET[:, :],
                          start=first, stop=(ck == nk - 1))
                first = False
            p.recip(DR[0:1, :], PP[7][0:1, 0:512])
            p.mm(PP[3][:, 0:512], ones_row[0:1, :], DR[0:1, 0:512])
            p.copy(OF[:, :], PP[6][:, 0:512])
            p.tt(OST2[:, Q0b:Q0b + 512], OF[:, :], PP[3][:, 0:512], ALU.mult)
        p.dma("sync", dout["o4_o"][r0:r0 + 128, :], OST2[:, :])

    ETok = VT2[:, 0:64 * 64]
    LG = p.sb("LG", [64, S], F32)
    EXL = p.sb("EXL", [64, S], BF16)
    MC = p.sb("MC", [64, 1], F32)
    DEN = p.sb("DEN", [64, 4], F32)
    DRW = p.sb("DRW", [1, 64], F32)

    XT = p.sb("XT", [128, D], BF16)
    SIf = p.sb("SIf", [128, 8 * 64], F32)
    H1b = p.sb("H1b", [128, 2 * 16], BF16)
    W1Bs = p.sb("W1Bs", [128, 256], BF16)
    W2Bs = p.sb("W2Bs", [128, 128], BF16)
    SO = p.sb("SO", [128, 8 * 64], F32)
    b1c = p.sb("b1c", [128, 4 * 2], F32)
    b2c = p.sb("b2c", [128, 4 * 8], F32)
    p.dma("sync", b1c[:, :], din["b1S"])
    p.dma("sync", b2c[:, :], din["b2S"])
    p.dma("sync", LG[:, :], din["lgS"])
    p.reduce(MC[:, 0:1], LG[:, :], ALU.max, negate=True)
    p.act(EXL[:, :], LG[:, :], AF.Exp, bias=MC[:, 0:1],
          accum_out=DEN[:, 0:1])
    p.copy(DEN[:, 1:2], DEN[:, 0:1])
    p.copy(DEN[:, 2:3], DEN[:, 0:1])
    p.copy(DEN[:, 3:4], DEN[:, 0:1])
    p.transpose(PP[2][0:4, 0:64], DEN[0:64, 0:4], idf[0:64, 0:64])
    p.copy(T1[0:4, 0:64], PP[2][0:4, 0:64])
    p.recip(DRW[0:1, 0:64], T1[0:1, 0:64])
    for c in range(64):
        p.transpose(PPB5[0:128, 0:64], EXL[:, c * 128:(c + 1) * 128],
                    idb[0:64, 0:64])
        p.copy(ETok[:, c * 64:(c + 1) * 64], PPB5[0:128, 0:64])
    for tc in range(64):
        p.dma("sync", XT[:, :], din["xnintok"][tc * 128:(tc + 1) * 128, :])
        for d in range(8):
            p.mm_more(PP[6][:, d * 64:(d + 1) * 64],
                      XT[:, d * 128:(d + 1) * 128],
                      ETok[:, tc * 64:(tc + 1) * 64],
                      start=(tc == 0), stop=(tc == 63))
    p.copy(SIf[:, :], PP[6][:, 0:512])
    p.mm(PP[3][:, 0:64], ones_row[0:1, :], DRW[0:1, 0:64])
    p.copy(T1[:, 0:64], PP[3][:, 0:64])
    for d in range(8):
        p.tt(SIf[:, d * 64:(d + 1) * 64], SIf[:, d * 64:(d + 1) * 64],
             T1[:, 0:64], ALU.mult)
    SIb = p.sb("SIb", [128, 8 * 64], BF16)
    p.copy(SIb[:, :], SIf[:, :], eng="scalar")
    for e in range(4):
        for hc in range(2):
            for k in range(8):
                r0_ = (e * 8 + k) * 128
                p.dma("gpsimd", W1Bs[:, hc * 128:(hc + 1) * 128],
                      din["w1S"][r0_:r0_ + 128, hc * 128:(hc + 1) * 128])
                p.mm_more(PP[0][:, 0:16],
                          W1Bs[:, hc * 128:(hc + 1) * 128],
                          SIb[:, k * 64 + e * 16: k * 64 + e * 16 + 16],
                          start=(k == 0), stop=(k == 7))
            p.act(H1b[:, hc * 16:(hc + 1) * 16], PP[0][:, 0:16], AF.Gelu,
                  bias=b1c[:, e * 2 + hc: e * 2 + hc + 1])
        for d in range(8):
            for hc in range(2):
                r0_ = (e * 2 + hc) * 128
                p.dma("gpsimd", W2Bs[:, 0:128],
                      din["w2S"][r0_:r0_ + 128, d * 128:(d + 1) * 128])
                p.mm_more(PP[1][:, 0:16], W2Bs[:, 0:128],
                          H1b[:, hc * 16:(hc + 1) * 16],
                          start=(hc == 0), stop=(hc == 1))
            p.act(SO[:, d * 64 + e * 16: d * 64 + e * 16 + 16],
                  PP[1][:, 0:16], AF.Identity,
                  bias=b2c[:, e * 8 + d: e * 8 + d + 1])
    for d in range(8):
        p.dma("sync", dout["soT"][d * 128:(d + 1) * 128, :],
              SO[:, d * 64:(d + 1) * 64])
    return p.finalize()


def host_l9(inputs, state):
    nc = get_nc("l9", build_l9)
    w1 = np.asarray(inputs["L3_ffn_w1"], np.float32)
    w2 = np.asarray(inputs["L3_ffn_w2"], np.float32)
    b1 = np.asarray(inputs["L3_ffn_b1"], np.float32)
    b2 = np.asarray(inputs["L3_ffn_b2"], np.float32)
    MKs = []
    for v in range(4):
        MKs.append((v * 128 + np.arange(128)[:, None]
                    <= np.arange(512)[None, :]).astype(np.float32))
    in_maps = []
    for c in range(NCORES):
        b, u = c // 4, c % 4
        cols = slice(b * S, (b + 1) * S)
        es = slice(u * 4, u * 4 + 4)
        m = dict(idf32=np.eye(128, dtype=np.float32))
        for v in range(4):
            m["MK%d" % v] = MKs[v]
        sc = state["sc4"][0, cols]
        m["scR"] = np.ascontiguousarray(sc.reshape(1, S))
        m["scP"] = np.ascontiguousarray(sc.reshape(64, 128).T)
        m["valS"] = np.ascontiguousarray(
            state["valT"][u * 256:(u + 1) * 256, cols]).astype(BF)
        m["lgS"] = np.ascontiguousarray(
            state["lgT"][b * S:(b + 1) * S, u * 64:(u + 1) * 64].T)
        m["xnintok"] = np.ascontiguousarray(
            state["xninT"][:, cols].T).astype(BF)
        m["w1S"] = np.ascontiguousarray(
            w1[es].reshape(4, 8, 128, 256).reshape(4 * 8 * 128, 256))
        m["w2S"] = np.ascontiguousarray(
            w2[es].reshape(4, 2, 128, 1024).reshape(4 * 2 * 128, 1024))
        b1S = np.zeros((128, 8), np.float32)
        b2S = np.zeros((128, 32), np.float32)
        for e in range(4):
            b1S[:, e * 2:(e + 1) * 2] = b1[u * 4 + e].reshape(2, 128).T
            b2S[:, e * 8:(e + 1) * 8] = b2[u * 4 + e].reshape(8, 128).T
        m["b1S"] = b1S
        m["b2S"] = b2S
        in_maps.append(m)
    res = run8(nc, in_maps)
    o4 = np.zeros((NB, D, S), np.float32)
    so = np.zeros((NB, D, 256), np.float32)
    for c in range(NCORES):
        b, u = c // 4, c % 4
        o4[b, u * 256:(u + 1) * 256] = np.asarray(res[c]["o4_o"], np.float32)
        so[b, :, u * 64:(u + 1) * 64] = res[c]["soT"]
    state["o4T"] = np.ascontiguousarray(np.concatenate([o4[0], o4[1]], 1))
    state["soA"] = so
    return state



def build_l10():
    p = Prog("l10")
    for name, shape, dt in [
        ("lgTok", [TOK, 256], F32), ("soTA", [256, D], F32),
        ("o4T", [D, TOK], BF16), ("woT4", [D, D], F32),
        ("x3T", [D, TOK], F32),
    ]:
        p.inp(name, shape, dt)
    p.out("x4T", [D, TOK], F32)
    din, dout = p.din, p.dout
    p.inp("idf32", [128, 128], F32)
    idf = p.sb("idf", [128, 128], F32)
    p.dma("sync", idf[:, :], din["idf32"])

    ones_row = p.sb("ones_row", [1, 128], F32)
    p.memset(ones_row[:, :], 1.0)
    HCH = 1024
    LGt = p.sb("LGt", [128, 256], F32)
    MC = p.sb("MCc", [128, 1], F32)
    SMc = p.sb("SMc", [128, 1], F32)
    RC = p.sb("RCc", [128, 1], F32)
    CMB = p.sb("CMB", [128, 256], F32)
    CT = p.sb("CT", [128, 2 * HCH], BF16)
    SOT = p.sb("SOT", [128, 2 * D], BF16)
    XB4 = p.sb("XB4", [128, 8 * HCH], BF16)
    FF4 = p.sb("FF4", [128, 8 * HCH], F32)
    SS4 = p.sb("SS4", [128, 8 * HCH], F32)
    WA = p.sb("WA", [128, 8 * D], BF16)
    PP = [p.ps("PP%d" % i) for i in range(6)]

    for k in range(2):
        p.dma("gpsimd", SOT[:, k * D:(k + 1) * D],
              din["soTA"][k * 128:(k + 1) * 128, :])
    for k in range(8):
        p.dma("gpsimd", WA[:, k * D:(k + 1) * D],
              din["woT4"][k * 128:(k + 1) * 128, :])
    for hf in range(2):
        xa = hf * HCH
        for i in range(HCH // 128):
            tb = i * 128
            p.dma("sync", LGt[:, :], din["lgTok"][xa + tb:xa + tb + 128, :])
            p.reduce(MC[:, 0:1], LGt[:, :], ALU.max, negate=True)
            p.act(CMB[:, :], LGt[:, :], AF.Exp, bias=MC[:, 0:1],
                  accum_out=SMc[:, 0:1])
            p.recip(RC[:, 0:1], SMc[:, 0:1])
            p.ts(CMB[:, :], CMB[:, :], RC[:, 0:1], None, ALU.mult)
            for ec in range(2):
                p.transpose(PP[4][:, 0:128], CMB[:, ec * 128:(ec + 1) * 128],
                            idf[:, :])
                p.copy(CT[:, ec * HCH + tb: ec * HCH + tb + 128],
                       PP[4][:, 0:128])
        for o in range(8):
            for (a, b) in col_splits(HCH, 512):
                w = b - a
                ps = PP[o % 2]
                p.mm(ps[:, 0:w],
                     [SOT[:, k * D + o * 128: k * D + o * 128 + 128]
                      for k in range(2)],
                     [CT[:, k * HCH + a: k * HCH + b] for k in range(2)])
                p.act(FF4[:, o * HCH + a: o * HCH + b], ps[:, 0:w], AF.Silu)
        for k in range(8):
            p.dma("sync", XB4[:, k * HCH:(k + 1) * HCH],
                  din["o4T"][k * 128:(k + 1) * 128, xa:xa + HCH])
        proj(p, WA, XB4, SS4, [PP[2], PP[3]], 8, 8, HCH)
        p.tt(FF4[:, :], FF4[:, :], SS4[:, :], ALU.mult)
        for k in range(8):
            p.dma("sync", SS4[:, k * HCH:(k + 1) * HCH],
                  din["x3T"][k * 128:(k + 1) * 128, xa:xa + HCH])
        p.tt(SS4[:, :], SS4[:, :], FF4[:, :], ALU.add)
        for k in range(8):
            p.dma("sync", dout["x4T"][k * 128:(k + 1) * 128, xa:xa + HCH],
                  SS4[:, k * HCH:(k + 1) * HCH])
    return p.finalize()


def host_l10(inputs, state):
    nc = get_nc("l10", build_l10)
    in_maps = []
    for c in range(NCORES):
        b = c // 4
        m = dict(
            lgTok=np.ascontiguousarray(state["lgT"][c * TOK:(c + 1) * TOK]),
            soTA=np.ascontiguousarray(state["soA"][b].T),
            o4T=np.ascontiguousarray(
                state["o4T"][:, c * TOK:(c + 1) * TOK]).astype(BF),
            woT4=_t(inputs["L3_attn_wo_w"]),
            x3T=np.ascontiguousarray(state["x3T"][:, c * TOK:(c + 1) * TOK]),
            idf32=np.eye(128, dtype=np.float32),
        )
        in_maps.append(m)
    res = run8(nc, in_maps)
    x4 = np.concatenate([res[c]["x4T"] for c in range(NCORES)], 1)
    state["x4T"] = x4
    return state



def kernel(**inputs):
    state = {}
    host_l1(inputs, state)
    host_l2(inputs, state)
    host_l3(inputs, state)
    host_l4(inputs, state)
    host_l5(inputs, state)
    host_l6a(inputs, state)
    host_l6b(inputs, state)
    host_l7(inputs, state)
    host_l8(inputs, state)
    host_l9(inputs, state)
    host_l10(inputs, state)
    x4 = state["x4T"]
    return np.ascontiguousarray(
        x4.reshape(D, NB, S).transpose(1, 2, 0)).astype(np.float32)


# version 07
# speedup vs baseline: 1.0793x; 1.0793x over previous
import math
from contextlib import ExitStack

import numpy as np
import ml_dtypes

import concourse.bass as bass
import concourse.mybir as mybir
from concourse.bass_utils import run_bass_kernel_spmd

F32 = mybir.dt.float32
BF16 = mybir.dt.bfloat16
I32 = mybir.dt.int32
AF = mybir.ActivationFunctionType
ALU = mybir.AluOpType
AX = mybir.AxisListType
BF = ml_dtypes.bfloat16

NCORES = 8
NB = 2
S = 8192
D = 1024
TOK = 2048
PI = math.pi
TWO_PI = 2.0 * math.pi

ENGINES = ("tensor", "vector", "scalar", "sync", "gpsimd")
DMA_ENGINES = {"sync", "gpsimd"}


class Prog:

    def __init__(self, name):
        self.nc = bass.Bass("TRN2", target_bir_lowering=False)
        self.ops = []
        self.st = ExitStack()
        self._names = set()
        self.din = {}
        self.dout = {}

    def inp(self, name, shape, dt=F32):
        ap = self.nc.dram_tensor(name, shape, dt, kind="ExternalInput").ap()
        self.din[name] = ap
        return ap

    def out(self, name, shape, dt=F32):
        ap = self.nc.dram_tensor(name, shape, dt, kind="ExternalOutput").ap()
        self.dout[name] = ap
        return ap

    def sb(self, name, shape, dt=F32):
        assert name not in self._names, name
        self._names.add(name)
        return self.st.enter_context(self.nc.sbuf_tensor(name, shape, dt))

    def ps(self, name, shape=None, dt=F32):
        assert name not in self._names, name
        self._names.add(name)
        return self.st.enter_context(
            self.nc.psum_tensor(name, shape or [128, 512], dt))

    @staticmethod
    def _key(x):
        t = getattr(x, "tensor", x)
        return id(t)

    def rec(self, engine, emit, ins, outs, is_dma=False):
        self.ops.append((engine, emit,
                         [self._key(a) for a in ins if a is not None],
                         [self._key(a) for a in outs if a is not None],
                         is_dma))

    def dma(self, q, dst, src):
        self.rec(q, lambda e: e.dma_start(out=dst, in_=src), [src], [dst],
                 is_dma=True)

    def mm(self, psum, lhsTs, rhss, extra_ins=()):
        if not isinstance(lhsTs, (list, tuple)):
            lhsTs, rhss = [lhsTs], [rhss]
        n = len(lhsTs)

        def emit(t):
            for i in range(n):
                inst = t.matmul(psum, lhsTs[i], rhss[i],
                                start=(i == 0), stop=(i == n - 1))
            return inst
        self.rec("tensor", emit, list(lhsTs) + list(rhss) + list(extra_ins),
                 [psum])

    def mm_more(self, psum, lhsTs, rhss, start, stop):
        if not isinstance(lhsTs, (list, tuple)):
            lhsTs, rhss = [lhsTs], [rhss]
        n = len(lhsTs)

        def emit(t):
            for i in range(n):
                inst = t.matmul(psum, lhsTs[i], rhss[i],
                                start=(start and i == 0),
                                stop=(stop and i == n - 1))
            return inst
        self.rec("tensor", emit, list(lhsTs) + list(rhss), [psum])

    def transpose(self, psum_out, in_ap, ident):
        self.rec("tensor", lambda t: t.transpose(psum_out, in_ap, ident),
                 [in_ap, ident], [psum_out])

    def act(self, out, in_, fn, bias=0.0, scale=1.0, accum_out=None):
        b = bias if not isinstance(bias, (int, float)) else None
        s_ = scale if not isinstance(scale, (int, float)) else None
        self.rec("scalar",
                 lambda s: s.activation(out, in_, fn, bias=bias, scale=scale,
                                        accum_out=accum_out),
                 [in_, b, s_], [out, accum_out])

    def tt(self, out, a, b, op, eng="vector"):
        self.rec(eng, lambda v: v.tensor_tensor(out, a, b, op), [a, b], [out])

    def ts(self, out, a, s1, s2, op0, op1=None, eng="vector", accum_out=None):
        if op1 is None:
            self.rec(eng, lambda v: v.tensor_scalar(out, a, s1, None, op0,
                                                    accum_out=accum_out),
                     [a, s1 if not np.isscalar(s1) else None], [out, accum_out])
        else:
            self.rec(eng, lambda v: v.tensor_scalar(out, a, s1, s2, op0, op1,
                                                    accum_out=accum_out),
                     [a, s1 if not np.isscalar(s1) else None,
                      s2 if not np.isscalar(s2) else None], [out, accum_out])

    def stt(self, out, in0, scalar, in1, op0, op1, eng="vector"):
        self.rec(eng,
                 lambda v: v.scalar_tensor_tensor(out, in0, scalar, in1,
                                                  op0, op1),
                 [in0, in1, scalar if not np.isscalar(scalar) else None],
                 [out])

    def copy(self, out, in_, eng="vector"):
        if eng == "scalar":
            self.rec(eng, lambda s: s.activation(out, in_, AF.Copy),
                     [in_], [out])
        else:
            self.rec(eng, lambda v: v.tensor_copy(out, in_), [in_], [out])

    def reduce(self, out, in_, op, absval=False, negate=False, eng="vector"):
        self.rec(eng,
                 lambda v: v.tensor_reduce(out, in_, AX.X, op,
                                           apply_absolute_value=absval,
                                           negate=negate),
                 [in_], [out])

    def scan(self, out, d0, d1, init, op0, op1):
        self.rec("vector",
                 lambda v: v.tensor_tensor_scan(out, d0, d1, init, op0, op1),
                 [d0, d1, init if not np.isscalar(init) else None], [out])

    def recip(self, out, in_):
        self.rec("scalar", lambda s: s.activation(out, in_, AF.Ln),
                 [in_], [out])
        cm1 = self._neg1()
        np_ = out.shape[0]
        self.rec("scalar",
                 lambda s: s.activation(out, out, AF.Exp,
                                        scale=cm1[0:np_, 0:1]),
                 [out, cm1], [out])

    def _neg1(self):
        if not hasattr(self, "_cneg1"):
            self._cneg1 = self.sb("_cneg1", [128, 1], F32)
            self.memset(self._cneg1[:, :], -1.0)
        return self._cneg1

    def memset(self, ap, c, eng="vector"):
        self.rec(eng, lambda v: v.memset(ap, c), [], [ap])

    def select(self, out, mask, tv, fv):
        self.rec("vector", lambda v: v.select(out, mask, tv, fv),
                 [mask, tv, fv], [out])

    def constcol(self, name, value, rows=128):
        t = self.sb(name, [rows, 1], F32)
        self.memset(t[:, :], value)
        return t

    def finalize(self):
        n = len(self.ops)
        last_write = {}
        readers = {}
        deps = [set() for _ in range(n)]
        for i, (eng, emit, ins, outs, is_dma) in enumerate(self.ops):
            for k in ins:
                if k in last_write:
                    deps[i].add(last_write[k])
            for k in outs:
                if k in last_write:
                    deps[i].add(last_write[k])
                for r in readers.get(k, ()):
                    deps[i].add(r)
            for k in ins:
                readers.setdefault(k, []).append(i)
            for k in outs:
                last_write[k] = i
                readers[k] = []

        engs = [o[0] for o in self.ops]
        isdma = [o[4] for o in self.ops]
        cnt = {}
        run_cnt = {e: 0 for e in ENGINES}
        for i in range(n):
            run_cnt[engs[i]] += 16 if isdma[i] else 1
            cnt[i] = run_cnt[engs[i]]
        need_wait = [[] for _ in range(n)]
        for i in range(n):
            for j in deps[i]:
                if (engs[j] == engs[i] == "tensor"
                        and not isdma[j] and not isdma[i]):
                    continue
                need_wait[i].append((engs[j], cnt[j]))

        sems = {e: self.st.enter_context(self.nc.semaphore("sem_" + e))
                for e in ENGINES}

        with self.nc.Block() as block:
            def run_engine(eng_name, eng):
                hi = {e: 0 for e in ENGINES}
                for i in range(n):
                    if engs[i] != eng_name:
                        continue
                    for (we, wv) in need_wait[i]:
                        if wv > hi[we]:
                            eng.wait_ge(sems[we], wv)
                            hi[we] = wv
                    inst = self.ops[i][1](eng)
                    inst.then_inc(sems[eng_name], 16 if isdma[i] else 1)
                for e in ENGINES:
                    if run_cnt[e] > hi[e] and e != eng_name:
                        eng.wait_ge(sems[e], run_cnt[e])

            @block.tensor
            def _(t):
                run_engine("tensor", t)

            @block.vector
            def _(v):
                run_engine("vector", v)

            @block.scalar
            def _(s):
                run_engine("scalar", s)

            @block.sync
            def _(s):
                run_engine("sync", s)

            @block.gpsimd
            def _(g):
                run_engine("gpsimd", g)

        self.st.close()
        return self.nc


def col_splits(n, maxn=512):
    k = (n + maxn - 1) // maxn
    base, rem = divmod(n, k)
    out = []
    i = 0
    for j in range(k):
        w = base + (1 if j < rem else 0)
        out.append((i, i + w))
        i += w
    return out


def _t(a):
    return np.ascontiguousarray(np.asarray(a, np.float32).T)


def _f(a):
    return np.ascontiguousarray(np.asarray(a, np.float32))


def _col(a):
    return np.ascontiguousarray(np.asarray(a, np.float32).reshape(-1, 1))


_BUILD_CACHE = {}


def get_nc(key, fn):
    if key not in _BUILD_CACHE:
        _BUILD_CACHE[key] = fn()
    return _BUILD_CACHE[key]


def run8(nc, in_maps):
    return run_bass_kernel_spmd(nc, in_maps, core_ids=list(range(NCORES))).results



def feat_stats(p, X, nch, cols, ones_col, sq, psA, psB, row_sum, row_ssq):
    for (a, b) in col_splits(cols, 512):
        w = b - a
        p.mm(psA[0:1, 0:w], [ones_col[:, 0:1]] * nch,
             [X[:, k * cols + a: k * cols + b] for k in range(nch)])
        for k in range(nch):
            p.act(sq[k % 2][:, 0:w], X[:, k * cols + a: k * cols + b],
                  AF.Square)
            p.mm_more(psB[0:1, 0:w], ones_col[:, 0:1], sq[k % 2][:, 0:w],
                      start=(k == 0), stop=(k == nch - 1))
        p.copy(row_sum[0:1, a:b], psA[0:1, 0:w])
        p.copy(row_ssq[0:1, a:b], psB[0:1, 0:w])


def ln_rows(p, cols, row_sum, row_ssq, row_rs, row_mr, d=D, eps=1e-5,
            rms=False):
    c = slice(0, cols)
    if not rms:
        p.ts(row_mr[0:1, c], row_sum[0:1, c], 1.0 / d, None, ALU.mult)
        p.tt(row_rs[0:1, c], row_mr[0:1, c], row_mr[0:1, c], ALU.mult)
        p.stt(row_rs[0:1, c], row_ssq[0:1, c], 1.0 / d, row_rs[0:1, c],
              ALU.mult, ALU.subtract)
        p.ts(row_rs[0:1, c], row_rs[0:1, c], eps, None, ALU.add)
    else:
        p.ts(row_rs[0:1, c], row_ssq[0:1, c], 1.0 / d, eps, ALU.mult, ALU.add)
    p.act(row_sum[0:1, c], row_rs[0:1, c], AF.Sqrt)
    p.recip(row_rs[0:1, c], row_sum[0:1, c])
    if not rms:
        p.tt(row_mr[0:1, c], row_mr[0:1, c], row_rs[0:1, c], ALU.mult)
        p.ts(row_mr[0:1, c], row_mr[0:1, c], -1.0, None, ALU.mult)


def apply_ln(p, Xin, Xout, nch, cols, ones_row, row_rs, row_mr, PS0, PS1,
             gain_cols=None, bias_cols=None):
    for (a, b) in col_splits(cols, 512):
        w = b - a
        p.mm(PS0[:, 0:w], ones_row[0:1, :], row_rs[0:1, a:b])
        if row_mr is not None:
            p.mm(PS1[:, 0:w], ones_row[0:1, :], row_mr[0:1, a:b])
        for k in range(nch):
            sl = slice(k * cols + a, k * cols + b)
            p.tt(Xout[:, sl], Xin[:, sl], PS0[:, 0:w], ALU.mult)
            if row_mr is not None:
                p.tt(Xout[:, sl], Xout[:, sl], PS1[:, 0:w], ALU.add)
            if gain_cols is not None:
                p.ts(Xout[:, sl], Xout[:, sl], gain_cols[:, k:k + 1], None,
                     ALU.mult)
            if bias_cols is not None:
                p.ts(Xout[:, sl], Xout[:, sl], bias_cols[:, k:k + 1], None,
                     ALU.add)


def proj(p, W, X, OUT, psums, nk, nout, cols, xcols=None, bias=None,
         act=None, scale=1.0, out_cols=None, obase=0, wout=None):
    xcols = xcols if xcols is not None else cols
    out_cols = out_cols if out_cols is not None else cols
    wout = wout if wout is not None else nout * 128
    for o in range(nout):
        for (a, b) in col_splits(cols, 512):
            w = b - a
            ps = psums[o % len(psums)]
            p.mm(ps[:, 0:w],
                 [W[:, k * wout + o * 128: k * wout + o * 128 + 128]
                  for k in range(nk)],
                 [X[:, k * xcols + a: k * xcols + b] for k in range(nk)])
            sl = slice(o * out_cols + obase + a, o * out_cols + obase + b)
            fn = act or (AF.Identity if bias is not None else AF.Copy)
            p.act(OUT[:, sl], ps[:, 0:w], fn,
                  bias=(bias[:, o:o + 1] if bias is not None else 0.0),
                  scale=scale)



CH = 512
NCH_T = TOK // CH
CB = CH + 1


def build_l1():
    p = Prog("l1")
    for name, shape in [
        ("xT", [D, TOK + 1]), ("ppT", [D, TOK + 1]),
        ("Wpv", [D, D]), ("Wk", [D, D]), ("Wkv", [D, D]),
        ("Wg1", [2 * D, D]), ("Wb1", [D, 512]),
        ("Wg2", [D, 1]), ("Wb2", [512, 2]), ("wqkvT", [D, 24]),
        ("bpv", [128, 8]), ("bk", [128, 8]), ("bkv", [128, 8]),
        ("bg1", [128, 8]), ("bb1", [128, 4]),
        ("bg2", [1, 1]), ("bb2a", [1, 1]), ("bb2b", [1, 1]),
        ("wg_sh", [352, D]), ("wu_sh", [352, D]), ("wd_sh", [128, 2816]),
        ("idf32", [128, 128]), ("fmask", [128, 1]),
    ]:
        p.inp(name, shape)
    for name, shape, dt in [
        ("xnT", [D, TOK], F32),
        ("pos_cT", [D, TOK], BF16), ("pos_sT", [D, TOK], BF16),
        ("kcT", [D, TOK], BF16), ("ksT", [D, TOK], BF16),
        ("a_cT", [D, TOK], BF16), ("a_sT", [D, TOK], BF16),
        ("m_cT", [D, TOK], BF16), ("m_sT", [D, TOK], BF16),
        ("vg", [1, TOK], F32), ("bwr", [2, TOK], F32),
        ("wr0", [8, TOK], F32),
        ("wgq_sh", [352, D], BF16), ("wuq_sh", [352, D], BF16),
        ("wdq_sh", [128, 2816], BF16), ("dbg", [128, 128], F32),
    ]:
        p.out(name, shape, dt)

    din, dout = p.din, p.dout
    wg1 = p.sb("wg1", [128, 16 * D], BF16)
    wb1 = p.sb("wb1", [128, 8 * 512], BF16)
    wg2 = p.sb("wg2", [128, 8 * 1], BF16)
    wb2 = p.sb("wb2", [128, 4 * 2], BF16)
    wqkv = p.sb("wqkv", [128, 8 * 24], F32)
    WA = p.sb("WA", [128, 8 * D], BF16)
    for nm, dst, kch, ow in [("Wg1", wg1, 16, D), ("Wb1", wb1, 8, 512),
                             ("Wg2", wg2, 8, 1), ("Wb2", wb2, 4, 2),
                             ("wqkvT", wqkv, 8, 24)]:
        for k in range(kch):
            p.dma("gpsimd", dst[:, k * ow:(k + 1) * ow],
                  din[nm][k * 128:(k + 1) * 128, :])
    for k in range(8):
        p.dma("sync", wqkv[:, k * 24:(k + 1) * 24],
              din["wqkvT"][k * 128:(k + 1) * 128, :])
    bias_s = {}
    for nm in ["bpv", "bk", "bkv", "bg1", "bb1", "bg2", "bb2a", "bb2b"]:
        shp = {"bb1": [128, 4], "bg2": [1, 1], "bb2a": [1, 1],
               "bb2b": [1, 1]}.get(nm, [128, 8])
        bias_s[nm] = p.sb(nm + "_s", shp, F32)
        p.dma("sync", bias_s[nm][:, :], din[nm])
    idf = p.sb("idf", [128, 128], F32)
    p.dma("sync", idf[:, :], din["idf32"])
    fmask_s = p.sb("fmask_s", [128, 1], F32)
    p.dma("sync", fmask_s[:, :], din["fmask"])
    ones_col = p.sb("ones_col", [128, 1], F32)
    ones_row = p.sb("ones_row", [1, 128], F32)
    p.memset(ones_col[:, :], 1.0)
    p.memset(ones_row[:, :], 1.0)
    C_PI2 = p.constcol("C_PI2", PI / 2)
    C_PI = p.constcol("C_PI", PI)

    X = p.sb("X", [128, 8 * CB], F32)
    XNf = p.sb("XNf", [128, 8 * CB], F32)
    XNb = p.sb("XNb", [128, 8 * CB], BF16)
    PC = p.sb("PC", [128, 8 * CB], F32)
    PS = p.sb("PS", [128, 8 * CB], F32)
    SC = p.sb("SC", [128, 8 * CB], F32)
    VV = p.sb("VV", [128, 8 * CB], F32)
    GH = p.sb("GH", [128, 8 * CH], BF16)
    BH = p.sb("BH", [128, 4 * CH], BF16)
    sq0_ = p.sb("sq0", [128, 512], F32)
    sq = [sq0_, sq0_]
    row_sum = p.sb("row_sum", [1, CB], F32)
    row_ssq = p.sb("row_ssq", [1, CB], F32)
    row_rs = p.sb("row_rs", [1, CB], F32)
    row_mr = p.sb("row_mr", [1, CB], F32)
    vg_row = p.sb("vg_row", [1, CB], F32)
    bw_a = p.sb("bw_a", [1, CB], F32)
    bw_b = p.sb("bw_b", [1, CB], F32)
    MIX = p.sb("MIX", [128, 24], F32)
    RAT = p.sb("RAT", [128, 64], F32)
    RSC = p.sb("RSC", [128, 64], F32)
    RM = p.sb("RM", [128, 8], F32)
    RL2 = p.sb("RL2", [128, 8], F32)
    RM2 = p.sb("RM2", [128, 8], F32)
    RV = p.sb("RV", [128, 10], F32)
    WRT = p.sb("WRT", [128, 8], F32)
    RRC = p.sb("RRC", [128, 8], F32)
    WRO = p.sb("WRO", [8, 128], F32)
    QA = XNf[:, 0:1024]
    QC1 = XNf[:, 1024:2048]
    QC2 = XNf[:, 2048:3072]
    QS = XNf[:, 3072:3136]
    QR = XNf[:, 3200:3264]
    QO = XNb[:, 0:1024]

    psA = p.ps("psA", [1, 512])
    psB = p.ps("psB", [1, 512])
    P0 = p.ps("P0")
    P1 = p.ps("P1")
    P2 = p.ps("P2")
    P3 = p.ps("P3")
    PR = p.ps("PR", [128, 128])

    def fp4_qd(A, Srec, R, C1, C2, O, rows, colw):
        nblk = colw // 16
        rs, cw, nb = slice(0, rows), slice(0, colw), slice(0, nblk)
        p.reduce(Srec[rs, nb], A[rs, cw].rearrange("p (g b) -> p g b", b=16),
                 ALU.max, absval=True)
        p.ts(Srec[rs, nb], Srec[rs, nb], 1.0 / 6.0, 1e-12, ALU.mult, ALU.max)
        p.recip(R[rs, nb], Srec[rs, nb])
        p.act(C1[rs, cw], A[rs, cw], AF.Abs)
        rb = R[rs, nb].rearrange("p (g one) -> p g one", one=1) \
            .broadcast_to([rows, nblk, 16])
        p.tt(C1[rs, cw].rearrange("p (g b) -> p g b", b=16),
             C1[rs, cw].rearrange("p (g b) -> p g b", b=16), rb, ALU.mult)
        p.ts(C2[rs, cw], C1[rs, cw], 0.25, None, ALU.is_gt)
        for m in (0.75, 1.25, 1.75):
            p.stt(C2[rs, cw], C1[rs, cw], m, C2[rs, cw], ALU.is_gt, ALU.add)
        p.ts(C2[rs, cw], C2[rs, cw], 0.5, None, ALU.mult)
        for m in (2.5, 3.5, 5.0, 5.0):
            p.stt(C2[rs, cw], C1[rs, cw], m, C2[rs, cw], ALU.is_gt, ALU.add)
        sb_ = Srec[rs, nb].rearrange("p (g one) -> p g one", one=1) \
            .broadcast_to([rows, nblk, 16])
        p.tt(C2[rs, cw].rearrange("p (g b) -> p g b", b=16),
             C2[rs, cw].rearrange("p (g b) -> p g b", b=16), sb_, ALU.mult)
        p.act(C1[rs, cw], A[rs, cw], AF.Sign)
        p.tt(O[rs, cw], C2[rs, cw], C1[rs, cw], ALU.mult)

    for (srcn, dstn, nrows, cw) in [("wg_sh", "wgq_sh", 352, 1024),
                                    ("wu_sh", "wuq_sh", 352, 1024),
                                    ("wd_sh", "wdq_sh", 128, 2816)]:
        pieces = [(0, 1024)] if cw == 1024 else \
            [(i * 704, (i + 1) * 704) for i in range(4)]
        r0 = 0
        while r0 < nrows:
            r1 = min(r0 + 128, nrows)
            rr = r1 - r0
            for (c0, c1) in pieces:
                pw = c1 - c0
                p.dma("sync", QA[0:rr, 0:pw], din[srcn][r0:r1, c0:c1])
                fp4_qd(QA, QS, QR, QC1, QC2, QO, rr, pw)
                p.dma("sync", dout[dstn][r0:r1, c0:c1], QO[0:rr, 0:pw])
            r0 = r1

    n8 = slice(0, 8 * CB)
    for c in range(NCH_T):
        xa = c * CH

        for k in range(8):
            p.dma("sync", X[:, k * CB:(k + 1) * CB],
                  din["xT"][k * 128:(k + 1) * 128, xa:xa + CB])
            p.dma("sync", PC[:, k * CB:(k + 1) * CB],
                  din["ppT"][k * 128:(k + 1) * 128, xa:xa + CB])

        feat_stats(p, X, 8, CB, ones_col, sq, psA, psB, row_sum, row_ssq)
        ln_rows(p, CB, row_sum, row_ssq, row_rs, row_mr)
        apply_ln(p, X, XNf, 8, CB, ones_row, row_rs, row_mr, P0, P1)
        p.copy(XNb[:, :], XNf[:, :], eng="scalar")
        for k in range(8):
            p.dma("sync", dout["xnT"][k * 128:(k + 1) * 128, xa:xa + CH],
                  XNf[:, k * CB + 1:(k + 1) * CB])

        p.ts(SC[:, n8], PC[:, n8], 1.0 / TWO_PI, None, ALU.mult)
        p.copy(VV[:, n8].bitcast(I32), SC[:, n8])
        p.copy(SC[:, n8], VV[:, n8].bitcast(I32))
        p.stt(PS[:, n8], SC[:, n8], -TWO_PI, PC[:, n8], ALU.mult, ALU.add)
        p.ts(SC[:, n8], PS[:, n8], PI / 2, -TWO_PI, ALU.is_gt, ALU.mult)
        p.tt(SC[:, n8], SC[:, n8], PS[:, n8], ALU.add)
        p.act(PC[:, n8], SC[:, n8], AF.Sin, bias=C_PI2[:, 0:1])
        p.act(PS[:, n8], PS[:, n8], AF.Sin)
        for k in range(8):
            p.dma("gpsimd", dout["pos_cT"][k * 128:(k + 1) * 128, xa:xa + CH],
                  PC[:, k * CB + 1:(k + 1) * CB])
            p.dma("gpsimd", dout["pos_sT"][k * 128:(k + 1) * 128, xa:xa + CH],
                  PS[:, k * CB + 1:(k + 1) * CB])

        for k in range(8):
            p.dma("gpsimd", WA[:, k * D:(k + 1) * D],
                  din["Wpv"][k * 128:(k + 1) * 128, :])
        proj(p, WA, XNb, VV, [P0, P1], 8, 8, CB, bias=bias_s["bpv"])
        for k in range(8):
            sl = slice(k * CB, (k + 1) * CB)
            p.tt(SC[:, sl], PC[:, sl], VV[:, sl], ALU.mult)
        for k in range(8):
            p.dma("gpsimd", dout["a_cT"][k * 128:(k + 1) * 128, xa:xa + CH],
                  SC[:, k * CB + 1:(k + 1) * CB])
        for k in range(8):
            sl = slice(k * CB, (k + 1) * CB)
            p.tt(PC[:, sl], PS[:, sl], VV[:, sl], ALU.mult)
        for k in range(8):
            p.dma("gpsimd", dout["a_sT"][k * 128:(k + 1) * 128, xa:xa + CH],
                  PC[:, k * CB + 1:(k + 1) * CB])

        for k in range(8):
            p.dma("gpsimd", WA[:, k * D:(k + 1) * D],
                  din["Wkv"][k * 128:(k + 1) * 128, :])
        proj(p, WA, XNb, VV, [P2, P3], 8, 8, CB, bias=bias_s["bkv"])

        for k in range(8):
            p.dma("gpsimd", WA[:, k * D:(k + 1) * D],
                  din["Wk"][k * 128:(k + 1) * 128, :])
        proj(p, WA, XNb, SC, [P0, P1], 8, 8, CB, bias=bias_s["bk"],
             act=AF.Tanh)
        p.act(PS[:, n8], SC[:, n8], AF.Sin, scale=C_PI[:, 0:1])
        p.ts(PC[:, n8], SC[:, n8], 0.5, -TWO_PI, ALU.is_gt, ALU.mult)
        p.stt(PC[:, n8], SC[:, n8], PI, PC[:, n8], ALU.mult, ALU.add)
        p.act(PC[:, n8], PC[:, n8], AF.Sin, bias=C_PI2[:, 0:1])
        for k in range(8):
            p.dma("gpsimd", dout["kcT"][k * 128:(k + 1) * 128, xa:xa + CH],
                  PC[:, k * CB + 1:(k + 1) * CB])
            p.dma("gpsimd", dout["ksT"][k * 128:(k + 1) * 128, xa:xa + CH],
                  PS[:, k * CB + 1:(k + 1) * CB])

        for o in range(8):
            ps = [P0, P1][o % 2]
            p.mm(ps[:, 0:CH],
                 [wg1[:, k * D + o * 128: k * D + o * 128 + 128]
                  for k in range(16)],
                 [XNb[:, k * CB + 1:(k + 1) * CB] for k in range(8)] +
                 [XNb[:, k * CB + 0:(k + 1) * CB - 1] for k in range(8)])
            p.act(GH[:, o * CH:(o + 1) * CH], ps[:, 0:CH], AF.Gelu,
                  bias=bias_s["bg1"][:, o:o + 1])
        p.mm(psA[0:1, 0:CH],
             [wg2[:, k:k + 1] for k in range(8)],
             [GH[:, k * CH:(k + 1) * CH] for k in range(8)])
        p.act(vg_row[0:1, 1:CB], psA[0:1, 0:CH], AF.Sigmoid,
              bias=bias_s["bg2"][0:1, 0:1])
        p.dma("sync", dout["vg"][0:1, xa:xa + CH], vg_row[0:1, 1:CB])

        for o in range(4):
            ps = [P2, P3][o % 2]
            p.mm(ps[:, 0:CH],
                 [wb1[:, k * 512 + o * 128: k * 512 + o * 128 + 128]
                  for k in range(8)],
                 [XNb[:, k * CB + 1:(k + 1) * CB] for k in range(8)])
            p.act(BH[:, o * CH:(o + 1) * CH], ps[:, 0:CH], AF.Gelu,
                  bias=bias_s["bb1"][:, o:o + 1])
        p.mm(psA[0:1, 0:CH],
             [wb2[:, 2 * k:2 * k + 1] for k in range(4)],
             [BH[:, k * CH:(k + 1) * CH] for k in range(4)])
        p.mm(psB[0:1, 0:CH],
             [wb2[:, 2 * k + 1:2 * k + 2] for k in range(4)],
             [BH[:, k * CH:(k + 1) * CH] for k in range(4)])
        p.act(bw_a[0:1, 1:CB], psA[0:1, 0:CH], AF.Identity,
              bias=bias_s["bb2a"][0:1, 0:1])
        p.act(bw_b[0:1, 1:CB], psB[0:1, 0:CH], AF.Identity,
              bias=bias_s["bb2b"][0:1, 0:1])
        p.tt(bw_a[0:1, 1:CB], bw_a[0:1, 1:CB], bw_b[0:1, 1:CB], ALU.subtract)
        p.act(bw_a[0:1, 1:CB], bw_a[0:1, 1:CB], AF.Sigmoid)
        p.ts(bw_b[0:1, 1:CB], bw_a[0:1, 1:CB], -1.0, 1.0, ALU.mult, ALU.add)
        p.dma("sync", dout["bwr"][0:1, xa:xa + CH], bw_a[0:1, 1:CB])
        p.dma("sync", dout["bwr"][1:2, xa:xa + CH], bw_b[0:1, 1:CB])

        p.mm(P2[:, 0:CH], ones_row[0:1, :], vg_row[0:1, 1:CB])
        for k in range(8):
            slt = slice(k * CB + 1, (k + 1) * CB)
            sls = slice(k * CB + 0, (k + 1) * CB - 1)
            p.tt(SC[:, slt], PC[:, sls], VV[:, slt], ALU.mult)
            p.tt(SC[:, slt], SC[:, slt], P2[:, 0:CH], ALU.mult)
            p.tt(X[:, slt], PS[:, sls], VV[:, slt], ALU.mult)
            p.tt(X[:, slt], X[:, slt], P2[:, 0:CH], ALU.mult)
        if c == 0:
            for k in range(8):
                p.ts(SC[:, k * CB + 1:k * CB + 2],
                     SC[:, k * CB + 1:k * CB + 2],
                     fmask_s[:, 0:1], None, ALU.mult)
                p.ts(X[:, k * CB + 1:k * CB + 2],
                     X[:, k * CB + 1:k * CB + 2],
                     fmask_s[:, 0:1], None, ALU.mult)
        for k in range(8):
            p.dma("gpsimd", dout["m_cT"][k * 128:(k + 1) * 128, xa:xa + CH],
                  SC[:, k * CB + 1:(k + 1) * CB])
            p.dma("gpsimd", dout["m_sT"][k * 128:(k + 1) * 128, xa:xa + CH],
                  X[:, k * CB + 1:(k + 1) * CB])

        for i in range(CH // 128):
            tbase = 1 + i * 128
            p.mm(PR[:, 0:24],
                 [XNf[:, k * CB + tbase: k * CB + tbase + 128]
                  for k in range(8)],
                 [wqkv[:, k * 24:(k + 1) * 24] for k in range(8)])
            p.copy(MIX[:, 0:24], PR[:, 0:24])
            for e in range(8):
                p.tt(RAT[:, 8 * e:8 * e + 8],
                     MIX[:, e:e + 1].broadcast_to([128, 8]),
                     MIX[:, 8:16], ALU.mult)
            g3 = RAT[:, 0:64].rearrange("p (e j) -> p e j", j=8)
            p.reduce(RM[:, 0:8], g3, ALU.max)
            p.tt(RSC[:, 0:64].rearrange("p (e j) -> p e j", j=8), g3,
                 RM[:, 0:8].rearrange("p (e one) -> p e one", one=1)
                 .broadcast_to([128, 8, 8]), ALU.subtract)
            p.act(RSC[:, 0:64], RSC[:, 0:64], AF.Exp)
            s3 = RSC[:, 0:64].rearrange("p (e j) -> p e j", j=8)
            p.reduce(RM[:, 0:8], s3, ALU.add)
            p.recip(RRC[:, 0:8], RM[:, 0:8])
            p.tt(RSC[:, 0:64].rearrange("p (e j) -> p e j", j=8), s3,
                 RRC[:, 0:8].rearrange("p (e one) -> p e one", one=1)
                 .broadcast_to([128, 8, 8]), ALU.mult)
            p.tt(RSC[:, 0:64].rearrange("p (e j) -> p e j", j=8), s3,
                 MIX[:, 16:24].rearrange("p (one j) -> p one j", one=1)
                 .broadcast_to([128, 8, 8]), ALU.mult)
            p.reduce(RM[:, 0:8], s3, ALU.add)
            p.reduce(RV[:, 0:1], RM[:, 0:8], ALU.max)
            p.ts(RL2[:, 0:8], RM[:, 0:8], RV[:, 0:1], None, ALU.is_equal)
            p.stt(RM2[:, 0:8], RL2[:, 0:8], -1e30, RM[:, 0:8],
                  ALU.mult, ALU.add)
            p.reduce(RV[:, 1:2], RM2[:, 0:8], ALU.max)
            p.ts(WRT[:, 0:8], RM2[:, 0:8], RV[:, 1:2], None, ALU.is_equal)
            p.tt(RV[:, 2:3], RV[:, 1:2], RV[:, 0:1], ALU.subtract)
            p.act(RV[:, 3:4], RV[:, 2:3], AF.Exp)
            p.ts(RV[:, 4:5], RV[:, 3:4], 1.0, None, ALU.add)
            p.recip(RV[:, 6:7], RV[:, 4:5])
            p.tt(RV[:, 5:6], RV[:, 3:4], RV[:, 6:7], ALU.mult)
            p.ts(RL2[:, 0:8], RL2[:, 0:8], RV[:, 6:7], None, ALU.mult)
            p.ts(WRT[:, 0:8], WRT[:, 0:8], RV[:, 5:6], None, ALU.mult)
            p.tt(WRT[:, 0:8], WRT[:, 0:8], RL2[:, 0:8], ALU.add)
            if c == 0 and i == 0:
                p.dma("sync", dout["dbg"][:, 0:24], MIX[:, 0:24])
                p.dma("sync", dout["dbg"][:, 24:88], RSC[:, 0:64])
                p.dma("sync", dout["dbg"][:, 88:96], RM[:, 0:8])
                p.dma("sync", dout["dbg"][:, 96:104], RL2[:, 0:8])
                p.dma("sync", dout["dbg"][:, 104:112], WRT[:, 0:8])
                p.dma("sync", dout["dbg"][:, 112:118], RV[:, 0:6])
            p.transpose(PR[0:8, 0:128], WRT[:, 0:8], idf[:, :])
            p.copy(WRO[0:8, 0:128], PR[0:8, 0:128])
            p.dma("sync", dout["wr0"][0:8, xa + i * 128: xa + (i + 1) * 128],
                  WRO[0:8, 0:128])

    return p.finalize()


def host_l1(inputs, state):
    nc = get_nc("l1", build_l1)
    x = np.asarray(inputs["L0_attn_x"], np.float32)
    pp = np.asarray(inputs["L0_attn_pos_phases"], np.float32)[:S]
    xf = x.reshape(NB * S, D)
    in_maps = []
    idf32 = np.eye(128, dtype=np.float32)
    Wg1 = _f(inputs["L0_attn_Wg1"])
    com = dict(
        Wpv=_f(inputs["L0_attn_Wpv"]), Wk=_f(inputs["L0_attn_Wk"]),
        Wkv=_f(inputs["L0_attn_Wkv"]), Wg1=Wg1,
        Wb1=_f(inputs["L0_attn_Wb1"]), Wg2=_f(inputs["L0_attn_Wg2"]),
        Wb2=_f(inputs["L0_attn_Wb2"]), wqkvT=_t(inputs["L0_ffn_wqkv"]),
        bpv=_f(inputs["L0_attn_bpv"]).reshape(8, 128).T.copy(),
        bk=_f(inputs["L0_attn_bk"]).reshape(8, 128).T.copy(),
        bkv=_f(inputs["L0_attn_bkv"]).reshape(8, 128).T.copy(),
        bg1=_f(inputs["L0_attn_bg1"]).reshape(8, 128).T.copy(),
        bb1=_f(inputs["L0_attn_bb1"]).reshape(4, 128).T.copy(),
        bg2=_f(inputs["L0_attn_bg2"]).reshape(1, 1),
        bb2a=_f(inputs["L0_attn_bb2"])[0].reshape(1, 1),
        bb2b=_f(inputs["L0_attn_bb2"])[1].reshape(1, 1),
        idf32=idf32,
    )
    wg = _f(inputs["L2_ffn_w_gate"])
    wu = _f(inputs["L2_ffn_w_up"])
    wd = _f(inputs["L2_ffn_w_down"])
    for c in range(NCORES):
        b, t0 = divmod(c * TOK, S)
        xs = np.zeros((D, TOK + 1), np.float32)
        xs[:, 1:] = xf[b * S + t0: b * S + t0 + TOK].T
        if t0 > 0:
            xs[:, 0] = xf[b * S + t0 - 1]
        pps = np.zeros((D, TOK + 1), np.float32)
        pps[:, 1:] = pp[t0:t0 + TOK].T
        if t0 > 0:
            pps[:, 0] = pp[t0 - 1]
        m = dict(com)
        m["xT"] = xs
        m["ppT"] = pps
        m["fmask"] = np.full((128, 1), 0.0 if t0 == 0 else 1.0, np.float32)
        m["wg_sh"] = wg[c * 352:(c + 1) * 352]
        m["wu_sh"] = wu[c * 352:(c + 1) * 352]
        m["wd_sh"] = wd[c * 128:(c + 1) * 128]
        in_maps.append(m)
    res = run8(nc, in_maps)

    state["l1"] = res
    for name in ["xnT", "pos_cT", "pos_sT", "kcT", "ksT",
                 "a_cT", "a_sT", "m_cT", "m_sT"]:
        state[name] = np.concatenate([np.asarray(res[c][name], np.float32)
                                      for c in range(NCORES)], axis=1)
    for name in ["vg", "bwr", "wr0"]:
        state[name] = np.concatenate([res[c][name] for c in range(NCORES)],
                                     axis=1)
    state["wgq"] = np.concatenate([res[c]["wgq_sh"] for c in range(NCORES)], 0)
    state["wuq"] = np.concatenate([res[c]["wuq_sh"] for c in range(NCORES)], 0)
    state["wdq"] = np.concatenate([res[c]["wdq_sh"] for c in range(NCORES)], 0)
    return state




def build_l2():
    p = Prog("l2")
    for name, shape, dt in [
        ("a_c", [256, S], BF16), ("a_s", [256, S], BF16),
        ("m_c", [256, S], BF16), ("m_s", [256, S], BF16),
        ("pc", [256, S], BF16), ("psn", [256, S], BF16),
        ("kc", [256, S], BF16), ("ks", [256, S], BF16),
        ("vgf", [1, S], F32), ("bwf", [2, S], F32),
    ]:
        p.inp(name, shape, dt)
    p.out("bl", [256, S], F32)
    din, dout = p.din, p.dout

    PW = 2048
    NP = S // PW
    ones_col = p.sb("ones_col", [128, 1], F32)
    ones_row = p.sb("ones_row", [1, 128], F32)
    p.memset(ones_col[:, :], 1.0)
    p.memset(ones_row[:, :], 1.0)
    ONES = p.sb("ONES", [128, PW], F32)
    p.memset(ONES[:, :], 1.0)
    ACC = p.sb("ACC", [128, PW], F32)
    SCN = p.sb("SCN", [128, PW], F32)
    SCI = p.sb("SCI", [128, PW], F32)
    PCF = p.sb("PCF", [128, PW], F32)
    INB = p.sb("INB", [128, PW], BF16)
    TMP = p.sb("TMP", [128, PW], F32)
    CAR = p.sb("CAR", [128, 16], F32)
    VGR = p.sb("VGR", [1, PW], F32)
    GCR = p.sb("GCR", [1, PW], F32)
    W0R = p.sb("W0R", [1, PW], F32)
    W1R = p.sb("W1R", [1, PW], F32)
    GCC = p.sb("GCC", [1, 2], F32)
    P0 = p.ps("P0")
    P1 = p.ps("P1")

    p.memset(CAR[:, :], 0.0)
    p.memset(GCC[:, :], 0.0)

    for pc_ in range(NP):
        a = pc_ * PW
        p.dma("sync", VGR[0:1, :], din["vgf"][0:1, a:a + PW])
        p.scan(GCR[0:1, :], ONES[0:1, :], VGR[0:1, :], GCC[0:1, 0:1],
               ALU.mult, ALU.add)
        p.copy(GCC[0:1, 0:1], GCR[0:1, PW - 1:PW])
        p.ts(TMP[0:1, 0:PW], GCR[0:1, :], 1.0, None, ALU.max)
        p.act(TMP[0:1, 0:PW], TMP[0:1, 0:PW], AF.Sqrt)
        p.recip(W1R[0:1, :], TMP[0:1, 0:PW])
        p.dma("sync", W0R[0:1, :], din["bwf"][0:1, a:a + PW])
        p.ts(W0R[0:1, :], W0R[0:1, :], 1.0 / 32.0, None, ALU.mult)
        p.dma("sync", TMP[0:1, 0:PW], din["bwf"][1:2, a:a + PW])
        p.tt(W1R[0:1, :], W1R[0:1, :], TMP[0:1, 0:PW], ALU.mult)
        p.ts(W1R[0:1, :], W1R[0:1, :], 1.0 / 32.0, None, ALU.mult)

        for half in range(2):
            r0 = half * 128
            carb = 8 * half

            def scan_in(name, car_i):
                p.dma("sync", INB[:, :], din[name][r0:r0 + 128, a:a + PW])
                p.copy(SCI[:, :], INB[:, :])
                p.scan(SCN[:, :], ONES[:, :], SCI[:, :],
                       CAR[:, carb + car_i:carb + car_i + 1],
                       ALU.mult, ALU.add)
                p.copy(CAR[:, carb + car_i:carb + car_i + 1],
                       SCN[:, PW - 1:PW])

            def load_f32(name):
                p.dma("sync", INB[:, :], din[name][r0:r0 + 128, a:a + PW])
                p.copy(PCF[:, :], INB[:, :])

            scan_in("a_c", 0)
            load_f32("pc")
            p.tt(ACC[:, :], SCN[:, :], PCF[:, :], ALU.mult)
            scan_in("a_s", 1)
            load_f32("psn")
            p.tt(TMP[:, :], SCN[:, :], PCF[:, :], ALU.mult)
            p.tt(ACC[:, :], ACC[:, :], TMP[:, :], ALU.add)
            for (sa, sb_) in col_splits(PW, 512):
                w = sb_ - sa
                p.mm(P0[:, 0:w], ones_row[0:1, :], W0R[0:1, sa:sb_])
                p.tt(ACC[:, sa:sb_], ACC[:, sa:sb_], P0[:, 0:w], ALU.mult)
            scan_in("m_c", 2)
            load_f32("kc")
            p.tt(TMP[:, :], SCN[:, :], PCF[:, :], ALU.mult)
            scan_in("m_s", 3)
            load_f32("ks")
            p.tt(PCF[:, :], SCN[:, :], PCF[:, :], ALU.mult)
            p.tt(TMP[:, :], TMP[:, :], PCF[:, :], ALU.add)
            for (sa, sb_) in col_splits(PW, 512):
                w = sb_ - sa
                p.mm(P1[:, 0:w], ones_row[0:1, :], W1R[0:1, sa:sb_])
                p.tt(TMP[:, sa:sb_], TMP[:, sa:sb_], P1[:, 0:w], ALU.mult)
            p.tt(ACC[:, :], ACC[:, :], TMP[:, :], ALU.add)
            p.dma("sync", dout["bl"][r0:r0 + 128, a:a + PW], ACC[:, :])
    return p.finalize()


def host_l2(inputs, state):
    nc = get_nc("l2", build_l2)
    in_maps = []
    for c in range(NCORES):
        b, f0 = c // 4, (c % 4) * 256
        m = {}
        for nm, key in [("a_c", "a_cT"), ("a_s", "a_sT"), ("m_c", "m_cT"),
                        ("m_s", "m_sT"), ("pc", "pos_cT"), ("psn", "pos_sT"),
                        ("kc", "kcT"), ("ks", "ksT")]:
            full = state[key]
            m[nm] = np.ascontiguousarray(
                full[f0:f0 + 256, b * S:(b + 1) * S]).astype(BF)
        m["vgf"] = np.ascontiguousarray(state["vg"][0:1, b * S:(b + 1) * S])
        m["bwf"] = np.ascontiguousarray(state["bwr"][:, b * S:(b + 1) * S])
        in_maps.append(m)
    res = run8(nc, in_maps)
    bl = np.zeros((NB, D, S), np.float32)
    for c in range(NCORES):
        b, f0 = c // 4, (c % 4) * 256
        bl[b, f0:f0 + 256] = res[c]["bl"]
    state["blT"] = np.ascontiguousarray(
        np.concatenate([bl[0], bl[1]], axis=1))
    return state



def build_l3():
    p = Prog("l3")
    for name, shape, dt in [
        ("blT", [D, TOK], F32), ("xnT", [D, TOK], F32), ("xT", [D, TOK], F32),
        ("wr0", [8, TOK], F32),
        ("Wo", [D, D], F32), ("bo", [128, 8], F32),
        ("ln_g", [128, 8], F32), ("ln_b", [128, 8], F32),
        ("w1L", [8 * 22 * 128, 8 * 256], F32),
        ("w2L", [8 * 8 * 128, 22 * 128], F32),
    ]:
        p.inp(name, shape, dt)
    p.out("x1T", [D, TOK], F32)
    p.out("xn1T", [D, TOK], F32)
    din, dout = p.din, p.dout

    ones_col = p.sb("ones_col", [128, 1], F32)
    ones_row = p.sb("ones_row", [1, 128], F32)
    p.memset(ones_col[:, :], 1.0)
    p.memset(ones_row[:, :], 1.0)
    HCH = 1024
    XN = p.sb("XN", [128, 8 * HCH], F32)
    XNb = p.sb("XNb", [128, 8 * HCH], BF16)
    BL = p.sb("BL", [128, 8 * HCH], F32)
    LNB = p.sb("LNB", [128, 8 * HCH], BF16)
    SS = p.sb("SS", [128, 8 * HCH], F32)
    MA = XN
    XO = BL
    HE = p.sb("HE", [128, 22 * 512], BF16)
    W1B_ = [p.sb("W1B0", [128, 8 * 256], BF16),
            p.sb("W1B1", [128, 8 * 256], BF16)]
    W2B = p.sb("W2B0", [128, 22 * 128], BF16)
    WRB = p.sb("WRB", [128, HCH], F32)
    T1 = p.sb("T1", [128, 512], F32)
    T2 = p.sb("T2", [128, 512], F32)
    sq0_ = p.sb("sq0", [128, 512], F32)
    sq = [sq0_, sq0_]
    row_sum = p.sb("row_sum", [1, HCH], F32)
    row_ssq = p.sb("row_ssq", [1, HCH], F32)
    row_rs = p.sb("row_rs", [1, HCH], F32)
    row_mr = p.sb("row_mr", [1, HCH], F32)
    gcol = {}
    for nm in ["ln_g", "ln_b", "bo"]:
        gcol[nm] = p.sb(nm + "_s", [128, 8], F32)
        p.dma("sync", gcol[nm][:, :], din[nm])
    WOB = p.sb("WOB", [128, 8 * D], BF16)
    for k in range(8):
        p.dma("gpsimd", WOB[:, k * D:(k + 1) * D],
              din["Wo"][k * 128:(k + 1) * 128, :])
    psA = p.ps("psA", [1, 512])
    psB = p.ps("psB", [1, 512])
    P0 = p.ps("P0")
    P1 = p.ps("P1")
    P2 = p.ps("P2")
    P3 = p.ps("P3")

    for hf in range(2):
        xa = hf * HCH
        for k in range(8):
            p.dma("sync", BL[:, k * HCH:(k + 1) * HCH],
                  din["blT"][k * 128:(k + 1) * 128, xa:xa + HCH])
            p.dma("sync", XN[:, k * HCH:(k + 1) * HCH],
                  din["xnT"][k * 128:(k + 1) * 128, xa:xa + HCH])
        p.copy(XNb[:, :], XN[:, :], eng="scalar")

        feat_stats(p, BL, 8, HCH, ones_col, sq, psA, psB, row_sum, row_ssq)
        ln_rows(p, HCH, row_sum, row_ssq, row_rs, row_mr)
        apply_ln(p, BL, BL, 8, HCH, ones_row, row_rs, row_mr, P0, P1,
                 gain_cols=gcol["ln_g"], bias_cols=gcol["ln_b"])
        p.copy(LNB[:, :], BL[:, :], eng="scalar")
        for o in range(8):
            for (a, b) in col_splits(HCH, 512):
                w = b - a
                ps = [P0, P1][o % 2]
                p.mm(ps[:, 0:w],
                     [WOB[:, k * D + o * 128: k * D + o * 128 + 128]
                      for k in range(8)],
                     [LNB[:, k * HCH + a: k * HCH + b] for k in range(8)])
                p.act(SS[:, o * HCH + a: o * HCH + b], ps[:, 0:w],
                      AF.Identity, bias=gcol["bo"][:, o:o + 1])
                p.tt(SS[:, o * HCH + a: o * HCH + b],
                     SS[:, o * HCH + a: o * HCH + b],
                     XN[:, o * HCH + a: o * HCH + b], ALU.add)

        p.memset(MA[:, :], 0.0)
        for e in range(8):
            p.dma("sync", row_mr[0:1, 0:HCH],
                  din["wr0"][e:e + 1, xa:xa + HCH])
            for (a, b) in col_splits(HCH, 512):
                w = b - a
                p.mm(P2[:, 0:w], ones_row[0:1, :], row_mr[0:1, a:b])
                p.copy(WRB[:, a:b], P2[:, 0:w])
            for tb in range(HCH // 512):
                ta = tb * 512
                for h in range(22):
                    r0_ = (e * 22 + h) * 128
                    W1B = W1B_[h % 2]
                    p.dma("gpsimd", W1B[:, :], din["w1L"][r0_:r0_ + 128, :])
                    p.mm(P2[:, 0:512],
                         [W1B[:, k * 256: k * 256 + 128] for k in range(8)],
                         [XNb[:, k * HCH + ta: k * HCH + ta + 512]
                          for k in range(8)])
                    p.mm(P3[:, 0:512],
                         [W1B[:, k * 256 + 128: k * 256 + 256]
                          for k in range(8)],
                         [XNb[:, k * HCH + ta: k * HCH + ta + 512]
                          for k in range(8)])
                    p.act(T1[:, 0:512], P2[:, 0:512], AF.Silu)
                    p.tt(T2[:, 0:512], T1[:, 0:512], P3[:, 0:512], ALU.mult)
                    p.tt(HE[:, h * 512:(h + 1) * 512], T2[:, 0:512],
                         WRB[:, ta:ta + 512], ALU.mult)
                for d in range(8):
                    r0_ = (e * 8 + d) * 128
                    p.dma("gpsimd", W2B[:, :], din["w2L"][r0_:r0_ + 128, :])
                    ps = [P2, P3][d % 2]
                    p.mm(ps[:, 0:512],
                         [W2B[:, h * 128:(h + 1) * 128] for h in range(22)],
                         [HE[:, h * 512:(h + 1) * 512] for h in range(22)])
                    p.tt(MA[:, d * HCH + ta: d * HCH + ta + 512],
                         MA[:, d * HCH + ta: d * HCH + ta + 512],
                         ps[:, 0:512], ALU.add)

        for k in range(8):
            p.dma("sync", XO[:, k * HCH:(k + 1) * HCH],
                  din["xT"][k * 128:(k + 1) * 128, xa:xa + HCH])
        p.act(MA[:, :], MA[:, :], AF.Silu)
        p.tt(MA[:, :], MA[:, :], SS[:, :], ALU.mult)
        p.tt(XO[:, :], XO[:, :], MA[:, :], ALU.add)
        for k in range(8):
            p.dma("sync", dout["x1T"][k * 128:(k + 1) * 128, xa:xa + HCH],
                  XO[:, k * HCH:(k + 1) * HCH])
        feat_stats(p, XO, 8, HCH, ones_col, sq, psA, psB, row_sum, row_ssq)
        ln_rows(p, HCH, row_sum, row_ssq, row_rs, row_mr)
        apply_ln(p, XO, XO, 8, HCH, ones_row, row_rs, row_mr, P0, P1)
        for k in range(8):
            p.dma("sync", dout["xn1T"][k * 128:(k + 1) * 128, xa:xa + HCH],
                  XO[:, k * HCH:(k + 1) * HCH])
    return p.finalize()


def host_l3(inputs, state):
    nc = get_nc("l3", build_l3)
    w1 = np.asarray(inputs["L0_ffn_w1"], np.float32)
    w2 = np.asarray(inputs["L0_ffn_w2"], np.float32)
    w1T = w1.transpose(0, 2, 1)
    w1L = np.zeros((8, 22, 128, 8 * 256), np.float32)
    w1a = w1T[:, :, :2816].reshape(8, 8, 128, 22, 128)
    w1g = w1T[:, :, 2816:].reshape(8, 8, 128, 22, 128)
    for h in range(22):
        for k in range(8):
            w1L[:, h, :, k * 256:k * 256 + 128] = w1a[:, k, :, h]
            w1L[:, h, :, k * 256 + 128:k * 256 + 256] = w1g[:, k, :, h]

    w2T = w2.transpose(0, 2, 1)
    w2L = np.zeros((8, 8, 128, 22 * 128), np.float32)
    w2r = w2T.reshape(8, 22, 128, 8, 128)
    for d in range(8):
        for h in range(22):
            w2L[:, d, :, h * 128:(h + 1) * 128] = w2r[:, h, :, d]
    x = np.asarray(inputs["L0_attn_x"], np.float32).reshape(NB * S, D)
    com = dict(
        Wo=_f(inputs["L0_attn_Wo"]),
        bo=_f(inputs["L0_attn_bo"]).reshape(8, 128).T.copy(),
        ln_g=_f(inputs["L0_attn_ln_g"]).reshape(8, 128).T.copy(),
        ln_b=_f(inputs["L0_attn_ln_b"]).reshape(8, 128).T.copy(),
        w1L=np.ascontiguousarray(w1L.reshape(8 * 22 * 128, 8 * 256)),
        w2L=np.ascontiguousarray(w2L.reshape(8 * 8 * 128, 22 * 128)),
    )
    in_maps = []
    for c in range(NCORES):
        m = dict(com)
        m["blT"] = np.ascontiguousarray(state["blT"][:, c * TOK:(c + 1) * TOK])
        m["xnT"] = np.ascontiguousarray(state["xnT"][:, c * TOK:(c + 1) * TOK])
        m["xT"] = np.ascontiguousarray(x[c * TOK:(c + 1) * TOK].T)
        m["wr0"] = np.ascontiguousarray(state["wr0"][:, c * TOK:(c + 1) * TOK])
        in_maps.append(m)
    res = run8(nc, in_maps)
    state["x1T"] = np.concatenate([res[c]["x1T"] for c in range(NCORES)], 1)
    state["xn1T"] = np.concatenate([res[c]["xn1T"] for c in range(NCORES)], 1)
    return state




def build_l4():
    p = Prog("l4")
    for name, shape, dt in [
        ("xn1T", [D, TOK], F32),
        ("wq", [D, D], F32), ("wk", [D, D], F32), ("wv", [D, D], F32),
        ("projN", [64, 64], F32),
        ("wqT", [D, D], F32), ("wqTr", [D, D], F32),
        ("wkT", [D, D], F32), ("wkTr", [D, D], F32),
        ("wvT", [D, D], F32),
        ("g1c", [128, 8], F32), ("b1c", [128, 8], F32),
        ("anwc", [128, 8], F32),
        ("COSF", [128, TOK], F32), ("SINF", [128, TOK], F32),
    ]:
        p.inp(name, shape, dt)
    for name in ["qpT", "kpT", "vT", "q2rT", "k2rT", "v2T"]:
        p.out(name, [D, TOK], BF16)
    din, dout = p.din, p.dout

    ones_col = p.sb("ones_col", [128, 1], F32)
    ones_row = p.sb("ones_row", [1, 128], F32)
    p.memset(ones_col[:, :], 1.0)
    p.memset(ones_row[:, :], 1.0)
    HCH = 1024
    XN1 = p.sb("XN1", [128, 8 * HCH], F32)
    HB = p.sb("HB", [128, 8 * HCH], BF16)
    Z1 = p.sb("Z1", [128, 8 * HCH], F32)
    Z1b = p.sb("Z1b", [128, 8 * HCH], BF16)
    OUTB = p.sb("OUTB", [128, 8 * HCH], BF16)
    T1 = p.sb("T1", [128, 512], F32)
    T2 = p.sb("T2", [128, 512], F32)
    WA = p.sb("WA", [128, 8 * D], BF16)
    PRJ = p.sb("PRJ", [128, 64], BF16)
    COSs = p.sb("COSs", [128, HCH], F32)
    SINs = p.sb("SINs", [128, HCH], F32)
    sq0_ = p.sb("sq0", [128, 512], F32)
    sq = [sq0_, sq0_]
    row_sum = p.sb("row_sum", [1, HCH], F32)
    row_ssq = p.sb("row_ssq", [1, HCH], F32)
    row_rs = p.sb("row_rs", [1, HCH], F32)
    row_mr = p.sb("row_mr", [1, HCH], F32)
    gc = {}
    for nm in ["g1c", "b1c", "anwc"]:
        gc[nm] = p.sb(nm + "_s", [128, 8], F32)
        p.dma("sync", gc[nm][:, :], din[nm])
    p.dma("gpsimd", PRJ[0:64, :], din["projN"])
    p.dma("gpsimd", PRJ[64:128, :], din["projN"])
    psA = p.ps("psA", [1, 512])
    psB = p.ps("psB", [1, 512])
    P0 = p.ps("P0")
    P1 = p.ps("P1")
    P2 = p.ps("P2")
    P3 = p.ps("P3")

    def loadW(nm):
        for k in range(8):
            p.dma("gpsimd", WA[:, k * D:(k + 1) * D],
                  din[nm][k * 128:(k + 1) * 128, :])

    for hf in range(2):
        xa = hf * HCH
        for k in range(8):
            p.dma("sync", XN1[:, k * HCH:(k + 1) * HCH],
                  din["xn1T"][k * 128:(k + 1) * 128, xa:xa + HCH])
        p.dma("sync", COSs[:, :], din["COSF"][:, xa:xa + HCH])
        p.dma("sync", SINs[:, :], din["SINF"][:, xa:xa + HCH])

        feat_stats(p, XN1, 8, HCH, ones_col, sq, psA, psB, row_sum, row_ssq)
        ln_rows(p, HCH, row_sum, row_ssq, row_rs, row_mr)
        apply_ln(p, XN1, Z1, 8, HCH, ones_row, row_rs, row_mr, P0, P1,
                 gain_cols=gc["g1c"], bias_cols=gc["b1c"])
        p.copy(HB[:, :], Z1[:, :], eng="scalar")

        for (wn, on) in [("wq", "qpT"), ("wk", "kpT")]:
            loadW(wn)
            proj(p, WA, HB, Z1b, [P0, P1], 8, 8, HCH)
            for h in range(16):
                o, r0 = h // 2, (h % 2) * 64
                for (a, b) in col_splits(HCH, 512):
                    w = b - a
                    ps = [P2, P3][h % 2]
                    p.mm(ps[0:64, 0:w], PRJ[r0:r0 + 64, :],
                         Z1b[r0:r0 + 64, o * HCH + a: o * HCH + b])
                    rsl = slice(r0, r0 + 64)
                    p.act(T1[rsl, 0:w], ps[0:64, 0:w], AF.Relu)
                    p.tt(T2[rsl, 0:w], ps[0:64, 0:w], T1[rsl, 0:w],
                         ALU.subtract)
                    p.act(T2[rsl, 0:w], T2[rsl, 0:w], AF.Exp)
                    p.tt(OUTB[rsl, o * HCH + a: o * HCH + b],
                         T2[rsl, 0:w], T1[rsl, 0:w], ALU.add)
            for k in range(8):
                p.dma("sync", dout[on][k * 128:(k + 1) * 128, xa:xa + HCH],
                      OUTB[:, k * HCH:(k + 1) * HCH])

        loadW("wv")
        proj(p, WA, HB, OUTB, [P0, P1], 8, 8, HCH)
        for k in range(8):
            p.dma("sync", dout["vT"][k * 128:(k + 1) * 128, xa:xa + HCH],
                  OUTB[:, k * HCH:(k + 1) * HCH])

        feat_stats(p, XN1, 8, HCH, ones_col, sq, psA, psB, row_sum, row_ssq)
        ln_rows(p, HCH, row_sum, row_ssq, row_rs, row_mr, eps=1e-6, rms=True)
        apply_ln(p, XN1, Z1, 8, HCH, ones_row, row_rs, None, P0, P1,
                 gain_cols=gc["anwc"])
        p.copy(HB[:, :], Z1[:, :], eng="scalar")

        for (w1n, w2n, on) in [("wqT", "wqTr", "q2rT"), ("wkT", "wkTr", "k2rT")]:
            loadW(w1n)
            proj(p, WA, HB, Z1, [P0, P1], 8, 8, HCH)
            loadW(w2n)
            for o in range(8):
                for (a, b) in col_splits(HCH, 512):
                    w = b - a
                    ps = [P2, P3][o % 2]
                    p.mm(ps[:, 0:w],
                         [WA[:, k * D + o * 128: k * D + o * 128 + 128]
                          for k in range(8)],
                         [HB[:, k * HCH + a: k * HCH + b] for k in range(8)])
                    sl = slice(o * HCH + a, o * HCH + b)
                    p.tt(T1[:, 0:w], ps[:, 0:w], SINs[:, a:b], ALU.mult)
                    p.tt(T2[:, 0:w], Z1[:, sl], COSs[:, a:b], ALU.mult)
                    p.tt(OUTB[:, sl], T1[:, 0:w], T2[:, 0:w], ALU.add)
            for k in range(8):
                p.dma("sync", dout[on][k * 128:(k + 1) * 128, xa:xa + HCH],
                      OUTB[:, k * HCH:(k + 1) * HCH])

        loadW("wvT")
        proj(p, WA, HB, OUTB, [P0, P1], 8, 8, HCH)
        for k in range(8):
            p.dma("sync", dout["v2T"][k * 128:(k + 1) * 128, xa:xa + HCH],
                  OUTB[:, k * HCH:(k + 1) * HCH])
    return p.finalize()


def host_l4(inputs, state):
    nc = get_nc("l4", build_l4)

    def rot_cols(w):
        wr = np.asarray(w, np.float32).copy()
        wr2 = wr.reshape(wr.shape[0], 16, 2, 32)
        return np.ascontiguousarray(wr2[:, :, ::-1].reshape(wr.shape))

    inv_freq = 1.0 / 10000.0 ** (np.arange(0, 64, 2, dtype=np.float32) / 64)
    pos = np.arange(S, dtype=np.float32)
    fr = pos[:, None] * inv_freq[None, :]
    cosf = np.cos(fr).T
    sinf = np.sin(fr).T
    wqT = _t(inputs["L1_ffn_wq"])
    wkT = _t(inputs["L1_ffn_wk"])
    com = dict(
        wq=_f(inputs["L1_attn_wq"]), wk=_f(inputs["L1_attn_wk"]),
        wv=_f(inputs["L1_attn_wv"]), projN=_f(inputs["L1_attn_proj"]),
        wqT=wqT, wqTr=rot_cols(wqT), wkT=wkT, wkTr=rot_cols(wkT),
        wvT=_t(inputs["L1_ffn_wv"]),
        g1c=_f(inputs["L1_attn_g1"]).reshape(8, 128).T.copy(),
        b1c=_f(inputs["L1_attn_beta1"]).reshape(8, 128).T.copy(),
        anwc=_f(inputs["L1_ffn_attn_norm_w"]).reshape(8, 128).T.copy(),
    )
    in_maps = []
    for c in range(NCORES):
        b, t0 = divmod(c * TOK, S)
        CO = np.zeros((128, TOK), np.float32)
        SI = np.zeros((128, TOK), np.float32)
        cs = cosf[:, t0:t0 + TOK]
        sn = sinf[:, t0:t0 + TOK]
        for q in range(4):
            CO[q * 32:(q + 1) * 32] = cs
            SI[q * 32:(q + 1) * 32] = (-sn if q % 2 == 0 else sn)
        m = dict(com)
        m["xn1T"] = np.ascontiguousarray(state["xn1T"][:, c * TOK:(c + 1) * TOK])
        m["COSF"] = CO
        m["SINF"] = SI
        in_maps.append(m)
    res = run8(nc, in_maps)
    for nm in ["qpT", "kpT", "vT", "q2rT", "k2rT", "v2T"]:
        state[nm] = np.concatenate(
            [np.asarray(res[c][nm], np.float32) for c in range(NCORES)], 1)
    return state




NQB = 16


def build_l5():
    p = Prog("l5")
    for name in ["qp", "kp", "vp", "q2", "k2", "v2"]:
        p.inp(name, [256, S], BF16)
    for v in range(4):
        p.inp("MBA%d" % v, [128, 512], F32)
        p.inp("MK%d" % v, [128, 512], F32)
    p.inp("TRIU", [128, 128], F32)
    p.inp("idf32", [128, 128], F32)
    p.out("ap_o", [256, S], BF16)
    p.out("ao_o", [256, S], BF16)
    din, dout = p.din, p.dout
    scr = p.nc.dram_tensor("scr", [8, 128], F32).ap()

    ones_row = p.sb("ones_row", [1, 128], F32)
    ones_bf = p.sb("ones_bf", [128, 1], BF16)
    p.memset(ones_row[:, :], 1.0)
    p.memset(ones_bf[:, :], 1.0)
    idf = p.sb("idf", [128, 128], F32)
    p.dma("sync", idf[:, :], din["idf32"])
    idb = p.sb("idb", [128, 128], BF16)
    p.copy(idb[:, :], idf[:, :])
    MBA = [p.sb("sMBA%d" % v, [128, 512], F32) for v in range(4)]
    MK = [p.sb("sMK%d" % v, [128, 512], F32) for v in range(4)]
    for v in range(4):
        p.dma("sync", MBA[v][:, :], din["MBA%d" % v])
        p.dma("sync", MK[v][:, :], din["MK%d" % v])
    TRI = p.sb("TRI", [128, 128], F32)
    p.dma("sync", TRI[:, :], din["TRIU"])

    QH = p.sb("QH", [64, S], BF16)
    KH = p.sb("KH", [64, S], BF16)
    VH = p.sb("VH", [64, S], BF16)
    VTOK = p.sb("VTOK", [128, 64 * 64], BF16)
    KTOK = p.sb("KTOK", [128, 64 * 64], BF16)
    STb = p.sb("STb", [128, 128], BF16)
    NUMF = p.sb("NUMF", [64, 128], F32)
    KVf = p.sb("KVf", [64, 64], F32)
    KVb = p.sb("KVb", [64, 64], BF16)
    KSf = p.sb("KSf", [64, 1], F32)
    KSb = p.sb("KSb", [64, 1], BF16)
    DRow = p.sb("DRow", [1, 128], F32)
    OST = p.sb("OST", [64, S], BF16)
    MXC = p.sb("MXC", [128, 4], F32)
    MXT = p.sb("MXT", [4, 128], F32)
    MR = p.sb("MR", [1, 512], F32)
    MBs = p.sb("MBs", [128, 512], F32)
    ET = p.sb("ET", [128, 512], BF16)
    T1 = p.sb("T1", [128, 512], F32)
    T2 = p.sb("T2", [128, 512], F32)
    OF = p.sb("OF", [64, 512], F32)
    DR = p.sb("DR", [1, 512], F32)
    PP = [p.ps("PP%d" % i) for i in range(8)]
    PPB5 = PP[5][:, 0:256].bitcast(BF16)

    for h in range(4):
        r0 = h * 64
        p.dma("sync", QH[:, :], din["qp"][r0:r0 + 64, :])
        p.dma("sync", KH[:, :], din["kp"][r0:r0 + 64, :])
        p.dma("sync", VH[:, :], din["vp"][r0:r0 + 64, :])
        for c in range(64):
            cs = slice(c * 128, (c + 1) * 128)
            p.transpose(PPB5[0:128, 0:64], VH[:, cs], idb[0:64, 0:64])
            p.copy(VTOK[:, c * 64:(c + 1) * 64], PPB5[0:128, 0:64])
            p.transpose(PPB5[0:128, 64:128], KH[:, cs], idb[0:64, 0:64])
            p.copy(KTOK[:, c * 64:(c + 1) * 64], PPB5[0:128, 64:128])
        p.memset(KVf[:, :], 0.0)
        p.memset(KVb[:, :], 0.0)
        p.memset(KSf[:, :], 0.0)
        p.memset(KSb[:, :], 0.0)
        for c in range(64):
            cs = slice(c * 128, (c + 1) * 128)
            p.mm(PP[0][:, 0:128], KH[:, cs], QH[:, cs])
            p.tt(STb[:, :], PP[0][:, 0:128], TRI[:, :], ALU.mult)

            def emitnum(t, cs=cs, c=c):
                t.matmul(PP[1][0:64, 0:128], VTOK[:, c * 64:(c + 1) * 64],
                         STb[:, :], start=True, stop=False)
                return t.matmul(PP[1][0:64, 0:128], KVb[:, :],
                                QH[:, cs], start=False, stop=True)
            p.rec("tensor", emitnum, [VTOK, STb, KVb, QH], [PP[1]])

            def emitden(t, cs=cs):
                t.matmul(PP[2][0:1, 0:128], ones_bf[:, 0:1],
                         STb[:, :], start=True, stop=False)
                return t.matmul(PP[2][0:1, 0:128], KSb[:, 0:1],
                                QH[:, cs], start=False, stop=True)
            p.rec("tensor", emitden, [ones_bf, STb, KSb, QH], [PP[2]])
            p.copy(NUMF[:, :], PP[1][0:64, 0:128])
            p.ts(DRow[0:1, :], PP[2][0:1, 0:128], 1e-6, None, ALU.add)
            p.recip(DRow[0:1, :], DRow[0:1, :])
            p.mm(PP[3][0:64, 0:128], ones_row[0:1, 0:64], DRow[0:1, :])
            p.tt(OST[:, cs], NUMF[:, :], PP[3][0:64, 0:128], ALU.mult)
            p.mm(PP[4][0:64, 0:64], KTOK[:, c * 64:(c + 1) * 64],
                 VTOK[:, c * 64:(c + 1) * 64])
            p.tt(KVf[:, :], KVf[:, :], PP[4][0:64, 0:64], ALU.add)
            p.copy(KVb[:, :], KVf[:, :])
            p.reduce(T2[0:64, 0:1], KH[:, cs], ALU.add)
            p.tt(KSf[:, 0:1], KSf[:, 0:1], T2[0:64, 0:1], ALU.add)
            p.copy(KSb[:, 0:1], KSf[:, 0:1])
        p.dma("sync", dout["ap_o"][r0:r0 + 64, :], OST[:, :])

    for h in range(4):
        r0 = h * 64
        p.dma("sync", QH[:, :], din["q2"][r0:r0 + 64, :])
        p.dma("sync", KH[:, :], din["k2"][r0:r0 + 64, :])
        p.dma("sync", VH[:, :], din["v2"][r0:r0 + 64, :])
        p.ts(QH[:, :], QH[:, :], 0.125, None, ALU.mult)
        for c in range(64):
            p.transpose(PPB5[0:128, 0:64], VH[:, c * 128:(c + 1) * 128],
                        idb[0:64, 0:64])
            p.copy(VTOK[:, c * 64:(c + 1) * 64], PPB5[0:128, 0:64])
        for qb in range(NQB):
            Q0b = qb * 512
            p.memset(MXC[:, :], -3e38)
            for qt in range(4):
                Q0 = Q0b + qt * 128
                for j in range(Q0 // 512 + 1):
                    ps = PP[(j % 2)]
                    p.mm(ps[:, 0:512], QH[:, Q0:Q0 + 128],
                         KH[:, j * 512:(j + 1) * 512])
                    v_ = (Q0 - j * 512) // 128
                    if 0 <= v_ <= 3:
                        p.tt(T1[:, 0:512], ps[:, 0:512], MBA[v_][:, :],
                             ALU.add)
                        p.reduce(T1[:, 0:1], T1[:, 0:512], ALU.max)
                    else:
                        p.reduce(T1[:, 0:1], ps[:, 0:512], ALU.max)
                    p.tt(MXC[:, qt:qt + 1], MXC[:, qt:qt + 1],
                         T1[:, 0:1], ALU.max)
            p.transpose(PP[2][0:4, 0:128], MXC[:, 0:4], idf[:, :])
            p.copy(MXT[0:4, :], PP[2][0:4, 0:128])
            p.dma("sync", scr[0:4, :], MXT[0:4, :])
            p.dma("sync", MR[0:1, 0:512],
                  scr[0:4, :].rearrange("a b -> (a b)")[None, :])
            p.ts(MR[0:1, :], MR[0:1, :], -1.0, None, ALU.mult)
            p.mm(PP[3][:, 0:512], ones_row[0:1, :], MR[0:1, 0:512])
            p.copy(MBs[:, :], PP[3][:, 0:512])
            nk = (Q0b + 512) // 128
            first = True
            for ck in range(nk):
                K0 = ck * 128
                ps = PP[(ck % 2)]
                p.mm(ps[:, 0:512], KH[:, K0:K0 + 128], QH[:, Q0b:Q0b + 512])
                p.tt(T1[:, 0:512], ps[:, 0:512], MBs[:, :], ALU.add)
                p.act(ET[:, :], T1[:, 0:512], AF.Exp)
                u = (K0 - Q0b) // 128
                if 0 <= u <= 3:
                    p.tt(ET[:, :], ET[:, :], MK[u][:, :], ALU.mult)
                p.mm_more(PP[6][0:64, 0:512], VTOK[:, ck * 64:(ck + 1) * 64],
                          ET[:, :], start=first, stop=(ck == nk - 1))
                p.mm_more(PP[7][0:1, 0:512], ones_bf[:, 0:1], ET[:, :],
                          start=first, stop=(ck == nk - 1))
                first = False
            p.recip(DR[0:1, :], PP[7][0:1, 0:512])
            p.mm(PP[3][0:64, 0:512], ones_row[0:1, 0:64], DR[0:1, 0:512])
            p.copy(OF[0:64, :], PP[6][0:64, 0:512])
            p.tt(OST[:, Q0b:Q0b + 512], OF[0:64, :], PP[3][0:64, 0:512],
                 ALU.mult)
        p.dma("sync", dout["ao_o"][r0:r0 + 64, :], OST[:, :])
    return p.finalize()


def host_l5(inputs, state):
    nc = get_nc("l5", build_l5)
    MBAs, MKs = [], []
    for v in range(4):
        qq = np.arange(128)[:, None]
        kk = np.arange(512)[None, :]
        MBAs.append(np.where(kk <= v * 128 + qq, 0.0, -1e30).astype(np.float32))
        MKs.append((v * 128 + np.arange(128)[:, None]
                    <= np.arange(512)[None, :]).astype(np.float32))
    tri = (np.arange(128)[:, None] <= np.arange(128)[None, :]) \
        .astype(np.float32)
    com = {"TRIU": tri, "idf32": np.eye(128, dtype=np.float32)}
    for v in range(4):
        com["MBA%d" % v] = MBAs[v]
        com["MK%d" % v] = MKs[v]
    in_maps = []
    for c in range(NCORES):
        b, hg = c // 4, c % 4
        rows = slice(hg * 256, hg * 256 + 256)
        cols = slice(b * S, (b + 1) * S)
        m = dict(com)
        for nm, key in [("qp", "qpT"), ("kp", "kpT"), ("vp", "vT"),
                        ("q2", "q2rT"), ("k2", "k2rT"), ("v2", "v2T")]:
            m[nm] = np.ascontiguousarray(state[key][rows, cols]).astype(BF)
        in_maps.append(m)
    res = run8(nc, in_maps)
    ap = np.zeros((NB, D, S), np.float32)
    ao = np.zeros((NB, D, S), np.float32)
    for c in range(NCORES):
        b, hg = c // 4, c % 4
        ap[b, hg * 256:hg * 256 + 256] = np.asarray(res[c]["ap_o"], np.float32)
        ao[b, hg * 256:hg * 256 + 256] = np.asarray(res[c]["ao_o"], np.float32)
    state["apT"] = np.ascontiguousarray(np.concatenate([ap[0], ap[1]], 1))
    state["aoT"] = np.ascontiguousarray(np.concatenate([ao[0], ao[1]], 1))
    return state




def emit_top2(p, PRps, MIX, RM, RL2, RM2, RV, WRT, RRC):
    p.copy(RM[:, 0:8], PRps)
    p.reduce(RV[:, 0:1], RM[:, 0:8], ALU.max)
    p.ts(RL2[:, 0:8], RM[:, 0:8], RV[:, 0:1], None, ALU.is_equal)
    p.stt(RM2[:, 0:8], RL2[:, 0:8], -1e30, RM[:, 0:8], ALU.mult, ALU.add)
    p.reduce(RV[:, 1:2], RM2[:, 0:8], ALU.max)
    p.ts(WRT[:, 0:8], RM2[:, 0:8], RV[:, 1:2], None, ALU.is_equal)
    p.tt(RV[:, 2:3], RV[:, 1:2], RV[:, 0:1], ALU.subtract)
    p.act(RV[:, 3:4], RV[:, 2:3], AF.Exp)
    p.ts(RV[:, 4:5], RV[:, 3:4], 1.0, None, ALU.add)
    p.recip(RV[:, 6:7], RV[:, 4:5])
    p.tt(RV[:, 5:6], RV[:, 3:4], RV[:, 6:7], ALU.mult)
    p.ts(RL2[:, 0:8], RL2[:, 0:8], RV[:, 6:7], None, ALU.mult)
    p.ts(WRT[:, 0:8], WRT[:, 0:8], RV[:, 5:6], None, ALU.mult)
    p.tt(WRT[:, 0:8], WRT[:, 0:8], RL2[:, 0:8], ALU.add)



def build_l6a():
    p = Prog("l6a")
    for name, shape, dt in [
        ("xn1T", [D, TOK], F32), ("x1T", [D, TOK], F32),
        ("apT", [D, TOK], BF16), ("aoT", [D, TOK], BF16),
        ("wo1", [D, D], F32), ("w1f", [D, 4096], F32),
        ("w2f", [4096, D], F32),
        ("b1fc", [128, 32], F32), ("b2fc", [128, 8], F32),
        ("g2c", [128, 8], F32), ("be2c", [128, 8], F32),
        ("woT2", [D, D], F32), ("mnwc", [128, 8], F32),
        ("gwT", [D, 8], F32),
        ("w1L2", [8 * 32 * 128, 8 * 128], F32),
        ("w2L2", [8 * 8 * 128, 32 * 128], F32),
        ("b1c2", [128, 8 * 32], F32), ("b2nat", [8, D], F32),
        ("idf32", [128, 128], F32),
    ]:
        p.inp(name, shape, dt)
    p.out("x2T", [D, TOK], F32)
    p.out("xn2T", [D, TOK], F32)
    din, dout = p.din, p.dout
    scr = p.nc.dram_tensor("scr2", [8, TOK], F32).ap()

    ones_col = p.sb("ones_col", [128, 1], F32)
    ones_row = p.sb("ones_row", [1, 128], F32)
    p.memset(ones_col[:, :], 1.0)
    p.memset(ones_row[:, :], 1.0)
    idf = p.sb("idf", [128, 128], F32)
    p.dma("sync", idf[:, :], din["idf32"])
    HCH = 1024
    BUF1 = p.sb("BUF1", [128, 8 * HCH], F32)
    XATT = p.sb("XATT", [128, 8 * HCH], F32)
    HRES = p.sb("HRES", [128, 8 * HCH], F32)
    ABuf = p.sb("ABuf", [128, 8 * HCH], BF16)
    HBb = p.sb("HBb", [128, 8 * HCH], BF16)
    HID = p.sb("HID", [128, 32 * 512], BF16)
    WA = p.sb("WA", [128, 8 * D], BF16)
    W1B_ = [p.sb("W1B0", [128, 8 * 128], BF16)] * 2
    W2B_ = [p.sb("W2B0", [128, 16 * 128], BF16)] * 2
    B2S = p.sb("B2S", [8, 128], F32)
    WRB = p.sb("WRB", [128, 512], F32)
    T1 = p.sb("T1", [128, 512], F32)
    GW = WRB[:, 384:448]
    RM = T1[:, 200:208]
    RL2 = T1[:, 208:216]
    RM2 = T1[:, 216:224]
    RV = T1[:, 224:234]
    WRT = T1[:, 234:242]
    MIX = RM
    RRC = RM
    sq0_ = p.sb("sq0", [128, 512], F32)
    sq = [sq0_, sq0_]
    row_sum = p.sb("row_sum", [1, HCH], F32)
    row_ssq = p.sb("row_ssq", [1, HCH], F32)
    row_rs = p.sb("row_rs", [1, HCH], F32)
    row_mr = p.sb("row_mr", [1, HCH], F32)
    gc = {}
    for nm in ["b1fc", "b2fc", "g2c", "be2c", "mnwc", "b1c2"]:
        shp = {"b1fc": [128, 32], "b1c2": [128, 8 * 32]}.get(nm, [128, 8])
        gc[nm] = p.sb(nm + "_s", shp, F32)
        p.dma("sync", gc[nm][:, :], din[nm])

    psA = p.ps("psA", [1, 512])
    psB = p.ps("psB", [1, 512])
    P0 = p.ps("P0")
    P1 = p.ps("P1")
    P2 = p.ps("P2")
    P3 = p.ps("P3")
    PR = p.ps("PR", [128, 128])

    def loadW(nm):
        for k in range(8):
            p.dma("gpsimd", WA[:, k * D:(k + 1) * D],
                  din[nm][k * 128:(k + 1) * 128, :])

    for hf in range(2):
        xa = hf * HCH
        for k in range(8):
            p.dma("sync", BUF1[:, k * HCH:(k + 1) * HCH],
                  din["xn1T"][k * 128:(k + 1) * 128, xa:xa + HCH])
            p.dma("sync", ABuf[:, k * HCH:(k + 1) * HCH],
                  din["apT"][k * 128:(k + 1) * 128, xa:xa + HCH])
        loadW("wo1")
        proj(p, WA, ABuf, XATT, [P0, P1], 8, 8, HCH)
        p.tt(XATT[:, :], XATT[:, :], BUF1[:, :], ALU.add)
        feat_stats(p, XATT, 8, HCH, ones_col, sq, psA, psB, row_sum, row_ssq)
        ln_rows(p, HCH, row_sum, row_ssq, row_rs, row_mr)
        apply_ln(p, XATT, HRES, 8, HCH, ones_row, row_rs, row_mr, P0, P1,
                 gain_cols=gc["g2c"], bias_cols=gc["be2c"])
        p.copy(HBb[:, :], HRES[:, :], eng="scalar")
        for tb in range(HCH // 512):
            ta = tb * 512
            for o in range(32):
                W1B = W1B_[o % 2]
                p.dma("gpsimd", W1B[:, :].rearrange("p (k n) -> p k n", n=128),
                      din["w1f"][0:D, o * 128:(o + 1) * 128]
                      .rearrange("(k p) n -> p k n", p=128))
                ps = [P0, P1][o % 2]
                p.mm(ps[:, 0:512],
                     [W1B[:, k * 128:(k + 1) * 128] for k in range(8)],
                     [HBb[:, k * HCH + ta: k * HCH + ta + 512]
                      for k in range(8)])
                p.act(HID[:, o * 512:(o + 1) * 512], ps[:, 0:512], AF.Gelu,
                      bias=gc["b1fc"][:, o:o + 1])
            for d in range(8):
                ps = [P2, P3][d % 2]
                for half in range(2):
                    hb = half * 16
                    W2B = W2B_[half]
                    p.dma("gpsimd",
                          W2B[:, :].rearrange("p (k n) -> p k n", n=128),
                          din["w2f"][hb * 128:(hb + 16) * 128,
                                     d * 128:(d + 1) * 128]
                          .rearrange("(k p) n -> p k n", p=128))
                    p.mm_more(ps[:, 0:512],
                              [W2B[:, h * 128:(h + 1) * 128]
                               for h in range(16)],
                              [HID[:, (hb + h) * 512:(hb + h + 1) * 512]
                               for h in range(16)],
                              start=(half == 0), stop=(half == 1))
                p.act(T1[:, 0:512], ps[:, 0:512], AF.Identity,
                      bias=gc["b2fc"][:, d:d + 1])
                p.tt(XATT[:, d * HCH + ta: d * HCH + ta + 512],
                     XATT[:, d * HCH + ta: d * HCH + ta + 512],
                     T1[:, 0:512], ALU.add)
        for k in range(8):
            p.dma("sync", ABuf[:, k * HCH:(k + 1) * HCH],
                  din["aoT"][k * 128:(k + 1) * 128, xa:xa + HCH])
        loadW("woT2")
        proj(p, WA, ABuf, HRES, [P0, P1], 8, 8, HCH)
        p.tt(HRES[:, :], HRES[:, :], BUF1[:, :], ALU.add)
        feat_stats(p, HRES, 8, HCH, ones_col, sq, psA, psB, row_sum, row_ssq)
        ln_rows(p, HCH, row_sum, row_ssq, row_rs, row_mr, eps=1e-6, rms=True)
        apply_ln(p, HRES, BUF1, 8, HCH, ones_row, row_rs, None, P0, P1,
                 gain_cols=gc["mnwc"])
        p.copy(HBb[:, :], BUF1[:, :], eng="scalar")
        for k in range(8):
            p.dma("sync", GW[:, k * 8:(k + 1) * 8],
                  din["gwT"][k * 128:(k + 1) * 128, :])
        for i in range(HCH // 128):
            tbase = i * 128
            p.mm(PR[:, 0:8],
                 [BUF1[:, k * HCH + tbase: k * HCH + tbase + 128]
                  for k in range(8)],
                 [GW[:, k * 8:(k + 1) * 8] for k in range(8)])
            emit_top2(p, PR[:, 0:8], MIX, RM, RL2, RM2, RV, WRT, RRC)
            p.transpose(PR[0:8, 0:128], WRT[:, 0:8], idf[:, :])
            p.copy(T1[0:8, 0:128], PR[0:8, 0:128])
            p.dma("sync", scr[0:8, xa + tbase: xa + tbase + 128],
                  T1[0:8, 0:128])
        for e in range(8):
            p.dma("sync", row_mr[0:1, 0:HCH], scr[e:e + 1, xa:xa + HCH])
            for tb in range(HCH // 512):
                ta = tb * 512
                p.mm(P2[:, 0:512], ones_row[0:1, :], row_mr[0:1, ta:ta + 512])
                p.copy(WRB[:, 0:512], P2[:, 0:512])
                for h in range(32):
                    r0_ = (e * 32 + h) * 128
                    W1B = W1B_[h % 2]
                    p.dma("gpsimd", W1B[:, :], din["w1L2"][r0_:r0_ + 128, :])
                    ps = [P0, P1][h % 2]
                    p.mm(ps[:, 0:512],
                         [W1B[:, k * 128:(k + 1) * 128] for k in range(8)],
                         [HBb[:, k * HCH + ta: k * HCH + ta + 512]
                          for k in range(8)])
                    p.act(T1[:, 0:512], ps[:, 0:512], AF.Gelu,
                          bias=gc["b1c2"][:, e * 32 + h: e * 32 + h + 1])
                    p.tt(HID[:, h * 512:(h + 1) * 512], T1[:, 0:512],
                         WRB[:, 0:512], ALU.mult)
                for d in range(8):
                    r0_ = (e * 8 + d) * 128
                    ps = [P2, P3][d % 2]
                    for half in range(2):
                        hb = half * 16
                        W2B = W2B_[half]
                        p.dma("gpsimd", W2B[:, :],
                              din["w2L2"][r0_:r0_ + 128,
                                          hb * 128:(hb + 16) * 128])
                        p.mm_more(ps[:, 0:512],
                                  [W2B[:, h * 128:(h + 1) * 128]
                                   for h in range(16)],
                                  [HID[:, (hb + h) * 512:(hb + h + 1) * 512]
                                   for h in range(16)],
                                  start=(half == 0), stop=(half == 1))
                    p.tt(HRES[:, d * HCH + ta: d * HCH + ta + 512],
                         HRES[:, d * HCH + ta: d * HCH + ta + 512],
                         ps[:, 0:512], ALU.add)
        for d in range(8):
            p.dma("sync", B2S[0:8, 0:128],
                  din["b2nat"][0:8, d * 128:(d + 1) * 128])
            for tb in range(HCH // 512):
                ta = tb * 512
                ps = [P0, P1][tb % 2]
                def eb2(t, ps=ps):
                    return t.matmul(ps[:, 0:512], B2S[0:8, 0:128],
                                    T1[0:8, 0:512], start=True,
                                    stop=True)
                p.dma("sync", T1[0:8, 0:512],
                      scr[0:8, xa + ta: xa + ta + 512])
                p.rec("tensor", eb2, [B2S, T1], [ps])
                p.tt(HRES[:, d * HCH + ta: d * HCH + ta + 512],
                     HRES[:, d * HCH + ta: d * HCH + ta + 512],
                     ps[:, 0:512], ALU.add)
        p.act(HRES[:, :], HRES[:, :], AF.Silu)
        p.tt(XATT[:, :], XATT[:, :], HRES[:, :], ALU.mult)
        for k in range(8):
            p.dma("sync", BUF1[:, k * HCH:(k + 1) * HCH],
                  din["x1T"][k * 128:(k + 1) * 128, xa:xa + HCH])
        p.tt(BUF1[:, :], BUF1[:, :], XATT[:, :], ALU.add)
        for k in range(8):
            p.dma("sync", dout["x2T"][k * 128:(k + 1) * 128, xa:xa + HCH],
                  BUF1[:, k * HCH:(k + 1) * HCH])
        feat_stats(p, BUF1, 8, HCH, ones_col, sq, psA, psB, row_sum, row_ssq)
        ln_rows(p, HCH, row_sum, row_ssq, row_rs, row_mr)
        apply_ln(p, BUF1, BUF1, 8, HCH, ones_row, row_rs, row_mr, P0, P1)
        for k in range(8):
            p.dma("sync", dout["xn2T"][k * 128:(k + 1) * 128, xa:xa + HCH],
                  BUF1[:, k * HCH:(k + 1) * HCH])
    return p.finalize()


def host_l6a(inputs, state):
    nc = get_nc("l6a", build_l6a)
    w1 = np.asarray(inputs["L1_ffn_w1"], np.float32)
    w2 = np.asarray(inputs["L1_ffn_w2"], np.float32)
    w1T = w1.transpose(0, 2, 1)
    w1L = np.zeros((8, 32, 128, 8 * 128), np.float32)
    w1r = w1T.reshape(8, 8, 128, 32, 128)
    for h in range(32):
        for k in range(8):
            w1L[:, h, :, k * 128:(k + 1) * 128] = w1r[:, k, :, h]
    w2T = w2.transpose(0, 2, 1)
    w2L = np.zeros((8, 8, 128, 32 * 128), np.float32)
    w2r = w2T.reshape(8, 32, 128, 8, 128)
    for d in range(8):
        for h in range(32):
            w2L[:, d, :, h * 128:(h + 1) * 128] = w2r[:, h, :, d]
    b1 = np.asarray(inputs["L1_ffn_b1"], np.float32)
    b1c2 = np.zeros((128, 8 * 32), np.float32)
    for e in range(8):
        b1c2[:, e * 32:(e + 1) * 32] = b1[e].reshape(32, 128).T
    com = dict(
        wo1=_f(inputs["L1_attn_wo"]), w1f=_f(inputs["L1_attn_w1"]),
        w2f=_f(inputs["L1_attn_w2"]),
        b1fc=_f(inputs["L1_attn_b1"]).reshape(32, 128).T.copy(),
        b2fc=_f(inputs["L1_attn_b2"]).reshape(8, 128).T.copy(),
        g2c=_f(inputs["L1_attn_g2"]).reshape(8, 128).T.copy(),
        be2c=_f(inputs["L1_attn_beta2"]).reshape(8, 128).T.copy(),
        woT2=_t(inputs["L1_ffn_wo"]),
        mnwc=_f(inputs["L1_ffn_moe_norm_w"]).reshape(8, 128).T.copy(),
        gwT=_t(inputs["L1_ffn_gate_w"]),
        w1L2=np.ascontiguousarray(w1L.reshape(8 * 32 * 128, 8 * 128)),
        w2L2=np.ascontiguousarray(w2L.reshape(8 * 8 * 128, 32 * 128)),
        b1c2=b1c2, b2nat=_f(inputs["L1_ffn_b2"]),
        idf32=np.eye(128, dtype=np.float32),
    )
    in_maps = []
    for c in range(NCORES):
        m = dict(com)
        for nm, key in [("xn1T", "xn1T"), ("x1T", "x1T"),
                        ("apT", "apT"), ("aoT", "aoT")]:
            arr = state[key][:, c * TOK:(c + 1) * TOK]
            m[nm] = np.ascontiguousarray(arr).astype(
                BF if nm in ("apT", "aoT") else np.float32)
        in_maps.append(m)
    res = run8(nc, in_maps)
    state["x2T"] = np.concatenate([res[c]["x2T"] for c in range(NCORES)], 1)
    state["xn2T"] = np.concatenate([res[c]["xn2T"] for c in range(NCORES)], 1)
    return state




def build_l6b():
    import os
    SKIP = os.environ.get("L6B_SKIP", "")
    p = Prog("l6b")
    for name, shape, dt in [
        ("xn2T", [D, TOK], F32),
        ("qwT", [D, D], F32), ("kwT", [D, D], F32), ("vwT", [D, D], F32),
        ("qbc", [128, 8], F32), ("kbc", [128, 8], F32), ("vbc", [128, 8], F32),
        ("qgc", [128, 8], F32), ("qBc", [128, 8], F32),
        ("kgc", [128, 8], F32), ("kBc", [128, 8], F32),
        ("vgc", [128, 8], F32), ("vBc", [128, 8], F32),
        ("idf32", [128, 128], F32),
    ]:
        p.inp(name, shape, dt)
    p.out("q3T", [D, TOK], BF16)
    p.out("k3T", [D, TOK], BF16)
    p.out("v3T", [D, TOK], BF16)
    p.out("xqT", [D, TOK], BF16)
    din, dout = p.din, p.dout

    ones_col = p.sb("ones_col", [128, 1], F32)
    ones_row = p.sb("ones_row", [1, 128], F32)
    p.memset(ones_col[:, :], 1.0)
    p.memset(ones_row[:, :], 1.0)
    idf = p.sb("idf", [128, 128], F32)
    p.dma("sync", idf[:, :], din["idf32"])
    idb = p.sb("idb", [128, 128], BF16)
    p.copy(idb[:, :], idf[:, :])
    HCH = 1024
    XN2 = p.sb("XN2", [128, 8 * HCH], F32)
    XNN = p.sb("XNN", [128, 8 * HCH], F32)
    QF = p.sb("QF", [128, 8 * HCH], F32)
    NB_ = p.sb("NBB", [128, 8 * HCH], BF16)
    WA = p.sb("WA", [128, 8 * D], BF16)
    T1 = p.sb("T1", [128, 512], F32)
    QT = p.sb("QT", [128, 1024], F32)
    QS = p.sb("QSc", [128, 64], F32)
    QR = p.sb("QRc", [128, 64], F32)
    QC1 = p.sb("QC1c", [128, 1024], F32)
    QC2 = p.sb("QC2c", [128, 1024], F32)
    QO = p.sb("QOc", [128, 1024], BF16)
    sq0_ = p.sb("sq0", [128, 512], F32)
    sq = [sq0_, sq0_]
    row_sum = p.sb("row_sum", [1, HCH], F32)
    row_ssq = p.sb("row_ssq", [1, HCH], F32)
    row_rs = p.sb("row_rs", [1, HCH], F32)
    row_mr = p.sb("row_mr", [1, HCH], F32)
    gc = {}
    for nm in ["qbc", "kbc", "vbc", "qgc", "qBc", "kgc", "kBc", "vgc", "vBc"]:
        gc[nm] = p.sb(nm + "_s", [128, 8], F32)
        p.dma("sync", gc[nm][:, :], din[nm])
    psA = p.ps("psA", [1, 512])
    psB = p.ps("psB", [1, 512])
    P0 = p.ps("P0")
    P1 = p.ps("P1")
    P2 = p.ps("P2")
    P3 = p.ps("P3")

    for hf in range(2):
        xa = hf * HCH
        for k in range(8):
            p.dma("sync", XN2[:, k * HCH:(k + 1) * HCH],
                  din["xn2T"][k * 128:(k + 1) * 128, xa:xa + HCH])
        feat_stats(p, XN2, 8, HCH, ones_col, sq, psA, psB, row_sum, row_ssq)
        ln_rows(p, HCH, row_sum, row_ssq, row_rs, row_mr)
        apply_ln(p, XN2, XNN, 8, HCH, ones_row, row_rs, row_mr, P0, P1)

        plist = [] if "proj" in SKIP else [
                ("qwT", "qbc", "qgc", "qBc", "q3T", True),
                ("kwT", "kbc", "kgc", "kBc", "k3T", True),
                ("vwT", "vbc", "vgc", "vBc", "v3T", False)]
        for (wn, bn, gn, Bn, on, do_norm) in plist:
            for k in range(8):
                sl = slice(k * HCH, (k + 1) * HCH)
                p.ts(QF[:, sl], XNN[:, sl], gc[gn][:, k:k + 1], None, ALU.mult)
                p.ts(QF[:, sl], QF[:, sl], gc[Bn][:, k:k + 1], None, ALU.add)
            p.copy(NB_[:, :], QF[:, :], eng="scalar")
            loadW(p, WA, din[wn])
            proj(p, WA, NB_, QF, [P0, P1], 8, 8, HCH, bias=gc[bn])
            if do_norm:
                for (a, b) in col_splits(HCH, 512):
                    w = b - a
                    for k in range(8):
                        p.act(sq[0][:, 0:w], QF[:, k * HCH + a: k * HCH + b],
                              AF.Square)
                        p.mm_more(psA[0:1, 0:w], ones_col[:, 0:1],
                                  sq[0][:, 0:w], start=(k == 0),
                                  stop=(k == 7))
                    p.copy(row_sum[0:1, a:b], psA[0:1, 0:w])
                p.act(row_sum[0:1, 0:HCH], row_sum[0:1, 0:HCH], AF.Sqrt)
                p.ts(row_sum[0:1, 0:HCH], row_sum[0:1, 0:HCH], 1e-6, None,
                     ALU.max)
                p.recip(row_rs[0:1, 0:HCH], row_sum[0:1, 0:HCH])
                apply_ln(p, QF, QF, 8, HCH, ones_row, row_rs, None, P2, P3)
            p.copy(NB_[:, :], QF[:, :], eng="scalar")
            for k in range(8):
                p.dma("sync", dout[on][k * 128:(k + 1) * 128, xa:xa + HCH],
                      NB_[:, k * HCH:(k + 1) * HCH])

        for tt_ in range([] if False else (0 if "quant" in SKIP else HCH // 128)):
            tb = tt_ * 128
            for k in range(8):
                p.transpose(P2[:, 0:128], XN2[:, k * HCH + tb: k * HCH + tb + 128],
                            idf[:, :])
                p.copy(QT[:, k * 128:(k + 1) * 128], P2[:, 0:128])
            if "fp4" not in SKIP:
                fp4_qd_generic(p, QT, QS, QR, QC1, QC2, None, 128, 1024,
                               out_f32=True)
            else:
                p.copy(QC2[:, :], QT[:, :])
            for k in range(8):
                if "back" in SKIP:
                    continue
                p.transpose(P3[:, 0:128], QC2[:, k * 128:(k + 1) * 128],
                            idf[:, :])
                p.copy(NB_[:, k * HCH + tb: k * HCH + tb + 128],
                       P3[:, 0:128])
        for k in range(8):
            p.dma("sync", dout["xqT"][k * 128:(k + 1) * 128, xa:xa + HCH],
                  NB_[:, k * HCH:(k + 1) * HCH])
    return p.finalize()


def loadW(p, WA, dsrc):
    for k in range(8):
        p.dma("gpsimd", WA[:, k * D:(k + 1) * D], dsrc[k * 128:(k + 1) * 128, :])


def fp4_qd_generic(p, A, Srec, R, C1, C2, O, rows, colw, out_f32=False):
    nblk = colw // 16
    rs, cw, nb = slice(0, rows), slice(0, colw), slice(0, nblk)
    p.reduce(Srec[rs, nb], A[rs, cw].rearrange("p (g b) -> p g b", b=16),
             ALU.max, absval=True)
    p.ts(Srec[rs, nb], Srec[rs, nb], 1.0 / 6.0, 1e-12, ALU.mult, ALU.max)
    p.recip(R[rs, nb], Srec[rs, nb])
    p.act(C1[rs, cw], A[rs, cw], AF.Abs)
    rb = R[rs, nb].rearrange("p (g one) -> p g one", one=1) \
        .broadcast_to([rows, nblk, 16])
    p.tt(C1[rs, cw].rearrange("p (g b) -> p g b", b=16),
         C1[rs, cw].rearrange("p (g b) -> p g b", b=16), rb, ALU.mult)
    p.ts(C2[rs, cw], C1[rs, cw], 0.25, None, ALU.is_gt)
    for m in (0.75, 1.25, 1.75):
        p.stt(C2[rs, cw], C1[rs, cw], m, C2[rs, cw], ALU.is_gt, ALU.add)
    p.ts(C2[rs, cw], C2[rs, cw], 0.5, None, ALU.mult)
    for m in (2.5, 3.5, 5.0, 5.0):
        p.stt(C2[rs, cw], C1[rs, cw], m, C2[rs, cw], ALU.is_gt, ALU.add)
    sb_ = Srec[rs, nb].rearrange("p (g one) -> p g one", one=1) \
        .broadcast_to([rows, nblk, 16])
    p.tt(C2[rs, cw].rearrange("p (g b) -> p g b", b=16),
         C2[rs, cw].rearrange("p (g b) -> p g b", b=16), sb_, ALU.mult)
    p.act(C1[rs, cw], A[rs, cw], AF.Sign)
    if out_f32:
        p.tt(C2[rs, cw], C2[rs, cw], C1[rs, cw], ALU.mult)
    else:
        p.tt(O[rs, cw], C2[rs, cw], C1[rs, cw], ALU.mult)


def host_l6b(inputs, state):
    nc = get_nc("l6b", build_l6b)

    def col8(v):
        return _f(v).reshape(8, 128).T.copy()

    com = dict(
        qwT=_t(inputs["L2_attn_q_w"]), kwT=_t(inputs["L2_attn_k_w"]),
        vwT=_t(inputs["L2_attn_v_w"]),
        qbc=col8(inputs["L2_attn_q_b"]), kbc=col8(inputs["L2_attn_k_b"]),
        vbc=col8(inputs["L2_attn_v_b"]),
        qgc=col8(inputs["L2_attn_ln_q_g"]), qBc=col8(inputs["L2_attn_ln_q_b"]),
        kgc=col8(inputs["L2_attn_ln_k_g"]), kBc=col8(inputs["L2_attn_ln_k_b"]),
        vgc=col8(inputs["L2_attn_ln_v_g"]), vBc=col8(inputs["L2_attn_ln_v_b"]),
        idf32=np.eye(128, dtype=np.float32),
    )
    in_maps = []
    for c in range(NCORES):
        m = dict(com)
        m["xn2T"] = np.ascontiguousarray(state["xn2T"][:, c * TOK:(c + 1) * TOK])
        in_maps.append(m)
    res = run8(nc, in_maps)
    for nm in ["q3T", "k3T", "v3T", "xqT"]:
        state[nm] = np.concatenate(
            [np.asarray(res[c][nm], np.float32) for c in range(NCORES)], 1)
    return state




def build_l7():
    p = Prog("l7")
    for name in ["q3", "k3", "v3"]:
        p.inp(name, [256, S], BF16)
    p.inp("betap", [1, 16], F32)
    p.inp("selh", [4, 128], F32)
    p.inp("idf32", [128, 128], F32)
    p.out("o3_o", [256, S], BF16)
    din, dout = p.din, p.dout

    ones_row = p.sb("ones_row", [1, 128], F32)
    ones_bf = p.sb("ones_bf", [128, 1], BF16)
    p.memset(ones_row[:, :], 1.0)
    p.memset(ones_bf[:, :], 1.0)
    idf = p.sb("idf", [128, 128], F32)
    p.dma("sync", idf[:, :], din["idf32"])
    idb = p.sb("idb", [128, 128], BF16)
    p.copy(idb[:, :], idf[:, :])
    QH = p.sb("QH", [64, S], BF16)
    KH = p.sb("KH", [64, S], BF16)
    VH = p.sb("VH", [64, S], BF16)
    VTOK = p.sb("VTOK", [128, 64 * 64], BF16)
    ET = p.sb("ET", [128, 512], BF16)
    OF = p.sb("OF", [64, 512], F32)
    DR = p.sb("DR", [1, 512], F32)
    OST = p.sb("OST", [64, S], BF16)
    BR = p.sb("BR", [1, 16], F32)
    BC = p.sb("BC", [64, 1], F32)
    SEL = p.sb("SEL", [4, 128], F32)
    T0 = p.sb("T0", [1, 16], F32)
    PP = [p.ps("PP%d" % i) for i in range(8)]
    PPB5 = PP[5][:, 0:256].bitcast(BF16)

    p.dma("sync", SEL[0:4, :], din["selh"])
    p.dma("sync", BR[0:1, :], din["betap"])
    p.act(T0[0:1, :], BR[0:1, :], AF.Abs)
    p.ts(T0[0:1, :], T0[0:1, :], -1.0, None, ALU.mult)
    p.act(T0[0:1, :], T0[0:1, :], AF.Exp)
    p.act(T0[0:1, :], T0[0:1, :], AF.Ln, bias=1.0)
    p.act(BR[0:1, :], BR[0:1, :], AF.Relu)
    p.tt(BR[0:1, :], BR[0:1, :], T0[0:1, :], ALU.add)
    p.ts(BR[0:1, :], BR[0:1, :], 0.1, None, ALU.max)
    p.ts(BR[0:1, :], BR[0:1, :], 10.0, None, ALU.min)
    p.ts(BR[0:1, :], BR[0:1, :], 0.125, None, ALU.mult)

    for h in range(4):
        r0 = h * 64
        p.dma("sync", QH[:, :], din["q3"][r0:r0 + 64, :])
        p.dma("sync", KH[:, :], din["k3"][r0:r0 + 64, :])
        p.dma("sync", VH[:, :], din["v3"][r0:r0 + 64, :])
        p.rec("tensor",
              lambda t, h=h: t.matmul(PP[3][0:64, 0:1], ones_row[0:1, 0:64],
                                      BR[0:1, h:h + 1], start=True, stop=True),
              [ones_row, BR], [PP[3]])
        p.copy(BC[0:64, 0:1], PP[3][0:64, 0:1])
        p.ts(QH[:, :], QH[:, :], BC[0:64, 0:1], None, ALU.mult)
        for c in range(64):
            p.transpose(PPB5[0:128, 0:64], VH[:, c * 128:(c + 1) * 128],
                        idb[0:64, 0:64])
            p.copy(VTOK[:, c * 64:(c + 1) * 64], PPB5[0:128, 0:64])
        for qb in range(NQB):
            Q0b = qb * 512
            first = True
            for ck in range(64):
                K0 = ck * 128
                ps = PP[(ck % 2)]
                p.mm(ps[:, 0:512], KH[:, K0:K0 + 128], QH[:, Q0b:Q0b + 512])
                p.act(ET[:, :], ps[:, 0:512], AF.Exp)
                p.mm_more(PP[6][0:64, 0:512], VTOK[:, ck * 64:(ck + 1) * 64],
                          ET[:, :], start=first, stop=(ck == 63))
                p.mm_more(PP[7][0:1, 0:512], ones_bf[:, 0:1], ET[:, :],
                          start=first, stop=(ck == 63))
                first = False
            p.recip(DR[0:1, :], PP[7][0:1, 0:512])
            p.mm(PP[3][0:64, 0:512], ones_row[0:1, 0:64], DR[0:1, 0:512])
            p.copy(OF[0:64, :], PP[6][0:64, 0:512])
            p.tt(OST[:, Q0b:Q0b + 512], OF[0:64, :], PP[3][0:64, 0:512],
                 ALU.mult)
        p.dma("sync", dout["o3_o"][r0:r0 + 64, :], OST[:, :])
    return p.finalize()


def host_l7(inputs, state):
    nc = get_nc("l7", build_l7)
    com = dict(idf32=np.eye(128, dtype=np.float32),
               betap=_f(inputs["L2_attn_beta_param"]).reshape(1, 16),
               selh=np.zeros((4, 128), np.float32))
    in_maps = []
    for c in range(NCORES):
        b, hg = c // 4, c % 4
        rows = slice(hg * 256, hg * 256 + 256)
        cols = slice(b * S, (b + 1) * S)
        m = dict(com)
        bp = _f(inputs["L2_attn_beta_param"]).reshape(16)
        m["betap"] = np.ascontiguousarray(
            np.concatenate([bp[hg * 4:hg * 4 + 4],
                            np.zeros(12, np.float32)]).reshape(1, 16))
        for nm, key in [("q3", "q3T"), ("k3", "k3T"), ("v3", "v3T")]:
            m[nm] = np.ascontiguousarray(state[key][rows, cols]).astype(BF)
        in_maps.append(m)
    res = run8(nc, in_maps)
    o3 = np.zeros((NB, D, S), np.float32)
    for c in range(NCORES):
        b, hg = c // 4, c % 4
        o3[b, hg * 256:hg * 256 + 256] = np.asarray(res[c]["o3_o"], np.float32)
    state["o3T"] = np.ascontiguousarray(np.concatenate([o3[0], o3[1]], 1))
    return state



def build_l8():
    p = Prog("l8")
    for name, shape, dt in [
        ("xn2T", [D, TOK], F32), ("x2T", [D, TOK], F32),
        ("o3T", [D, TOK], BF16), ("xqT", [D, TOK], BF16),
        ("owT", [D, D], F32), ("obc", [128, 8], F32),
        ("wgq", [D, 2816], BF16), ("wuq", [D, 2816], BF16),
        ("wdqT", [2816, D], BF16),
        ("wjT", [D, D], F32), ("wjN", [D, D], F32), ("wjvT", [D, D], F32),
        ("gamc", [128, 8], F32), ("betc", [128, 8], F32),
        ("muN", [D, 256], F32), ("scl", [1, 1], F32),
        ("idf32", [128, 128], F32),
    ]:
        p.inp(name, shape, dt)
    p.out("x3T", [D, TOK], F32)
    p.out("xn3T", [D, TOK], F32)
    p.out("sc4", [1, TOK], F32)
    p.out("valT", [D, TOK], BF16)
    p.out("xninT", [D, TOK], BF16)
    p.out("lgT", [TOK, 256], F32)
    din, dout = p.din, p.dout

    ones_col = p.sb("ones_col", [128, 1], F32)
    ones_row = p.sb("ones_row", [1, 128], F32)
    p.memset(ones_col[:, :], 1.0)
    p.memset(ones_row[:, :], 1.0)
    idf = p.sb("idf", [128, 128], F32)
    p.dma("sync", idf[:, :], din["idf32"])
    HCH = 1024
    XN2 = p.sb("XN2", [128, 8 * HCH], F32)
    SS = p.sb("SS", [128, 8 * HCH], F32)
    FF = p.sb("FF", [128, 8 * HCH], F32)
    XB = p.sb("XB", [128, 8 * HCH], BF16)
    HQ = p.sb("HQB", [128, 22 * 512], BF16)
    QS = p.sb("QSc", [128, 176], F32)
    QR = p.sb("QRc", [128, 176], F32)
    QC2 = p.sb("QC2c", [128, 2816], F32)
    WA = p.sb("WA", [128, 8 * D], BF16)
    WG_ = [p.sb("WGc0", [128, 8 * 512], BF16),
           p.sb("WGc1", [128, 8 * 512], BF16)]
    T1 = p.sb("T1", [128, 512], F32)
    MUU = p.sb("MUU", [128, 8 * 256], BF16)
    sq0_ = p.sb("sq0", [128, 512], F32)
    sq = [sq0_, sq0_]
    row_sum = p.sb("row_sum", [1, HCH], F32)
    row_ssq = p.sb("row_ssq", [1, HCH], F32)
    row_rs = p.sb("row_rs", [1, HCH], F32)
    row_mr = p.sb("row_mr", [1, HCH], F32)
    mrow = p.sb("mrow", [1, 256], F32)
    QT = XN2[:, 0:2816]
    XUB = HQ[:, 0:8 * HCH]
    QC1 = XN2[:, 2816 + 256:2816 + 256 + 2816]
    MUF = FF[:, 0:8 * 256]
    gc = {}
    for nm in ["obc", "gamc", "betc"]:
        gc[nm] = p.sb(nm + "_s", [128, 8], F32)
        p.dma("sync", gc[nm][:, :], din[nm])
    scl_s = p.sb("scl_s", [1, 1], F32)
    p.dma("sync", scl_s[:, :], din["scl"])
    psA = p.ps("psA", [1, 512])
    psB = p.ps("psB", [1, 512])
    P0 = p.ps("P0")
    P1 = p.ps("P1")
    P2 = p.ps("P2")
    P3 = p.ps("P3")
    PR = p.ps("PR", [128, 256])

    for k in range(8):
        p.dma("sync", MUF[:, k * 256:(k + 1) * 256],
              din["muN"][k * 128:(k + 1) * 128, :])
    for k in range(8):
        p.act(sq[0][:, 0:256], MUF[:, k * 256:(k + 1) * 256], AF.Square)
        p.mm_more(psA[0:1, 0:256], ones_col[:, 0:1], sq[0][:, 0:256],
                  start=(k == 0), stop=(k == 7))
    p.copy(mrow[0:1, 0:256], psA[0:1, 0:256])
    p.act(mrow[0:1, 0:256], mrow[0:1, 0:256], AF.Sqrt)
    p.ts(mrow[0:1, 0:256], mrow[0:1, 0:256], 1e-12, None, ALU.max)
    p.recip(row_mr[0:1, 0:256], mrow[0:1, 0:256])
    p.mm(PR[:, 0:256], ones_row[0:1, :], row_mr[0:1, 0:256])
    for k in range(8):
        p.tt(MUU[:, k * 256:(k + 1) * 256], MUF[:, k * 256:(k + 1) * 256],
             PR[:, 0:256], ALU.mult)

    for hf in range(2):
        xa = hf * HCH
        for k in range(8):
            p.dma("sync", XN2[:, k * HCH:(k + 1) * HCH],
                  din["xn2T"][k * 128:(k + 1) * 128, xa:xa + HCH])
            p.dma("sync", XB[:, k * HCH:(k + 1) * HCH],
                  din["o3T"][k * 128:(k + 1) * 128, xa:xa + HCH])
        loadW(p, WA, din["owT"])
        proj(p, WA, XB, SS, [P0, P1], 8, 8, HCH, bias=gc["obc"])
        p.tt(SS[:, :], SS[:, :], XN2[:, :], ALU.add)
        for k in range(8):
            p.dma("sync", XB[:, k * HCH:(k + 1) * HCH],
                  din["xqT"][k * 128:(k + 1) * 128, xa:xa + HCH])
        for tt_ in range(HCH // 128):
            tb = tt_ * 128
            for gu in range(2):
                wname = "wgq" if gu == 0 else "wuq"
                for hs in range(6):
                    ha = hs * 512
                    hw = min(512, 2816 - ha)
                    WG = WG_[hs % 2]
                    p.dma("gpsimd",
                          WG[:, 0:8 * hw].rearrange("p (k n) -> p k n", n=hw),
                          din[wname][:, ha:ha + hw]
                          .rearrange("(k p) n -> p k n", p=128))
                    ps = [P0, P1][gu]
                    p.mm(ps[:, 0:hw],
                         [XB[:, k * HCH + tb: k * HCH + tb + 128]
                          for k in range(8)],
                         [WG[:, k * hw:(k + 1) * hw] for k in range(8)])
                    if gu == 0:
                        p.act(QT[:, ha:ha + hw], ps[:, 0:hw], AF.Silu)
                    else:
                        p.tt(QT[:, ha:ha + hw], QT[:, ha:ha + hw],
                             ps[:, 0:hw], ALU.mult)
            fp4_qd_generic(p, QT, QS, QR, QC1, QC2, None, 128, 2816,
                           out_f32=True)
            for k in range(22):
                p.transpose(P2[:, 0:128], QC2[:, k * 128:(k + 1) * 128],
                            idf[:, :])
                p.copy(HQ[:, k * 512 + tt_ % 4 * 128:
                          k * 512 + tt_ % 4 * 128 + 128], P2[:, 0:128])
            if tt_ % 4 == 3:
                ta = (tt_ - 3) * 128
                for d in range(8):
                    p.dma("gpsimd",
                          WA[:, 0:22 * 128]
                          .rearrange("p (k n) -> p k n", n=128),
                          din["wdqT"][:, d * 128:(d + 1) * 128]
                          .rearrange("(k p) n -> p k n", p=128))
                    ps = [P2, P3][d % 2]
                    p.mm(ps[:, 0:512],
                         [WA[:, h * 128:(h + 1) * 128] for h in range(22)],
                         [HQ[:, h * 512:(h + 1) * 512] for h in range(22)])
                    p.copy(FF[:, d * HCH + ta: d * HCH + ta + 512],
                           ps[:, 0:512])
        p.act(FF[:, :], FF[:, :], AF.Silu)
        p.tt(FF[:, :], FF[:, :], SS[:, :], ALU.mult)
        for k in range(8):
            p.dma("sync", XN2[:, k * HCH:(k + 1) * HCH],
                  din["x2T"][k * 128:(k + 1) * 128, xa:xa + HCH])
        p.tt(XN2[:, :], XN2[:, :], FF[:, :], ALU.add)
        for k in range(8):
            p.dma("sync", dout["x3T"][k * 128:(k + 1) * 128, xa:xa + HCH],
                  XN2[:, k * HCH:(k + 1) * HCH])
        feat_stats(p, XN2, 8, HCH, ones_col, sq, psA, psB, row_sum, row_ssq)
        ln_rows(p, HCH, row_sum, row_ssq, row_rs, row_mr)
        apply_ln(p, XN2, XN2, 8, HCH, ones_row, row_rs, row_mr, P0, P1)
        for k in range(8):
            p.dma("sync", dout["xn3T"][k * 128:(k + 1) * 128, xa:xa + HCH],
                  XN2[:, k * HCH:(k + 1) * HCH])
        XN3 = XN2
        p.copy(XB[:, :], XN3[:, :], eng="scalar")
        loadW(p, WA, din["wjT"])
        proj(p, WA, XB, FF, [P0, P1], 8, 8, HCH)
        p.copy(XUB[:, :], FF[:, :], eng="scalar")
        loadW(p, WA, din["wjN"])
        proj(p, WA, XUB, FF, [P2, P3], 8, 8, HCH)
        p.tt(FF[:, :], FF[:, :], XN3[:, :], ALU.mult)
        for (a, b) in col_splits(HCH, 512):
            w = b - a
            p.mm(psA[0:1, 0:w], [ones_col[:, 0:1]] * 8,
                 [FF[:, k * HCH + a: k * HCH + b] for k in range(8)])
            p.ts(row_sum[0:1, a:b], psA[0:1, 0:w], 1.0 / 32.0, None, ALU.mult)
        p.dma("sync", dout["sc4"][0:1, xa:xa + HCH], row_sum[0:1, 0:HCH])
        loadW(p, WA, din["wjvT"])
        proj(p, WA, XB, XUB, [P0, P1], 8, 8, HCH)
        for k in range(8):
            p.dma("sync", dout["valT"][k * 128:(k + 1) * 128, xa:xa + HCH],
                  XUB[:, k * HCH:(k + 1) * HCH])
        feat_stats(p, XN3, 8, HCH, ones_col, sq, psA, psB, row_sum, row_ssq)
        ln_rows(p, HCH, row_sum, row_ssq, row_rs, row_mr)
        apply_ln(p, XN3, FF, 8, HCH, ones_row, row_rs, row_mr, P0, P1,
                 gain_cols=gc["gamc"], bias_cols=gc["betc"])
        p.copy(XB[:, :], FF[:, :], eng="scalar")
        for k in range(8):
            p.dma("sync", dout["xninT"][k * 128:(k + 1) * 128, xa:xa + HCH],
                  XB[:, k * HCH:(k + 1) * HCH])
        for (a, b) in col_splits(HCH, 512):
            w = b - a
            for k in range(8):
                p.act(sq[0][:, 0:w], FF[:, k * HCH + a: k * HCH + b],
                      AF.Square)
                p.mm_more(psA[0:1, 0:w], ones_col[:, 0:1], sq[0][:, 0:w],
                          start=(k == 0), stop=(k == 7))
            p.copy(row_sum[0:1, a:b], psA[0:1, 0:w])
        p.act(row_sum[0:1, 0:HCH], row_sum[0:1, 0:HCH], AF.Sqrt)
        p.ts(row_sum[0:1, 0:HCH], row_sum[0:1, 0:HCH], 1e-12, None, ALU.max)
        p.recip(row_rs[0:1, 0:HCH], row_sum[0:1, 0:HCH])
        p.ts(row_rs[0:1, 0:HCH], row_rs[0:1, 0:HCH], scl_s[0:1, 0:1], None,
             ALU.mult)
        apply_ln(p, FF, FF, 8, HCH, ones_row, row_rs, None, P2, P3)
        p.copy(XUB[:, :], FF[:, :], eng="scalar")
        for i in range(HCH // 128):
            tbase = i * 128
            p.mm(PR[:, 0:256],
                 [XUB[:, k * HCH + tbase: k * HCH + tbase + 128]
                  for k in range(8)],
                 [MUU[:, k * 256:(k + 1) * 256] for k in range(8)])
            p.copy(T1[:, 0:256], PR[:, 0:256])
            p.dma("sync", dout["lgT"][xa + tbase: xa + tbase + 128, :],
                  T1[:, 0:256])
    return p.finalize()


def host_l8(inputs, state):
    nc = get_nc("l8", build_l8)
    wgq = state["wgq"].astype(np.float32)
    wuq = state["wuq"].astype(np.float32)
    wdq = state["wdq"].astype(np.float32)
    com = dict(
        owT=_t(inputs["L2_attn_o_w"]),
        obc=_f(inputs["L2_attn_o_b"]).reshape(8, 128).T.copy(),
        wgq=np.ascontiguousarray(wgq.T).astype(BF),
        wuq=np.ascontiguousarray(wuq.T).astype(BF),
        wdqT=np.ascontiguousarray(wdq.T).astype(BF),
        wjT=_t(inputs["L3_attn_wj_w"]), wjN=_f(inputs["L3_attn_wj_w"]),
        wjvT=_t(inputs["L3_attn_wjv_w"]),
        gamc=_f(inputs["L3_ffn_ln_gamma"]).reshape(8, 128).T.copy(),
        betc=_f(inputs["L3_ffn_ln_beta"]).reshape(8, 128).T.copy(),
        muN=np.ascontiguousarray(
            np.asarray(inputs["L3_ffn_mu"], np.float32).reshape(D, 256)),
        scl=_f(inputs["L3_ffn_scale"]).reshape(1, 1),
        idf32=np.eye(128, dtype=np.float32),
    )
    in_maps = []
    for c in range(NCORES):
        m = dict(com)
        for nm in ["xn2T", "x2T"]:
            m[nm] = np.ascontiguousarray(state[nm][:, c * TOK:(c + 1) * TOK])
        for nm in ["o3T", "xqT"]:
            m[nm] = np.ascontiguousarray(
                state[nm][:, c * TOK:(c + 1) * TOK]).astype(BF)
        in_maps.append(m)
    res = run8(nc, in_maps)
    for nm in ["x3T", "xn3T", "valT", "xninT"]:
        state[nm] = np.concatenate(
            [np.asarray(res[c][nm], np.float32) for c in range(NCORES)], 1)
    state["sc4"] = np.concatenate([res[c]["sc4"] for c in range(NCORES)], 1)
    state["lgT"] = np.concatenate([res[c]["lgT"] for c in range(NCORES)], 0)
    return state




def build_l9():
    p = Prog("l9")
    p.inp("scR", [1, S], F32)
    p.inp("scP", [128, 64], F32)
    p.inp("valS", [256, S], BF16)
    p.inp("lgS", [64, S], F32)
    p.inp("xnintok", [S, D], BF16)
    p.inp("w1S", [4 * 8 * 128, 256], F32)
    p.inp("w2S", [4 * 2 * 128, 1024], F32)
    p.inp("b1S", [128, 4 * 2], F32)
    p.inp("b2S", [128, 4 * 8], F32)
    for v in range(4):
        p.inp("MK%d" % v, [128, 512], F32)
    p.inp("idf32", [128, 128], F32)
    p.out("o4_o", [256, S], BF16)
    p.out("soT", [D, 64], F32)
    din, dout = p.din, p.dout

    ones_row = p.sb("ones_row", [1, 128], F32)
    ones_bf = p.sb("ones_bf", [128, 1], BF16)
    p.memset(ones_row[:, :], 1.0)
    p.memset(ones_bf[:, :], 1.0)
    idf = p.sb("idf", [128, 128], F32)
    p.dma("sync", idf[:, :], din["idf32"])
    idb = p.sb("idb", [128, 128], BF16)
    p.copy(idb[:, :], idf[:, :])
    MK = [p.sb("sMK%d" % v, [128, 512], F32) for v in range(4)]
    for v in range(4):
        p.dma("sync", MK[v][:, :], din["MK%d" % v])
    SCR = p.sb("SCR", [1, S], F32)
    PMX = p.sb("PMX", [1, 2048], F32)
    PMN = p.sb("PMN", [1, 2048], F32)
    RMX = p.sb("RMX", [1, S], F32)
    CARY = p.sb("CARY", [1, 4], F32)
    SCP = p.sb("SCP", [128, 64], F32)
    VH2 = p.sb("VH2", [128, S], BF16)
    VT2 = p.sb("VT2", [128, 64 * 128], BF16)
    SQB = p.sb("SQB", [128, 512], F32)
    MBX = p.sb("MBX", [128, 512], F32)
    ET = p.sb("ET", [128, 512], BF16)
    T1 = p.sb("T1", [128, 512], F32)
    OF = T1
    DR = p.sb("DR", [1, 512], F32)
    OST2 = p.sb("OST2", [128, S], BF16)
    PP = [p.ps("PP%d" % i) for i in range(8)]
    PPB5 = PP[5][:, 0:256].bitcast(BF16)

    p.dma("sync", SCR[0:1, :], din["scR"])
    p.dma("sync", SCP[:, :], din["scP"])
    p.memset(CARY[0:1, :], -3e38)
    for pc_ in range(S // 2048):
        a = pc_ * 2048
        sl = slice(a, a + 2048)
        p.scan(PMX[0:1, :], SCR[0:1, sl], SCR[0:1, sl], CARY[0:1, 0:1],
               ALU.max, ALU.bypass)
        p.copy(CARY[0:1, 0:1], PMX[0:1, 2047:2048])
        p.ts(RMX[0:1, sl], SCR[0:1, sl], -1.0, None, ALU.mult)
        p.scan(PMN[0:1, :], RMX[0:1, sl], RMX[0:1, sl], CARY[0:1, 1:2],
               ALU.max, ALU.bypass)
        p.copy(CARY[0:1, 1:2], PMN[0:1, 2047:2048])
        p.ts(PMN[0:1, :], PMN[0:1, :], -1.0, None, ALU.mult)
        p.tt(PMX[0:1, :], SCR[0:1, sl], PMX[0:1, :], ALU.mult)
        p.tt(PMN[0:1, :], SCR[0:1, sl], PMN[0:1, :], ALU.mult)
        p.tt(PMX[0:1, :], PMX[0:1, :], PMN[0:1, :], ALU.max)
        p.ts(RMX[0:1, sl], PMX[0:1, :], -1.0, None, ALU.mult)

    for half in range(2):
        r0 = half * 128
        p.dma("sync", VH2[:, :], din["valS"][r0:r0 + 128, :])
        for c in range(64):
            p.transpose(PPB5[0:128, 0:128], VH2[:, c * 128:(c + 1) * 128],
                        idb[:, :])
            p.copy(VT2[:, c * 128:(c + 1) * 128], PPB5[0:128, 0:128])
        for qb in range(NQB):
            Q0b = qb * 512
            p.mm(PP[3][:, 0:512], ones_row[0:1, :], SCR[0:1, Q0b:Q0b + 512])
            p.copy(SQB[:, :], PP[3][:, 0:512])
            p.mm(PP[3][:, 0:512], ones_row[0:1, :], RMX[0:1, Q0b:Q0b + 512])
            p.copy(MBX[:, :], PP[3][:, 0:512])
            nk = (Q0b + 512) // 128
            first = True
            for ck in range(nk):
                K0 = ck * 128
                p.ts(T1[:, 0:512], SQB[:, :], SCP[:, ck:ck + 1], None,
                     ALU.mult)
                p.tt(T1[:, 0:512], T1[:, 0:512], MBX[:, :], ALU.add)
                p.act(ET[:, :], T1[:, 0:512], AF.Exp)
                u = (K0 - Q0b) // 128
                if 0 <= u <= 3:
                    p.tt(ET[:, :], ET[:, :], MK[u][:, :], ALU.mult)
                p.mm_more(PP[6][:, 0:512], VT2[:, ck * 128:(ck + 1) * 128],
                          ET[:, :], start=first, stop=(ck == nk - 1))
                p.mm_more(PP[7][0:1, 0:512], ones_bf[:, 0:1], ET[:, :],
                          start=first, stop=(ck == nk - 1))
                first = False
            p.recip(DR[0:1, :], PP[7][0:1, 0:512])
            p.mm(PP[3][:, 0:512], ones_row[0:1, :], DR[0:1, 0:512])
            p.copy(OF[:, :], PP[6][:, 0:512])
            p.tt(OST2[:, Q0b:Q0b + 512], OF[:, :], PP[3][:, 0:512], ALU.mult)
        p.dma("sync", dout["o4_o"][r0:r0 + 128, :], OST2[:, :])

    ETok = VT2[:, 0:64 * 64]
    LG = p.sb("LG", [64, S], F32)
    EXL = p.sb("EXL", [64, S], BF16)
    MC = p.sb("MC", [64, 1], F32)
    DEN = p.sb("DEN", [64, 4], F32)
    DRW = p.sb("DRW", [1, 64], F32)

    XT = p.sb("XT", [128, D], BF16)
    SIf = p.sb("SIf", [128, 8 * 64], F32)
    H1b = p.sb("H1b", [128, 2 * 16], BF16)
    W1Bs = p.sb("W1Bs", [128, 256], BF16)
    W2Bs = p.sb("W2Bs", [128, 128], BF16)
    SO = p.sb("SO", [128, 8 * 64], F32)
    b1c = p.sb("b1c", [128, 4 * 2], F32)
    b2c = p.sb("b2c", [128, 4 * 8], F32)
    p.dma("sync", b1c[:, :], din["b1S"])
    p.dma("sync", b2c[:, :], din["b2S"])
    p.dma("sync", LG[:, :], din["lgS"])
    p.reduce(MC[:, 0:1], LG[:, :], ALU.max, negate=True)
    p.act(EXL[:, :], LG[:, :], AF.Exp, bias=MC[:, 0:1],
          accum_out=DEN[:, 0:1])
    p.copy(DEN[:, 1:2], DEN[:, 0:1])
    p.copy(DEN[:, 2:3], DEN[:, 0:1])
    p.copy(DEN[:, 3:4], DEN[:, 0:1])
    p.transpose(PP[2][0:4, 0:64], DEN[0:64, 0:4], idf[0:64, 0:64])
    p.copy(T1[0:4, 0:64], PP[2][0:4, 0:64])
    p.recip(DRW[0:1, 0:64], T1[0:1, 0:64])
    for c in range(64):
        p.transpose(PPB5[0:128, 0:64], EXL[:, c * 128:(c + 1) * 128],
                    idb[0:64, 0:64])
        p.copy(ETok[:, c * 64:(c + 1) * 64], PPB5[0:128, 0:64])
    for tc in range(64):
        p.dma("sync", XT[:, :], din["xnintok"][tc * 128:(tc + 1) * 128, :])
        for d in range(8):
            p.mm_more(PP[6][:, d * 64:(d + 1) * 64],
                      XT[:, d * 128:(d + 1) * 128],
                      ETok[:, tc * 64:(tc + 1) * 64],
                      start=(tc == 0), stop=(tc == 63))
    p.copy(SIf[:, :], PP[6][:, 0:512])
    p.mm(PP[3][:, 0:64], ones_row[0:1, :], DRW[0:1, 0:64])
    p.copy(T1[:, 0:64], PP[3][:, 0:64])
    for d in range(8):
        p.tt(SIf[:, d * 64:(d + 1) * 64], SIf[:, d * 64:(d + 1) * 64],
             T1[:, 0:64], ALU.mult)
    SIb = p.sb("SIb", [128, 8 * 64], BF16)
    p.copy(SIb[:, :], SIf[:, :], eng="scalar")
    for e in range(4):
        for hc in range(2):
            for k in range(8):
                r0_ = (e * 8 + k) * 128
                p.dma("gpsimd", W1Bs[:, hc * 128:(hc + 1) * 128],
                      din["w1S"][r0_:r0_ + 128, hc * 128:(hc + 1) * 128])
                p.mm_more(PP[0][:, 0:16],
                          W1Bs[:, hc * 128:(hc + 1) * 128],
                          SIb[:, k * 64 + e * 16: k * 64 + e * 16 + 16],
                          start=(k == 0), stop=(k == 7))
            p.act(H1b[:, hc * 16:(hc + 1) * 16], PP[0][:, 0:16], AF.Gelu,
                  bias=b1c[:, e * 2 + hc: e * 2 + hc + 1])
        for d in range(8):
            for hc in range(2):
                r0_ = (e * 2 + hc) * 128
                p.dma("gpsimd", W2Bs[:, 0:128],
                      din["w2S"][r0_:r0_ + 128, d * 128:(d + 1) * 128])
                p.mm_more(PP[1][:, 0:16], W2Bs[:, 0:128],
                          H1b[:, hc * 16:(hc + 1) * 16],
                          start=(hc == 0), stop=(hc == 1))
            p.act(SO[:, d * 64 + e * 16: d * 64 + e * 16 + 16],
                  PP[1][:, 0:16], AF.Identity,
                  bias=b2c[:, e * 8 + d: e * 8 + d + 1])
    for d in range(8):
        p.dma("sync", dout["soT"][d * 128:(d + 1) * 128, :],
              SO[:, d * 64:(d + 1) * 64])
    return p.finalize()


def host_l9(inputs, state):
    nc = get_nc("l9", build_l9)
    w1 = np.asarray(inputs["L3_ffn_w1"], np.float32)
    w2 = np.asarray(inputs["L3_ffn_w2"], np.float32)
    b1 = np.asarray(inputs["L3_ffn_b1"], np.float32)
    b2 = np.asarray(inputs["L3_ffn_b2"], np.float32)
    MKs = []
    for v in range(4):
        MKs.append((v * 128 + np.arange(128)[:, None]
                    <= np.arange(512)[None, :]).astype(np.float32))
    in_maps = []
    for c in range(NCORES):
        b, u = c // 4, c % 4
        cols = slice(b * S, (b + 1) * S)
        es = slice(u * 4, u * 4 + 4)
        m = dict(idf32=np.eye(128, dtype=np.float32))
        for v in range(4):
            m["MK%d" % v] = MKs[v]
        sc = state["sc4"][0, cols]
        m["scR"] = np.ascontiguousarray(sc.reshape(1, S))
        m["scP"] = np.ascontiguousarray(sc.reshape(64, 128).T)
        m["valS"] = np.ascontiguousarray(
            state["valT"][u * 256:(u + 1) * 256, cols]).astype(BF)
        m["lgS"] = np.ascontiguousarray(
            state["lgT"][b * S:(b + 1) * S, u * 64:(u + 1) * 64].T)
        m["xnintok"] = np.ascontiguousarray(
            state["xninT"][:, cols].T).astype(BF)
        m["w1S"] = np.ascontiguousarray(
            w1[es].reshape(4, 8, 128, 256).reshape(4 * 8 * 128, 256))
        m["w2S"] = np.ascontiguousarray(
            w2[es].reshape(4, 2, 128, 1024).reshape(4 * 2 * 128, 1024))
        b1S = np.zeros((128, 8), np.float32)
        b2S = np.zeros((128, 32), np.float32)
        for e in range(4):
            b1S[:, e * 2:(e + 1) * 2] = b1[u * 4 + e].reshape(2, 128).T
            b2S[:, e * 8:(e + 1) * 8] = b2[u * 4 + e].reshape(8, 128).T
        m["b1S"] = b1S
        m["b2S"] = b2S
        in_maps.append(m)
    res = run8(nc, in_maps)
    o4 = np.zeros((NB, D, S), np.float32)
    so = np.zeros((NB, D, 256), np.float32)
    for c in range(NCORES):
        b, u = c // 4, c % 4
        o4[b, u * 256:(u + 1) * 256] = np.asarray(res[c]["o4_o"], np.float32)
        so[b, :, u * 64:(u + 1) * 64] = res[c]["soT"]
    state["o4T"] = np.ascontiguousarray(np.concatenate([o4[0], o4[1]], 1))
    state["soA"] = so
    return state



def build_l10():
    p = Prog("l10")
    for name, shape, dt in [
        ("lgTok", [TOK, 256], F32), ("soTA", [256, D], F32),
        ("o4T", [D, TOK], BF16), ("woT4", [D, D], F32),
        ("x3T", [D, TOK], F32),
    ]:
        p.inp(name, shape, dt)
    p.out("x4T", [D, TOK], F32)
    din, dout = p.din, p.dout
    p.inp("idf32", [128, 128], F32)
    idf = p.sb("idf", [128, 128], F32)
    p.dma("sync", idf[:, :], din["idf32"])

    ones_row = p.sb("ones_row", [1, 128], F32)
    p.memset(ones_row[:, :], 1.0)
    HCH = 1024
    LGt = p.sb("LGt", [128, 256], F32)
    MC = p.sb("MCc", [128, 1], F32)
    SMc = p.sb("SMc", [128, 1], F32)
    RC = p.sb("RCc", [128, 1], F32)
    CMB = p.sb("CMB", [128, 256], F32)
    CT = p.sb("CT", [128, 2 * HCH], BF16)
    SOT = p.sb("SOT", [128, 2 * D], BF16)
    XB4 = p.sb("XB4", [128, 8 * HCH], BF16)
    FF4 = p.sb("FF4", [128, 8 * HCH], F32)
    SS4 = p.sb("SS4", [128, 8 * HCH], F32)
    WA = p.sb("WA", [128, 8 * D], BF16)
    PP = [p.ps("PP%d" % i) for i in range(6)]

    for k in range(2):
        p.dma("gpsimd", SOT[:, k * D:(k + 1) * D],
              din["soTA"][k * 128:(k + 1) * 128, :])
    for k in range(8):
        p.dma("gpsimd", WA[:, k * D:(k + 1) * D],
              din["woT4"][k * 128:(k + 1) * 128, :])
    for hf in range(2):
        xa = hf * HCH
        for i in range(HCH // 128):
            tb = i * 128
            p.dma("sync", LGt[:, :], din["lgTok"][xa + tb:xa + tb + 128, :])
            p.reduce(MC[:, 0:1], LGt[:, :], ALU.max, negate=True)
            p.act(CMB[:, :], LGt[:, :], AF.Exp, bias=MC[:, 0:1],
                  accum_out=SMc[:, 0:1])
            p.recip(RC[:, 0:1], SMc[:, 0:1])
            p.ts(CMB[:, :], CMB[:, :], RC[:, 0:1], None, ALU.mult)
            for ec in range(2):
                p.transpose(PP[4][:, 0:128], CMB[:, ec * 128:(ec + 1) * 128],
                            idf[:, :])
                p.copy(CT[:, ec * HCH + tb: ec * HCH + tb + 128],
                       PP[4][:, 0:128])
        for o in range(8):
            for (a, b) in col_splits(HCH, 512):
                w = b - a
                ps = PP[o % 2]
                p.mm(ps[:, 0:w],
                     [SOT[:, k * D + o * 128: k * D + o * 128 + 128]
                      for k in range(2)],
                     [CT[:, k * HCH + a: k * HCH + b] for k in range(2)])
                p.act(FF4[:, o * HCH + a: o * HCH + b], ps[:, 0:w], AF.Silu)
        for k in range(8):
            p.dma("sync", XB4[:, k * HCH:(k + 1) * HCH],
                  din["o4T"][k * 128:(k + 1) * 128, xa:xa + HCH])
        proj(p, WA, XB4, SS4, [PP[2], PP[3]], 8, 8, HCH)
        p.tt(FF4[:, :], FF4[:, :], SS4[:, :], ALU.mult)
        for k in range(8):
            p.dma("sync", SS4[:, k * HCH:(k + 1) * HCH],
                  din["x3T"][k * 128:(k + 1) * 128, xa:xa + HCH])
        p.tt(SS4[:, :], SS4[:, :], FF4[:, :], ALU.add)
        for k in range(8):
            p.dma("sync", dout["x4T"][k * 128:(k + 1) * 128, xa:xa + HCH],
                  SS4[:, k * HCH:(k + 1) * HCH])
    return p.finalize()


def host_l10(inputs, state):
    nc = get_nc("l10", build_l10)
    in_maps = []
    for c in range(NCORES):
        b = c // 4
        m = dict(
            lgTok=np.ascontiguousarray(state["lgT"][c * TOK:(c + 1) * TOK]),
            soTA=np.ascontiguousarray(state["soA"][b].T),
            o4T=np.ascontiguousarray(
                state["o4T"][:, c * TOK:(c + 1) * TOK]).astype(BF),
            woT4=_t(inputs["L3_attn_wo_w"]),
            x3T=np.ascontiguousarray(state["x3T"][:, c * TOK:(c + 1) * TOK]),
            idf32=np.eye(128, dtype=np.float32),
        )
        in_maps.append(m)
    res = run8(nc, in_maps)
    x4 = np.concatenate([res[c]["x4T"] for c in range(NCORES)], 1)
    state["x4T"] = x4
    return state



def kernel(**inputs):
    state = {}
    host_l1(inputs, state)
    host_l2(inputs, state)
    host_l3(inputs, state)
    host_l4(inputs, state)
    host_l5(inputs, state)
    host_l6a(inputs, state)
    host_l6b(inputs, state)
    host_l7(inputs, state)
    host_l8(inputs, state)
    host_l9(inputs, state)
    host_l10(inputs, state)
    x4 = state["x4T"]
    return np.ascontiguousarray(
        x4.reshape(D, NB, S).transpose(1, 2, 0)).astype(np.float32)


# version 08
# speedup vs baseline: 1.1684x; 1.0826x over previous
import math
from contextlib import ExitStack

import numpy as np
import ml_dtypes

import concourse.bass as bass
import concourse.mybir as mybir
from concourse.bass_utils import run_bass_kernel_spmd

F32 = mybir.dt.float32
BF16 = mybir.dt.bfloat16
I32 = mybir.dt.int32
AF = mybir.ActivationFunctionType
ALU = mybir.AluOpType
AX = mybir.AxisListType
BF = ml_dtypes.bfloat16

NCORES = 8
NB = 2
S = 8192
D = 1024
TOK = 2048
PI = math.pi
TWO_PI = 2.0 * math.pi

ENGINES = ("tensor", "vector", "scalar", "sync", "gpsimd")
DMA_ENGINES = {"sync", "gpsimd"}


class Prog:

    def __init__(self, name):
        self.nc = bass.Bass("TRN2", target_bir_lowering=False)
        self.ops = []
        self.st = ExitStack()
        self._names = set()
        self.din = {}
        self.dout = {}

    def inp(self, name, shape, dt=F32):
        ap = self.nc.dram_tensor(name, shape, dt, kind="ExternalInput").ap()
        self.din[name] = ap
        return ap

    def out(self, name, shape, dt=F32):
        ap = self.nc.dram_tensor(name, shape, dt, kind="ExternalOutput").ap()
        self.dout[name] = ap
        return ap

    def sb(self, name, shape, dt=F32):
        assert name not in self._names, name
        self._names.add(name)
        return self.st.enter_context(self.nc.sbuf_tensor(name, shape, dt))

    def ps(self, name, shape=None, dt=F32):
        assert name not in self._names, name
        self._names.add(name)
        return self.st.enter_context(
            self.nc.psum_tensor(name, shape or [128, 512], dt))

    @staticmethod
    def _key(x):
        t = getattr(x, "tensor", x)
        return id(t)

    def rec(self, engine, emit, ins, outs, is_dma=False):
        self.ops.append((engine, emit,
                         [self._key(a) for a in ins if a is not None],
                         [self._key(a) for a in outs if a is not None],
                         is_dma))

    def dma(self, q, dst, src):
        self.rec(q, lambda e: e.dma_start(out=dst, in_=src), [src], [dst],
                 is_dma=True)

    def mm(self, psum, lhsTs, rhss, extra_ins=()):
        if not isinstance(lhsTs, (list, tuple)):
            lhsTs, rhss = [lhsTs], [rhss]
        n = len(lhsTs)

        def emit(t):
            for i in range(n):
                inst = t.matmul(psum, lhsTs[i], rhss[i],
                                start=(i == 0), stop=(i == n - 1))
            return inst
        self.rec("tensor", emit, list(lhsTs) + list(rhss) + list(extra_ins),
                 [psum])

    def mm_more(self, psum, lhsTs, rhss, start, stop):
        if not isinstance(lhsTs, (list, tuple)):
            lhsTs, rhss = [lhsTs], [rhss]
        n = len(lhsTs)

        def emit(t):
            for i in range(n):
                inst = t.matmul(psum, lhsTs[i], rhss[i],
                                start=(start and i == 0),
                                stop=(stop and i == n - 1))
            return inst
        self.rec("tensor", emit, list(lhsTs) + list(rhss), [psum])

    def transpose(self, psum_out, in_ap, ident):
        self.rec("tensor", lambda t: t.transpose(psum_out, in_ap, ident),
                 [in_ap, ident], [psum_out])

    def act(self, out, in_, fn, bias=0.0, scale=1.0, accum_out=None):
        b = bias if not isinstance(bias, (int, float)) else None
        s_ = scale if not isinstance(scale, (int, float)) else None
        self.rec("scalar",
                 lambda s: s.activation(out, in_, fn, bias=bias, scale=scale,
                                        accum_out=accum_out),
                 [in_, b, s_], [out, accum_out])

    def tt(self, out, a, b, op, eng="vector"):
        self.rec(eng, lambda v: v.tensor_tensor(out, a, b, op), [a, b], [out])

    def ts(self, out, a, s1, s2, op0, op1=None, eng="vector", accum_out=None):
        if op1 is None:
            self.rec(eng, lambda v: v.tensor_scalar(out, a, s1, None, op0,
                                                    accum_out=accum_out),
                     [a, s1 if not np.isscalar(s1) else None], [out, accum_out])
        else:
            self.rec(eng, lambda v: v.tensor_scalar(out, a, s1, s2, op0, op1,
                                                    accum_out=accum_out),
                     [a, s1 if not np.isscalar(s1) else None,
                      s2 if not np.isscalar(s2) else None], [out, accum_out])

    def stt(self, out, in0, scalar, in1, op0, op1, eng="vector"):
        self.rec(eng,
                 lambda v: v.scalar_tensor_tensor(out, in0, scalar, in1,
                                                  op0, op1),
                 [in0, in1, scalar if not np.isscalar(scalar) else None],
                 [out])

    def copy(self, out, in_, eng="vector"):
        if eng == "scalar":
            self.rec(eng, lambda s: s.activation(out, in_, AF.Copy),
                     [in_], [out])
        else:
            self.rec(eng, lambda v: v.tensor_copy(out, in_), [in_], [out])

    def reduce(self, out, in_, op, absval=False, negate=False, eng="vector"):
        self.rec(eng,
                 lambda v: v.tensor_reduce(out, in_, AX.X, op,
                                           apply_absolute_value=absval,
                                           negate=negate),
                 [in_], [out])

    def scan(self, out, d0, d1, init, op0, op1):
        self.rec("vector",
                 lambda v: v.tensor_tensor_scan(out, d0, d1, init, op0, op1),
                 [d0, d1, init if not np.isscalar(init) else None], [out])

    def recip(self, out, in_):
        self.rec("scalar", lambda s: s.activation(out, in_, AF.Ln),
                 [in_], [out])
        cm1 = self._neg1()
        np_ = out.shape[0]
        self.rec("scalar",
                 lambda s: s.activation(out, out, AF.Exp,
                                        scale=cm1[0:np_, 0:1]),
                 [out, cm1], [out])

    def _neg1(self):
        if not hasattr(self, "_cneg1"):
            self._cneg1 = self.sb("_cneg1", [128, 1], F32)
            self.memset(self._cneg1[:, :], -1.0)
        return self._cneg1

    def memset(self, ap, c, eng="vector"):
        self.rec(eng, lambda v: v.memset(ap, c), [], [ap])

    def select(self, out, mask, tv, fv):
        self.rec("vector", lambda v: v.select(out, mask, tv, fv),
                 [mask, tv, fv], [out])

    def constcol(self, name, value, rows=128):
        t = self.sb(name, [rows, 1], F32)
        self.memset(t[:, :], value)
        return t

    def finalize(self):
        n = len(self.ops)
        last_write = {}
        readers = {}
        deps = [set() for _ in range(n)]
        for i, (eng, emit, ins, outs, is_dma) in enumerate(self.ops):
            for k in ins:
                if k in last_write:
                    deps[i].add(last_write[k])
            for k in outs:
                if k in last_write:
                    deps[i].add(last_write[k])
                for r in readers.get(k, ()):
                    deps[i].add(r)
            for k in ins:
                readers.setdefault(k, []).append(i)
            for k in outs:
                last_write[k] = i
                readers[k] = []

        engs = [o[0] for o in self.ops]
        isdma = [o[4] for o in self.ops]
        cnt = {}
        run_cnt = {e: 0 for e in ENGINES}
        for i in range(n):
            run_cnt[engs[i]] += 16 if isdma[i] else 1
            cnt[i] = run_cnt[engs[i]]
        need_wait = [[] for _ in range(n)]
        for i in range(n):
            for j in deps[i]:
                if (engs[j] == engs[i] == "tensor"
                        and not isdma[j] and not isdma[i]):
                    continue
                need_wait[i].append((engs[j], cnt[j]))

        sems = {e: self.st.enter_context(self.nc.semaphore("sem_" + e))
                for e in ENGINES}

        with self.nc.Block() as block:
            def run_engine(eng_name, eng):
                hi = {e: 0 for e in ENGINES}
                for i in range(n):
                    if engs[i] != eng_name:
                        continue
                    for (we, wv) in need_wait[i]:
                        if wv > hi[we]:
                            eng.wait_ge(sems[we], wv)
                            hi[we] = wv
                    inst = self.ops[i][1](eng)
                    inst.then_inc(sems[eng_name], 16 if isdma[i] else 1)
                for e in ENGINES:
                    if run_cnt[e] > hi[e] and e != eng_name:
                        eng.wait_ge(sems[e], run_cnt[e])

            @block.tensor
            def _(t):
                run_engine("tensor", t)

            @block.vector
            def _(v):
                run_engine("vector", v)

            @block.scalar
            def _(s):
                run_engine("scalar", s)

            @block.sync
            def _(s):
                run_engine("sync", s)

            @block.gpsimd
            def _(g):
                run_engine("gpsimd", g)

        self.st.close()
        return self.nc


def col_splits(n, maxn=512):
    k = (n + maxn - 1) // maxn
    base, rem = divmod(n, k)
    out = []
    i = 0
    for j in range(k):
        w = base + (1 if j < rem else 0)
        out.append((i, i + w))
        i += w
    return out


def _t(a):
    return np.ascontiguousarray(np.asarray(a, np.float32).T)


def _f(a):
    return np.ascontiguousarray(np.asarray(a, np.float32))


def _col(a):
    return np.ascontiguousarray(np.asarray(a, np.float32).reshape(-1, 1))


_BUILD_CACHE = {}


def get_nc(key, fn):
    if key not in _BUILD_CACHE:
        _BUILD_CACHE[key] = fn()
    return _BUILD_CACHE[key]


def run8(nc, in_maps):
    return run_bass_kernel_spmd(nc, in_maps, core_ids=list(range(NCORES))).results



def feat_stats(p, X, nch, cols, ones_col, sq, psA, psB, row_sum, row_ssq):
    for (a, b) in col_splits(cols, 512):
        w = b - a
        p.mm(psA[0:1, 0:w], [ones_col[:, 0:1]] * nch,
             [X[:, k * cols + a: k * cols + b] for k in range(nch)])
        for k in range(nch):
            p.act(sq[k % 2][:, 0:w], X[:, k * cols + a: k * cols + b],
                  AF.Square)
            p.mm_more(psB[0:1, 0:w], ones_col[:, 0:1], sq[k % 2][:, 0:w],
                      start=(k == 0), stop=(k == nch - 1))
        p.copy(row_sum[0:1, a:b], psA[0:1, 0:w])
        p.copy(row_ssq[0:1, a:b], psB[0:1, 0:w])


def ln_rows(p, cols, row_sum, row_ssq, row_rs, row_mr, d=D, eps=1e-5,
            rms=False):
    c = slice(0, cols)
    if not rms:
        p.ts(row_mr[0:1, c], row_sum[0:1, c], 1.0 / d, None, ALU.mult)
        p.tt(row_rs[0:1, c], row_mr[0:1, c], row_mr[0:1, c], ALU.mult)
        p.stt(row_rs[0:1, c], row_ssq[0:1, c], 1.0 / d, row_rs[0:1, c],
              ALU.mult, ALU.subtract)
        p.ts(row_rs[0:1, c], row_rs[0:1, c], eps, None, ALU.add)
    else:
        p.ts(row_rs[0:1, c], row_ssq[0:1, c], 1.0 / d, eps, ALU.mult, ALU.add)
    p.act(row_sum[0:1, c], row_rs[0:1, c], AF.Sqrt)
    p.recip(row_rs[0:1, c], row_sum[0:1, c])
    if not rms:
        p.tt(row_mr[0:1, c], row_mr[0:1, c], row_rs[0:1, c], ALU.mult)
        p.ts(row_mr[0:1, c], row_mr[0:1, c], -1.0, None, ALU.mult)


def apply_ln(p, Xin, Xout, nch, cols, ones_row, row_rs, row_mr, PS0, PS1,
             gain_cols=None, bias_cols=None):
    for (a, b) in col_splits(cols, 512):
        w = b - a
        p.mm(PS0[:, 0:w], ones_row[0:1, :], row_rs[0:1, a:b])
        if row_mr is not None:
            p.mm(PS1[:, 0:w], ones_row[0:1, :], row_mr[0:1, a:b])
        for k in range(nch):
            sl = slice(k * cols + a, k * cols + b)
            p.tt(Xout[:, sl], Xin[:, sl], PS0[:, 0:w], ALU.mult)
            if row_mr is not None:
                p.tt(Xout[:, sl], Xout[:, sl], PS1[:, 0:w], ALU.add)
            if gain_cols is not None:
                p.ts(Xout[:, sl], Xout[:, sl], gain_cols[:, k:k + 1], None,
                     ALU.mult)
            if bias_cols is not None:
                p.ts(Xout[:, sl], Xout[:, sl], bias_cols[:, k:k + 1], None,
                     ALU.add)


def proj(p, W, X, OUT, psums, nk, nout, cols, xcols=None, bias=None,
         act=None, scale=1.0, out_cols=None, obase=0, wout=None):
    xcols = xcols if xcols is not None else cols
    out_cols = out_cols if out_cols is not None else cols
    wout = wout if wout is not None else nout * 128
    for o in range(nout):
        for (a, b) in col_splits(cols, 512):
            w = b - a
            ps = psums[o % len(psums)]
            p.mm(ps[:, 0:w],
                 [W[:, k * wout + o * 128: k * wout + o * 128 + 128]
                  for k in range(nk)],
                 [X[:, k * xcols + a: k * xcols + b] for k in range(nk)])
            sl = slice(o * out_cols + obase + a, o * out_cols + obase + b)
            fn = act or (AF.Identity if bias is not None else AF.Copy)
            p.act(OUT[:, sl], ps[:, 0:w], fn,
                  bias=(bias[:, o:o + 1] if bias is not None else 0.0),
                  scale=scale)



CH = 512
NCH_T = TOK // CH
CB = CH + 1


def build_l1():
    p = Prog("l1")
    for name, shape in [
        ("xT", [D, TOK + 1]), ("ppT", [D, TOK + 1]),
        ("Wpv", [D, D]), ("Wk", [D, D]), ("Wkv", [D, D]),
        ("Wg1", [2 * D, D]), ("Wb1", [D, 512]),
        ("Wg2", [D, 1]), ("Wb2", [512, 2]), ("wqkvT", [D, 24]),
        ("bpv", [128, 8]), ("bk", [128, 8]), ("bkv", [128, 8]),
        ("bg1", [128, 8]), ("bb1", [128, 4]),
        ("bg2", [1, 1]), ("bb2a", [1, 1]), ("bb2b", [1, 1]),
        ("wg_sh", [352, D]), ("wu_sh", [352, D]), ("wd_sh", [128, 2816]),
        ("idf32", [128, 128]), ("fmask", [128, 1]),
    ]:
        p.inp(name, shape)
    for name, shape, dt in [
        ("xnT", [D, TOK], F32),
        ("pos_cT", [D, TOK], BF16), ("pos_sT", [D, TOK], BF16),
        ("kcT", [D, TOK], BF16), ("ksT", [D, TOK], BF16),
        ("a_cT", [D, TOK], BF16), ("a_sT", [D, TOK], BF16),
        ("m_cT", [D, TOK], BF16), ("m_sT", [D, TOK], BF16),
        ("vg", [1, TOK], F32), ("bwr", [2, TOK], F32),
        ("wr0", [8, TOK], F32),
        ("wgq_sh", [352, D], BF16), ("wuq_sh", [352, D], BF16),
        ("wdq_sh", [128, 2816], BF16), ("dbg", [128, 128], F32),
    ]:
        p.out(name, shape, dt)

    din, dout = p.din, p.dout
    wg1 = p.sb("wg1", [128, 16 * D], BF16)
    wb1 = p.sb("wb1", [128, 8 * 512], BF16)
    wg2 = p.sb("wg2", [128, 8 * 1], BF16)
    wb2 = p.sb("wb2", [128, 4 * 2], BF16)
    wqkv = p.sb("wqkv", [128, 8 * 24], F32)
    WA = p.sb("WA", [128, 8 * D], BF16)
    for nm, dst, kch, ow in [("Wg1", wg1, 16, D), ("Wb1", wb1, 8, 512),
                             ("Wg2", wg2, 8, 1), ("Wb2", wb2, 4, 2),
                             ("wqkvT", wqkv, 8, 24)]:
        for k in range(kch):
            p.dma("gpsimd", dst[:, k * ow:(k + 1) * ow],
                  din[nm][k * 128:(k + 1) * 128, :])
    for k in range(8):
        p.dma("sync", wqkv[:, k * 24:(k + 1) * 24],
              din["wqkvT"][k * 128:(k + 1) * 128, :])
    bias_s = {}
    for nm in ["bpv", "bk", "bkv", "bg1", "bb1", "bg2", "bb2a", "bb2b"]:
        shp = {"bb1": [128, 4], "bg2": [1, 1], "bb2a": [1, 1],
               "bb2b": [1, 1]}.get(nm, [128, 8])
        bias_s[nm] = p.sb(nm + "_s", shp, F32)
        p.dma("sync", bias_s[nm][:, :], din[nm])
    idf = p.sb("idf", [128, 128], F32)
    p.dma("sync", idf[:, :], din["idf32"])
    fmask_s = p.sb("fmask_s", [128, 1], F32)
    p.dma("sync", fmask_s[:, :], din["fmask"])
    ones_col = p.sb("ones_col", [128, 1], F32)
    ones_row = p.sb("ones_row", [1, 128], F32)
    p.memset(ones_col[:, :], 1.0)
    p.memset(ones_row[:, :], 1.0)
    C_PI2 = p.constcol("C_PI2", PI / 2)
    C_PI = p.constcol("C_PI", PI)

    X = p.sb("X", [128, 8 * CB], F32)
    XNf = p.sb("XNf", [128, 8 * CB], F32)
    XNb = p.sb("XNb", [128, 8 * CB], BF16)
    PC = p.sb("PC", [128, 8 * CB], F32)
    PS = p.sb("PS", [128, 8 * CB], F32)
    SC = p.sb("SC", [128, 8 * CB], F32)
    VV = p.sb("VV", [128, 8 * CB], F32)
    GH = p.sb("GH", [128, 8 * CH], BF16)
    BH = p.sb("BH", [128, 4 * CH], BF16)
    sq0_ = p.sb("sq0", [128, 512], F32)
    sq = [sq0_, sq0_]
    row_sum = p.sb("row_sum", [1, CB], F32)
    row_ssq = p.sb("row_ssq", [1, CB], F32)
    row_rs = p.sb("row_rs", [1, CB], F32)
    row_mr = p.sb("row_mr", [1, CB], F32)
    vg_row = p.sb("vg_row", [1, CB], F32)
    bw_a = p.sb("bw_a", [1, CB], F32)
    bw_b = p.sb("bw_b", [1, CB], F32)
    MIX = p.sb("MIX", [128, 24], F32)
    RAT = p.sb("RAT", [128, 64], F32)
    RSC = p.sb("RSC", [128, 64], F32)
    RM = p.sb("RM", [128, 8], F32)
    RL2 = p.sb("RL2", [128, 8], F32)
    RM2 = p.sb("RM2", [128, 8], F32)
    RV = p.sb("RV", [128, 10], F32)
    WRT = p.sb("WRT", [128, 8], F32)
    RRC = p.sb("RRC", [128, 8], F32)
    WRO = p.sb("WRO", [8, 128], F32)
    QA = XNf[:, 0:1024]
    QC1 = XNf[:, 1024:2048]
    QC2 = XNf[:, 2048:3072]
    QS = XNf[:, 3072:3136]
    QR = XNf[:, 3200:3264]
    QO = XNb[:, 0:1024]

    psA = p.ps("psA", [1, 512])
    psB = p.ps("psB", [1, 512])
    P0 = p.ps("P0")
    P1 = p.ps("P1")
    P2 = p.ps("P2")
    P3 = p.ps("P3")
    PR = p.ps("PR", [128, 128])

    def fp4_qd(A, Srec, R, C1, C2, O, rows, colw):
        nblk = colw // 16
        rs, cw, nb = slice(0, rows), slice(0, colw), slice(0, nblk)
        p.reduce(Srec[rs, nb], A[rs, cw].rearrange("p (g b) -> p g b", b=16),
                 ALU.max, absval=True)
        p.ts(Srec[rs, nb], Srec[rs, nb], 1.0 / 6.0, 1e-12, ALU.mult, ALU.max)
        p.recip(R[rs, nb], Srec[rs, nb])
        p.act(C1[rs, cw], A[rs, cw], AF.Abs)
        rb = R[rs, nb].rearrange("p (g one) -> p g one", one=1) \
            .broadcast_to([rows, nblk, 16])
        p.tt(C1[rs, cw].rearrange("p (g b) -> p g b", b=16),
             C1[rs, cw].rearrange("p (g b) -> p g b", b=16), rb, ALU.mult)
        p.ts(C2[rs, cw], C1[rs, cw], 0.25, None, ALU.is_gt)
        for m in (0.75, 1.25, 1.75):
            p.stt(C2[rs, cw], C1[rs, cw], m, C2[rs, cw], ALU.is_gt, ALU.add)
        p.ts(C2[rs, cw], C2[rs, cw], 0.5, None, ALU.mult)
        for m in (2.5, 3.5, 5.0, 5.0):
            p.stt(C2[rs, cw], C1[rs, cw], m, C2[rs, cw], ALU.is_gt, ALU.add)
        sb_ = Srec[rs, nb].rearrange("p (g one) -> p g one", one=1) \
            .broadcast_to([rows, nblk, 16])
        p.tt(C2[rs, cw].rearrange("p (g b) -> p g b", b=16),
             C2[rs, cw].rearrange("p (g b) -> p g b", b=16), sb_, ALU.mult)
        p.act(C1[rs, cw], A[rs, cw], AF.Sign)
        p.tt(O[rs, cw], C2[rs, cw], C1[rs, cw], ALU.mult)

    for (srcn, dstn, nrows, cw) in [("wg_sh", "wgq_sh", 352, 1024),
                                    ("wu_sh", "wuq_sh", 352, 1024),
                                    ("wd_sh", "wdq_sh", 128, 2816)]:
        pieces = [(0, 1024)] if cw == 1024 else \
            [(i * 704, (i + 1) * 704) for i in range(4)]
        r0 = 0
        while r0 < nrows:
            r1 = min(r0 + 128, nrows)
            rr = r1 - r0
            for (c0, c1) in pieces:
                pw = c1 - c0
                p.dma("sync", QA[0:rr, 0:pw], din[srcn][r0:r1, c0:c1])
                fp4_qd(QA, QS, QR, QC1, QC2, QO, rr, pw)
                p.dma("sync", dout[dstn][r0:r1, c0:c1], QO[0:rr, 0:pw])
            r0 = r1

    n8 = slice(0, 8 * CB)
    for c in range(NCH_T):
        xa = c * CH

        for k in range(8):
            p.dma("sync", X[:, k * CB:(k + 1) * CB],
                  din["xT"][k * 128:(k + 1) * 128, xa:xa + CB])
            p.dma("sync", PC[:, k * CB:(k + 1) * CB],
                  din["ppT"][k * 128:(k + 1) * 128, xa:xa + CB])

        feat_stats(p, X, 8, CB, ones_col, sq, psA, psB, row_sum, row_ssq)
        ln_rows(p, CB, row_sum, row_ssq, row_rs, row_mr)
        apply_ln(p, X, XNf, 8, CB, ones_row, row_rs, row_mr, P0, P1)
        p.copy(XNb[:, :], XNf[:, :], eng="scalar")
        for k in range(8):
            p.dma("sync", dout["xnT"][k * 128:(k + 1) * 128, xa:xa + CH],
                  XNf[:, k * CB + 1:(k + 1) * CB])

        p.ts(SC[:, n8], PC[:, n8], 1.0 / TWO_PI, None, ALU.mult)
        p.copy(VV[:, n8].bitcast(I32), SC[:, n8])
        p.copy(SC[:, n8], VV[:, n8].bitcast(I32))
        p.stt(PS[:, n8], SC[:, n8], -TWO_PI, PC[:, n8], ALU.mult, ALU.add)
        p.ts(SC[:, n8], PS[:, n8], PI / 2, -TWO_PI, ALU.is_gt, ALU.mult)
        p.tt(SC[:, n8], SC[:, n8], PS[:, n8], ALU.add)
        p.act(PC[:, n8], SC[:, n8], AF.Sin, bias=C_PI2[:, 0:1])
        p.act(PS[:, n8], PS[:, n8], AF.Sin)
        for k in range(8):
            p.dma("gpsimd", dout["pos_cT"][k * 128:(k + 1) * 128, xa:xa + CH],
                  PC[:, k * CB + 1:(k + 1) * CB])
            p.dma("gpsimd", dout["pos_sT"][k * 128:(k + 1) * 128, xa:xa + CH],
                  PS[:, k * CB + 1:(k + 1) * CB])

        for k in range(8):
            p.dma("gpsimd", WA[:, k * D:(k + 1) * D],
                  din["Wpv"][k * 128:(k + 1) * 128, :])
        proj(p, WA, XNb, VV, [P0, P1], 8, 8, CB, bias=bias_s["bpv"])
        for k in range(8):
            sl = slice(k * CB, (k + 1) * CB)
            p.tt(SC[:, sl], PC[:, sl], VV[:, sl], ALU.mult)
        for k in range(8):
            p.dma("gpsimd", dout["a_cT"][k * 128:(k + 1) * 128, xa:xa + CH],
                  SC[:, k * CB + 1:(k + 1) * CB])
        for k in range(8):
            sl = slice(k * CB, (k + 1) * CB)
            p.tt(PC[:, sl], PS[:, sl], VV[:, sl], ALU.mult)
        for k in range(8):
            p.dma("gpsimd", dout["a_sT"][k * 128:(k + 1) * 128, xa:xa + CH],
                  PC[:, k * CB + 1:(k + 1) * CB])

        for k in range(8):
            p.dma("gpsimd", WA[:, k * D:(k + 1) * D],
                  din["Wkv"][k * 128:(k + 1) * 128, :])
        proj(p, WA, XNb, VV, [P2, P3], 8, 8, CB, bias=bias_s["bkv"])

        for k in range(8):
            p.dma("gpsimd", WA[:, k * D:(k + 1) * D],
                  din["Wk"][k * 128:(k + 1) * 128, :])
        proj(p, WA, XNb, SC, [P0, P1], 8, 8, CB, bias=bias_s["bk"],
             act=AF.Tanh)
        p.act(PS[:, n8], SC[:, n8], AF.Sin, scale=C_PI[:, 0:1])
        p.ts(PC[:, n8], SC[:, n8], 0.5, -TWO_PI, ALU.is_gt, ALU.mult)
        p.stt(PC[:, n8], SC[:, n8], PI, PC[:, n8], ALU.mult, ALU.add)
        p.act(PC[:, n8], PC[:, n8], AF.Sin, bias=C_PI2[:, 0:1])
        for k in range(8):
            p.dma("gpsimd", dout["kcT"][k * 128:(k + 1) * 128, xa:xa + CH],
                  PC[:, k * CB + 1:(k + 1) * CB])
            p.dma("gpsimd", dout["ksT"][k * 128:(k + 1) * 128, xa:xa + CH],
                  PS[:, k * CB + 1:(k + 1) * CB])

        for o in range(8):
            ps = [P0, P1][o % 2]
            p.mm(ps[:, 0:CH],
                 [wg1[:, k * D + o * 128: k * D + o * 128 + 128]
                  for k in range(16)],
                 [XNb[:, k * CB + 1:(k + 1) * CB] for k in range(8)] +
                 [XNb[:, k * CB + 0:(k + 1) * CB - 1] for k in range(8)])
            p.act(GH[:, o * CH:(o + 1) * CH], ps[:, 0:CH], AF.Gelu,
                  bias=bias_s["bg1"][:, o:o + 1])
        p.mm(psA[0:1, 0:CH],
             [wg2[:, k:k + 1] for k in range(8)],
             [GH[:, k * CH:(k + 1) * CH] for k in range(8)])
        p.act(vg_row[0:1, 1:CB], psA[0:1, 0:CH], AF.Sigmoid,
              bias=bias_s["bg2"][0:1, 0:1])
        p.dma("sync", dout["vg"][0:1, xa:xa + CH], vg_row[0:1, 1:CB])

        for o in range(4):
            ps = [P2, P3][o % 2]
            p.mm(ps[:, 0:CH],
                 [wb1[:, k * 512 + o * 128: k * 512 + o * 128 + 128]
                  for k in range(8)],
                 [XNb[:, k * CB + 1:(k + 1) * CB] for k in range(8)])
            p.act(BH[:, o * CH:(o + 1) * CH], ps[:, 0:CH], AF.Gelu,
                  bias=bias_s["bb1"][:, o:o + 1])
        p.mm(psA[0:1, 0:CH],
             [wb2[:, 2 * k:2 * k + 1] for k in range(4)],
             [BH[:, k * CH:(k + 1) * CH] for k in range(4)])
        p.mm(psB[0:1, 0:CH],
             [wb2[:, 2 * k + 1:2 * k + 2] for k in range(4)],
             [BH[:, k * CH:(k + 1) * CH] for k in range(4)])
        p.act(bw_a[0:1, 1:CB], psA[0:1, 0:CH], AF.Identity,
              bias=bias_s["bb2a"][0:1, 0:1])
        p.act(bw_b[0:1, 1:CB], psB[0:1, 0:CH], AF.Identity,
              bias=bias_s["bb2b"][0:1, 0:1])
        p.tt(bw_a[0:1, 1:CB], bw_a[0:1, 1:CB], bw_b[0:1, 1:CB], ALU.subtract)
        p.act(bw_a[0:1, 1:CB], bw_a[0:1, 1:CB], AF.Sigmoid)
        p.ts(bw_b[0:1, 1:CB], bw_a[0:1, 1:CB], -1.0, 1.0, ALU.mult, ALU.add)
        p.dma("sync", dout["bwr"][0:1, xa:xa + CH], bw_a[0:1, 1:CB])
        p.dma("sync", dout["bwr"][1:2, xa:xa + CH], bw_b[0:1, 1:CB])

        p.mm(P2[:, 0:CH], ones_row[0:1, :], vg_row[0:1, 1:CB])
        for k in range(8):
            slt = slice(k * CB + 1, (k + 1) * CB)
            sls = slice(k * CB + 0, (k + 1) * CB - 1)
            p.tt(SC[:, slt], PC[:, sls], VV[:, slt], ALU.mult)
            p.tt(SC[:, slt], SC[:, slt], P2[:, 0:CH], ALU.mult)
            p.tt(X[:, slt], PS[:, sls], VV[:, slt], ALU.mult)
            p.tt(X[:, slt], X[:, slt], P2[:, 0:CH], ALU.mult)
        if c == 0:
            for k in range(8):
                p.ts(SC[:, k * CB + 1:k * CB + 2],
                     SC[:, k * CB + 1:k * CB + 2],
                     fmask_s[:, 0:1], None, ALU.mult)
                p.ts(X[:, k * CB + 1:k * CB + 2],
                     X[:, k * CB + 1:k * CB + 2],
                     fmask_s[:, 0:1], None, ALU.mult)
        for k in range(8):
            p.dma("gpsimd", dout["m_cT"][k * 128:(k + 1) * 128, xa:xa + CH],
                  SC[:, k * CB + 1:(k + 1) * CB])
            p.dma("gpsimd", dout["m_sT"][k * 128:(k + 1) * 128, xa:xa + CH],
                  X[:, k * CB + 1:(k + 1) * CB])

        for i in range(CH // 128):
            tbase = 1 + i * 128
            p.mm(PR[:, 0:24],
                 [XNf[:, k * CB + tbase: k * CB + tbase + 128]
                  for k in range(8)],
                 [wqkv[:, k * 24:(k + 1) * 24] for k in range(8)])
            p.copy(MIX[:, 0:24], PR[:, 0:24])
            for e in range(8):
                p.tt(RAT[:, 8 * e:8 * e + 8],
                     MIX[:, e:e + 1].broadcast_to([128, 8]),
                     MIX[:, 8:16], ALU.mult)
            g3 = RAT[:, 0:64].rearrange("p (e j) -> p e j", j=8)
            p.reduce(RM[:, 0:8], g3, ALU.max)
            p.tt(RSC[:, 0:64].rearrange("p (e j) -> p e j", j=8), g3,
                 RM[:, 0:8].rearrange("p (e one) -> p e one", one=1)
                 .broadcast_to([128, 8, 8]), ALU.subtract)
            p.act(RSC[:, 0:64], RSC[:, 0:64], AF.Exp)
            s3 = RSC[:, 0:64].rearrange("p (e j) -> p e j", j=8)
            p.reduce(RM[:, 0:8], s3, ALU.add)
            p.recip(RRC[:, 0:8], RM[:, 0:8])
            p.tt(RSC[:, 0:64].rearrange("p (e j) -> p e j", j=8), s3,
                 RRC[:, 0:8].rearrange("p (e one) -> p e one", one=1)
                 .broadcast_to([128, 8, 8]), ALU.mult)
            p.tt(RSC[:, 0:64].rearrange("p (e j) -> p e j", j=8), s3,
                 MIX[:, 16:24].rearrange("p (one j) -> p one j", one=1)
                 .broadcast_to([128, 8, 8]), ALU.mult)
            p.reduce(RM[:, 0:8], s3, ALU.add)
            p.reduce(RV[:, 0:1], RM[:, 0:8], ALU.max)
            p.ts(RL2[:, 0:8], RM[:, 0:8], RV[:, 0:1], None, ALU.is_equal)
            p.stt(RM2[:, 0:8], RL2[:, 0:8], -1e30, RM[:, 0:8],
                  ALU.mult, ALU.add)
            p.reduce(RV[:, 1:2], RM2[:, 0:8], ALU.max)
            p.ts(WRT[:, 0:8], RM2[:, 0:8], RV[:, 1:2], None, ALU.is_equal)
            p.tt(RV[:, 2:3], RV[:, 1:2], RV[:, 0:1], ALU.subtract)
            p.act(RV[:, 3:4], RV[:, 2:3], AF.Exp)
            p.ts(RV[:, 4:5], RV[:, 3:4], 1.0, None, ALU.add)
            p.recip(RV[:, 6:7], RV[:, 4:5])
            p.tt(RV[:, 5:6], RV[:, 3:4], RV[:, 6:7], ALU.mult)
            p.ts(RL2[:, 0:8], RL2[:, 0:8], RV[:, 6:7], None, ALU.mult)
            p.ts(WRT[:, 0:8], WRT[:, 0:8], RV[:, 5:6], None, ALU.mult)
            p.tt(WRT[:, 0:8], WRT[:, 0:8], RL2[:, 0:8], ALU.add)
            if c == 0 and i == 0:
                p.dma("sync", dout["dbg"][:, 0:24], MIX[:, 0:24])
                p.dma("sync", dout["dbg"][:, 24:88], RSC[:, 0:64])
                p.dma("sync", dout["dbg"][:, 88:96], RM[:, 0:8])
                p.dma("sync", dout["dbg"][:, 96:104], RL2[:, 0:8])
                p.dma("sync", dout["dbg"][:, 104:112], WRT[:, 0:8])
                p.dma("sync", dout["dbg"][:, 112:118], RV[:, 0:6])
            p.transpose(PR[0:8, 0:128], WRT[:, 0:8], idf[:, :])
            p.copy(WRO[0:8, 0:128], PR[0:8, 0:128])
            p.dma("sync", dout["wr0"][0:8, xa + i * 128: xa + (i + 1) * 128],
                  WRO[0:8, 0:128])

    return p.finalize()


def host_l1(inputs, state):
    nc = get_nc("l1", build_l1)
    x = np.asarray(inputs["L0_attn_x"], np.float32)
    pp = np.asarray(inputs["L0_attn_pos_phases"], np.float32)[:S]
    xf = x.reshape(NB * S, D)
    in_maps = []
    idf32 = np.eye(128, dtype=np.float32)
    Wg1 = _f(inputs["L0_attn_Wg1"])
    com = dict(
        Wpv=_f(inputs["L0_attn_Wpv"]), Wk=_f(inputs["L0_attn_Wk"]),
        Wkv=_f(inputs["L0_attn_Wkv"]), Wg1=Wg1,
        Wb1=_f(inputs["L0_attn_Wb1"]), Wg2=_f(inputs["L0_attn_Wg2"]),
        Wb2=_f(inputs["L0_attn_Wb2"]), wqkvT=_t(inputs["L0_ffn_wqkv"]),
        bpv=_f(inputs["L0_attn_bpv"]).reshape(8, 128).T.copy(),
        bk=_f(inputs["L0_attn_bk"]).reshape(8, 128).T.copy(),
        bkv=_f(inputs["L0_attn_bkv"]).reshape(8, 128).T.copy(),
        bg1=_f(inputs["L0_attn_bg1"]).reshape(8, 128).T.copy(),
        bb1=_f(inputs["L0_attn_bb1"]).reshape(4, 128).T.copy(),
        bg2=_f(inputs["L0_attn_bg2"]).reshape(1, 1),
        bb2a=_f(inputs["L0_attn_bb2"])[0].reshape(1, 1),
        bb2b=_f(inputs["L0_attn_bb2"])[1].reshape(1, 1),
        idf32=idf32,
    )
    wg = _f(inputs["L2_ffn_w_gate"])
    wu = _f(inputs["L2_ffn_w_up"])
    wd = _f(inputs["L2_ffn_w_down"])
    for c in range(NCORES):
        b, t0 = divmod(c * TOK, S)
        xs = np.zeros((D, TOK + 1), np.float32)
        xs[:, 1:] = xf[b * S + t0: b * S + t0 + TOK].T
        if t0 > 0:
            xs[:, 0] = xf[b * S + t0 - 1]
        pps = np.zeros((D, TOK + 1), np.float32)
        pps[:, 1:] = pp[t0:t0 + TOK].T
        if t0 > 0:
            pps[:, 0] = pp[t0 - 1]
        m = dict(com)
        m["xT"] = xs
        m["ppT"] = pps
        m["fmask"] = np.full((128, 1), 0.0 if t0 == 0 else 1.0, np.float32)
        m["wg_sh"] = wg[c * 352:(c + 1) * 352]
        m["wu_sh"] = wu[c * 352:(c + 1) * 352]
        m["wd_sh"] = wd[c * 128:(c + 1) * 128]
        in_maps.append(m)
    res = run8(nc, in_maps)

    state["l1"] = res
    for name in ["xnT", "pos_cT", "pos_sT", "kcT", "ksT",
                 "a_cT", "a_sT", "m_cT", "m_sT"]:
        state[name] = np.concatenate([np.asarray(res[c][name], np.float32)
                                      for c in range(NCORES)], axis=1)
    for name in ["vg", "bwr", "wr0"]:
        state[name] = np.concatenate([res[c][name] for c in range(NCORES)],
                                     axis=1)
    state["wgq"] = np.concatenate([res[c]["wgq_sh"] for c in range(NCORES)], 0)
    state["wuq"] = np.concatenate([res[c]["wuq_sh"] for c in range(NCORES)], 0)
    state["wdq"] = np.concatenate([res[c]["wdq_sh"] for c in range(NCORES)], 0)
    return state




def build_l2():
    p = Prog("l2")
    for name, shape, dt in [
        ("a_c", [256, S], BF16), ("a_s", [256, S], BF16),
        ("m_c", [256, S], BF16), ("m_s", [256, S], BF16),
        ("pc", [256, S], BF16), ("psn", [256, S], BF16),
        ("kc", [256, S], BF16), ("ks", [256, S], BF16),
        ("vgf", [1, S], F32), ("bwf", [2, S], F32),
    ]:
        p.inp(name, shape, dt)
    p.out("bl", [256, S], F32)
    din, dout = p.din, p.dout

    PW = 2048
    NP = S // PW
    ones_col = p.sb("ones_col", [128, 1], F32)
    ones_row = p.sb("ones_row", [1, 128], F32)
    p.memset(ones_col[:, :], 1.0)
    p.memset(ones_row[:, :], 1.0)
    ONES = p.sb("ONES", [128, PW], F32)
    p.memset(ONES[:, :], 1.0)
    ACC = p.sb("ACC", [128, PW], F32)
    SCN = p.sb("SCN", [128, PW], F32)
    SCI = p.sb("SCI", [128, PW], F32)
    PCF = p.sb("PCF", [128, PW], F32)
    INB = p.sb("INB", [128, PW], BF16)
    TMP = p.sb("TMP", [128, PW], F32)
    CAR = p.sb("CAR", [128, 16], F32)
    VGR = p.sb("VGR", [1, PW], F32)
    GCR = p.sb("GCR", [1, PW], F32)
    W0R = p.sb("W0R", [1, PW], F32)
    W1R = p.sb("W1R", [1, PW], F32)
    GCC = p.sb("GCC", [1, 2], F32)
    P0 = p.ps("P0")
    P1 = p.ps("P1")

    p.memset(CAR[:, :], 0.0)
    p.memset(GCC[:, :], 0.0)

    for pc_ in range(NP):
        a = pc_ * PW
        p.dma("sync", VGR[0:1, :], din["vgf"][0:1, a:a + PW])
        p.scan(GCR[0:1, :], ONES[0:1, :], VGR[0:1, :], GCC[0:1, 0:1],
               ALU.mult, ALU.add)
        p.copy(GCC[0:1, 0:1], GCR[0:1, PW - 1:PW])
        p.ts(TMP[0:1, 0:PW], GCR[0:1, :], 1.0, None, ALU.max)
        p.act(TMP[0:1, 0:PW], TMP[0:1, 0:PW], AF.Sqrt)
        p.recip(W1R[0:1, :], TMP[0:1, 0:PW])
        p.dma("sync", W0R[0:1, :], din["bwf"][0:1, a:a + PW])
        p.ts(W0R[0:1, :], W0R[0:1, :], 1.0 / 32.0, None, ALU.mult)
        p.dma("sync", TMP[0:1, 0:PW], din["bwf"][1:2, a:a + PW])
        p.tt(W1R[0:1, :], W1R[0:1, :], TMP[0:1, 0:PW], ALU.mult)
        p.ts(W1R[0:1, :], W1R[0:1, :], 1.0 / 32.0, None, ALU.mult)

        for half in range(2):
            r0 = half * 128
            carb = 8 * half

            def scan_in(name, car_i):
                p.dma("sync", INB[:, :], din[name][r0:r0 + 128, a:a + PW])
                p.copy(SCI[:, :], INB[:, :])
                p.scan(SCN[:, :], ONES[:, :], SCI[:, :],
                       CAR[:, carb + car_i:carb + car_i + 1],
                       ALU.mult, ALU.add)
                p.copy(CAR[:, carb + car_i:carb + car_i + 1],
                       SCN[:, PW - 1:PW])

            def load_f32(name):
                p.dma("sync", INB[:, :], din[name][r0:r0 + 128, a:a + PW])
                p.copy(PCF[:, :], INB[:, :])

            scan_in("a_c", 0)
            load_f32("pc")
            p.tt(ACC[:, :], SCN[:, :], PCF[:, :], ALU.mult)
            scan_in("a_s", 1)
            load_f32("psn")
            p.tt(TMP[:, :], SCN[:, :], PCF[:, :], ALU.mult)
            p.tt(ACC[:, :], ACC[:, :], TMP[:, :], ALU.add)
            for (sa, sb_) in col_splits(PW, 512):
                w = sb_ - sa
                p.mm(P0[:, 0:w], ones_row[0:1, :], W0R[0:1, sa:sb_])
                p.tt(ACC[:, sa:sb_], ACC[:, sa:sb_], P0[:, 0:w], ALU.mult)
            scan_in("m_c", 2)
            load_f32("kc")
            p.tt(TMP[:, :], SCN[:, :], PCF[:, :], ALU.mult)
            scan_in("m_s", 3)
            load_f32("ks")
            p.tt(PCF[:, :], SCN[:, :], PCF[:, :], ALU.mult)
            p.tt(TMP[:, :], TMP[:, :], PCF[:, :], ALU.add)
            for (sa, sb_) in col_splits(PW, 512):
                w = sb_ - sa
                p.mm(P1[:, 0:w], ones_row[0:1, :], W1R[0:1, sa:sb_])
                p.tt(TMP[:, sa:sb_], TMP[:, sa:sb_], P1[:, 0:w], ALU.mult)
            p.tt(ACC[:, :], ACC[:, :], TMP[:, :], ALU.add)
            p.dma("sync", dout["bl"][r0:r0 + 128, a:a + PW], ACC[:, :])
    return p.finalize()


def host_l2(inputs, state):
    nc = get_nc("l2", build_l2)
    in_maps = []
    for c in range(NCORES):
        b, f0 = c // 4, (c % 4) * 256
        m = {}
        for nm, key in [("a_c", "a_cT"), ("a_s", "a_sT"), ("m_c", "m_cT"),
                        ("m_s", "m_sT"), ("pc", "pos_cT"), ("psn", "pos_sT"),
                        ("kc", "kcT"), ("ks", "ksT")]:
            full = state[key]
            m[nm] = np.ascontiguousarray(
                full[f0:f0 + 256, b * S:(b + 1) * S]).astype(BF)
        m["vgf"] = np.ascontiguousarray(state["vg"][0:1, b * S:(b + 1) * S])
        m["bwf"] = np.ascontiguousarray(state["bwr"][:, b * S:(b + 1) * S])
        in_maps.append(m)
    res = run8(nc, in_maps)
    bl = np.zeros((NB, D, S), np.float32)
    for c in range(NCORES):
        b, f0 = c // 4, (c % 4) * 256
        bl[b, f0:f0 + 256] = res[c]["bl"]
    state["blT"] = np.ascontiguousarray(
        np.concatenate([bl[0], bl[1]], axis=1))
    return state



def build_l3():
    p = Prog("l3")
    for name, shape, dt in [
        ("blT", [D, TOK], F32), ("xnT", [D, TOK], F32), ("xT", [D, TOK], F32),
        ("wr0", [8, TOK], F32),
        ("Wo", [D, D], F32), ("bo", [128, 8], F32),
        ("ln_g", [128, 8], F32), ("ln_b", [128, 8], F32),
        ("w1L", [8 * 22 * 128, 8 * 256], F32),
        ("w2L", [8 * 8 * 128, 22 * 128], F32),
    ]:
        p.inp(name, shape, dt)
    p.out("x1T", [D, TOK], F32)
    p.out("xn1T", [D, TOK], F32)
    din, dout = p.din, p.dout

    ones_col = p.sb("ones_col", [128, 1], F32)
    ones_row = p.sb("ones_row", [1, 128], F32)
    p.memset(ones_col[:, :], 1.0)
    p.memset(ones_row[:, :], 1.0)
    HCH = 1024
    XN = p.sb("XN", [128, 8 * HCH], F32)
    XNb = p.sb("XNb", [128, 8 * HCH], BF16)
    BL = p.sb("BL", [128, 8 * HCH], F32)
    LNB = p.sb("LNB", [128, 8 * HCH], BF16)
    SS = p.sb("SS", [128, 8 * HCH], F32)
    MA = XN
    XO = BL
    HE = p.sb("HE", [128, 22 * 512], BF16)
    W1B_ = [p.sb("W1B0", [128, 8 * 256], BF16),
            p.sb("W1B1", [128, 8 * 256], BF16)]
    W2B = p.sb("W2B0", [128, 22 * 128], BF16)
    WRB = p.sb("WRB", [128, HCH], F32)
    T1 = p.sb("T1", [128, 512], F32)
    T2 = p.sb("T2", [128, 512], F32)
    sq0_ = p.sb("sq0", [128, 512], F32)
    sq = [sq0_, sq0_]
    row_sum = p.sb("row_sum", [1, HCH], F32)
    row_ssq = p.sb("row_ssq", [1, HCH], F32)
    row_rs = p.sb("row_rs", [1, HCH], F32)
    row_mr = p.sb("row_mr", [1, HCH], F32)
    gcol = {}
    for nm in ["ln_g", "ln_b", "bo"]:
        gcol[nm] = p.sb(nm + "_s", [128, 8], F32)
        p.dma("sync", gcol[nm][:, :], din[nm])
    WOB = p.sb("WOB", [128, 8 * D], BF16)
    for k in range(8):
        p.dma("gpsimd", WOB[:, k * D:(k + 1) * D],
              din["Wo"][k * 128:(k + 1) * 128, :])
    psA = p.ps("psA", [1, 512])
    psB = p.ps("psB", [1, 512])
    P0 = p.ps("P0")
    P1 = p.ps("P1")
    P2 = p.ps("P2")
    P3 = p.ps("P3")

    for hf in range(2):
        xa = hf * HCH
        for k in range(8):
            p.dma("sync", BL[:, k * HCH:(k + 1) * HCH],
                  din["blT"][k * 128:(k + 1) * 128, xa:xa + HCH])
            p.dma("sync", XN[:, k * HCH:(k + 1) * HCH],
                  din["xnT"][k * 128:(k + 1) * 128, xa:xa + HCH])
        p.copy(XNb[:, :], XN[:, :], eng="scalar")

        feat_stats(p, BL, 8, HCH, ones_col, sq, psA, psB, row_sum, row_ssq)
        ln_rows(p, HCH, row_sum, row_ssq, row_rs, row_mr)
        apply_ln(p, BL, BL, 8, HCH, ones_row, row_rs, row_mr, P0, P1,
                 gain_cols=gcol["ln_g"], bias_cols=gcol["ln_b"])
        p.copy(LNB[:, :], BL[:, :], eng="scalar")
        for o in range(8):
            for (a, b) in col_splits(HCH, 512):
                w = b - a
                ps = [P0, P1][o % 2]
                p.mm(ps[:, 0:w],
                     [WOB[:, k * D + o * 128: k * D + o * 128 + 128]
                      for k in range(8)],
                     [LNB[:, k * HCH + a: k * HCH + b] for k in range(8)])
                p.act(SS[:, o * HCH + a: o * HCH + b], ps[:, 0:w],
                      AF.Identity, bias=gcol["bo"][:, o:o + 1])
                p.tt(SS[:, o * HCH + a: o * HCH + b],
                     SS[:, o * HCH + a: o * HCH + b],
                     XN[:, o * HCH + a: o * HCH + b], ALU.add)

        p.memset(MA[:, :], 0.0)
        for e in range(8):
            p.dma("sync", row_mr[0:1, 0:HCH],
                  din["wr0"][e:e + 1, xa:xa + HCH])
            for (a, b) in col_splits(HCH, 512):
                w = b - a
                p.mm(P2[:, 0:w], ones_row[0:1, :], row_mr[0:1, a:b])
                p.copy(WRB[:, a:b], P2[:, 0:w])
            for tb in range(HCH // 512):
                ta = tb * 512
                for h in range(22):
                    r0_ = (e * 22 + h) * 128
                    W1B = W1B_[h % 2]
                    p.dma("gpsimd", W1B[:, :], din["w1L"][r0_:r0_ + 128, :])
                    p.mm(P2[:, 0:512],
                         [W1B[:, k * 256: k * 256 + 128] for k in range(8)],
                         [XNb[:, k * HCH + ta: k * HCH + ta + 512]
                          for k in range(8)])
                    p.mm(P3[:, 0:512],
                         [W1B[:, k * 256 + 128: k * 256 + 256]
                          for k in range(8)],
                         [XNb[:, k * HCH + ta: k * HCH + ta + 512]
                          for k in range(8)])
                    p.act(T1[:, 0:512], P2[:, 0:512], AF.Silu)
                    p.tt(T2[:, 0:512], T1[:, 0:512], P3[:, 0:512], ALU.mult)
                    p.tt(HE[:, h * 512:(h + 1) * 512], T2[:, 0:512],
                         WRB[:, ta:ta + 512], ALU.mult)
                for d in range(8):
                    r0_ = (e * 8 + d) * 128
                    p.dma("gpsimd", W2B[:, :], din["w2L"][r0_:r0_ + 128, :])
                    ps = [P2, P3][d % 2]
                    p.mm(ps[:, 0:512],
                         [W2B[:, h * 128:(h + 1) * 128] for h in range(22)],
                         [HE[:, h * 512:(h + 1) * 512] for h in range(22)])
                    p.tt(MA[:, d * HCH + ta: d * HCH + ta + 512],
                         MA[:, d * HCH + ta: d * HCH + ta + 512],
                         ps[:, 0:512], ALU.add)

        for k in range(8):
            p.dma("sync", XO[:, k * HCH:(k + 1) * HCH],
                  din["xT"][k * 128:(k + 1) * 128, xa:xa + HCH])
        p.act(MA[:, :], MA[:, :], AF.Silu)
        p.tt(MA[:, :], MA[:, :], SS[:, :], ALU.mult)
        p.tt(XO[:, :], XO[:, :], MA[:, :], ALU.add)
        for k in range(8):
            p.dma("sync", dout["x1T"][k * 128:(k + 1) * 128, xa:xa + HCH],
                  XO[:, k * HCH:(k + 1) * HCH])
        feat_stats(p, XO, 8, HCH, ones_col, sq, psA, psB, row_sum, row_ssq)
        ln_rows(p, HCH, row_sum, row_ssq, row_rs, row_mr)
        apply_ln(p, XO, XO, 8, HCH, ones_row, row_rs, row_mr, P0, P1)
        for k in range(8):
            p.dma("sync", dout["xn1T"][k * 128:(k + 1) * 128, xa:xa + HCH],
                  XO[:, k * HCH:(k + 1) * HCH])
    return p.finalize()


def host_l3(inputs, state):
    nc = get_nc("l3", build_l3)
    w1 = np.asarray(inputs["L0_ffn_w1"], np.float32)
    w2 = np.asarray(inputs["L0_ffn_w2"], np.float32)
    w1T = w1.transpose(0, 2, 1)
    w1L = np.zeros((8, 22, 128, 8 * 256), np.float32)
    w1a = w1T[:, :, :2816].reshape(8, 8, 128, 22, 128)
    w1g = w1T[:, :, 2816:].reshape(8, 8, 128, 22, 128)
    for h in range(22):
        for k in range(8):
            w1L[:, h, :, k * 256:k * 256 + 128] = w1a[:, k, :, h]
            w1L[:, h, :, k * 256 + 128:k * 256 + 256] = w1g[:, k, :, h]

    w2T = w2.transpose(0, 2, 1)
    w2L = np.zeros((8, 8, 128, 22 * 128), np.float32)
    w2r = w2T.reshape(8, 22, 128, 8, 128)
    for d in range(8):
        for h in range(22):
            w2L[:, d, :, h * 128:(h + 1) * 128] = w2r[:, h, :, d]
    x = np.asarray(inputs["L0_attn_x"], np.float32).reshape(NB * S, D)
    com = dict(
        Wo=_f(inputs["L0_attn_Wo"]),
        bo=_f(inputs["L0_attn_bo"]).reshape(8, 128).T.copy(),
        ln_g=_f(inputs["L0_attn_ln_g"]).reshape(8, 128).T.copy(),
        ln_b=_f(inputs["L0_attn_ln_b"]).reshape(8, 128).T.copy(),
        w1L=np.ascontiguousarray(w1L.reshape(8 * 22 * 128, 8 * 256)),
        w2L=np.ascontiguousarray(w2L.reshape(8 * 8 * 128, 22 * 128)),
    )
    in_maps = []
    for c in range(NCORES):
        m = dict(com)
        m["blT"] = np.ascontiguousarray(state["blT"][:, c * TOK:(c + 1) * TOK])
        m["xnT"] = np.ascontiguousarray(state["xnT"][:, c * TOK:(c + 1) * TOK])
        m["xT"] = np.ascontiguousarray(x[c * TOK:(c + 1) * TOK].T)
        m["wr0"] = np.ascontiguousarray(state["wr0"][:, c * TOK:(c + 1) * TOK])
        in_maps.append(m)
    res = run8(nc, in_maps)
    state["x1T"] = np.concatenate([res[c]["x1T"] for c in range(NCORES)], 1)
    state["xn1T"] = np.concatenate([res[c]["xn1T"] for c in range(NCORES)], 1)
    return state




def build_l4():
    p = Prog("l4")
    for name, shape, dt in [
        ("xn1T", [D, TOK], F32),
        ("wq", [D, D], F32), ("wk", [D, D], F32), ("wv", [D, D], F32),
        ("projN", [64, 64], F32),
        ("wqT", [D, D], F32), ("wqTr", [D, D], F32),
        ("wkT", [D, D], F32), ("wkTr", [D, D], F32),
        ("wvT", [D, D], F32),
        ("g1c", [128, 8], F32), ("b1c", [128, 8], F32),
        ("anwc", [128, 8], F32),
        ("COSF", [128, TOK], F32), ("SINF", [128, TOK], F32),
    ]:
        p.inp(name, shape, dt)
    for name in ["qpT", "kpT", "vT", "q2rT", "k2rT", "v2T"]:
        p.out(name, [D, TOK], BF16)
    din, dout = p.din, p.dout

    ones_col = p.sb("ones_col", [128, 1], F32)
    ones_row = p.sb("ones_row", [1, 128], F32)
    p.memset(ones_col[:, :], 1.0)
    p.memset(ones_row[:, :], 1.0)
    HCH = 1024
    XN1 = p.sb("XN1", [128, 8 * HCH], F32)
    HB = p.sb("HB", [128, 8 * HCH], BF16)
    Z1 = p.sb("Z1", [128, 8 * HCH], F32)
    Z1b = p.sb("Z1b", [128, 8 * HCH], BF16)
    OUTB = p.sb("OUTB", [128, 8 * HCH], BF16)
    T1 = p.sb("T1", [128, 512], F32)
    T2 = p.sb("T2", [128, 512], F32)
    WA = p.sb("WA", [128, 8 * D], BF16)
    PRJ = p.sb("PRJ", [128, 64], BF16)
    COSs = p.sb("COSs", [128, HCH], F32)
    SINs = p.sb("SINs", [128, HCH], F32)
    sq0_ = p.sb("sq0", [128, 512], F32)
    sq = [sq0_, sq0_]
    row_sum = p.sb("row_sum", [1, HCH], F32)
    row_ssq = p.sb("row_ssq", [1, HCH], F32)
    row_rs = p.sb("row_rs", [1, HCH], F32)
    row_mr = p.sb("row_mr", [1, HCH], F32)
    gc = {}
    for nm in ["g1c", "b1c", "anwc"]:
        gc[nm] = p.sb(nm + "_s", [128, 8], F32)
        p.dma("sync", gc[nm][:, :], din[nm])
    p.dma("gpsimd", PRJ[0:64, :], din["projN"])
    p.dma("gpsimd", PRJ[64:128, :], din["projN"])
    psA = p.ps("psA", [1, 512])
    psB = p.ps("psB", [1, 512])
    P0 = p.ps("P0")
    P1 = p.ps("P1")
    P2 = p.ps("P2")
    P3 = p.ps("P3")

    def loadW(nm):
        for k in range(8):
            p.dma("gpsimd", WA[:, k * D:(k + 1) * D],
                  din[nm][k * 128:(k + 1) * 128, :])

    for hf in range(2):
        xa = hf * HCH
        for k in range(8):
            p.dma("sync", XN1[:, k * HCH:(k + 1) * HCH],
                  din["xn1T"][k * 128:(k + 1) * 128, xa:xa + HCH])
        p.dma("sync", COSs[:, :], din["COSF"][:, xa:xa + HCH])
        p.dma("sync", SINs[:, :], din["SINF"][:, xa:xa + HCH])

        feat_stats(p, XN1, 8, HCH, ones_col, sq, psA, psB, row_sum, row_ssq)
        ln_rows(p, HCH, row_sum, row_ssq, row_rs, row_mr)
        apply_ln(p, XN1, Z1, 8, HCH, ones_row, row_rs, row_mr, P0, P1,
                 gain_cols=gc["g1c"], bias_cols=gc["b1c"])
        p.copy(HB[:, :], Z1[:, :], eng="scalar")

        for (wn, on) in [("wq", "qpT"), ("wk", "kpT")]:
            loadW(wn)
            proj(p, WA, HB, Z1b, [P0, P1], 8, 8, HCH)
            for h in range(16):
                o, r0 = h // 2, (h % 2) * 64
                for (a, b) in col_splits(HCH, 512):
                    w = b - a
                    ps = [P2, P3][h % 2]
                    p.mm(ps[0:64, 0:w], PRJ[r0:r0 + 64, :],
                         Z1b[r0:r0 + 64, o * HCH + a: o * HCH + b])
                    rsl = slice(r0, r0 + 64)
                    p.act(T1[rsl, 0:w], ps[0:64, 0:w], AF.Relu)
                    p.tt(T2[rsl, 0:w], ps[0:64, 0:w], T1[rsl, 0:w],
                         ALU.subtract)
                    p.act(T2[rsl, 0:w], T2[rsl, 0:w], AF.Exp)
                    p.tt(OUTB[rsl, o * HCH + a: o * HCH + b],
                         T2[rsl, 0:w], T1[rsl, 0:w], ALU.add)
            for k in range(8):
                p.dma("sync", dout[on][k * 128:(k + 1) * 128, xa:xa + HCH],
                      OUTB[:, k * HCH:(k + 1) * HCH])

        loadW("wv")
        proj(p, WA, HB, OUTB, [P0, P1], 8, 8, HCH)
        for k in range(8):
            p.dma("sync", dout["vT"][k * 128:(k + 1) * 128, xa:xa + HCH],
                  OUTB[:, k * HCH:(k + 1) * HCH])

        feat_stats(p, XN1, 8, HCH, ones_col, sq, psA, psB, row_sum, row_ssq)
        ln_rows(p, HCH, row_sum, row_ssq, row_rs, row_mr, eps=1e-6, rms=True)
        apply_ln(p, XN1, Z1, 8, HCH, ones_row, row_rs, None, P0, P1,
                 gain_cols=gc["anwc"])
        p.copy(HB[:, :], Z1[:, :], eng="scalar")

        for (w1n, w2n, on) in [("wqT", "wqTr", "q2rT"), ("wkT", "wkTr", "k2rT")]:
            loadW(w1n)
            proj(p, WA, HB, Z1, [P0, P1], 8, 8, HCH)
            loadW(w2n)
            for o in range(8):
                for (a, b) in col_splits(HCH, 512):
                    w = b - a
                    ps = [P2, P3][o % 2]
                    p.mm(ps[:, 0:w],
                         [WA[:, k * D + o * 128: k * D + o * 128 + 128]
                          for k in range(8)],
                         [HB[:, k * HCH + a: k * HCH + b] for k in range(8)])
                    sl = slice(o * HCH + a, o * HCH + b)
                    p.tt(T1[:, 0:w], ps[:, 0:w], SINs[:, a:b], ALU.mult)
                    p.tt(T2[:, 0:w], Z1[:, sl], COSs[:, a:b], ALU.mult)
                    p.tt(OUTB[:, sl], T1[:, 0:w], T2[:, 0:w], ALU.add)
            for k in range(8):
                p.dma("sync", dout[on][k * 128:(k + 1) * 128, xa:xa + HCH],
                      OUTB[:, k * HCH:(k + 1) * HCH])

        loadW("wvT")
        proj(p, WA, HB, OUTB, [P0, P1], 8, 8, HCH)
        for k in range(8):
            p.dma("sync", dout["v2T"][k * 128:(k + 1) * 128, xa:xa + HCH],
                  OUTB[:, k * HCH:(k + 1) * HCH])
    return p.finalize()


def host_l4(inputs, state):
    nc = get_nc("l4", build_l4)

    def rot_cols(w):
        wr = np.asarray(w, np.float32).copy()
        wr2 = wr.reshape(wr.shape[0], 16, 2, 32)
        return np.ascontiguousarray(wr2[:, :, ::-1].reshape(wr.shape))

    inv_freq = 1.0 / 10000.0 ** (np.arange(0, 64, 2, dtype=np.float32) / 64)
    pos = np.arange(S, dtype=np.float32)
    fr = pos[:, None] * inv_freq[None, :]
    cosf = np.cos(fr).T
    sinf = np.sin(fr).T
    wqT = _t(inputs["L1_ffn_wq"])
    wkT = _t(inputs["L1_ffn_wk"])
    com = dict(
        wq=_f(inputs["L1_attn_wq"]), wk=_f(inputs["L1_attn_wk"]),
        wv=_f(inputs["L1_attn_wv"]), projN=_f(inputs["L1_attn_proj"]),
        wqT=wqT, wqTr=rot_cols(wqT), wkT=wkT, wkTr=rot_cols(wkT),
        wvT=_t(inputs["L1_ffn_wv"]),
        g1c=_f(inputs["L1_attn_g1"]).reshape(8, 128).T.copy(),
        b1c=_f(inputs["L1_attn_beta1"]).reshape(8, 128).T.copy(),
        anwc=_f(inputs["L1_ffn_attn_norm_w"]).reshape(8, 128).T.copy(),
    )
    in_maps = []
    for c in range(NCORES):
        b, t0 = divmod(c * TOK, S)
        CO = np.zeros((128, TOK), np.float32)
        SI = np.zeros((128, TOK), np.float32)
        cs = cosf[:, t0:t0 + TOK]
        sn = sinf[:, t0:t0 + TOK]
        for q in range(4):
            CO[q * 32:(q + 1) * 32] = cs
            SI[q * 32:(q + 1) * 32] = (-sn if q % 2 == 0 else sn)
        m = dict(com)
        m["xn1T"] = np.ascontiguousarray(state["xn1T"][:, c * TOK:(c + 1) * TOK])
        m["COSF"] = CO
        m["SINF"] = SI
        in_maps.append(m)
    res = run8(nc, in_maps)
    for nm in ["qpT", "kpT", "vT", "q2rT", "k2rT", "v2T"]:
        state[nm] = np.concatenate(
            [np.asarray(res[c][nm], np.float32) for c in range(NCORES)], 1)
    return state




NQB = 16


def build_l5():
    p = Prog("l5")
    for name in ["qp", "kp", "vp", "q2", "k2", "v2"]:
        p.inp(name, [256, S], BF16)
    for v in range(4):
        p.inp("MBA%d" % v, [128, 512], F32)
        p.inp("MK%d" % v, [128, 512], F32)
    p.inp("TRIU", [128, 128], F32)
    p.inp("idf32", [128, 128], F32)
    p.out("ap_o", [256, S], BF16)
    p.out("ao_o", [256, S], BF16)
    din, dout = p.din, p.dout
    scr = p.nc.dram_tensor("scr", [8, 128], F32).ap()

    ones_row = p.sb("ones_row", [1, 128], F32)
    ones_bf = p.sb("ones_bf", [128, 1], BF16)
    p.memset(ones_row[:, :], 1.0)
    p.memset(ones_bf[:, :], 1.0)
    idf = p.sb("idf", [128, 128], F32)
    p.dma("sync", idf[:, :], din["idf32"])
    idb = p.sb("idb", [128, 128], BF16)
    p.copy(idb[:, :], idf[:, :])
    MBA = [p.sb("sMBA%d" % v, [128, 512], F32) for v in range(4)]
    MK = [p.sb("sMK%d" % v, [128, 512], F32) for v in range(4)]
    for v in range(4):
        p.dma("sync", MBA[v][:, :], din["MBA%d" % v])
        p.dma("sync", MK[v][:, :], din["MK%d" % v])
    TRI = p.sb("TRI", [128, 128], F32)
    p.dma("sync", TRI[:, :], din["TRIU"])

    QH = p.sb("QH", [64, S], BF16)
    KH = p.sb("KH", [64, S], BF16)
    VH = p.sb("VH", [64, S], BF16)
    VTOK = p.sb("VTOK", [128, 64 * 64], BF16)
    KTOK = p.sb("KTOK", [128, 64 * 64], BF16)
    STb = p.sb("STb", [128, 128], BF16)
    NUMF = p.sb("NUMF", [64, 128], F32)
    KVf = p.sb("KVf", [64, 64], F32)
    KVb = p.sb("KVb", [64, 64], BF16)
    KSf = p.sb("KSf", [64, 1], F32)
    KSb = p.sb("KSb", [64, 1], BF16)
    DRow = p.sb("DRow", [1, 128], F32)
    OST = p.sb("OST", [64, S], BF16)
    MXC = p.sb("MXC", [128, 4], F32)
    MXT = p.sb("MXT", [4, 128], F32)
    MR = p.sb("MR", [1, 512], F32)
    MBs = p.sb("MBs", [128, 512], F32)
    ET = p.sb("ET", [128, 512], BF16)
    T1 = p.sb("T1", [128, 512], F32)
    T2 = p.sb("T2", [128, 512], F32)
    OF = p.sb("OF", [64, 512], F32)
    DR = p.sb("DR", [1, 512], F32)
    PP = [p.ps("PP%d" % i) for i in range(8)]
    PPB5 = PP[5][:, 0:256].bitcast(BF16)

    for h in range(4):
        r0 = h * 64
        p.dma("sync", QH[:, :], din["qp"][r0:r0 + 64, :])
        p.dma("sync", KH[:, :], din["kp"][r0:r0 + 64, :])
        p.dma("sync", VH[:, :], din["vp"][r0:r0 + 64, :])
        for c in range(64):
            cs = slice(c * 128, (c + 1) * 128)
            p.transpose(PPB5[0:128, 0:64], VH[:, cs], idb[0:64, 0:64])
            p.copy(VTOK[:, c * 64:(c + 1) * 64], PPB5[0:128, 0:64])
            p.transpose(PPB5[0:128, 64:128], KH[:, cs], idb[0:64, 0:64])
            p.copy(KTOK[:, c * 64:(c + 1) * 64], PPB5[0:128, 64:128])
        p.memset(KVf[:, :], 0.0)
        p.memset(KVb[:, :], 0.0)
        p.memset(KSf[:, :], 0.0)
        p.memset(KSb[:, :], 0.0)
        for c in range(64):
            cs = slice(c * 128, (c + 1) * 128)
            p.mm(PP[0][:, 0:128], KH[:, cs], QH[:, cs])
            p.tt(STb[:, :], PP[0][:, 0:128], TRI[:, :], ALU.mult)

            def emitnum(t, cs=cs, c=c):
                t.matmul(PP[1][0:64, 0:128], VTOK[:, c * 64:(c + 1) * 64],
                         STb[:, :], start=True, stop=False)
                return t.matmul(PP[1][0:64, 0:128], KVb[:, :],
                                QH[:, cs], start=False, stop=True)
            p.rec("tensor", emitnum, [VTOK, STb, KVb, QH], [PP[1]])

            def emitden(t, cs=cs):
                t.matmul(PP[2][0:1, 0:128], ones_bf[:, 0:1],
                         STb[:, :], start=True, stop=False)
                return t.matmul(PP[2][0:1, 0:128], KSb[:, 0:1],
                                QH[:, cs], start=False, stop=True)
            p.rec("tensor", emitden, [ones_bf, STb, KSb, QH], [PP[2]])
            p.copy(NUMF[:, :], PP[1][0:64, 0:128])
            p.ts(DRow[0:1, :], PP[2][0:1, 0:128], 1e-6, None, ALU.add)
            p.recip(DRow[0:1, :], DRow[0:1, :])
            p.mm(PP[3][0:64, 0:128], ones_row[0:1, 0:64], DRow[0:1, :])
            p.tt(OST[:, cs], NUMF[:, :], PP[3][0:64, 0:128], ALU.mult)
            p.mm(PP[4][0:64, 0:64], KTOK[:, c * 64:(c + 1) * 64],
                 VTOK[:, c * 64:(c + 1) * 64])
            p.tt(KVf[:, :], KVf[:, :], PP[4][0:64, 0:64], ALU.add)
            p.copy(KVb[:, :], KVf[:, :])
            p.reduce(T2[0:64, 0:1], KH[:, cs], ALU.add)
            p.tt(KSf[:, 0:1], KSf[:, 0:1], T2[0:64, 0:1], ALU.add)
            p.copy(KSb[:, 0:1], KSf[:, 0:1])
        p.dma("sync", dout["ap_o"][r0:r0 + 64, :], OST[:, :])

    for h in range(4):
        r0 = h * 64
        p.dma("sync", QH[:, :], din["q2"][r0:r0 + 64, :])
        p.dma("sync", KH[:, :], din["k2"][r0:r0 + 64, :])
        p.dma("sync", VH[:, :], din["v2"][r0:r0 + 64, :])
        p.ts(QH[:, :], QH[:, :], 0.125, None, ALU.mult)
        for c in range(64):
            p.transpose(PPB5[0:128, 0:64], VH[:, c * 128:(c + 1) * 128],
                        idb[0:64, 0:64])
            p.copy(VTOK[:, c * 64:(c + 1) * 64], PPB5[0:128, 0:64])
        for qb in range(NQB):
            Q0b = qb * 512
            p.memset(MXC[:, :], -3e38)
            for qt in range(4):
                Q0 = Q0b + qt * 128
                for j in range(Q0 // 512 + 1):
                    ps = PP[(j % 2)]
                    p.mm(ps[:, 0:512], QH[:, Q0:Q0 + 128],
                         KH[:, j * 512:(j + 1) * 512])
                    v_ = (Q0 - j * 512) // 128
                    if 0 <= v_ <= 3:
                        p.tt(T1[:, 0:512], ps[:, 0:512], MBA[v_][:, :],
                             ALU.add)
                        p.reduce(T1[:, 0:1], T1[:, 0:512], ALU.max)
                    else:
                        p.reduce(T1[:, 0:1], ps[:, 0:512], ALU.max)
                    p.tt(MXC[:, qt:qt + 1], MXC[:, qt:qt + 1],
                         T1[:, 0:1], ALU.max)
            p.transpose(PP[2][0:4, 0:128], MXC[:, 0:4], idf[:, :])
            p.copy(MXT[0:4, :], PP[2][0:4, 0:128])
            p.dma("sync", scr[0:4, :], MXT[0:4, :])
            p.dma("sync", MR[0:1, 0:512],
                  scr[0:4, :].rearrange("a b -> (a b)")[None, :])
            p.ts(MR[0:1, :], MR[0:1, :], -1.0, None, ALU.mult)
            p.mm(PP[3][:, 0:512], ones_row[0:1, :], MR[0:1, 0:512])
            p.copy(MBs[:, :], PP[3][:, 0:512])
            nk = (Q0b + 512) // 128
            first = True
            for ck in range(nk):
                K0 = ck * 128
                ps = PP[(ck % 2)]
                p.mm(ps[:, 0:512], KH[:, K0:K0 + 128], QH[:, Q0b:Q0b + 512])
                p.tt(T1[:, 0:512], ps[:, 0:512], MBs[:, :], ALU.add)
                p.act(ET[:, :], T1[:, 0:512], AF.Exp)
                u = (K0 - Q0b) // 128
                if 0 <= u <= 3:
                    p.tt(ET[:, :], ET[:, :], MK[u][:, :], ALU.mult)
                p.mm_more(PP[6][0:64, 0:512], VTOK[:, ck * 64:(ck + 1) * 64],
                          ET[:, :], start=first, stop=(ck == nk - 1))
                p.mm_more(PP[7][0:1, 0:512], ones_bf[:, 0:1], ET[:, :],
                          start=first, stop=(ck == nk - 1))
                first = False
            p.recip(DR[0:1, :], PP[7][0:1, 0:512])
            p.mm(PP[3][0:64, 0:512], ones_row[0:1, 0:64], DR[0:1, 0:512])
            p.copy(OF[0:64, :], PP[6][0:64, 0:512])
            p.tt(OST[:, Q0b:Q0b + 512], OF[0:64, :], PP[3][0:64, 0:512],
                 ALU.mult)
        p.dma("sync", dout["ao_o"][r0:r0 + 64, :], OST[:, :])
    return p.finalize()


def host_l5(inputs, state):
    nc = get_nc("l5", build_l5)
    MBAs, MKs = [], []
    for v in range(4):
        qq = np.arange(128)[:, None]
        kk = np.arange(512)[None, :]
        MBAs.append(np.where(kk <= v * 128 + qq, 0.0, -1e30).astype(np.float32))
        MKs.append((v * 128 + np.arange(128)[:, None]
                    <= np.arange(512)[None, :]).astype(np.float32))
    tri = (np.arange(128)[:, None] <= np.arange(128)[None, :]) \
        .astype(np.float32)
    com = {"TRIU": tri, "idf32": np.eye(128, dtype=np.float32)}
    for v in range(4):
        com["MBA%d" % v] = MBAs[v]
        com["MK%d" % v] = MKs[v]
    in_maps = []
    for c in range(NCORES):
        b, hg = c // 4, c % 4
        rows = slice(hg * 256, hg * 256 + 256)
        cols = slice(b * S, (b + 1) * S)
        m = dict(com)
        for nm, key in [("qp", "qpT"), ("kp", "kpT"), ("vp", "vT"),
                        ("q2", "q2rT"), ("k2", "k2rT"), ("v2", "v2T")]:
            m[nm] = np.ascontiguousarray(state[key][rows, cols]).astype(BF)
        in_maps.append(m)
    res = run8(nc, in_maps)
    ap = np.zeros((NB, D, S), np.float32)
    ao = np.zeros((NB, D, S), np.float32)
    for c in range(NCORES):
        b, hg = c // 4, c % 4
        ap[b, hg * 256:hg * 256 + 256] = np.asarray(res[c]["ap_o"], np.float32)
        ao[b, hg * 256:hg * 256 + 256] = np.asarray(res[c]["ao_o"], np.float32)
    state["apT"] = np.ascontiguousarray(np.concatenate([ap[0], ap[1]], 1))
    state["aoT"] = np.ascontiguousarray(np.concatenate([ao[0], ao[1]], 1))
    return state




def emit_top2(p, PRps, MIX, RM, RL2, RM2, RV, WRT, RRC):
    p.copy(RM[:, 0:8], PRps)
    p.reduce(RV[:, 0:1], RM[:, 0:8], ALU.max)
    p.ts(RL2[:, 0:8], RM[:, 0:8], RV[:, 0:1], None, ALU.is_equal)
    p.stt(RM2[:, 0:8], RL2[:, 0:8], -1e30, RM[:, 0:8], ALU.mult, ALU.add)
    p.reduce(RV[:, 1:2], RM2[:, 0:8], ALU.max)
    p.ts(WRT[:, 0:8], RM2[:, 0:8], RV[:, 1:2], None, ALU.is_equal)
    p.tt(RV[:, 2:3], RV[:, 1:2], RV[:, 0:1], ALU.subtract)
    p.act(RV[:, 3:4], RV[:, 2:3], AF.Exp)
    p.ts(RV[:, 4:5], RV[:, 3:4], 1.0, None, ALU.add)
    p.recip(RV[:, 6:7], RV[:, 4:5])
    p.tt(RV[:, 5:6], RV[:, 3:4], RV[:, 6:7], ALU.mult)
    p.ts(RL2[:, 0:8], RL2[:, 0:8], RV[:, 6:7], None, ALU.mult)
    p.ts(WRT[:, 0:8], WRT[:, 0:8], RV[:, 5:6], None, ALU.mult)
    p.tt(WRT[:, 0:8], WRT[:, 0:8], RL2[:, 0:8], ALU.add)



def build_l6a():
    p = Prog("l6a")
    for name, shape, dt in [
        ("xn1T", [D, TOK], F32), ("x1T", [D, TOK], F32),
        ("apT", [D, TOK], BF16), ("aoT", [D, TOK], BF16),
        ("wo1", [D, D], F32), ("w1f", [D, 4096], F32),
        ("w2f", [4096, D], F32),
        ("b1fc", [128, 32], F32), ("b2fc", [128, 8], F32),
        ("g2c", [128, 8], F32), ("be2c", [128, 8], F32),
        ("woT2", [D, D], F32), ("mnwc", [128, 8], F32),
        ("gwT", [D, 8], F32),
        ("w1L2", [8 * 32 * 128, 8 * 128], F32),
        ("w2L2", [8 * 8 * 128, 32 * 128], F32),
        ("b1c2", [128, 8 * 32], F32), ("b2nat", [8, D], F32),
        ("idf32", [128, 128], F32),
    ]:
        p.inp(name, shape, dt)
    p.out("x2T", [D, TOK], F32)
    p.out("xn2T", [D, TOK], F32)
    din, dout = p.din, p.dout
    scr = p.nc.dram_tensor("scr2", [8, TOK], F32).ap()

    ones_col = p.sb("ones_col", [128, 1], F32)
    ones_row = p.sb("ones_row", [1, 128], F32)
    p.memset(ones_col[:, :], 1.0)
    p.memset(ones_row[:, :], 1.0)
    idf = p.sb("idf", [128, 128], F32)
    p.dma("sync", idf[:, :], din["idf32"])
    HCH = 1024
    BUF1 = p.sb("BUF1", [128, 8 * HCH], F32)
    XATT = p.sb("XATT", [128, 8 * HCH], F32)
    HRES = p.sb("HRES", [128, 8 * HCH], F32)
    ABuf = p.sb("ABuf", [128, 8 * HCH], BF16)
    HBb = p.sb("HBb", [128, 8 * HCH], BF16)
    HID = p.sb("HID", [128, 32 * 512], BF16)
    WA = p.sb("WA", [128, 8 * D], BF16)
    W1B0 = p.sb("W1B0", [128, 8 * 128], BF16)
    W2B0 = p.sb("W2B0", [128, 16 * 128], BF16)
    W1B_ = [W1B0, W2B0[:, 0:1024]]
    W2B_ = [W2B0] * 2
    B2S = p.sb("B2S", [8, 128], F32)
    WRB = p.sb("WRB", [128, 512], F32)
    T1 = p.sb("T1", [128, 512], F32)
    GW = WRB[:, 384:448]
    RM = T1[:, 200:208]
    RL2 = T1[:, 208:216]
    RM2 = T1[:, 216:224]
    RV = T1[:, 224:234]
    WRT = T1[:, 234:242]
    MIX = RM
    RRC = RM
    sq0_ = p.sb("sq0", [128, 512], F32)
    sq = [sq0_, sq0_]
    row_sum = p.sb("row_sum", [1, HCH], F32)
    row_ssq = p.sb("row_ssq", [1, HCH], F32)
    row_rs = p.sb("row_rs", [1, HCH], F32)
    row_mr = p.sb("row_mr", [1, HCH], F32)
    gc = {}
    for nm in ["b1fc", "b2fc", "g2c", "be2c", "mnwc", "b1c2"]:
        shp = {"b1fc": [128, 32], "b1c2": [128, 8 * 32]}.get(nm, [128, 8])
        gc[nm] = p.sb(nm + "_s", shp, F32)
        p.dma("sync", gc[nm][:, :], din[nm])

    psA = p.ps("psA", [1, 512])
    psB = p.ps("psB", [1, 512])
    P0 = p.ps("P0")
    P1 = p.ps("P1")
    P2 = p.ps("P2")
    P3 = p.ps("P3")
    PR = p.ps("PR", [128, 128])

    def loadW(nm):
        for k in range(8):
            p.dma("gpsimd", WA[:, k * D:(k + 1) * D],
                  din[nm][k * 128:(k + 1) * 128, :])

    for hf in range(2):
        xa = hf * HCH
        for k in range(8):
            p.dma("sync", BUF1[:, k * HCH:(k + 1) * HCH],
                  din["xn1T"][k * 128:(k + 1) * 128, xa:xa + HCH])
            p.dma("sync", ABuf[:, k * HCH:(k + 1) * HCH],
                  din["apT"][k * 128:(k + 1) * 128, xa:xa + HCH])
        loadW("wo1")
        proj(p, WA, ABuf, XATT, [P0, P1], 8, 8, HCH)
        p.tt(XATT[:, :], XATT[:, :], BUF1[:, :], ALU.add)
        feat_stats(p, XATT, 8, HCH, ones_col, sq, psA, psB, row_sum, row_ssq)
        ln_rows(p, HCH, row_sum, row_ssq, row_rs, row_mr)
        apply_ln(p, XATT, HRES, 8, HCH, ones_row, row_rs, row_mr, P0, P1,
                 gain_cols=gc["g2c"], bias_cols=gc["be2c"])
        p.copy(HBb[:, :], HRES[:, :], eng="scalar")
        for tb in range(HCH // 512):
            ta = tb * 512
            for o in range(32):
                W1B = W1B_[o % 2]
                p.dma("gpsimd", W1B[:, :].rearrange("p (k n) -> p k n", n=128),
                      din["w1f"][0:D, o * 128:(o + 1) * 128]
                      .rearrange("(k p) n -> p k n", p=128))
                ps = [P0, P1][o % 2]
                p.mm(ps[:, 0:512],
                     [W1B[:, k * 128:(k + 1) * 128] for k in range(8)],
                     [HBb[:, k * HCH + ta: k * HCH + ta + 512]
                      for k in range(8)])
                p.act(HID[:, o * 512:(o + 1) * 512], ps[:, 0:512], AF.Gelu,
                      bias=gc["b1fc"][:, o:o + 1])
            for d in range(8):
                ps = [P2, P3][d % 2]
                for half in range(2):
                    hb = half * 16
                    W2B = W2B_[half]
                    p.dma("gpsimd",
                          W2B[:, :].rearrange("p (k n) -> p k n", n=128),
                          din["w2f"][hb * 128:(hb + 16) * 128,
                                     d * 128:(d + 1) * 128]
                          .rearrange("(k p) n -> p k n", p=128))
                    p.mm_more(ps[:, 0:512],
                              [W2B[:, h * 128:(h + 1) * 128]
                               for h in range(16)],
                              [HID[:, (hb + h) * 512:(hb + h + 1) * 512]
                               for h in range(16)],
                              start=(half == 0), stop=(half == 1))
                p.act(T1[:, 0:512], ps[:, 0:512], AF.Identity,
                      bias=gc["b2fc"][:, d:d + 1])
                p.tt(XATT[:, d * HCH + ta: d * HCH + ta + 512],
                     XATT[:, d * HCH + ta: d * HCH + ta + 512],
                     T1[:, 0:512], ALU.add)
        for k in range(8):
            p.dma("sync", ABuf[:, k * HCH:(k + 1) * HCH],
                  din["aoT"][k * 128:(k + 1) * 128, xa:xa + HCH])
        loadW("woT2")
        proj(p, WA, ABuf, HRES, [P0, P1], 8, 8, HCH)
        p.tt(HRES[:, :], HRES[:, :], BUF1[:, :], ALU.add)
        feat_stats(p, HRES, 8, HCH, ones_col, sq, psA, psB, row_sum, row_ssq)
        ln_rows(p, HCH, row_sum, row_ssq, row_rs, row_mr, eps=1e-6, rms=True)
        apply_ln(p, HRES, BUF1, 8, HCH, ones_row, row_rs, None, P0, P1,
                 gain_cols=gc["mnwc"])
        p.copy(HBb[:, :], BUF1[:, :], eng="scalar")
        for k in range(8):
            p.dma("sync", GW[:, k * 8:(k + 1) * 8],
                  din["gwT"][k * 128:(k + 1) * 128, :])
        for i in range(HCH // 128):
            tbase = i * 128
            p.mm(PR[:, 0:8],
                 [BUF1[:, k * HCH + tbase: k * HCH + tbase + 128]
                  for k in range(8)],
                 [GW[:, k * 8:(k + 1) * 8] for k in range(8)])
            emit_top2(p, PR[:, 0:8], MIX, RM, RL2, RM2, RV, WRT, RRC)
            p.transpose(PR[0:8, 0:128], WRT[:, 0:8], idf[:, :])
            p.copy(T1[0:8, 0:128], PR[0:8, 0:128])
            p.dma("sync", scr[0:8, xa + tbase: xa + tbase + 128],
                  T1[0:8, 0:128])
        for e in range(8):
            p.dma("sync", row_mr[0:1, 0:HCH], scr[e:e + 1, xa:xa + HCH])
            for tb in range(HCH // 512):
                ta = tb * 512
                p.mm(P2[:, 0:512], ones_row[0:1, :], row_mr[0:1, ta:ta + 512])
                p.copy(WRB[:, 0:512], P2[:, 0:512])
                for h in range(32):
                    r0_ = (e * 32 + h) * 128
                    W1B = W1B_[h % 2]
                    p.dma("gpsimd", W1B[:, :], din["w1L2"][r0_:r0_ + 128, :])
                    ps = [P0, P1][h % 2]
                    p.mm(ps[:, 0:512],
                         [W1B[:, k * 128:(k + 1) * 128] for k in range(8)],
                         [HBb[:, k * HCH + ta: k * HCH + ta + 512]
                          for k in range(8)])
                    p.act(T1[:, 0:512], ps[:, 0:512], AF.Gelu,
                          bias=gc["b1c2"][:, e * 32 + h: e * 32 + h + 1])
                    p.tt(HID[:, h * 512:(h + 1) * 512], T1[:, 0:512],
                         WRB[:, 0:512], ALU.mult)
                for d in range(8):
                    r0_ = (e * 8 + d) * 128
                    ps = [P2, P3][d % 2]
                    for half in range(2):
                        hb = half * 16
                        W2B = W2B_[half]
                        p.dma("gpsimd", W2B[:, :],
                              din["w2L2"][r0_:r0_ + 128,
                                          hb * 128:(hb + 16) * 128])
                        p.mm_more(ps[:, 0:512],
                                  [W2B[:, h * 128:(h + 1) * 128]
                                   for h in range(16)],
                                  [HID[:, (hb + h) * 512:(hb + h + 1) * 512]
                                   for h in range(16)],
                                  start=(half == 0), stop=(half == 1))
                    p.tt(HRES[:, d * HCH + ta: d * HCH + ta + 512],
                         HRES[:, d * HCH + ta: d * HCH + ta + 512],
                         ps[:, 0:512], ALU.add)
        for d in range(8):
            p.dma("sync", B2S[0:8, 0:128],
                  din["b2nat"][0:8, d * 128:(d + 1) * 128])
            for tb in range(HCH // 512):
                ta = tb * 512
                ps = [P0, P1][tb % 2]
                def eb2(t, ps=ps):
                    return t.matmul(ps[:, 0:512], B2S[0:8, 0:128],
                                    T1[0:8, 0:512], start=True,
                                    stop=True)
                p.dma("sync", T1[0:8, 0:512],
                      scr[0:8, xa + ta: xa + ta + 512])
                p.rec("tensor", eb2, [B2S, T1], [ps])
                p.tt(HRES[:, d * HCH + ta: d * HCH + ta + 512],
                     HRES[:, d * HCH + ta: d * HCH + ta + 512],
                     ps[:, 0:512], ALU.add)
        p.act(HRES[:, :], HRES[:, :], AF.Silu)
        p.tt(XATT[:, :], XATT[:, :], HRES[:, :], ALU.mult)
        for k in range(8):
            p.dma("sync", BUF1[:, k * HCH:(k + 1) * HCH],
                  din["x1T"][k * 128:(k + 1) * 128, xa:xa + HCH])
        p.tt(BUF1[:, :], BUF1[:, :], XATT[:, :], ALU.add)
        for k in range(8):
            p.dma("sync", dout["x2T"][k * 128:(k + 1) * 128, xa:xa + HCH],
                  BUF1[:, k * HCH:(k + 1) * HCH])
        feat_stats(p, BUF1, 8, HCH, ones_col, sq, psA, psB, row_sum, row_ssq)
        ln_rows(p, HCH, row_sum, row_ssq, row_rs, row_mr)
        apply_ln(p, BUF1, BUF1, 8, HCH, ones_row, row_rs, row_mr, P0, P1)
        for k in range(8):
            p.dma("sync", dout["xn2T"][k * 128:(k + 1) * 128, xa:xa + HCH],
                  BUF1[:, k * HCH:(k + 1) * HCH])
    return p.finalize()


def host_l6a(inputs, state):
    nc = get_nc("l6a", build_l6a)
    w1 = np.asarray(inputs["L1_ffn_w1"], np.float32)
    w2 = np.asarray(inputs["L1_ffn_w2"], np.float32)
    w1T = w1.transpose(0, 2, 1)
    w1L = np.zeros((8, 32, 128, 8 * 128), np.float32)
    w1r = w1T.reshape(8, 8, 128, 32, 128)
    for h in range(32):
        for k in range(8):
            w1L[:, h, :, k * 128:(k + 1) * 128] = w1r[:, k, :, h]
    w2T = w2.transpose(0, 2, 1)
    w2L = np.zeros((8, 8, 128, 32 * 128), np.float32)
    w2r = w2T.reshape(8, 32, 128, 8, 128)
    for d in range(8):
        for h in range(32):
            w2L[:, d, :, h * 128:(h + 1) * 128] = w2r[:, h, :, d]
    b1 = np.asarray(inputs["L1_ffn_b1"], np.float32)
    b1c2 = np.zeros((128, 8 * 32), np.float32)
    for e in range(8):
        b1c2[:, e * 32:(e + 1) * 32] = b1[e].reshape(32, 128).T
    com = dict(
        wo1=_f(inputs["L1_attn_wo"]), w1f=_f(inputs["L1_attn_w1"]),
        w2f=_f(inputs["L1_attn_w2"]),
        b1fc=_f(inputs["L1_attn_b1"]).reshape(32, 128).T.copy(),
        b2fc=_f(inputs["L1_attn_b2"]).reshape(8, 128).T.copy(),
        g2c=_f(inputs["L1_attn_g2"]).reshape(8, 128).T.copy(),
        be2c=_f(inputs["L1_attn_beta2"]).reshape(8, 128).T.copy(),
        woT2=_t(inputs["L1_ffn_wo"]),
        mnwc=_f(inputs["L1_ffn_moe_norm_w"]).reshape(8, 128).T.copy(),
        gwT=_t(inputs["L1_ffn_gate_w"]),
        w1L2=np.ascontiguousarray(w1L.reshape(8 * 32 * 128, 8 * 128)),
        w2L2=np.ascontiguousarray(w2L.reshape(8 * 8 * 128, 32 * 128)),
        b1c2=b1c2, b2nat=_f(inputs["L1_ffn_b2"]),
        idf32=np.eye(128, dtype=np.float32),
    )
    in_maps = []
    for c in range(NCORES):
        m = dict(com)
        for nm, key in [("xn1T", "xn1T"), ("x1T", "x1T"),
                        ("apT", "apT"), ("aoT", "aoT")]:
            arr = state[key][:, c * TOK:(c + 1) * TOK]
            m[nm] = np.ascontiguousarray(arr).astype(
                BF if nm in ("apT", "aoT") else np.float32)
        in_maps.append(m)
    res = run8(nc, in_maps)
    state["x2T"] = np.concatenate([res[c]["x2T"] for c in range(NCORES)], 1)
    state["xn2T"] = np.concatenate([res[c]["xn2T"] for c in range(NCORES)], 1)
    return state




def build_l6b():
    import os
    SKIP = os.environ.get("L6B_SKIP", "")
    p = Prog("l6b")
    for name, shape, dt in [
        ("xn2T", [D, TOK], F32),
        ("qwT", [D, D], F32), ("kwT", [D, D], F32), ("vwT", [D, D], F32),
        ("qbc", [128, 8], F32), ("kbc", [128, 8], F32), ("vbc", [128, 8], F32),
        ("qgc", [128, 8], F32), ("qBc", [128, 8], F32),
        ("kgc", [128, 8], F32), ("kBc", [128, 8], F32),
        ("vgc", [128, 8], F32), ("vBc", [128, 8], F32),
        ("idf32", [128, 128], F32),
    ]:
        p.inp(name, shape, dt)
    p.out("q3T", [D, TOK], BF16)
    p.out("k3T", [D, TOK], BF16)
    p.out("v3T", [D, TOK], BF16)
    p.out("xqT", [D, TOK], BF16)
    din, dout = p.din, p.dout

    ones_col = p.sb("ones_col", [128, 1], F32)
    ones_row = p.sb("ones_row", [1, 128], F32)
    p.memset(ones_col[:, :], 1.0)
    p.memset(ones_row[:, :], 1.0)
    idf = p.sb("idf", [128, 128], F32)
    p.dma("sync", idf[:, :], din["idf32"])
    idb = p.sb("idb", [128, 128], BF16)
    p.copy(idb[:, :], idf[:, :])
    HCH = 1024
    XN2 = p.sb("XN2", [128, 8 * HCH], F32)
    XNN = p.sb("XNN", [128, 8 * HCH], F32)
    QF = p.sb("QF", [128, 8 * HCH], F32)
    NB_ = p.sb("NBB", [128, 8 * HCH], BF16)
    WA = p.sb("WA", [128, 8 * D], BF16)
    T1 = p.sb("T1", [128, 512], F32)
    QT = p.sb("QT", [128, 1024], F32)
    QS = p.sb("QSc", [128, 64], F32)
    QR = p.sb("QRc", [128, 64], F32)
    QC1 = p.sb("QC1c", [128, 1024], F32)
    QC2 = p.sb("QC2c", [128, 1024], F32)
    QO = p.sb("QOc", [128, 1024], BF16)
    sq0_ = p.sb("sq0", [128, 512], F32)
    sq = [sq0_, sq0_]
    row_sum = p.sb("row_sum", [1, HCH], F32)
    row_ssq = p.sb("row_ssq", [1, HCH], F32)
    row_rs = p.sb("row_rs", [1, HCH], F32)
    row_mr = p.sb("row_mr", [1, HCH], F32)
    gc = {}
    for nm in ["qbc", "kbc", "vbc", "qgc", "qBc", "kgc", "kBc", "vgc", "vBc"]:
        gc[nm] = p.sb(nm + "_s", [128, 8], F32)
        p.dma("sync", gc[nm][:, :], din[nm])
    psA = p.ps("psA", [1, 512])
    psB = p.ps("psB", [1, 512])
    P0 = p.ps("P0")
    P1 = p.ps("P1")
    P2 = p.ps("P2")
    P3 = p.ps("P3")

    for hf in range(2):
        xa = hf * HCH
        for k in range(8):
            p.dma("sync", XN2[:, k * HCH:(k + 1) * HCH],
                  din["xn2T"][k * 128:(k + 1) * 128, xa:xa + HCH])
        feat_stats(p, XN2, 8, HCH, ones_col, sq, psA, psB, row_sum, row_ssq)
        ln_rows(p, HCH, row_sum, row_ssq, row_rs, row_mr)
        apply_ln(p, XN2, XNN, 8, HCH, ones_row, row_rs, row_mr, P0, P1)

        plist = [] if "proj" in SKIP else [
                ("qwT", "qbc", "qgc", "qBc", "q3T", True),
                ("kwT", "kbc", "kgc", "kBc", "k3T", True),
                ("vwT", "vbc", "vgc", "vBc", "v3T", False)]
        for (wn, bn, gn, Bn, on, do_norm) in plist:
            for k in range(8):
                sl = slice(k * HCH, (k + 1) * HCH)
                p.ts(QF[:, sl], XNN[:, sl], gc[gn][:, k:k + 1], None, ALU.mult)
                p.ts(QF[:, sl], QF[:, sl], gc[Bn][:, k:k + 1], None, ALU.add)
            p.copy(NB_[:, :], QF[:, :], eng="scalar")
            loadW(p, WA, din[wn])
            proj(p, WA, NB_, QF, [P0, P1], 8, 8, HCH, bias=gc[bn])
            if do_norm:
                for (a, b) in col_splits(HCH, 512):
                    w = b - a
                    for k in range(8):
                        p.act(sq[0][:, 0:w], QF[:, k * HCH + a: k * HCH + b],
                              AF.Square)
                        p.mm_more(psA[0:1, 0:w], ones_col[:, 0:1],
                                  sq[0][:, 0:w], start=(k == 0),
                                  stop=(k == 7))
                    p.copy(row_sum[0:1, a:b], psA[0:1, 0:w])
                p.act(row_sum[0:1, 0:HCH], row_sum[0:1, 0:HCH], AF.Sqrt)
                p.ts(row_sum[0:1, 0:HCH], row_sum[0:1, 0:HCH], 1e-6, None,
                     ALU.max)
                p.recip(row_rs[0:1, 0:HCH], row_sum[0:1, 0:HCH])
                apply_ln(p, QF, QF, 8, HCH, ones_row, row_rs, None, P2, P3)
            p.copy(NB_[:, :], QF[:, :], eng="scalar")
            for k in range(8):
                p.dma("sync", dout[on][k * 128:(k + 1) * 128, xa:xa + HCH],
                      NB_[:, k * HCH:(k + 1) * HCH])

        for tt_ in range([] if False else (0 if "quant" in SKIP else HCH // 128)):
            tb = tt_ * 128
            for k in range(8):
                p.transpose(P2[:, 0:128], XN2[:, k * HCH + tb: k * HCH + tb + 128],
                            idf[:, :])
                p.copy(QT[:, k * 128:(k + 1) * 128], P2[:, 0:128])
            if "fp4" not in SKIP:
                fp4_qd_generic(p, QT, QS, QR, QC1, QC2, None, 128, 1024,
                               out_f32=True)
            else:
                p.copy(QC2[:, :], QT[:, :])
            for k in range(8):
                if "back" in SKIP:
                    continue
                p.transpose(P3[:, 0:128], QC2[:, k * 128:(k + 1) * 128],
                            idf[:, :])
                p.copy(NB_[:, k * HCH + tb: k * HCH + tb + 128],
                       P3[:, 0:128])
        for k in range(8):
            p.dma("sync", dout["xqT"][k * 128:(k + 1) * 128, xa:xa + HCH],
                  NB_[:, k * HCH:(k + 1) * HCH])
    return p.finalize()


def loadW(p, WA, dsrc):
    for k in range(8):
        p.dma("gpsimd", WA[:, k * D:(k + 1) * D], dsrc[k * 128:(k + 1) * 128, :])


def fp4_qd_generic(p, A, Srec, R, C1, C2, O, rows, colw, out_f32=False):
    nblk = colw // 16
    rs, cw, nb = slice(0, rows), slice(0, colw), slice(0, nblk)
    p.reduce(Srec[rs, nb], A[rs, cw].rearrange("p (g b) -> p g b", b=16),
             ALU.max, absval=True)
    p.ts(Srec[rs, nb], Srec[rs, nb], 1.0 / 6.0, 1e-12, ALU.mult, ALU.max)
    p.recip(R[rs, nb], Srec[rs, nb])
    p.act(C1[rs, cw], A[rs, cw], AF.Abs)
    rb = R[rs, nb].rearrange("p (g one) -> p g one", one=1) \
        .broadcast_to([rows, nblk, 16])
    p.tt(C1[rs, cw].rearrange("p (g b) -> p g b", b=16),
         C1[rs, cw].rearrange("p (g b) -> p g b", b=16), rb, ALU.mult)
    p.ts(C2[rs, cw], C1[rs, cw], 0.25, None, ALU.is_gt)
    for m in (0.75, 1.25, 1.75):
        p.stt(C2[rs, cw], C1[rs, cw], m, C2[rs, cw], ALU.is_gt, ALU.add)
    p.ts(C2[rs, cw], C2[rs, cw], 0.5, None, ALU.mult)
    for m in (2.5, 3.5, 5.0, 5.0):
        p.stt(C2[rs, cw], C1[rs, cw], m, C2[rs, cw], ALU.is_gt, ALU.add)
    sb_ = Srec[rs, nb].rearrange("p (g one) -> p g one", one=1) \
        .broadcast_to([rows, nblk, 16])
    p.tt(C2[rs, cw].rearrange("p (g b) -> p g b", b=16),
         C2[rs, cw].rearrange("p (g b) -> p g b", b=16), sb_, ALU.mult)
    p.act(C1[rs, cw], A[rs, cw], AF.Sign)
    if out_f32:
        p.tt(C2[rs, cw], C2[rs, cw], C1[rs, cw], ALU.mult)
    else:
        p.tt(O[rs, cw], C2[rs, cw], C1[rs, cw], ALU.mult)


def host_l6b(inputs, state):
    nc = get_nc("l6b", build_l6b)

    def col8(v):
        return _f(v).reshape(8, 128).T.copy()

    com = dict(
        qwT=_t(inputs["L2_attn_q_w"]), kwT=_t(inputs["L2_attn_k_w"]),
        vwT=_t(inputs["L2_attn_v_w"]),
        qbc=col8(inputs["L2_attn_q_b"]), kbc=col8(inputs["L2_attn_k_b"]),
        vbc=col8(inputs["L2_attn_v_b"]),
        qgc=col8(inputs["L2_attn_ln_q_g"]), qBc=col8(inputs["L2_attn_ln_q_b"]),
        kgc=col8(inputs["L2_attn_ln_k_g"]), kBc=col8(inputs["L2_attn_ln_k_b"]),
        vgc=col8(inputs["L2_attn_ln_v_g"]), vBc=col8(inputs["L2_attn_ln_v_b"]),
        idf32=np.eye(128, dtype=np.float32),
    )
    in_maps = []
    for c in range(NCORES):
        m = dict(com)
        m["xn2T"] = np.ascontiguousarray(state["xn2T"][:, c * TOK:(c + 1) * TOK])
        in_maps.append(m)
    res = run8(nc, in_maps)
    for nm in ["q3T", "k3T", "v3T", "xqT"]:
        state[nm] = np.concatenate(
            [np.asarray(res[c][nm], np.float32) for c in range(NCORES)], 1)
    return state




def build_l7():
    p = Prog("l7")
    for name in ["q3", "k3", "v3"]:
        p.inp(name, [256, S], BF16)
    p.inp("betap", [1, 16], F32)
    p.inp("selh", [4, 128], F32)
    p.inp("idf32", [128, 128], F32)
    p.out("o3_o", [256, S], BF16)
    din, dout = p.din, p.dout

    ones_row = p.sb("ones_row", [1, 128], F32)
    ones_bf = p.sb("ones_bf", [128, 1], BF16)
    p.memset(ones_row[:, :], 1.0)
    p.memset(ones_bf[:, :], 1.0)
    idf = p.sb("idf", [128, 128], F32)
    p.dma("sync", idf[:, :], din["idf32"])
    idb = p.sb("idb", [128, 128], BF16)
    p.copy(idb[:, :], idf[:, :])
    QH = p.sb("QH", [64, S], BF16)
    KH = p.sb("KH", [64, S], BF16)
    VH = p.sb("VH", [64, S], BF16)
    VTOK = p.sb("VTOK", [128, 64 * 64], BF16)
    ET = p.sb("ET", [128, 512], BF16)
    OF = p.sb("OF", [64, 512], F32)
    DR = p.sb("DR", [1, 512], F32)
    OST = p.sb("OST", [64, S], BF16)
    BR = p.sb("BR", [1, 16], F32)
    BC = p.sb("BC", [64, 1], F32)
    SEL = p.sb("SEL", [4, 128], F32)
    T0 = p.sb("T0", [1, 16], F32)
    PP = [p.ps("PP%d" % i) for i in range(8)]
    PPB5 = PP[5][:, 0:256].bitcast(BF16)

    p.dma("sync", SEL[0:4, :], din["selh"])
    p.dma("sync", BR[0:1, :], din["betap"])
    p.act(T0[0:1, :], BR[0:1, :], AF.Abs)
    p.ts(T0[0:1, :], T0[0:1, :], -1.0, None, ALU.mult)
    p.act(T0[0:1, :], T0[0:1, :], AF.Exp)
    p.act(T0[0:1, :], T0[0:1, :], AF.Ln, bias=1.0)
    p.act(BR[0:1, :], BR[0:1, :], AF.Relu)
    p.tt(BR[0:1, :], BR[0:1, :], T0[0:1, :], ALU.add)
    p.ts(BR[0:1, :], BR[0:1, :], 0.1, None, ALU.max)
    p.ts(BR[0:1, :], BR[0:1, :], 10.0, None, ALU.min)
    p.ts(BR[0:1, :], BR[0:1, :], 0.125, None, ALU.mult)

    for h in range(4):
        r0 = h * 64
        p.dma("sync", QH[:, :], din["q3"][r0:r0 + 64, :])
        p.dma("sync", KH[:, :], din["k3"][r0:r0 + 64, :])
        p.dma("sync", VH[:, :], din["v3"][r0:r0 + 64, :])
        p.rec("tensor",
              lambda t, h=h: t.matmul(PP[3][0:64, 0:1], ones_row[0:1, 0:64],
                                      BR[0:1, h:h + 1], start=True, stop=True),
              [ones_row, BR], [PP[3]])
        p.copy(BC[0:64, 0:1], PP[3][0:64, 0:1])
        p.ts(QH[:, :], QH[:, :], BC[0:64, 0:1], None, ALU.mult)
        for c in range(64):
            p.transpose(PPB5[0:128, 0:64], VH[:, c * 128:(c + 1) * 128],
                        idb[0:64, 0:64])
            p.copy(VTOK[:, c * 64:(c + 1) * 64], PPB5[0:128, 0:64])
        for qb in range(NQB):
            Q0b = qb * 512
            first = True
            for ck in range(64):
                K0 = ck * 128
                ps = PP[(ck % 2)]
                p.mm(ps[:, 0:512], KH[:, K0:K0 + 128], QH[:, Q0b:Q0b + 512])
                p.act(ET[:, :], ps[:, 0:512], AF.Exp)
                p.mm_more(PP[6][0:64, 0:512], VTOK[:, ck * 64:(ck + 1) * 64],
                          ET[:, :], start=first, stop=(ck == 63))
                p.mm_more(PP[7][0:1, 0:512], ones_bf[:, 0:1], ET[:, :],
                          start=first, stop=(ck == 63))
                first = False
            p.recip(DR[0:1, :], PP[7][0:1, 0:512])
            p.mm(PP[3][0:64, 0:512], ones_row[0:1, 0:64], DR[0:1, 0:512])
            p.copy(OF[0:64, :], PP[6][0:64, 0:512])
            p.tt(OST[:, Q0b:Q0b + 512], OF[0:64, :], PP[3][0:64, 0:512],
                 ALU.mult)
        p.dma("sync", dout["o3_o"][r0:r0 + 64, :], OST[:, :])
    return p.finalize()


def host_l7(inputs, state):
    nc = get_nc("l7", build_l7)
    com = dict(idf32=np.eye(128, dtype=np.float32),
               betap=_f(inputs["L2_attn_beta_param"]).reshape(1, 16),
               selh=np.zeros((4, 128), np.float32))
    in_maps = []
    for c in range(NCORES):
        b, hg = c // 4, c % 4
        rows = slice(hg * 256, hg * 256 + 256)
        cols = slice(b * S, (b + 1) * S)
        m = dict(com)
        bp = _f(inputs["L2_attn_beta_param"]).reshape(16)
        m["betap"] = np.ascontiguousarray(
            np.concatenate([bp[hg * 4:hg * 4 + 4],
                            np.zeros(12, np.float32)]).reshape(1, 16))
        for nm, key in [("q3", "q3T"), ("k3", "k3T"), ("v3", "v3T")]:
            m[nm] = np.ascontiguousarray(state[key][rows, cols]).astype(BF)
        in_maps.append(m)
    res = run8(nc, in_maps)
    o3 = np.zeros((NB, D, S), np.float32)
    for c in range(NCORES):
        b, hg = c // 4, c % 4
        o3[b, hg * 256:hg * 256 + 256] = np.asarray(res[c]["o3_o"], np.float32)
    state["o3T"] = np.ascontiguousarray(np.concatenate([o3[0], o3[1]], 1))
    return state



def build_l8():
    p = Prog("l8")
    for name, shape, dt in [
        ("xn2T", [D, TOK], F32), ("x2T", [D, TOK], F32),
        ("o3T", [D, TOK], BF16), ("xqT", [D, TOK], BF16),
        ("owT", [D, D], F32), ("obc", [128, 8], F32),
        ("wgq", [D, 2816], BF16), ("wuq", [D, 2816], BF16),
        ("wdqT", [2816, D], BF16),
        ("wjT", [D, D], F32), ("wjN", [D, D], F32), ("wjvT", [D, D], F32),
        ("gamc", [128, 8], F32), ("betc", [128, 8], F32),
        ("muN", [D, 256], F32), ("scl", [1, 1], F32),
        ("idf32", [128, 128], F32),
    ]:
        p.inp(name, shape, dt)
    p.out("x3T", [D, TOK], F32)
    p.out("xn3T", [D, TOK], F32)
    p.out("sc4", [1, TOK], F32)
    p.out("valT", [D, TOK], BF16)
    p.out("xninT", [D, TOK], BF16)
    p.out("lgT", [TOK, 256], F32)
    din, dout = p.din, p.dout

    ones_col = p.sb("ones_col", [128, 1], F32)
    ones_row = p.sb("ones_row", [1, 128], F32)
    p.memset(ones_col[:, :], 1.0)
    p.memset(ones_row[:, :], 1.0)
    idf = p.sb("idf", [128, 128], F32)
    p.dma("sync", idf[:, :], din["idf32"])
    HCH = 1024
    XN2 = p.sb("XN2", [128, 8 * HCH], F32)
    SS = p.sb("SS", [128, 8 * HCH], F32)
    FF = p.sb("FF", [128, 8 * HCH], F32)
    XB = p.sb("XB", [128, 8 * HCH], BF16)
    HQ = p.sb("HQB", [128, 22 * 512], BF16)
    QS = p.sb("QSc", [128, 176], F32)
    QR = p.sb("QRc", [128, 176], F32)
    QC2 = p.sb("QC2c", [128, 2816], F32)
    WA = p.sb("WA", [128, 8 * D], BF16)
    WG_ = [p.sb("WGc0", [128, 8 * 512], BF16),
           p.sb("WGc1", [128, 8 * 512], BF16)]
    T1 = p.sb("T1", [128, 512], F32)
    MUU = p.sb("MUU", [128, 8 * 256], BF16)
    sq0_ = p.sb("sq0", [128, 512], F32)
    sq = [sq0_, sq0_]
    row_sum = p.sb("row_sum", [1, HCH], F32)
    row_ssq = p.sb("row_ssq", [1, HCH], F32)
    row_rs = p.sb("row_rs", [1, HCH], F32)
    row_mr = p.sb("row_mr", [1, HCH], F32)
    mrow = p.sb("mrow", [1, 256], F32)
    QT = XN2[:, 0:2816]
    XUB = HQ[:, 0:8 * HCH]
    QC1 = XN2[:, 2816 + 256:2816 + 256 + 2816]
    MUF = FF[:, 0:8 * 256]
    gc = {}
    for nm in ["obc", "gamc", "betc"]:
        gc[nm] = p.sb(nm + "_s", [128, 8], F32)
        p.dma("sync", gc[nm][:, :], din[nm])
    scl_s = p.sb("scl_s", [1, 1], F32)
    p.dma("sync", scl_s[:, :], din["scl"])
    psA = p.ps("psA", [1, 512])
    psB = p.ps("psB", [1, 512])
    P0 = p.ps("P0")
    P1 = p.ps("P1")
    P2 = p.ps("P2")
    P3 = p.ps("P3")
    PR = p.ps("PR", [128, 256])

    for k in range(8):
        p.dma("sync", MUF[:, k * 256:(k + 1) * 256],
              din["muN"][k * 128:(k + 1) * 128, :])
    for k in range(8):
        p.act(sq[0][:, 0:256], MUF[:, k * 256:(k + 1) * 256], AF.Square)
        p.mm_more(psA[0:1, 0:256], ones_col[:, 0:1], sq[0][:, 0:256],
                  start=(k == 0), stop=(k == 7))
    p.copy(mrow[0:1, 0:256], psA[0:1, 0:256])
    p.act(mrow[0:1, 0:256], mrow[0:1, 0:256], AF.Sqrt)
    p.ts(mrow[0:1, 0:256], mrow[0:1, 0:256], 1e-12, None, ALU.max)
    p.recip(row_mr[0:1, 0:256], mrow[0:1, 0:256])
    p.mm(PR[:, 0:256], ones_row[0:1, :], row_mr[0:1, 0:256])
    for k in range(8):
        p.tt(MUU[:, k * 256:(k + 1) * 256], MUF[:, k * 256:(k + 1) * 256],
             PR[:, 0:256], ALU.mult)

    for hf in range(2):
        xa = hf * HCH
        for k in range(8):
            p.dma("sync", XN2[:, k * HCH:(k + 1) * HCH],
                  din["xn2T"][k * 128:(k + 1) * 128, xa:xa + HCH])
            p.dma("sync", XB[:, k * HCH:(k + 1) * HCH],
                  din["o3T"][k * 128:(k + 1) * 128, xa:xa + HCH])
        loadW(p, WA, din["owT"])
        proj(p, WA, XB, SS, [P0, P1], 8, 8, HCH, bias=gc["obc"])
        p.tt(SS[:, :], SS[:, :], XN2[:, :], ALU.add)
        for k in range(8):
            p.dma("sync", XB[:, k * HCH:(k + 1) * HCH],
                  din["xqT"][k * 128:(k + 1) * 128, xa:xa + HCH])
        for tt_ in range(HCH // 128):
            tb = tt_ * 128
            for gu in range(2):
                wname = "wgq" if gu == 0 else "wuq"
                for hs in range(6):
                    ha = hs * 512
                    hw = min(512, 2816 - ha)
                    WG = WG_[hs % 2]
                    p.dma("gpsimd",
                          WG[:, 0:8 * hw].rearrange("p (k n) -> p k n", n=hw),
                          din[wname][:, ha:ha + hw]
                          .rearrange("(k p) n -> p k n", p=128))
                    ps = [P0, P1][gu]
                    p.mm(ps[:, 0:hw],
                         [XB[:, k * HCH + tb: k * HCH + tb + 128]
                          for k in range(8)],
                         [WG[:, k * hw:(k + 1) * hw] for k in range(8)])
                    if gu == 0:
                        p.act(QT[:, ha:ha + hw], ps[:, 0:hw], AF.Silu)
                    else:
                        p.tt(QT[:, ha:ha + hw], QT[:, ha:ha + hw],
                             ps[:, 0:hw], ALU.mult)
            fp4_qd_generic(p, QT, QS, QR, QC1, QC2, None, 128, 2816,
                           out_f32=True)
            for k in range(22):
                p.transpose(P2[:, 0:128], QC2[:, k * 128:(k + 1) * 128],
                            idf[:, :])
                p.copy(HQ[:, k * 512 + tt_ % 4 * 128:
                          k * 512 + tt_ % 4 * 128 + 128], P2[:, 0:128])
            if tt_ % 4 == 3:
                ta = (tt_ - 3) * 128
                for d in range(8):
                    p.dma("gpsimd",
                          WA[:, 0:22 * 128]
                          .rearrange("p (k n) -> p k n", n=128),
                          din["wdqT"][:, d * 128:(d + 1) * 128]
                          .rearrange("(k p) n -> p k n", p=128))
                    ps = [P2, P3][d % 2]
                    p.mm(ps[:, 0:512],
                         [WA[:, h * 128:(h + 1) * 128] for h in range(22)],
                         [HQ[:, h * 512:(h + 1) * 512] for h in range(22)])
                    p.copy(FF[:, d * HCH + ta: d * HCH + ta + 512],
                           ps[:, 0:512])
        p.act(FF[:, :], FF[:, :], AF.Silu)
        p.tt(FF[:, :], FF[:, :], SS[:, :], ALU.mult)
        for k in range(8):
            p.dma("sync", XN2[:, k * HCH:(k + 1) * HCH],
                  din["x2T"][k * 128:(k + 1) * 128, xa:xa + HCH])
        p.tt(XN2[:, :], XN2[:, :], FF[:, :], ALU.add)
        for k in range(8):
            p.dma("sync", dout["x3T"][k * 128:(k + 1) * 128, xa:xa + HCH],
                  XN2[:, k * HCH:(k + 1) * HCH])
        feat_stats(p, XN2, 8, HCH, ones_col, sq, psA, psB, row_sum, row_ssq)
        ln_rows(p, HCH, row_sum, row_ssq, row_rs, row_mr)
        apply_ln(p, XN2, XN2, 8, HCH, ones_row, row_rs, row_mr, P0, P1)
        for k in range(8):
            p.dma("sync", dout["xn3T"][k * 128:(k + 1) * 128, xa:xa + HCH],
                  XN2[:, k * HCH:(k + 1) * HCH])
        XN3 = XN2
        p.copy(XB[:, :], XN3[:, :], eng="scalar")
        loadW(p, WA, din["wjT"])
        proj(p, WA, XB, FF, [P0, P1], 8, 8, HCH)
        p.copy(XUB[:, :], FF[:, :], eng="scalar")
        loadW(p, WA, din["wjN"])
        proj(p, WA, XUB, FF, [P2, P3], 8, 8, HCH)
        p.tt(FF[:, :], FF[:, :], XN3[:, :], ALU.mult)
        for (a, b) in col_splits(HCH, 512):
            w = b - a
            p.mm(psA[0:1, 0:w], [ones_col[:, 0:1]] * 8,
                 [FF[:, k * HCH + a: k * HCH + b] for k in range(8)])
            p.ts(row_sum[0:1, a:b], psA[0:1, 0:w], 1.0 / 32.0, None, ALU.mult)
        p.dma("sync", dout["sc4"][0:1, xa:xa + HCH], row_sum[0:1, 0:HCH])
        loadW(p, WA, din["wjvT"])
        proj(p, WA, XB, XUB, [P0, P1], 8, 8, HCH)
        for k in range(8):
            p.dma("sync", dout["valT"][k * 128:(k + 1) * 128, xa:xa + HCH],
                  XUB[:, k * HCH:(k + 1) * HCH])
        feat_stats(p, XN3, 8, HCH, ones_col, sq, psA, psB, row_sum, row_ssq)
        ln_rows(p, HCH, row_sum, row_ssq, row_rs, row_mr)
        apply_ln(p, XN3, FF, 8, HCH, ones_row, row_rs, row_mr, P0, P1,
                 gain_cols=gc["gamc"], bias_cols=gc["betc"])
        p.copy(XB[:, :], FF[:, :], eng="scalar")
        for k in range(8):
            p.dma("sync", dout["xninT"][k * 128:(k + 1) * 128, xa:xa + HCH],
                  XB[:, k * HCH:(k + 1) * HCH])
        for (a, b) in col_splits(HCH, 512):
            w = b - a
            for k in range(8):
                p.act(sq[0][:, 0:w], FF[:, k * HCH + a: k * HCH + b],
                      AF.Square)
                p.mm_more(psA[0:1, 0:w], ones_col[:, 0:1], sq[0][:, 0:w],
                          start=(k == 0), stop=(k == 7))
            p.copy(row_sum[0:1, a:b], psA[0:1, 0:w])
        p.act(row_sum[0:1, 0:HCH], row_sum[0:1, 0:HCH], AF.Sqrt)
        p.ts(row_sum[0:1, 0:HCH], row_sum[0:1, 0:HCH], 1e-12, None, ALU.max)
        p.recip(row_rs[0:1, 0:HCH], row_sum[0:1, 0:HCH])
        p.ts(row_rs[0:1, 0:HCH], row_rs[0:1, 0:HCH], scl_s[0:1, 0:1], None,
             ALU.mult)
        apply_ln(p, FF, FF, 8, HCH, ones_row, row_rs, None, P2, P3)
        p.copy(XUB[:, :], FF[:, :], eng="scalar")
        for i in range(HCH // 128):
            tbase = i * 128
            p.mm(PR[:, 0:256],
                 [XUB[:, k * HCH + tbase: k * HCH + tbase + 128]
                  for k in range(8)],
                 [MUU[:, k * 256:(k + 1) * 256] for k in range(8)])
            p.copy(T1[:, 0:256], PR[:, 0:256])
            p.dma("sync", dout["lgT"][xa + tbase: xa + tbase + 128, :],
                  T1[:, 0:256])
    return p.finalize()


def host_l8(inputs, state):
    nc = get_nc("l8", build_l8)
    wgq = state["wgq"].astype(np.float32)
    wuq = state["wuq"].astype(np.float32)
    wdq = state["wdq"].astype(np.float32)
    com = dict(
        owT=_t(inputs["L2_attn_o_w"]),
        obc=_f(inputs["L2_attn_o_b"]).reshape(8, 128).T.copy(),
        wgq=np.ascontiguousarray(wgq.T).astype(BF),
        wuq=np.ascontiguousarray(wuq.T).astype(BF),
        wdqT=np.ascontiguousarray(wdq.T).astype(BF),
        wjT=_t(inputs["L3_attn_wj_w"]), wjN=_f(inputs["L3_attn_wj_w"]),
        wjvT=_t(inputs["L3_attn_wjv_w"]),
        gamc=_f(inputs["L3_ffn_ln_gamma"]).reshape(8, 128).T.copy(),
        betc=_f(inputs["L3_ffn_ln_beta"]).reshape(8, 128).T.copy(),
        muN=np.ascontiguousarray(
            np.asarray(inputs["L3_ffn_mu"], np.float32).reshape(D, 256)),
        scl=_f(inputs["L3_ffn_scale"]).reshape(1, 1),
        idf32=np.eye(128, dtype=np.float32),
    )
    in_maps = []
    for c in range(NCORES):
        m = dict(com)
        for nm in ["xn2T", "x2T"]:
            m[nm] = np.ascontiguousarray(state[nm][:, c * TOK:(c + 1) * TOK])
        for nm in ["o3T", "xqT"]:
            m[nm] = np.ascontiguousarray(
                state[nm][:, c * TOK:(c + 1) * TOK]).astype(BF)
        in_maps.append(m)
    res = run8(nc, in_maps)
    for nm in ["x3T", "xn3T", "valT", "xninT"]:
        state[nm] = np.concatenate(
            [np.asarray(res[c][nm], np.float32) for c in range(NCORES)], 1)
    state["sc4"] = np.concatenate([res[c]["sc4"] for c in range(NCORES)], 1)
    state["lgT"] = np.concatenate([res[c]["lgT"] for c in range(NCORES)], 0)
    return state




def build_l9():
    p = Prog("l9")
    p.inp("scR", [1, S], F32)
    p.inp("scP", [128, 64], F32)
    p.inp("valS", [256, S], BF16)
    p.inp("lgS", [64, S], F32)
    p.inp("xnintok", [S, D], BF16)
    p.inp("w1S", [4 * 8 * 128, 256], F32)
    p.inp("w2S", [4 * 2 * 128, 1024], F32)
    p.inp("b1S", [128, 4 * 2], F32)
    p.inp("b2S", [128, 4 * 8], F32)
    for v in range(4):
        p.inp("MK%d" % v, [128, 512], F32)
    p.inp("idf32", [128, 128], F32)
    p.out("o4_o", [256, S], BF16)
    p.out("soT", [D, 64], F32)
    din, dout = p.din, p.dout

    ones_row = p.sb("ones_row", [1, 128], F32)
    ones_bf = p.sb("ones_bf", [128, 1], BF16)
    p.memset(ones_row[:, :], 1.0)
    p.memset(ones_bf[:, :], 1.0)
    idf = p.sb("idf", [128, 128], F32)
    p.dma("sync", idf[:, :], din["idf32"])
    idb = p.sb("idb", [128, 128], BF16)
    p.copy(idb[:, :], idf[:, :])
    MK = [p.sb("sMK%d" % v, [128, 512], F32) for v in range(4)]
    for v in range(4):
        p.dma("sync", MK[v][:, :], din["MK%d" % v])
    SCR = p.sb("SCR", [1, S], F32)
    PMX = p.sb("PMX", [1, 2048], F32)
    PMN = p.sb("PMN", [1, 2048], F32)
    RMX = p.sb("RMX", [1, S], F32)
    CARY = p.sb("CARY", [1, 4], F32)
    SCP = p.sb("SCP", [128, 64], F32)
    VH2 = p.sb("VH2", [128, S], BF16)
    VT2 = p.sb("VT2", [128, 64 * 128], BF16)
    SQB = p.sb("SQB", [128, 512], F32)
    MBX = p.sb("MBX", [128, 512], F32)
    ET = p.sb("ET", [128, 512], BF16)
    T1 = p.sb("T1", [128, 512], F32)
    OF = T1
    DR = p.sb("DR", [1, 512], F32)
    OST2 = p.sb("OST2", [128, S], BF16)
    PP = [p.ps("PP%d" % i) for i in range(8)]
    PPB5 = PP[5][:, 0:256].bitcast(BF16)

    p.dma("sync", SCR[0:1, :], din["scR"])
    p.dma("sync", SCP[:, :], din["scP"])
    p.memset(CARY[0:1, :], -3e38)
    for pc_ in range(S // 2048):
        a = pc_ * 2048
        sl = slice(a, a + 2048)
        p.scan(PMX[0:1, :], SCR[0:1, sl], SCR[0:1, sl], CARY[0:1, 0:1],
               ALU.max, ALU.bypass)
        p.copy(CARY[0:1, 0:1], PMX[0:1, 2047:2048])
        p.ts(RMX[0:1, sl], SCR[0:1, sl], -1.0, None, ALU.mult)
        p.scan(PMN[0:1, :], RMX[0:1, sl], RMX[0:1, sl], CARY[0:1, 1:2],
               ALU.max, ALU.bypass)
        p.copy(CARY[0:1, 1:2], PMN[0:1, 2047:2048])
        p.ts(PMN[0:1, :], PMN[0:1, :], -1.0, None, ALU.mult)
        p.tt(PMX[0:1, :], SCR[0:1, sl], PMX[0:1, :], ALU.mult)
        p.tt(PMN[0:1, :], SCR[0:1, sl], PMN[0:1, :], ALU.mult)
        p.tt(PMX[0:1, :], PMX[0:1, :], PMN[0:1, :], ALU.max)
        p.ts(RMX[0:1, sl], PMX[0:1, :], -1.0, None, ALU.mult)

    for half in range(2):
        r0 = half * 128
        p.dma("sync", VH2[:, :], din["valS"][r0:r0 + 128, :])
        for c in range(64):
            p.transpose(PPB5[0:128, 0:128], VH2[:, c * 128:(c + 1) * 128],
                        idb[:, :])
            p.copy(VT2[:, c * 128:(c + 1) * 128], PPB5[0:128, 0:128])
        for qb in range(NQB):
            Q0b = qb * 512
            p.mm(PP[3][:, 0:512], ones_row[0:1, :], SCR[0:1, Q0b:Q0b + 512])
            p.copy(SQB[:, :], PP[3][:, 0:512])
            p.mm(PP[3][:, 0:512], ones_row[0:1, :], RMX[0:1, Q0b:Q0b + 512])
            p.copy(MBX[:, :], PP[3][:, 0:512])
            nk = (Q0b + 512) // 128
            first = True
            for ck in range(nk):
                K0 = ck * 128
                p.ts(T1[:, 0:512], SQB[:, :], SCP[:, ck:ck + 1], None,
                     ALU.mult)
                p.tt(T1[:, 0:512], T1[:, 0:512], MBX[:, :], ALU.add)
                p.act(ET[:, :], T1[:, 0:512], AF.Exp)
                u = (K0 - Q0b) // 128
                if 0 <= u <= 3:
                    p.tt(ET[:, :], ET[:, :], MK[u][:, :], ALU.mult)
                p.mm_more(PP[6][:, 0:512], VT2[:, ck * 128:(ck + 1) * 128],
                          ET[:, :], start=first, stop=(ck == nk - 1))
                p.mm_more(PP[7][0:1, 0:512], ones_bf[:, 0:1], ET[:, :],
                          start=first, stop=(ck == nk - 1))
                first = False
            p.recip(DR[0:1, :], PP[7][0:1, 0:512])
            p.mm(PP[3][:, 0:512], ones_row[0:1, :], DR[0:1, 0:512])
            p.copy(OF[:, :], PP[6][:, 0:512])
            p.tt(OST2[:, Q0b:Q0b + 512], OF[:, :], PP[3][:, 0:512], ALU.mult)
        p.dma("sync", dout["o4_o"][r0:r0 + 128, :], OST2[:, :])

    ETok = VT2[:, 0:64 * 64]
    LG = p.sb("LG", [64, S], F32)
    EXL = p.sb("EXL", [64, S], BF16)
    MC = p.sb("MC", [64, 1], F32)
    DEN = p.sb("DEN", [64, 4], F32)
    DRW = p.sb("DRW", [1, 64], F32)

    XT = p.sb("XT", [128, D], BF16)
    SIf = p.sb("SIf", [128, 8 * 64], F32)
    H1b = p.sb("H1b", [128, 2 * 16], BF16)
    W1Bs = p.sb("W1Bs", [128, 256], BF16)
    W2Bs = p.sb("W2Bs", [128, 128], BF16)
    SO = p.sb("SO", [128, 8 * 64], F32)
    b1c = p.sb("b1c", [128, 4 * 2], F32)
    b2c = p.sb("b2c", [128, 4 * 8], F32)
    p.dma("sync", b1c[:, :], din["b1S"])
    p.dma("sync", b2c[:, :], din["b2S"])
    p.dma("sync", LG[:, :], din["lgS"])
    p.reduce(MC[:, 0:1], LG[:, :], ALU.max, negate=True)
    p.act(EXL[:, :], LG[:, :], AF.Exp, bias=MC[:, 0:1],
          accum_out=DEN[:, 0:1])
    p.copy(DEN[:, 1:2], DEN[:, 0:1])
    p.copy(DEN[:, 2:3], DEN[:, 0:1])
    p.copy(DEN[:, 3:4], DEN[:, 0:1])
    p.transpose(PP[2][0:4, 0:64], DEN[0:64, 0:4], idf[0:64, 0:64])
    p.copy(T1[0:4, 0:64], PP[2][0:4, 0:64])
    p.recip(DRW[0:1, 0:64], T1[0:1, 0:64])
    for c in range(64):
        p.transpose(PPB5[0:128, 0:64], EXL[:, c * 128:(c + 1) * 128],
                    idb[0:64, 0:64])
        p.copy(ETok[:, c * 64:(c + 1) * 64], PPB5[0:128, 0:64])
    for tc in range(64):
        p.dma("sync", XT[:, :], din["xnintok"][tc * 128:(tc + 1) * 128, :])
        for d in range(8):
            p.mm_more(PP[6][:, d * 64:(d + 1) * 64],
                      XT[:, d * 128:(d + 1) * 128],
                      ETok[:, tc * 64:(tc + 1) * 64],
                      start=(tc == 0), stop=(tc == 63))
    p.copy(SIf[:, :], PP[6][:, 0:512])
    p.mm(PP[3][:, 0:64], ones_row[0:1, :], DRW[0:1, 0:64])
    p.copy(T1[:, 0:64], PP[3][:, 0:64])
    for d in range(8):
        p.tt(SIf[:, d * 64:(d + 1) * 64], SIf[:, d * 64:(d + 1) * 64],
             T1[:, 0:64], ALU.mult)
    SIb = p.sb("SIb", [128, 8 * 64], BF16)
    p.copy(SIb[:, :], SIf[:, :], eng="scalar")
    for e in range(4):
        for hc in range(2):
            for k in range(8):
                r0_ = (e * 8 + k) * 128
                p.dma("gpsimd", W1Bs[:, hc * 128:(hc + 1) * 128],
                      din["w1S"][r0_:r0_ + 128, hc * 128:(hc + 1) * 128])
                p.mm_more(PP[0][:, 0:16],
                          W1Bs[:, hc * 128:(hc + 1) * 128],
                          SIb[:, k * 64 + e * 16: k * 64 + e * 16 + 16],
                          start=(k == 0), stop=(k == 7))
            p.act(H1b[:, hc * 16:(hc + 1) * 16], PP[0][:, 0:16], AF.Gelu,
                  bias=b1c[:, e * 2 + hc: e * 2 + hc + 1])
        for d in range(8):
            for hc in range(2):
                r0_ = (e * 2 + hc) * 128
                p.dma("gpsimd", W2Bs[:, 0:128],
                      din["w2S"][r0_:r0_ + 128, d * 128:(d + 1) * 128])
                p.mm_more(PP[1][:, 0:16], W2Bs[:, 0:128],
                          H1b[:, hc * 16:(hc + 1) * 16],
                          start=(hc == 0), stop=(hc == 1))
            p.act(SO[:, d * 64 + e * 16: d * 64 + e * 16 + 16],
                  PP[1][:, 0:16], AF.Identity,
                  bias=b2c[:, e * 8 + d: e * 8 + d + 1])
    for d in range(8):
        p.dma("sync", dout["soT"][d * 128:(d + 1) * 128, :],
              SO[:, d * 64:(d + 1) * 64])
    return p.finalize()


def host_l9(inputs, state):
    nc = get_nc("l9", build_l9)
    w1 = np.asarray(inputs["L3_ffn_w1"], np.float32)
    w2 = np.asarray(inputs["L3_ffn_w2"], np.float32)
    b1 = np.asarray(inputs["L3_ffn_b1"], np.float32)
    b2 = np.asarray(inputs["L3_ffn_b2"], np.float32)
    MKs = []
    for v in range(4):
        MKs.append((v * 128 + np.arange(128)[:, None]
                    <= np.arange(512)[None, :]).astype(np.float32))
    in_maps = []
    for c in range(NCORES):
        b, u = c // 4, c % 4
        cols = slice(b * S, (b + 1) * S)
        es = slice(u * 4, u * 4 + 4)
        m = dict(idf32=np.eye(128, dtype=np.float32))
        for v in range(4):
            m["MK%d" % v] = MKs[v]
        sc = state["sc4"][0, cols]
        m["scR"] = np.ascontiguousarray(sc.reshape(1, S))
        m["scP"] = np.ascontiguousarray(sc.reshape(64, 128).T)
        m["valS"] = np.ascontiguousarray(
            state["valT"][u * 256:(u + 1) * 256, cols]).astype(BF)
        m["lgS"] = np.ascontiguousarray(
            state["lgT"][b * S:(b + 1) * S, u * 64:(u + 1) * 64].T)
        m["xnintok"] = np.ascontiguousarray(
            state["xninT"][:, cols].T).astype(BF)
        m["w1S"] = np.ascontiguousarray(
            w1[es].reshape(4, 8, 128, 256).reshape(4 * 8 * 128, 256))
        m["w2S"] = np.ascontiguousarray(
            w2[es].reshape(4, 2, 128, 1024).reshape(4 * 2 * 128, 1024))
        b1S = np.zeros((128, 8), np.float32)
        b2S = np.zeros((128, 32), np.float32)
        for e in range(4):
            b1S[:, e * 2:(e + 1) * 2] = b1[u * 4 + e].reshape(2, 128).T
            b2S[:, e * 8:(e + 1) * 8] = b2[u * 4 + e].reshape(8, 128).T
        m["b1S"] = b1S
        m["b2S"] = b2S
        in_maps.append(m)
    res = run8(nc, in_maps)
    o4 = np.zeros((NB, D, S), np.float32)
    so = np.zeros((NB, D, 256), np.float32)
    for c in range(NCORES):
        b, u = c // 4, c % 4
        o4[b, u * 256:(u + 1) * 256] = np.asarray(res[c]["o4_o"], np.float32)
        so[b, :, u * 64:(u + 1) * 64] = res[c]["soT"]
    state["o4T"] = np.ascontiguousarray(np.concatenate([o4[0], o4[1]], 1))
    state["soA"] = so
    return state



def build_l10():
    p = Prog("l10")
    for name, shape, dt in [
        ("lgTok", [TOK, 256], F32), ("soTA", [256, D], F32),
        ("o4T", [D, TOK], BF16), ("woT4", [D, D], F32),
        ("x3T", [D, TOK], F32),
    ]:
        p.inp(name, shape, dt)
    p.out("x4T", [D, TOK], F32)
    din, dout = p.din, p.dout
    p.inp("idf32", [128, 128], F32)
    idf = p.sb("idf", [128, 128], F32)
    p.dma("sync", idf[:, :], din["idf32"])

    ones_row = p.sb("ones_row", [1, 128], F32)
    p.memset(ones_row[:, :], 1.0)
    HCH = 1024
    LGt = p.sb("LGt", [128, 256], F32)
    MC = p.sb("MCc", [128, 1], F32)
    SMc = p.sb("SMc", [128, 1], F32)
    RC = p.sb("RCc", [128, 1], F32)
    CMB = p.sb("CMB", [128, 256], F32)
    CT = p.sb("CT", [128, 2 * HCH], BF16)
    SOT = p.sb("SOT", [128, 2 * D], BF16)
    XB4 = p.sb("XB4", [128, 8 * HCH], BF16)
    FF4 = p.sb("FF4", [128, 8 * HCH], F32)
    SS4 = p.sb("SS4", [128, 8 * HCH], F32)
    WA = p.sb("WA", [128, 8 * D], BF16)
    PP = [p.ps("PP%d" % i) for i in range(6)]

    for k in range(2):
        p.dma("gpsimd", SOT[:, k * D:(k + 1) * D],
              din["soTA"][k * 128:(k + 1) * 128, :])
    for k in range(8):
        p.dma("gpsimd", WA[:, k * D:(k + 1) * D],
              din["woT4"][k * 128:(k + 1) * 128, :])
    for hf in range(2):
        xa = hf * HCH
        for i in range(HCH // 128):
            tb = i * 128
            p.dma("sync", LGt[:, :], din["lgTok"][xa + tb:xa + tb + 128, :])
            p.reduce(MC[:, 0:1], LGt[:, :], ALU.max, negate=True)
            p.act(CMB[:, :], LGt[:, :], AF.Exp, bias=MC[:, 0:1],
                  accum_out=SMc[:, 0:1])
            p.recip(RC[:, 0:1], SMc[:, 0:1])
            p.ts(CMB[:, :], CMB[:, :], RC[:, 0:1], None, ALU.mult)
            for ec in range(2):
                p.transpose(PP[4][:, 0:128], CMB[:, ec * 128:(ec + 1) * 128],
                            idf[:, :])
                p.copy(CT[:, ec * HCH + tb: ec * HCH + tb + 128],
                       PP[4][:, 0:128])
        for o in range(8):
            for (a, b) in col_splits(HCH, 512):
                w = b - a
                ps = PP[o % 2]
                p.mm(ps[:, 0:w],
                     [SOT[:, k * D + o * 128: k * D + o * 128 + 128]
                      for k in range(2)],
                     [CT[:, k * HCH + a: k * HCH + b] for k in range(2)])
                p.act(FF4[:, o * HCH + a: o * HCH + b], ps[:, 0:w], AF.Silu)
        for k in range(8):
            p.dma("sync", XB4[:, k * HCH:(k + 1) * HCH],
                  din["o4T"][k * 128:(k + 1) * 128, xa:xa + HCH])
        proj(p, WA, XB4, SS4, [PP[2], PP[3]], 8, 8, HCH)
        p.tt(FF4[:, :], FF4[:, :], SS4[:, :], ALU.mult)
        for k in range(8):
            p.dma("sync", SS4[:, k * HCH:(k + 1) * HCH],
                  din["x3T"][k * 128:(k + 1) * 128, xa:xa + HCH])
        p.tt(SS4[:, :], SS4[:, :], FF4[:, :], ALU.add)
        for k in range(8):
            p.dma("sync", dout["x4T"][k * 128:(k + 1) * 128, xa:xa + HCH],
                  SS4[:, k * HCH:(k + 1) * HCH])
    return p.finalize()


def host_l10(inputs, state):
    nc = get_nc("l10", build_l10)
    in_maps = []
    for c in range(NCORES):
        b = c // 4
        m = dict(
            lgTok=np.ascontiguousarray(state["lgT"][c * TOK:(c + 1) * TOK]),
            soTA=np.ascontiguousarray(state["soA"][b].T),
            o4T=np.ascontiguousarray(
                state["o4T"][:, c * TOK:(c + 1) * TOK]).astype(BF),
            woT4=_t(inputs["L3_attn_wo_w"]),
            x3T=np.ascontiguousarray(state["x3T"][:, c * TOK:(c + 1) * TOK]),
            idf32=np.eye(128, dtype=np.float32),
        )
        in_maps.append(m)
    res = run8(nc, in_maps)
    x4 = np.concatenate([res[c]["x4T"] for c in range(NCORES)], 1)
    state["x4T"] = x4
    return state



def kernel(**inputs):
    state = {}
    host_l1(inputs, state)
    host_l2(inputs, state)
    host_l3(inputs, state)
    host_l4(inputs, state)
    host_l5(inputs, state)
    host_l6a(inputs, state)
    host_l6b(inputs, state)
    host_l7(inputs, state)
    host_l8(inputs, state)
    host_l9(inputs, state)
    host_l10(inputs, state)
    x4 = state["x4T"]
    return np.ascontiguousarray(
        x4.reshape(D, NB, S).transpose(1, 2, 0)).astype(np.float32)
